# version 07
# baseline (speedup 1.0000x reference)
import contextlib

import numpy as np
import ml_dtypes
import concourse.bass as bass
import concourse.mybir as mybir
from concourse.bass_utils import run_bass_kernel_spmd

F32 = mybir.dt.float32
BF16 = mybir.dt.bfloat16
AF = mybir.ActivationFunctionType
OP = mybir.AluOpType
BF = ml_dtypes.bfloat16

GELU_C = 0.044715
GELU_S = 2.0 * 0.7978845608028654
EPS = 1e-5
NEG = -1e30


class CFG:
    def __init__(self, L=4096, NB=2, T=512, n_cores=8, bisect_iters=52,
                 debug=None):
        self.L, self.NB, self.T, self.C = L, NB, T, 128
        self.D, self.EI, self.CV, self.DIP = 1024, 2048, 2176, 4240
        self.NH, self.HD, self.NS, self.FF = 16, 128, 64, 4096
        self.n_cores = n_cores
        self.bisect_iters = bisect_iters
        self.debug = debug
        assert L % T == 0 and T % self.C == 0


ENGINES = ("tensor", "vector", "scalar", "gpsimd", "sync")
DSEMS = ("dxb", "dwi0", "dwi1", "dwi2", "dwo0", "dwo1", "dw2i0", "dw2i1", "dw2o0", "dw2o1", "dx1b",
         "dxl3", "dy0t", "dp0r", "dw20", "dw21", "dwst", "dcn0", "dcn1", "dc20", "dc21")
SEMS = ("T", "V", "S", "G", "DS", "DA", "DG", "DC") + DSEMS
ENG_SEM = {"tensor": "T", "vector": "V", "scalar": "S", "gpsimd": "G"}
DMA_SEM = {"sync": "DS", "scalar": "DA", "gpsimd": "DG"}


class Prog:

    def __init__(self, nc):
        self.nc = nc
        self.ops = {e: [] for e in ENGINES}
        self.cnt = {s: 0 for s in SEMS}
        self.hiw = {e: {s: 0 for s in SEMS} for e in ENGINES}
        self.last_w = {}
        self.readers = {}
        self.sems = {}
        self.pending_rd = {e: [] for e in ENGINES}

    def make_sems(self, es):
        for s in SEMS:
            self.sems[s] = es.enter_context(self.nc.semaphore("sem_" + s))

    def patch_total(self, keys, sem):
        tok = (sem, self.cnt[sem])
        for k in keys:
            self.last_w[k] = tok
            self.readers[k] = []

    def emit(self, eng, fn, *args, rd=(), wr=(), after=(), inc=True, dma=False,
             collective=False, dsem=None, **kw):
        toks = [t for t in after if t is not None]
        for k in rd:
            t = self.last_w.get(k)
            if t is not None:
                toks.append(t)
        for k in wr:
            t = self.last_w.get(k)
            if t is not None:
                toks.append(t)
            toks.extend(self.readers.get(k, ()))
        waits = []
        own = ENG_SEM.get(eng)
        for (sem, val) in toks:
            if sem == own and eng == "tensor":
                continue
            if val > self.hiw[eng][sem]:
                self.hiw[eng][sem] = val
                waits.append((sem, val))
        if dma:
            sem = dsem or ("DC" if collective else DMA_SEM[eng])
            amt = 16
        else:
            sem = ENG_SEM[eng]
            amt = 1
        if inc:
            self.cnt[sem] += amt
        tok = (sem, self.cnt[sem])
        self.ops[eng].append((fn, args, kw, waits, (sem, amt) if inc else None))
        if inc:
            for k in rd:
                self.readers.setdefault(k, []).append(tok)
            for k in self.pending_rd[eng]:
                self.readers.setdefault(k, []).append(tok)
            self.pending_rd[eng] = []
        else:
            self.pending_rd[eng].extend(rd)
        for k in wr:
            for e2 in ENGINES:
                assert k not in self.pending_rd[e2], \
                    f"write to {k} while {e2} has un-inc'd pending read"
            self.last_w[k] = tok
            self.readers[k] = []
        return tok

    def barrier(self):
        snap = dict(self.cnt)
        for e in ENGINES:
            for s in SEMS:
                v = snap[s]
                if s != ENG_SEM.get(e) and v > self.hiw[e][s]:
                    self.hiw[e][s] = v
                    self.ops[e].append((None, None, None, [(s, v)], None))

    def wait_all(self, eng):
        for s in SEMS:
            v = self.cnt[s]
            if v > self.hiw[eng][s]:
                self.hiw[eng][s] = v
                self.ops[eng].append((None, None, None, [(s, v)], None))

    def replay(self, block):
        prog = self

        def section(name):
            def run(eng):
                for (fn, args, kw, waits, inc) in prog.ops[name]:
                    for (sem, val) in waits:
                        eng.wait_ge(prog.sems[sem], val)
                    if fn is None:
                        continue
                    inst = getattr(eng, fn)(*args, **kw)
                    if inc is not None:
                        inst.then_inc(prog.sems[inc[0]], inc[1])
            return run

        block.tensor(section("tensor"))
        block.vector(section("vector"))
        block.scalar(section("scalar"))
        block.gpsimd(section("gpsimd"))
        block.sync(section("sync"))


class Arena:
    def __init__(self, handle, words):
        self.h = handle
        self.words = words
        self.off = 0
        self.marks = []

    def _take(self, n_words):
        n_words = (n_words + 1) & ~1
        assert self.off + n_words <= self.words, \
            f"arena overflow: {self.off}+{n_words} > {self.words}"
        o = self.off
        self.off += n_words
        return o

    def f32(self, parts, n):
        o = self._take(n)
        return self.h[0:parts, o:o + n]

    def bf16(self, parts, n):
        o = self._take((n + 1) // 2)
        return self.h[:].bitcast(BF16)[0:parts, 2 * o:2 * o + n]

    def push(self):
        self.marks.append(self.off)

    def pop(self):
        self.off = self.marks.pop()


def build_program(cfg: CFG):
    nc = bass.Bass("TRN2", target_bir_lowering=False)
    L, NB, T, C = cfg.L, cfg.NB, cfg.T, cfg.C
    D, EI, CV, DIP = cfg.D, cfg.EI, cfg.CV, cfg.DIP
    NH, HD, NS, FF = cfg.NH, cfg.HD, cfg.NS, cfg.FF
    ND, NE, NCV, NFF = D // 128, EI // 128, CV // 128, FF // 128
    NIP = 35
    NCVT = 18
    NT = L // T
    TOK = NB * L
    NTT = TOK // 512
    NCH = T // C

    def din(name, shape, dt=F32):
        return nc.dram_tensor(name, list(shape), dt, kind="ExternalInput").ap()

    def dint(name, shape, dt=F32):
        return nc.dram_tensor(name, list(shape), dt, kind="Internal").ap()

    x_fm = din("x_fm", [NB, D, L])
    w_in_tiles = din("w_in_tiles", [NIP, ND, 128, 128])
    w_outT = din("w_outT", [EI, D])
    w1 = din("w1", [D, FF])
    w2 = din("w2", [FF, D])
    conv_w4 = din("conv_w4", [128, NCVT * 4])
    conv_b17 = din("conv_b17", [128, NCVT])
    dtb_in = din("dtb", [NH, 1])
    alog_in = din("alog", [NH, 1])
    dprm_row = din("dprm_row", [1, NH])
    normw_in = din("normw", [128, NE])
    b1c_in = din("b1c", [128, NFF])
    b2c_in = din("b2c", [128, ND])
    lgam_in = din("lgam", [128, ND])
    lbet_in = din("lbet", [128, ND])
    wr_in = din("wr_in", [128, (D // 128) * 8])
    br_in = din("br_in", [8, 1])
    identf_in = din("identf_in", [128, 128])
    identb_in = din("identb_in", [128, 128], BF16)
    trineg_in = din("trineg_in", [128, 128])
    e16_in = din("e16_in", [NH, NH * 128])
    kvec_in = din("kvec_in", [1, 8])
    bnds_in = din("bnds_in", [1, 16])

    out_fm = nc.dram_tensor("out_fm", [NB, D, L], F32, kind="ExternalOutput").ap()

    x1h = dint("x1h", [NB, D, L])
    xln3h = dint("xln3h", [NB, D, L], BF16)
    y0h = dint("y0h", [NB, D, L], BF16)
    p0h = dint("p0h", [1, TOK])
    w2bh = dint("w2bh", [FF, D], BF16)
    w_inbh = dint("w_inbh", [NIP, ND, 128, 128], BF16)
    aw1bh = dint("aw1bh", [D, FF], BF16)
    aw2bh = dint("aw2bh", [FF, D], BF16)
    w_outbh = dint("w_outbh", [EI, D], BF16)
    dbgh = dint("dbgh", [1, 64])
    pw2bh = dint("pw2bh", [FF, D], BF16)

    es = contextlib.ExitStack()
    AR_P = Arena(es.enter_context(nc.sbuf_tensor("arP", [128, 4200], F32)), 4200)
    AR_A = Arena(es.enter_context(nc.sbuf_tensor("arA", [128, 48700], F32)), 48700)

    p = Prog(nc)
    p.make_sems(es)
    pb = [es.enter_context(nc.psum_tensor(f"pb{i}", [128, 512], F32))
          for i in range(8)]

    def pbank(i, parts=128, n=512):
        return pb[i][0:parts, 0:n]

    def pbank_b(i, parts=128, n=1024):
        return pb[i][:].bitcast(BF16)[0:parts, 0:n]

    identf = AR_P.f32(128, 128)
    identb = AR_P.bf16(128, 128)
    trineg = AR_P.f32(128, 128)
    trineg_b = AR_P.bf16(128, 128)
    e16 = AR_P.f32(NH, NH * 128)
    cw4 = AR_P.f32(128, NCVT * 4)
    cb17 = AR_P.f32(128, NCVT)
    dt_bias = AR_P.f32(NH, 1)
    a_log = AR_P.f32(NH, 1)
    d_row = AR_P.f32(1, NH)
    nw = AR_P.f32(128, NE)
    b1c = AR_P.f32(128, NFF)
    b2c = AR_P.f32(128, ND)
    gam = AR_P.f32(128, ND)
    bet = AR_P.f32(128, ND)
    wr_sb = AR_P.f32(128, ND * 8)
    br_sb = AR_P.f32(8, 1)
    kvec = AR_P.f32(1, 8)
    ones1 = AR_P.f32(1, 128)
    ones1b = AR_P.bf16(1, 128)
    onescol = AR_P.f32(128, 1)
    onescol_b = AR_P.bf16(128, 2)[:, 0:1]
    zero16 = AR_P.f32(NH, 1)
    eps_col = AR_P.f32(128, 1)
    one_col = AR_P.f32(128, 1)
    a_col = AR_P.f32(NH, 1)
    d_b = AR_P.f32(128, NH)
    wrg = AR_P.bf16(128, ND * 8)
    cgam = AR_P.f32(8, 1)
    cbet = AR_P.f32(8, 1)
    b1p = AR_P.f32(128, NFF)
    b2p = AR_P.f32(128, ND)
    blo = AR_P.f32(1, 8)
    bhi = AR_P.f32(1, 8)
    bmid = AR_P.f32(1, 8)
    bcols = AR_P.f32(128, 8)
    bst = AR_P.f32(1, 24)
    ab1 = AR_P.f32(128, NFF)
    ab2 = AR_P.f32(128, ND)

    g, s_, v_, t_, sc = "gpsimd", "sync", "vector", "tensor", "scalar"

    def dma(eng, out, in_, rd=(), wr=(), after=(), dsem=None):
        return p.emit(eng, "dma_start", rd=rd, wr=wr, after=after, dma=True,
                      dsem=dsem, out=out, in_=in_)

    def fm3(hbm_ap):
        return hbm_ap.rearrange("(k p) t -> p k t", p=128)

    def sb3(sbuf_ap, t):
        return sbuf_ap.rearrange("p (k t) -> p k t", t=t)

    dma(s_, identf[:], identf_in[:], wr=["identf"])
    dma(s_, trineg[:], trineg_in[:], wr=["trineg"])
    dma(s_, e16[:], e16_in[:], wr=["e16"])
    dma(s_, cw4[:], conv_w4[:], wr=["cw4"])
    dma(s_, cb17[:], conv_b17[:], wr=["cb17"])
    dma(s_, dt_bias[:], dtb_in[:], wr=["dtb"])
    dma(s_, a_log[:], alog_in[:], wr=["alog"])
    dma(s_, d_row[:], dprm_row[:], wr=["drow"])
    dma(s_, nw[:], normw_in[:], wr=["nw"])
    dma(s_, b1c[:], b1c_in[:], wr=["b1c"])
    dma(s_, b2c[:], b2c_in[:], wr=["b2c"])
    dma(s_, gam[:], lgam_in[:], wr=["gam"])
    dma(s_, bet[:], lbet_in[:], wr=["bet"])
    dma(s_, wr_sb[:], wr_in[:], wr=["wr"])
    dma(s_, br_sb[:], br_in[:], wr=["br"])
    dma(s_, kvec[:], kvec_in[:], wr=["kvec"])
    dma(s_, blo[:], bnds_in[:, 0:8], wr=["blo"])
    dma(s_, bhi[:], bnds_in[:, 8:16], wr=["bhi"])
    dma(s_, ab1[:], b1c_in[:], wr=["ab1"])
    dma(s_, ab2[:], b2c_in[:], wr=["ab2"])
    dma(g, identb[:], identb_in[:], wr=["identb"])
    p.patch_total(["identf", "trineg", "e16", "cw4", "cb17", "dtb", "alog",
                   "drow", "nw", "b1c", "b2c", "gam", "bet", "wr", "br",
                   "kvec", "blo", "bhi", "ab1", "ab2"], "DS")
    p.patch_total(["identb"], "DG")

    p.emit(g, "memset", ones1[:], 1.0, wr=["ones1"])
    p.emit(g, "memset", ones1b[:], 1.0, wr=["ones1b"])
    p.emit(g, "memset", onescol[:], 1.0, wr=["onescol"])
    p.emit(g, "memset", zero16[:], 0.0, wr=["zero16"])
    p.emit(g, "memset", bmid[:], 0.0, wr=["bmid"])
    p.emit(g, "memset", eps_col[:], float(EPS), wr=["eps_col"])
    p.emit(g, "memset", one_col[:], 1.0, wr=["one_col"])
    p.emit(v_, "tensor_copy", onescol_b[:], onescol[:], rd=["onescol"],
           wr=["onescolb"])
    p.emit(sc, "activation", a_col[:], a_log[:], AF.Exp, rd=["alog"],
           wr=["a_col"])
    p.emit(v_, "tensor_scalar", a_col[:], a_col[:], -1.0, None, op0=OP.mult,
           rd=["a_col"], wr=["a_col"])
    p.emit(t_, "matmul", pbank(6, 128, NH), ones1[:], d_row[:], start=True,
           stop=True, rd=["ones1", "drow", ("pbk", 6)], wr=[("pbk", 6)])
    p.emit(v_, "tensor_copy", d_b[:], pbank(6, 128, NH), rd=[("pbk", 6)],
           wr=["d_b"])
    p.emit(v_, "tensor_copy", trineg_b[:], trineg[:], rd=["trineg"],
           wr=["trineg_b"])
    p.emit(sc, "activation", ab1[:], ab1[:], AF.Abs,
           rd=["ab1"], wr=["ab1"])
    p.emit(sc, "activation", ab2[:], ab2[:], AF.Abs,
           rd=["ab2"], wr=["ab2"])

    w2stg = [AR_P.bf16(128, 1024) for _ in range(2)]
    mxst = AR_P.f32(128, 3)
    p.emit(g, "memset", mxst[:], 0.0, wr=["mxst"])
    for kf in range(NFF):
        pe = kf % 2
        dma(g, w2stg[pe][:], w2[kf * 128:(kf + 1) * 128, :],
            wr=[("w2stg", pe)], dsem="dw2i%d" % pe)
        dma(s_, w2bh[kf * 128:(kf + 1) * 128, :], w2stg[pe][:],
            rd=[("w2stg", pe)], wr=[("w2bh", kf), ("w2stg", pe)],
            dsem="dw2o%d" % pe)
        p.emit(sc, "activation", w2stg[pe][:], w2stg[pe][:], AF.Abs,
               rd=[("w2stg", pe)], wr=[("w2stg", pe)])
        p.emit(v_, "tensor_reduce", mxst[:, 0:1], w2stg[pe][:],
               axis=mybir.AxisListType.X, op=OP.max, rd=[("w2stg", pe)],
               wr=["mxs0"])
        p.emit(v_, "tensor_tensor", mxst[:, 2:3], mxst[:, 2:3],
               mxst[:, 0:1], op=OP.max, rd=["mxs0", "mxst"], wr=["mxst"])
        dma(s_, aw2bh[kf * 128:(kf + 1) * 128, :], w2stg[pe][:],
            rd=[("w2stg", pe)], wr=[("aw2bh", kf), ("w2stg", pe)],
            dsem="dw2o%d" % pe)
    for kd in range(ND):
        for jj in range(4):
            pe = (kd * 4 + jj) % 2
            dma(g, w2stg[pe][:],
                w1[kd * 128:(kd + 1) * 128, jj * 1024:(jj + 1) * 1024],
                wr=[("w2stg", pe)], dsem="dw2i%d" % pe)
            p.emit(sc, "activation", w2stg[pe][:], w2stg[pe][:], AF.Abs,
                   rd=[("w2stg", pe)], wr=[("w2stg", pe)])
            p.emit(v_, "tensor_reduce", mxst[:, 0:1], w2stg[pe][:],
                   axis=mybir.AxisListType.X, op=OP.max, rd=[("w2stg", pe)],
                   wr=["mxs0"])
            p.emit(v_, "tensor_tensor", mxst[:, 1:2], mxst[:, 1:2],
                   mxst[:, 0:1], op=OP.max, rd=["mxs0", "mxst"], wr=["mxst"])
            dma(s_, aw1bh[kd * 128:(kd + 1) * 128,
                          jj * 1024:(jj + 1) * 1024], w2stg[pe][:],
                rd=[("w2stg", pe)], wr=[("aw1bh", kd, jj), ("w2stg", pe)],
                dsem="dw2o%d" % pe)
    for e in range(NIP):
        pe = e % 2
        dma(g, w2stg[pe][:].rearrange("p (k q) -> p k q", q=128),
            w_in_tiles[e].rearrange("k p q -> p k q"),
            wr=[("w2stg", pe)], dsem="dw2i%d" % pe)
        dma(s_, w_inbh[e].rearrange("k p q -> p k q"),
            w2stg[pe][:].rearrange("p (k q) -> p k q", q=128),
            rd=[("w2stg", pe)], wr=[("wibh", e), ("w2stg", pe)], dsem="dw2o%d" % pe)
    for ke in range(NE):
        pe = ke % 2
        dma(g, w2stg[pe][:], w_outT[ke * 128:(ke + 1) * 128, :],
            wr=[("w2stg", pe)], dsem="dw2i%d" % pe)
        dma(s_, w_outbh[ke * 128:(ke + 1) * 128, :], w2stg[pe][:],
            rd=[("w2stg", pe)], wr=[("wobh", ke), ("w2stg", pe)],
            dsem="dw2o%d" % pe)

    ns = {}

    def ln_full(tag, src_fn, n_tiles, n_feat, tcols, src_bf16=False):
        arow, brow, crow = ns["arow"], ns["brow"], ns["crow"]
        sqp = ns["sq"]
        oc, ock = (onescol_b, "onescolb") if src_bf16 else (onescol, "onescol")
        s1 = pbank(2, 1, tcols)
        s2 = pbank(3, 1, tcols)
        srckeys = ns["srckeys"]
        for i in range(n_tiles):
            p.emit(t_, "matmul", s1, oc[:], src_fn(i),
                   start=(i == 0), stop=(i == n_tiles - 1),
                   rd=[ock, srckeys[i]],
                   wr=[("pbk", 2)] if i in (0, n_tiles - 1) else (),
                   inc=(i == n_tiles - 1))
        for i in range(n_tiles):
            sq_scr = sqp[i % 2]
            p.emit(sc, "activation", sq_scr[:, 0:tcols], src_fn(i), AF.Square,
                   rd=[srckeys[i]], wr=[("sq_scr", i % 2)])
            p.emit(t_, "matmul", s2, onescol_b[:], sq_scr[:, 0:tcols],
                   start=(i == 0), stop=(i == n_tiles - 1),
                   rd=["onescolb", ("sq_scr", i % 2)],
                   wr=[("pbk", 3)] if i in (0, n_tiles - 1) else (), inc=True)
        mu = arow[:, 0:tcols]
        p.emit(v_, "tensor_scalar", mu, s1, 1.0 / n_feat, None, op0=OP.mult,
               rd=[("pbk", 2)], wr=["arow"])
        mu2 = crow[:, 0:tcols]
        p.emit(v_, "tensor_tensor", mu2, mu, mu, op=OP.mult, rd=["arow"],
               wr=["crow"])
        var = brow[:, 0:tcols]
        p.emit(v_, "scalar_tensor_tensor", var, s2, 1.0 / n_feat, mu2,
               op0=OP.mult, op1=OP.subtract, rd=[("pbk", 3), "crow"],
               wr=["brow"])
        p.emit(sc, "activation", var, var, AF.Ln, bias=eps_col[0:1, :],
               rd=["brow", "eps_col"], wr=["brow"])
        rstd_b = brow[:].bitcast(BF16)[0:1, 0:tcols]
        p.emit(sc, "activation", rstd_b, var, AF.Exp, scale=-0.5, rd=["brow"],
               wr=["brow"])
        brw_b = crow[:].bitcast(BF16)[0:1, 0:tcols]
        p.emit(v_, "scalar_tensor_tensor", brw_b, mu, -1.0, rstd_b,
               op0=OP.mult, op1=OP.mult, rd=["arow", "brow"], wr=["crow"])
        bca = pbank(2, 128, tcols)
        bcb = pbank(3, 128, tcols)
        p.emit(t_, "matmul", bca, ones1b[:], rstd_b, start=True, stop=True,
               rd=["ones1b", "brow", ("pbk", 2)], wr=[("pbk", 2)])
        p.emit(t_, "matmul", bcb, ones1b[:], brw_b, start=True, stop=True,
               rd=["ones1b", "crow", ("pbk", 3)], wr=[("pbk", 3)])
        return bca, bcb

    AR_A.push()
    wi_buf = [AR_A.bf16(128, ND * 128) for _ in range(3)]
    d_id = AR_A.bf16(128, NH * 128)
    wo_buf = [AR_A.bf16(128, 1024) for _ in range(2)]
    diag = [AR_A.bf16(128, 4 * 128) for _ in range(2)]
    xb = AR_A.f32(128, ND * T)
    xln1 = AR_A.bf16(128, ND * T)
    z_silu = AR_A.bf16(128, NE * T)
    xbc = AR_A.bf16(128, NCVT * (T + 3))
    y_n = AR_A.bf16(128, NE * T)
    m_o = AR_A.f32(128, ND * T)
    sqA = AR_A.bf16(128, 512)
    halo = AR_A.bf16(128, NCVT * 4)
    dt_sp = AR_A.f32(NH, T)
    s_all = AR_A.f32(NH, T)
    sig_all = AR_A.f32(NH, T)
    esc_c = AR_A.f32(NH, 1)
    es_b = [AR_A.f32(NS, C) for _ in range(2)]
    w_st = AR_A.f32(NH, C)
    sT = AR_A.f32(128, NH)
    sigT = AR_A.f32(128, NH)
    wT = AR_A.f32(128, NH)
    escb = AR_A.f32(NS, NH)
    g_s = AR_A.f32(128, C)
    mask_b = [AR_A.bf16(128, C) for _ in range(2)]
    gmb = [AR_A.bf16(128, C) for _ in range(2)]
    x_tm = [AR_A.bf16(128, C) for _ in range(2)]
    x_til = [AR_A.bf16(128, C) for _ in range(2)]
    b_tm = AR_A.bf16(128, NS)
    c_til = [AR_A.bf16(NS, C) for _ in range(2)]
    ht = AR_A.f32(NS, NH * HD)
    ht_b = [AR_A.bf16(NS, HD) for _ in range(2)]
    y_g = AR_A.bf16(128, NH * C)
    dx = [AR_A.f32(128, C) for _ in range(2)]
    apA = [AR_A.f32(128, T) for _ in range(2)]
    rsc = AR_A.f32(1, 16)
    arowA = AR_A.f32(1, T)
    browA = AR_A.f32(1, T)
    crowA = AR_A.f32(1, T)
    sqA2 = AR_A.bf16(128, 512)
    ns.update(arow=arowA, brow=browA, crow=crowA, sq=(sqA, sqA2))
    for h in range(NH):
        p.emit(v_, "tensor_scalar", d_id[:, h * 128:(h + 1) * 128],
               identb[:], d_b[:, h:h + 1], None, op0=OP.mult,
               rd=["identb", "d_b"], wr=["d_id"])

    for mt in range(NB * NT):
        b, m = mt // NT, mt % NT
        t0 = m * T

        dma(s_, sb3(xb[:], T), fm3(x_fm[b, :, t0:t0 + T]),
            rd=["xb"], wr=["xb"], dsem="dxb")
        if m == 0:
            p.emit(v_, "memset", ht[:], 0.0,
                   rd=[("ht", h) for h in range(NH)],
                   wr=[("ht", h) for h in range(NH)])

        def xsrc(i):
            return xb[:, i * T:(i + 1) * T]

        ns["srckeys"] = ["xb"] * ND
        bca, bcb = ln_full("ln1", xsrc, ND, D, T)
        for i in range(ND):
            ap_ = apA[i % 2]
            p.emit(v_, "tensor_tensor", ap_[:], xsrc(i), bca, op=OP.mult,
                   rd=["xb", ("pbk", 2)], wr=[("apA", i % 2)])
            p.emit(v_, "tensor_tensor", xln1[:, i * T:(i + 1) * T], ap_[:],
                   bcb, op=OP.add, rd=[("apA", i % 2), ("pbk", 3)],
                   wr=[("xln1", i)])

        for e in range(NIP):
            pe = e % 3
            dma(s_, wi_buf[pe][:].rearrange("p (k q) -> p k q", q=128),
                w_inbh[e].rearrange("k p q -> p k q"),
                after=[p.last_w.get(("wibh", e))],
                wr=[("wi", pe)], dsem="dwi%d" % pe)
            rows_e = 128 if e < 32 else (64 if e < 34 else NH)
            ps = pbank(pe, rows_e, T)
            for kd in range(ND):
                p.emit(t_, "matmul", ps,
                       wi_buf[pe][:, kd * 128:kd * 128 + rows_e],
                       xln1[:, kd * T:(kd + 1) * T],
                       start=(kd == 0), stop=(kd == ND - 1),
                       rd=[("wi", pe), ("xln1", kd)],
                       wr=([("pbk", pe)] if kd in (0, ND - 1) else []) +
                          ([("wi", pe)] if kd == ND - 1 else []),
                       inc=(kd == ND - 1))
            if e < NE:
                p.emit(sc, "activation", z_silu[:, e * T:(e + 1) * T], ps,
                       AF.Silu, rd=[("pbk", pe)], wr=[("zs", e)])
            elif e < 34:
                c = e - NE
                p.emit(v_, "tensor_copy",
                       xbc[0:rows_e, c * (T + 3) + 3:c * (T + 3) + 3 + T], ps,
                       rd=[("pbk", pe)], wr=[("xbc", c)])
            else:
                p.emit(sc, "activation", dt_sp[:, 0:T], ps, AF.Relu,
                       bias=dt_bias[:], rd=[("pbk", pe), "dtb"], wr=["dt_sp"])
                p.emit(sc, "activation", s_all[:, 0:T], ps, AF.Abs,
                       bias=dt_bias[:], rd=[("pbk", pe), "dtb"], wr=["s_all"])
                p.emit(sc, "activation", sig_all[:, 0:T], s_all[:, 0:T],
                       AF.Exp, scale=-1.0, rd=["s_all"], wr=["sig_all"])
                p.emit(sc, "activation", sig_all[:, 0:T], sig_all[:, 0:T],
                       AF.Ln, bias=one_col[0:NH, :], rd=["sig_all", "one_col"],
                       wr=["sig_all"])
                p.emit(v_, "tensor_tensor", dt_sp[:, 0:T], dt_sp[:, 0:T],
                       sig_all[:, 0:T], op=OP.add, rd=["dt_sp", "sig_all"],
                       wr=["dt_sp"])

        for c in range(NCVT):
            base = c * (T + 3)
            rc = 128 if c < 16 else 64
            if m == 0:
                p.emit(v_, "memset", xbc[0:rc, base:base + 3], 0.0,
                       rd=[("xbc", c)], wr=[("xbc", c)])
            else:
                p.emit(v_, "tensor_copy", xbc[0:rc, base:base + 3],
                       halo[0:rc, c * 4:c * 4 + 3], rd=[("halo", c), ("xbc", c)],
                       wr=[("xbc", c)])
        for c in range(NCVT):
            base = c * (T + 3)
            rc = 128 if c < 16 else 64
            pe = c % 2
            for j in range(4):
                p.emit(v_, "tensor_scalar",
                       diag[pe][0:rc, j * 128:j * 128 + rc],
                       identb[0:rc, 0:rc], cw4[0:rc, c * 4 + j:c * 4 + j + 1],
                       None, op0=OP.mult, rd=["identb", "cw4", ("diag", pe)],
                       wr=[("diag", pe)] if j == 3 else ())
            ps = pbank(pe, rc, T)
            for j in range(4):
                p.emit(t_, "matmul", ps, diag[pe][0:rc, j * 128:j * 128 + rc],
                       xbc[0:rc, base + j:base + j + T],
                       start=(j == 0), stop=(j == 3),
                       rd=[("diag", pe), ("xbc", c)],
                       wr=([("pbk", pe)] if j in (0, 3) else []) +
                          ([("diag", pe)] if j == 3 else []),
                       inc=(j == 3))
            p.emit(v_, "tensor_copy", halo[0:rc, c * 4:c * 4 + 3],
                   xbc[0:rc, base + T:base + T + 3], rd=[("xbc", c)],
                   wr=[("halo", c)])
            p.emit(sc, "activation", xbc[0:rc, base + 3:base + 3 + T], ps,
                   AF.Silu, bias=cb17[0:rc, c:c + 1],
                   rd=[("pbk", pe), "cb17", ("halo", c)], wr=[("xbc", c)])

        p.emit(v_, "tensor_scalar", s_all[:, 0:T], dt_sp[:, 0:T], a_col[:],
               None, op0=OP.mult, rd=["dt_sp", "a_col"], wr=["s_all"])
        p.emit(sc, "activation", sig_all[:, 0:T], dt_sp[:, 0:T], AF.Ln,
               rd=["dt_sp"], wr=["sig_all"])
        for ch in range(NCH):
            sl = slice(ch * C, (ch + 1) * C)
            p.emit(v_, "tensor_tensor_scan", s_all[:, sl], s_all[:, sl],
                   s_all[:, sl], zero16[:], op0=OP.add, op1=OP.bypass,
                   rd=["s_all", "zero16"], wr=["s_all"])
        p.emit(v_, "tensor_tensor", sig_all[:, 0:T], sig_all[:, 0:T],
               s_all[:, 0:T], op=OP.subtract, rd=["s_all", "sig_all"],
               wr=["sig_all"])

        for ch in range(NCH):
            t_sl = slice(ch * C, (ch + 1) * C)
            p.emit(t_, "transpose", pbank(4, 128, NH), sig_all[:, t_sl],
                   identf[0:NH, 0:NH], rd=["sig_all", "identf", ("pbk", 4)],
                   wr=[("pbk", 4)])
            p.emit(v_, "tensor_copy", sigT[:, 0:NH], pbank(4, 128, NH),
                   rd=[("pbk", 4)], wr=["sigT"])
            p.emit(v_, "tensor_scalar", w_st[:], sig_all[:, t_sl],
                   s_all[:, ch * C + C - 1:ch * C + C], None, op0=OP.add,
                   rd=["sig_all", "s_all"], wr=["w_st"])
            p.emit(sc, "activation", w_st[:], w_st[:], AF.Exp, rd=["w_st"],
                   wr=["w_st"])
            p.emit(t_, "transpose", pbank(4, 128, NH), w_st[:],
                   identf[0:NH, 0:NH], rd=["w_st", "identf", ("pbk", 4)],
                   wr=[("pbk", 4)])
            p.emit(v_, "tensor_copy", wT[:, 0:NH], pbank(4, 128, NH),
                   rd=[("pbk", 4)], wr=["wT"])
            p.emit(sc, "activation", esc_c[:], s_all[:, ch * C + C - 1:ch * C + C],
                   AF.Exp, rd=["s_all"], wr=["esc_c"])
            p.emit(t_, "transpose", pbank(4, 1, NH), esc_c[:],
                   identf[0:NH, 0:NH],
                   rd=["esc_c", "identf", ("pbk", 4)], wr=[("pbk", 4)])
            p.emit(v_, "tensor_copy", rsc[:, 0:NH], pbank(4, 1, NH),
                   rd=[("pbk", 4)], wr=["rsc"])
            p.emit(t_, "matmul", pbank(5, NS, NH), ones1[:, 0:NS],
                   rsc[:, 0:NH], start=True, stop=True,
                   rd=["ones1", "rsc", ("pbk", 5)], wr=[("pbk", 5)])
            p.emit(v_, "tensor_copy", escb[:, 0:NH], pbank(5, NS, NH),
                   rd=[("pbk", 5)], wr=["escb"])
            bbB = 16 * (T + 3) + 3 + ch * C
            bbC = 17 * (T + 3) + 3 + ch * C
            B_sl = xbc[0:NS, bbB:bbB + C]
            C_sl = xbc[0:NS, bbC:bbC + C]
            p.emit(t_, "matmul", pbank(4, 128, C), B_sl, C_sl, start=True,
                   stop=True, rd=[("xbc", 16), ("xbc", 17), ("pbk", 4)], wr=[("pbk", 4)])
            p.emit(v_, "tensor_copy", g_s[:], pbank(4, 128, C),
                   rd=[("pbk", 4)], wr=["g_s"])
            p.emit(t_, "transpose", pbank_b(4, 128, NS), B_sl,
                   identb[0:NS, 0:NS], rd=[("xbc", 16), "identb", ("pbk", 4)],
                   wr=[("pbk", 4)])
            p.emit(v_, "tensor_copy", b_tm[:, 0:NS], pbank_b(4, 128, NS),
                   rd=[("pbk", 4)], wr=["b_tm"])

            for h in range(NH):
                pe = h % 2
                yb = 6 + pe
                tb = 4 + pe
                rb = 0 + pe
                cb = 2 + pe
                xoff = h * (T + 3) + 3 + ch * C
                p.emit(t_, "transpose", pbank_b(tb, 128, C),
                       xbc[:, xoff:xoff + C], identb[:],
                       rd=[("xbc", h), "identb", ("pbk", tb)], wr=[("pbk", tb)])
                p.emit(sc, "activation", x_tm[pe][:], pbank_b(tb, 128, C),
                       AF.Identity, rd=[("pbk", tb)], wr=[("x_tm", pe)])
                p.emit(t_, "matmul", pbank(rb, 128, C),
                       e16[:, h * 128:(h + 1) * 128], s_all[:, t_sl],
                       start=True, stop=False,
                       rd=["e16", "s_all"], wr=[("pbk", rb)],
                       inc=False)
                p.emit(t_, "matmul", pbank(rb, 128, C), identb[:],
                       trineg_b[:], start=False, stop=True,
                       rd=["identb", "trineg_b"], wr=[("pbk", rb)])
                p.emit(sc, "activation", mask_b[pe][:], pbank(rb, 128, C),
                       AF.Exp, bias=sigT[:, h:h + 1],
                       rd=[("pbk", rb), "sigT"], wr=[("mask", pe)])
                p.emit(v_, "tensor_tensor", gmb[pe][:], g_s[:], mask_b[pe][:],
                       op=OP.mult, rd=["g_s", ("mask", pe)], wr=[("gmb", pe)])
                p.emit(t_, "matmul", pbank(cb, NS, C),
                       e16[:, h * 128:h * 128 + NS], s_all[:, t_sl],
                       start=True, stop=True,
                       rd=["e16", "s_all", ("pbk", cb)], wr=[("pbk", cb)])
                p.emit(sc, "activation", es_b[pe][:], pbank(cb, NS, C),
                       AF.Exp, rd=[("pbk", cb)], wr=[("esb", pe)])
                p.emit(v_, "tensor_tensor", c_til[pe][:, 0:C], es_b[pe][:],
                       C_sl, op=OP.mult, rd=[("esb", pe), ("xbc", 17)],
                       wr=[("ctil", pe)])
                p.emit(v_, "tensor_copy", ht_b[pe][:],
                       ht[:, h * HD:(h + 1) * HD], rd=[("ht", h)],
                       wr=[("htb", pe)])
                p.emit(t_, "matmul", pbank(yb, 128, C), x_tm[pe][:], gmb[pe][:],
                       start=True, stop=False,
                       rd=[("x_tm", pe), ("gmb", pe)], wr=[("pbk", yb)],
                       inc=False)
                p.emit(t_, "matmul", pbank(yb, 128, C), ht_b[pe][:],
                       c_til[pe][:, 0:C], start=False, stop=False,
                       rd=[("htb", pe), ("ctil", pe)], inc=False)
                p.emit(t_, "matmul", pbank(yb, 128, C),
                       d_id[:, h * 128:(h + 1) * 128],
                       xbc[:, xoff:xoff + C], start=False, stop=True,
                       rd=["d_id", ("xbc", h)], wr=[("pbk", yb)])
                p.emit(v_, "tensor_scalar", x_til[pe][:], x_tm[pe][:],
                       wT[:, h:h + 1], None, op0=OP.mult,
                       rd=[("x_tm", pe), "wT"], wr=[("xtil", pe)])
                p.emit(t_, "matmul", pbank(rb, NS, HD), b_tm[:, 0:NS],
                       x_til[pe][:], start=True, stop=True,
                       rd=["b_tm", ("xtil", pe), ("pbk", rb)], wr=[("pbk", rb)])
                p.emit(v_, "scalar_tensor_tensor", ht[:, h * HD:(h + 1) * HD],
                       ht[:, h * HD:(h + 1) * HD], escb[:, h:h + 1],
                       pbank(rb, NS, HD), op0=OP.mult, op1=OP.add,
                       rd=[("ht", h), "escb", ("pbk", rb), ("htb", pe)],
                       wr=[("ht", h)])
                p.emit(v_, "tensor_tensor", y_g[:, h * C:(h + 1) * C],
                       pbank(yb, 128, C),
                       z_silu[:, h * T + ch * C:h * T + ch * C + C],
                       op=OP.mult, rd=[("pbk", yb), ("zs", h)],
                       wr=[("y_g", h), ("pbk", yb)])
            for h in range(NH):
                sq_ = (sqA, sqA2)[h % 2]
                p.emit(sc, "activation", sq_[:, 0:C], y_g[:, h * C:(h + 1) * C],
                       AF.Square, rd=[("y_g", h)], wr=[("sq_scr", h % 2)])
                p.emit(t_, "matmul", pbank(5, 1, C), onescol_b[:],
                       sq_[:, 0:C], start=(h == 0), stop=(h == NH - 1),
                       rd=["onescolb", ("sq_scr", h % 2)],
                       wr=[("pbk", 5)] if h in (0, NH - 1) else (), inc=True)
            p.emit(v_, "tensor_scalar", crowA[0:1, 0:C], pbank(5, 1, C),
                   1.0 / EI, None, op0=OP.mult, rd=[("pbk", 5)], wr=["crow"])
            p.emit(sc, "activation", crowA[0:1, 0:C], crowA[0:1, 0:C], AF.Ln,
                   bias=eps_col[0:1, :], rd=["crow", "eps_col"], wr=["crow"])
            rrb_b = crowA[:].bitcast(BF16)[0:1, 0:C]
            p.emit(sc, "activation", rrb_b, crowA[0:1, 0:C], AF.Exp,
                   scale=-0.5, rd=["crow"], wr=["crow"])
            p.emit(t_, "matmul", pbank(5, 128, C), ones1b[:], rrb_b,
                   start=True, stop=True, rd=["ones1b", "crow", ("pbk", 5)],
                   wr=[("pbk", 5)])
            for h in range(NH):
                p.emit(v_, "scalar_tensor_tensor",
                       y_n[:, h * T + ch * C:h * T + ch * C + C],
                       y_g[:, h * C:(h + 1) * C], nw[:, h:h + 1],
                       pbank(5, 128, C), op0=OP.mult, op1=OP.mult,
                       rd=[("y_g", h), "nw", ("pbk", 5)], wr=[("y_n", h)])

        for ke in range(NE):
            pe = ke % 2
            dma(s_, wo_buf[pe][:], w_outbh[ke * 128:(ke + 1) * 128, :],
                after=[p.last_w.get(("wobh", ke))],
                wr=[("wo", pe)], dsem="dwo%d" % pe)
            for dd in range(ND):
                p.emit(t_, "matmul", pbank(dd, 128, T),
                       wo_buf[pe][:, dd * 128:(dd + 1) * 128],
                       y_n[:, ke * T:(ke + 1) * T],
                       start=(ke == 0), stop=(ke == NE - 1),
                       rd=[("wo", pe), ("y_n", ke)],
                       wr=([("pbk", dd)] if ke in (0, NE - 1) else []) +
                          ([("wo", pe)] if dd == ND - 1 else []),
                       inc=(ke == NE - 1 or dd == ND - 1))
        for dd in range(ND):
            p.emit(v_, "tensor_copy", m_o[:, dd * T:(dd + 1) * T],
                   pbank(dd, 128, T), rd=[("pbk", dd)], wr=[("m_o", dd)])

        def mosrc(i):
            return m_o[:, i * T:(i + 1) * T]

        ns["srckeys"] = [("m_o", i) for i in range(ND)]
        bca, bcb = ln_full("ln2", mosrc, ND, D, T)
        for i in range(ND):
            ap_ = apA[i % 2]
            p.emit(v_, "tensor_tensor", ap_[:], mosrc(i), bca, op=OP.mult,
                   rd=[("m_o", i), ("pbk", 2)], wr=[("apA", i % 2)])
            p.emit(v_, "tensor_tensor", ap_[:], ap_[:], bcb, op=OP.add,
                   rd=[("apA", i % 2), ("pbk", 3)], wr=[("apA", i % 2)])
            p.emit(v_, "tensor_tensor", xb[:, i * T:(i + 1) * T], xsrc(i),
                   ap_[:], op=OP.add, rd=[("apA", i % 2), "xb"], wr=["xb"])
        tgt = out_fm if cfg.debug == "x1" else x1h
        dma(s_, fm3(tgt[b, :, t0:t0 + T]), sb3(xb[:], T),
            rd=["xb"], wr=[("x1h", mt)])
        p.last_w["xb"] = p.last_w[("x1h", mt)]

    AR_A.pop()

    if cfg.debug == "x1":
        p.wait_all("gpsimd")
        block = es.enter_context(nc.Block())
        p.replay(block)
        es.close()
        return nc

    p.barrier()
    AR_A.push()
    w1b = AR_A.bf16(128, ND * FF)
    w2buf = [AR_A.bf16(128, 1024) for _ in range(2)]
    x1b = AR_A.f32(128, ND * 512)
    xln3 = AR_A.bf16(128, ND * 512)
    act = AR_A.bf16(128, NFF * 512)
    h1b = [AR_A.bf16(128, 512) for _ in range(2)]
    x2b = [AR_A.bf16(128, 512) for _ in range(2)]
    ab_ = [AR_A.bf16(128, 512) for _ in range(2)]
    sgb = [AR_A.bf16(128, 512) for _ in range(2)]
    y0st = AR_A.bf16(128, ND * 512)
    lg_sb = AR_A.f32(8, 512)
    lt_sb = AR_A.f32(128, 8)
    nmax = AR_A.f32(128, 1)
    qex = AR_A.f32(128, 8)
    ssum = AR_A.f32(128, 1)
    p0c = AR_A.f32(128, 1)
    p0row = AR_A.f32(1, 512)
    sqB = AR_A.bf16(128, 512)
    sqB2 = AR_A.bf16(128, 512)
    apB = [AR_A.f32(128, 512) for _ in range(2)]
    arowB = AR_A.f32(1, 512)
    browB = AR_A.f32(1, 512)
    crowB = AR_A.f32(1, 512)
    cntc = AR_A.f32(128, 1)
    cnt_loc = AR_A.f32(1, 8)
    cacc = AR_A.f32(1, 128)
    cacc2 = AR_A.f32(128, 64)
    wblk = AR_A.f32(128, 8192)
    wcnt1 = [wblk[:, 0:2048], wblk[:, 2048:4096]]
    wcnt2 = [wblk[:, 4096:6144], wblk[:, 6144:8192]]
    wstgA = wblk[:, 4096:5120]
    absA = wblk[:, 5120:6144]
    wstgB = wblk[:].bitcast(BF16)[:, 2 * 6144:2 * 6144 + 1024]
    y1s = wblk[:, 0:4096]
    ns.update(arow=arowB, brow=browB, crow=crowB, sq=(sqB, sqB2))

    for kd in range(ND):
        dma(g, w1b[:, kd * FF:(kd + 1) * FF], w1[kd * 128:(kd + 1) * 128, :],
            wr=[("w1b", kd)])
    p.patch_total([("w1b", kd) for kd in range(ND)], "DG")

    for i in range(ND):
        p.emit(v_, "tensor_scalar", wrg[:, i * 8:(i + 1) * 8],
               wr_sb[:, i * 8:(i + 1) * 8], gam[:, i:i + 1], None, op0=OP.mult,
               rd=["wr", "gam"], wr=[("wrg", i)])
        p.emit(t_, "matmul", pbank(4, 8, 1), wr_sb[:, i * 8:(i + 1) * 8],
               gam[:, i:i + 1], start=(i == 0), stop=(i == ND - 1),
               rd=["wr", "gam"],
               wr=[("pbk", 4)] if i in (0, ND - 1) else (), inc=(i == ND - 1))
    p.emit(v_, "tensor_copy", cgam[:], pbank(4, 8, 1), rd=[("pbk", 4)],
           wr=["cgam"])
    for i in range(ND):
        p.emit(t_, "matmul", pbank(4, 8, 1), wr_sb[:, i * 8:(i + 1) * 8],
               bet[:, i:i + 1], start=(i == 0), stop=(i == ND - 1),
               rd=["wr", "bet"],
               wr=[("pbk", 4)] if i in (0, ND - 1) else (), inc=(i == ND - 1))
    p.emit(v_, "tensor_tensor", cbet[:], pbank(4, 8, 1), br_sb[:], op=OP.add,
           rd=[("pbk", 4), "br"], wr=["cbet"])

    def gelu_ffn(tag, xln3_keys, w1_ap, w1_keys, bias_ap, bias_key, w2_src,
                 w2_keys, bias2_ap, bias2_key, ydst, ydst_f32):
        for f in range(NFF):
            bk = f % 2
            ps = pbank(bk, 128, 512)
            for kd in range(ND):
                p.emit(t_, "matmul", ps,
                       w1_ap[:, kd * FF + f * 128:kd * FF + (f + 1) * 128],
                       xln3[:, kd * 512:(kd + 1) * 512],
                       start=(kd == 0), stop=(kd == ND - 1),
                       rd=[w1_keys[kd], ("xln3", kd)],
                       wr=[("pbk", bk)] if kd in (0, ND - 1) else (),
                       inc=(kd == ND - 1))
            p.emit(sc, "activation", h1b[bk][:], ps, AF.Identity,
                   bias=bias_ap[:, f:f + 1], rd=[("pbk", bk), bias_key],
                   wr=[("h1b", bk), ("pbk", bk)])
            p.emit(v_, "tensor_tensor", x2b[bk][:], h1b[bk][:], h1b[bk][:],
                   op=OP.mult, rd=[("h1b", bk)], wr=[("x2b", bk)])
            p.emit(v_, "tensor_scalar", ab_[bk][:], x2b[bk][:], GELU_C, 1.0,
                   op0=OP.mult, op1=OP.add, rd=[("x2b", bk)], wr=[("ab_", bk)])
            p.emit(v_, "tensor_tensor", ab_[bk][:], ab_[bk][:], h1b[bk][:],
                   op=OP.mult, rd=[("ab_", bk), ("h1b", bk)], wr=[("ab_", bk)])
            p.emit(sc, "activation", sgb[bk][:], ab_[bk][:], AF.Sigmoid,
                   scale=GELU_S, rd=[("ab_", bk)], wr=[("sgb", bk)])
            p.emit(v_, "tensor_tensor", act[:, f * 512:(f + 1) * 512],
                   h1b[bk][:], sgb[bk][:], op=OP.mult,
                   rd=[("h1b", bk), ("sgb", bk)], wr=[("act", f)])
        for kf in range(NFF):
            pe = kf % 2
            dma(s_, w2buf[pe][:], w2_src[kf * 128:(kf + 1) * 128, :],
                after=[p.last_w.get(w2_keys(kf))],
                wr=[("w2buf", pe)], dsem="dw2%d" % pe)
            for hh in range(ND):
                p.emit(t_, "matmul", pbank(hh, 128, 512),
                       w2buf[pe][:, hh * 128:(hh + 1) * 128],
                       act[:, kf * 512:(kf + 1) * 512],
                       start=(kf == 0), stop=(kf == NFF - 1),
                       rd=[("w2buf", pe), ("act", kf)],
                       wr=([("pbk", hh)] if kf in (0, NFF - 1) else []) +
                          ([("w2buf", pe)] if hh == ND - 1 else []),
                       inc=(kf == NFF - 1 or hh == ND - 1))
        for hh in range(ND):
            p.emit(sc, "activation", ydst[:, hh * 512:(hh + 1) * 512],
                   pbank(hh, 128, 512), AF.Identity,
                   bias=bias2_ap[:, hh:hh + 1],
                   rd=[("pbk", hh), bias2_key], wr=[(tag + "_y", hh)])

    aw2bv = aw2bh.rearrange("(a p) d -> p a d", p=128)
    AWTOK = [("dw2o0", p.cnt["dw2o0"]), ("dw2o1", p.cnt["dw2o1"])]

    def wdat_b(buf, cols=2048):
        return buf[:].bitcast(BF16)[:, 0:cols]

    def wmsk(buf, cols):
        return buf[:].bitcast(BF16)[:, 2048:2048 + cols]

    def bis_mid(js):
        sl = slice(js[0], js[-1] + 1)
        p.emit(v_, "tensor_tensor", bmid[:, sl], blo[:, sl], bhi[:, sl],
               op=OP.add, rd=["blo", "bhi"], wr=["bmid"])
        p.emit(v_, "tensor_scalar", bmid[:, sl], bmid[:, sl], 0.5, None,
               op0=OP.mult, rd=["bmid"], wr=["bmid"])
        p.emit(t_, "matmul", pbank(4, 128, 8), ones1[:], bmid[:], start=True,
               stop=True, rd=["ones1", "bmid", ("pbk", 4)], wr=[("pbk", 4)])
        p.emit(v_, "tensor_copy", bcols[:], pbank(4, 128, 8),
               rd=[("pbk", 4)], wr=["bcols"])

    def bis_update(js):
        pr, inv, tmp = bst[:, 0:8], bst[:, 8:16], bst[:, 16:24]
        sl = slice(js[0], js[-1] + 1)
        p.emit(v_, "tensor_tensor", pr[:, sl], cnt_loc[:, sl], kvec[:, sl],
               op=OP.is_ge, rd=["cnt_loc", "kvec"], wr=["pr"])
        p.emit(v_, "tensor_scalar", inv[:, sl], pr[:, sl], -1.0, 1.0,
               op0=OP.mult, op1=OP.add, rd=["pr"], wr=["inv"])
        p.emit(v_, "tensor_tensor", tmp[:, sl], pr[:, sl], bmid[:, sl],
               op=OP.mult, rd=["pr", "bmid"], wr=["tmp"])
        p.emit(v_, "tensor_tensor", bhi[:, sl], bhi[:, sl], inv[:, sl],
               op=OP.mult, rd=["bhi", "inv"], wr=["bhi"])
        p.emit(v_, "tensor_tensor", bhi[:, sl], bhi[:, sl], tmp[:, sl],
               op=OP.add, rd=["bhi", "tmp"], wr=["bhi"])
        p.emit(v_, "tensor_tensor", tmp[:, sl], inv[:, sl], bmid[:, sl],
               op=OP.mult, rd=["inv", "bmid"], wr=["tmp"])
        p.emit(v_, "tensor_tensor", blo[:, sl], blo[:, sl], pr[:, sl],
               op=OP.mult, rd=["blo", "pr"], wr=["blo"])
        p.emit(v_, "tensor_tensor", blo[:, sl], blo[:, sl], tmp[:, sl],
               op=OP.add, rd=["blo", "tmp"], wr=["blo"])

    def bis_cnt_small(j, srcap, nn, rdk):
        p.emit(v_, "memset", cntc[:], 0.0, rd=["cntc"], wr=["cntc"])
        p.emit(v_, "tensor_scalar", wmsk(wcnt1[0], 1024)[:, 0:nn], srcap,
               bcols[:, j:j + 1], 0.0, op0=OP.is_lt, op1=OP.add,
               accum_out=cntc[:], rd=rdk + ["bcols", "cntc"],
               wr=["cntc", ("msk1", 0)])
        p.emit(t_, "matmul", pbank(5, 1, 1), cntc[:], onescol[:],
               start=True, stop=True, rd=["cntc", "onescol", ("pbk", 5)],
               wr=[("pbk", 5)])
        p.emit(v_, "tensor_copy", cnt_loc[:, j:j + 1], pbank(5, 1, 1),
               rd=[("pbk", 5)], wr=["cnt_loc"])

    def bis_b_iter():
        bis_mid((4, 5, 6, 7))
        for j, (src_, nn_, key) in ((4, (ab1, NFF, "ab1")),
                                    (5, (ab1, NFF, "ab1")),
                                    (6, (ab2, ND, "ab2")),
                                    (7, (ab2, ND, "ab2"))):
            bis_cnt_small(j, src_[:, 0:nn_], nn_, [key])
        bis_update((4, 5, 6, 7))

    def bis_s1_iter():
        bis_mid((0, 1, 2, 3))
        p.emit(v_, "memset", cacc2[:], 0.0, rd=["cacc2"], wr=["cacc2"])
        for hc in range(16):
            pe = hc % 2
            dma(g, wdat_b(wcnt1[pe]),
                aw1bh[(hc // 2) * 128:(hc // 2) * 128 + 128,
                      (hc % 2) * 2048:(hc % 2) * 2048 + 2048],
                after=AWTOK, wr=[("wcb", pe)], dsem="dcn%d" % pe)
            for j in (0, 1):
                p.emit(v_, "tensor_scalar", wmsk(wcnt1[pe], 2048),
                       wdat_b(wcnt1[pe]), bcols[:, j:j + 1], 0.0,
                       op0=OP.is_lt, op1=OP.add,
                       accum_out=cacc2[:, j * 16 + hc:j * 16 + hc + 1],
                       rd=[("wcb", pe), "bcols"],
                       wr=[("msk1", pe), "cacc2"])
            p.last_w[("wcb", pe)] = p.last_w["cacc2"]
            dma(g, wdat_b(wcnt2[pe]).rearrange("p (a d) -> p a d", d=1024),
                aw2bv[:, 2 * hc:2 * hc + 2, :],
                after=AWTOK, wr=[("wc2", pe)], dsem="dc2%d" % pe)
            for j in (2, 3):
                p.emit(v_, "tensor_scalar", wmsk(wcnt2[pe], 2048),
                       wdat_b(wcnt2[pe]), bcols[:, j:j + 1], 0.0,
                       op0=OP.is_lt, op1=OP.add,
                       accum_out=cacc2[:, 32 + (j - 2) * 16 + hc:
                                       32 + (j - 2) * 16 + hc + 1],
                       rd=[("wc2", pe), "bcols"],
                       wr=[("msk2", pe), "cacc2"])
            p.last_w[("wc2", pe)] = p.last_w["cacc2"]
        for j in (0, 1, 2, 3):
            base = j * 16 if j < 2 else 32 + (j - 2) * 16
            p.emit(v_, "tensor_reduce", cntc[:],
                   cacc2[:, base:base + 16], axis=mybir.AxisListType.X,
                   op=OP.add, rd=["cacc2"], wr=["cntc"])
            p.emit(t_, "matmul", pbank(5, 1, 1), cntc[:], onescol[:],
                   start=True, stop=True,
                   rd=["cntc", "onescol", ("pbk", 5)], wr=[("pbk", 5)])
            p.emit(v_, "tensor_copy", cnt_loc[:, j:j + 1], pbank(5, 1, 1),
                   rd=[("pbk", 5)], wr=["cnt_loc"])
        bis_update((0, 1, 2, 3))

    for (colsrc, bhicol) in ((1, 0), (2, 2)):
        p.emit(t_, "transpose", pbank(4, 1, 128), mxst[:, colsrc:colsrc + 1],
               identf[:], rd=["mxst", "identf", ("pbk", 4)], wr=[("pbk", 4)])
        p.emit(v_, "tensor_reduce", bhi[:, bhicol:bhicol + 1],
               pbank(4, 1, 128), axis=mybir.AxisListType.X, op=OP.max,
               rd=[("pbk", 4)], wr=["bhi"])
    for (src_, nn_, col, key) in ((ab1, NFF, 4, "ab1"), (ab2, ND, 6, "ab2")):
        p.emit(v_, "tensor_reduce", cntc[:], src_[:, 0:nn_],
               axis=mybir.AxisListType.X, op=OP.max, rd=[key], wr=["cntc"])
        p.emit(t_, "transpose", pbank(4, 1, 128), cntc[:], identf[:],
               rd=["cntc", "identf", ("pbk", 4)], wr=[("pbk", 4)])
        p.emit(v_, "tensor_reduce", bhi[:, col:col + 1], pbank(4, 1, 128),
               axis=mybir.AxisListType.X, op=OP.max, rd=[("pbk", 4)],
               wr=["bhi"])
    for col in (0, 2, 4, 6):
        p.emit(v_, "tensor_copy", bhi[:, col + 1:col + 2],
               bhi[:, col:col + 1], rd=["bhi"], wr=["bhi"])
    p.emit(v_, "tensor_scalar", bhi[:], bhi[:], 1.01, None, op0=OP.mult,
           rd=["bhi"], wr=["bhi"])
    n_b_done = [0]
    n_s1_done = [0]

    for tt in range(NTT):
        b = (tt * 512) // L
        t0 = (tt * 512) % L
        dma(s_, sb3(x1b[:], 512), fm3(x1h[b, :, t0:t0 + 512]),
            rd=["x1b"], wr=["x1b"], dsem="dx1b")

        def x1src(i):
            return x1b[:, i * 512:(i + 1) * 512]

        ns["srckeys"] = ["x1b"] * ND
        bca, bcb = ln_full("ln3", x1src, ND, D, 512)
        for i in range(ND):
            ap_ = apB[i % 2]
            p.emit(v_, "tensor_tensor", ap_[:], x1src(i), bca, op=OP.mult,
                   rd=["x1b", ("pbk", 2)], wr=[("apB", i % 2)])
            p.emit(v_, "tensor_tensor", xln3[:, i * 512:(i + 1) * 512],
                   ap_[:], bcb, op=OP.add, rd=[("apB", i % 2), ("pbk", 3)],
                   wr=[("xln3", i)])
        dma(s_, fm3(xln3h[b, :, t0:t0 + 512]), sb3(xln3[:], 512),
            rd=[("xln3", i) for i in range(ND)], wr=[("xln3h", tt)])

        def x3src(i):
            return xln3[:, i * 512:(i + 1) * 512]

        ns["srckeys"] = [("xln3", i) for i in range(ND)]
        bca_r, bcb_r = ln_full("lnr", x3src, ND, D, 512, src_bf16=True)
        lps = pbank(4, 8, 512)
        for i in range(ND):
            p.emit(t_, "matmul", lps, wrg[:, i * 8:(i + 1) * 8], x3src(i),
                   start=(i == 0), stop=(i == ND - 1),
                   rd=[("wrg", i), ("xln3", i)],
                   wr=[("pbk", 4)] if i in (0, ND - 1) else (), inc=(i == ND - 1))
        p.emit(v_, "tensor_copy", lg_sb[:], lps, rd=[("pbk", 4)], wr=["lg"])
        p.emit(v_, "tensor_tensor", lg_sb[:], lg_sb[:], pbank(2, 8, 512),
               op=OP.mult, rd=["lg", ("pbk", 2), ("pbk", 4)], wr=["lg"])
        p.emit(v_, "scalar_tensor_tensor", lg_sb[:], pbank(3, 8, 512),
               cgam[:], lg_sb[:], op0=OP.mult, op1=OP.add,
               rd=[("pbk", 3), "cgam", "lg", ("pbk", 4)], wr=["lg"])
        p.emit(v_, "tensor_scalar", lg_sb[:], lg_sb[:], cbet[:], None,
               op0=OP.add, rd=["lg", "cbet"], wr=["lg"])
        for st in range(4):
            p.emit(t_, "transpose", pbank(5, 128, 8),
                   lg_sb[:, st * 128:(st + 1) * 128], identf[0:8, 0:8],
                   rd=["lg", "identf", ("pbk", 5)], wr=[("pbk", 5)])
            p.emit(v_, "tensor_copy", lt_sb[:], pbank(5, 128, 8),
                   rd=[("pbk", 5)], wr=["lt"])
            p.emit(v_, "tensor_reduce", nmax[:], lt_sb[:],
                   axis=mybir.AxisListType.X, op=OP.max, rd=["lt"],
                   wr=["nmax"])
            p.emit(v_, "tensor_scalar", nmax[:], nmax[:], -1.0, None,
                   op0=OP.mult, rd=["nmax"], wr=["nmax"])
            p.emit(sc, "activation", qex[:], lt_sb[:], AF.Exp, bias=nmax[:],
                   rd=["lt", "nmax"], wr=["qex"])
            p.emit(v_, "tensor_reduce", ssum[:], qex[:],
                   axis=mybir.AxisListType.X, op=OP.add, rd=["qex"],
                   wr=["ssum"])
            p.emit(v_, "reciprocal", ssum[:], ssum[:], rd=["ssum"],
                   wr=["ssum"])
            p.emit(v_, "tensor_tensor", p0c[:], qex[:, 0:1], ssum[:],
                   op=OP.mult, rd=["qex", "ssum"], wr=["p0c"])
            p.emit(t_, "transpose", pbank(5, 1, 128), p0c[:], identf[:],
                   rd=["p0c", "identf", ("pbk", 5)], wr=[("pbk", 5)])
            p.emit(v_, "tensor_copy", p0row[:, st * 128:(st + 1) * 128],
                   pbank(5, 1, 128), rd=[("pbk", 5)], wr=["p0row"])
        dma(s_, p0h[:, tt * 512:(tt + 1) * 512], p0row[:], rd=["p0row"],
            wr=[("p0h", tt)])

        gelu_ffn("y0", None, w1b, [("w1b", kd) for kd in range(ND)], b1c,
                 "b1c", w2bh, lambda kf: ("w2bh", kf), b2c, "b2c", y0st, None)
        dma(s_, fm3(y0h[b, :, t0:t0 + 512]), sb3(y0st[:], 512),
            rd=[("y0_y", hh) for hh in range(ND)], wr=[("y0h", tt)])
        for hh in range(ND):
            p.last_w[("y0_y", hh)] = p.last_w[("y0h", tt)]
        bq = -(-40 // NTT)
        sq_ = -(-16 // NTT)
        while n_b_done[0] < 40 and n_b_done[0] < (tt + 1) * bq:
            bis_b_iter()
            n_b_done[0] += 1
        while n_s1_done[0] < 16 and n_s1_done[0] < (tt + 1) * sq_:
            bis_s1_iter()
            n_s1_done[0] += 1

    p.barrier()
    w1v = w1.rearrange("(a p) f -> p a f", p=128)
    w2v = w2.rearrange("(a p) d -> p a d", p=128)
    w2bv = w2bh.rearrange("(a p) d -> p a d", p=128)

    def bis_update():
        pr, inv, tmp = bst[:, 0:8], bst[:, 8:16], bst[:, 16:24]
        p.emit(v_, "tensor_tensor", pr, cnt_loc[:], kvec[:], op=OP.is_ge,
               rd=["cnt_loc", "kvec"], wr=["pr"])
        p.emit(v_, "tensor_scalar", inv, pr, -1.0, 1.0, op0=OP.mult,
               op1=OP.add, rd=["pr"], wr=["inv"])
        p.emit(v_, "tensor_tensor", tmp, pr, bmid[:], op=OP.mult,
               rd=["pr", "bmid"], wr=["tmp"])
        p.emit(v_, "tensor_tensor", bhi[:], bhi[:], inv, op=OP.mult,
               rd=["bhi", "inv"], wr=["bhi"])
        p.emit(v_, "tensor_tensor", bhi[:], bhi[:], tmp, op=OP.add,
               rd=["bhi", "tmp"], wr=["bhi"])
        p.emit(v_, "tensor_tensor", tmp, inv, bmid[:], op=OP.mult,
               rd=["inv", "bmid"], wr=["tmp"])
        p.emit(v_, "tensor_tensor", blo[:], blo[:], pr, op=OP.mult,
               rd=["blo", "pr"], wr=["blo"])
        p.emit(v_, "tensor_tensor", blo[:], blo[:], tmp, op=OP.add,
               rd=["blo", "tmp"], wr=["blo"])

    def bis_mid():
        p.emit(v_, "tensor_tensor", bmid[:], blo[:], bhi[:], op=OP.add,
               rd=["blo", "bhi"], wr=["bmid"])
        p.emit(v_, "tensor_scalar", bmid[:], bmid[:], 0.5, None, op0=OP.mult,
               rd=["bmid"], wr=["bmid"])
        p.emit(t_, "matmul", pbank(4, 128, 8), ones1[:], bmid[:], start=True,
               stop=True, rd=["ones1", "bmid", ("pbk", 4)], wr=[("pbk", 4)])
        p.emit(v_, "tensor_copy", bcols[:], pbank(4, 128, 8),
               rd=[("pbk", 4)], wr=["bcols"])

    def wdat_b(buf, cols=2048):
        return buf[:].bitcast(BF16)[:, 0:cols]

    def wdat_f(buf, cols=1024):
        return buf[:, 0:cols]

    def wmsk(buf, cols):
        return buf[:].bitcast(BF16)[:, 2048:2048 + cols]

    def cnt_mask_mm(j, buf, srcap, rdk, mcols, first, last, mkey, bank):
        p.emit(v_, "tensor_scalar", wmsk(buf, mcols), srcap,
               bcols[:, j:j + 1], 0.0, op0=OP.is_lt, op1=OP.add,
               rd=rdk + ["bcols"], wr=[mkey])
        nmm = mcols // 512
        for sb in range(nmm):
            p.emit(t_, "matmul", pbank(bank, 1, 512), onescol_b[:],
                   wmsk(buf, mcols)[:, sb * 512:(sb + 1) * 512],
                   start=(first and sb == 0), stop=(last and sb == nmm - 1),
                   rd=["onescolb", mkey], wr=[("pbk", bank)],
                   inc=(sb == nmm - 1))

    def bis_tally(j):
        p.emit(v_, "tensor_reduce", cnt_loc[:, j:j + 1],
               pbank(j, 1, 512), axis=mybir.AxisListType.X, op=OP.add,
               rd=[("pbk", j)], wr=["cnt_loc", ("pbk", j)])

    def bis_cnt_small(j, srcap, nn, rdk):
        p.emit(v_, "memset", cntc[:], 0.0, rd=["cntc"], wr=["cntc"])
        p.emit(v_, "tensor_scalar", wmsk(wcnt1[0], 1024)[:, 0:nn], srcap,
               bcols[:, j:j + 1], 0.0, op0=OP.is_lt, op1=OP.add,
               accum_out=cntc[:], rd=rdk + ["bcols", "cntc"],
               wr=["cntc", ("msk1", 0)])
        p.emit(t_, "matmul", pbank(5, 1, 1), cntc[:], onescol[:],
               start=True, stop=True, rd=["cntc", "onescol", ("pbk", 5)],
               wr=[("pbk", 5)])
        p.emit(v_, "tensor_copy", cnt_loc[:, j:j + 1], pbank(5, 1, 1),
               rd=[("pbk", 5)], wr=["cnt_loc"])

    def bis_update(js):
        pr, inv, tmp = bst[:, 0:8], bst[:, 8:16], bst[:, 16:24]
        sl = slice(js[0], js[-1] + 1)
        p.emit(v_, "tensor_tensor", pr[:, sl], cnt_loc[:, sl], kvec[:, sl],
               op=OP.is_ge, rd=["cnt_loc", "kvec"], wr=["pr"])
        p.emit(v_, "tensor_scalar", inv[:, sl], pr[:, sl], -1.0, 1.0,
               op0=OP.mult, op1=OP.add, rd=["pr"], wr=["inv"])
        p.emit(v_, "tensor_tensor", tmp[:, sl], pr[:, sl], bmid[:, sl],
               op=OP.mult, rd=["pr", "bmid"], wr=["tmp"])
        p.emit(v_, "tensor_tensor", bhi[:, sl], bhi[:, sl], inv[:, sl],
               op=OP.mult, rd=["bhi", "inv"], wr=["bhi"])
        p.emit(v_, "tensor_tensor", bhi[:, sl], bhi[:, sl], tmp[:, sl],
               op=OP.add, rd=["bhi", "tmp"], wr=["bhi"])
        p.emit(v_, "tensor_tensor", tmp[:, sl], inv[:, sl], bmid[:, sl],
               op=OP.mult, rd=["inv", "bmid"], wr=["tmp"])
        p.emit(v_, "tensor_tensor", blo[:, sl], blo[:, sl], pr[:, sl],
               op=OP.mult, rd=["blo", "pr"], wr=["blo"])
        p.emit(v_, "tensor_tensor", blo[:, sl], blo[:, sl], tmp[:, sl],
               op=OP.add, rd=["blo", "tmp"], wr=["blo"])

    def bis_mid(js):
        sl = slice(js[0], js[-1] + 1)
        p.emit(v_, "tensor_tensor", bmid[:, sl], blo[:, sl], bhi[:, sl],
               op=OP.add, rd=["blo", "bhi"], wr=["bmid"])
        p.emit(v_, "tensor_scalar", bmid[:, sl], bmid[:, sl], 0.5, None,
               op0=OP.mult, rd=["bmid"], wr=["bmid"])
        p.emit(t_, "matmul", pbank(4, 128, 8), ones1[:], bmid[:], start=True,
               stop=True, rd=["ones1", "bmid", ("pbk", 4)], wr=[("pbk", 4)])
        p.emit(v_, "tensor_copy", bcols[:], pbank(4, 128, 8),
               rd=[("pbk", 4)], wr=["bcols"])

    p.emit(v_, "tensor_scalar", bhi[:, 0:4], blo[:, 0:4], 1.0078125,
           None, op0=OP.mult, rd=["blo"], wr=["bhi"])
    p.emit(v_, "tensor_scalar", blo[:, 0:4], blo[:, 0:4], 0.9921875,
           None, op0=OP.mult, rd=["blo"], wr=["blo"])
    for it in range(18):
        bis_mid((0, 1, 2, 3))
        for hc in range(32):
            pe = hc % 2
            first, last = (hc == 0), (hc == 31)
            dma(g, wcnt1[pe][:, 0:1024],
                w1[(hc // 4) * 128:(hc // 4) * 128 + 128,
                   (hc % 4) * 1024:(hc % 4) * 1024 + 1024],
                wr=[("wcb", pe)], dsem="dcn%d" % pe)
            p.emit(sc, "activation", wcnt1[pe][:, 0:1024],
                   wcnt1[pe][:, 0:1024], AF.Abs, rd=[("wcb", pe)],
                   wr=[("wcb", pe)])
            for j in (0, 1):
                p.emit(v_, "tensor_scalar", wmsk(wcnt1[pe], 1024),
                       wcnt1[pe][:, 0:1024], bcols[:, j:j + 1], 0.0,
                       op0=OP.is_lt, op1=OP.add, rd=[("wcb", pe), "bcols"],
                       wr=[("msk1", pe)])
                for sb in range(2):
                    p.emit(t_, "matmul", pbank(j, 1, 512), onescol_b[:],
                           wmsk(wcnt1[pe], 1024)[:, sb * 512:(sb + 1) * 512],
                           start=(first and sb == 0), stop=(last and sb == 1),
                           rd=["onescolb", ("msk1", pe)], wr=[("pbk", j)],
                           inc=(sb == 1))
            p.last_w[("wcb", pe)] = p.last_w[("msk1", pe)]
            dma(g, wcnt2[pe][:, 0:1024].rearrange("p (a d) -> p a d", d=1024),
                w2v[:, hc:hc + 1, :], wr=[("wc2", pe)], dsem="dc2%d" % pe)
            p.emit(sc, "activation", wcnt2[pe][:, 0:1024],
                   wcnt2[pe][:, 0:1024], AF.Abs, rd=[("wc2", pe)],
                   wr=[("wc2", pe)])
            for j in (2, 3):
                p.emit(v_, "tensor_scalar", wmsk(wcnt2[pe], 1024),
                       wcnt2[pe][:, 0:1024], bcols[:, j:j + 1], 0.0,
                       op0=OP.is_lt, op1=OP.add, rd=[("wc2", pe), "bcols"],
                       wr=[("msk2", pe)])
                for sb in range(2):
                    p.emit(t_, "matmul", pbank(j, 1, 512), onescol_b[:],
                           wmsk(wcnt2[pe], 1024)[:, sb * 512:(sb + 1) * 512],
                           start=(first and sb == 0), stop=(last and sb == 1),
                           rd=["onescolb", ("msk2", pe)], wr=[("pbk", j)],
                           inc=(sb == 1))
            p.last_w[("wc2", pe)] = p.last_w[("msk2", pe)]
        for j in (0, 1, 2, 3):
            p.emit(v_, "tensor_reduce", cnt_loc[:, j:j + 1],
                   pbank(j, 1, 512), axis=mybir.AxisListType.X, op=OP.add,
                   rd=[("pbk", j)], wr=["cnt_loc", ("pbk", j)])
        bis_update((0, 1, 2, 3))
    p.emit(t_, "matmul", pbank(4, 128, 8), ones1[:], blo[:], start=True,
           stop=True, rd=["ones1", "blo", ("pbk", 4)], wr=[("pbk", 4)])
    p.emit(v_, "tensor_copy", bcols[:], pbank(4, 128, 8), rd=[("pbk", 4)],
           wr=["bcols"])
    p.barrier()

    def perturb(dst, dst_keys, src, src_key, absrc, abs_key, n, jb, jt,
                m1, m2):
        p.emit(v_, "tensor_scalar", m1[:, 0:n], absrc[:, 0:n],
               bcols[:, jb:jb + 1], 0.8, op0=OP.is_le, op1=OP.mult,
               rd=[abs_key, "bcols"], wr=["pm1"])
        p.emit(v_, "tensor_scalar", m2[:, 0:n], absrc[:, 0:n],
               bcols[:, jt:jt + 1], 0.8, op0=OP.is_ge, op1=OP.mult,
               rd=[abs_key, "bcols"], wr=["pm2"])
        p.emit(v_, "tensor_tensor", m1[:, 0:n], m1[:, 0:n], m2[:, 0:n],
               op=OP.subtract, rd=["pm1", "pm2"], wr=["pm1"])
        p.emit(v_, "tensor_scalar", m1[:, 0:n], m1[:, 0:n], 1.0, None,
               op0=OP.add, rd=["pm1"], wr=["pm1"])
        p.emit(v_, "tensor_tensor", dst[:, 0:n], src[:, 0:n], m1[:, 0:n],
               op=OP.mult, rd=["pm1", src_key], wr=dst_keys)

    perturb(b1p, ["b1p"], b1c, "b1c", ab1, "ab1", NFF, 4, 5,
            y1s[:, 0:64], y1s[:, 64:128])
    perturb(b2p, ["b2p"], b2c, "b2c", ab2, "ab2", ND, 6, 7,
            y1s[:, 0:64], y1s[:, 64:128])

    for kf in range(NFF):
        dma(s_, wstgA[:], w2[kf * 128:(kf + 1) * 128, :], rd=["wstgA"],
            wr=["wstgA"], dsem="dwst")
        p.emit(sc, "activation", absA[:], wstgA[:], AF.Abs,
               rd=["wstgA"], wr=["absA"])
        m1 = y1s[:, 0:1024]
        m2 = y1s[:, 1024:2048]
        p.emit(v_, "tensor_scalar", m1, absA[:], bcols[:, 2:3], 0.8,
               op0=OP.is_le, op1=OP.mult, rd=["absA", "bcols"], wr=["pm1"])
        p.emit(v_, "tensor_scalar", m2, absA[:], bcols[:, 3:4], 0.8,
               op0=OP.is_ge, op1=OP.mult, rd=["absA", "bcols"], wr=["pm2"])
        p.emit(v_, "tensor_tensor", m1, m1, m2, op=OP.subtract,
               rd=["pm1", "pm2"], wr=["pm1"])
        p.emit(v_, "tensor_scalar", m1, m1, 1.0, None, op0=OP.add,
               rd=["pm1"], wr=["pm1"])
        p.emit(v_, "tensor_tensor", wstgB[:], wstgA[:], m1, op=OP.mult,
               rd=["pm1", "wstgA"], wr=["wstgB"])
        dma(s_, pw2bh[kf * 128:(kf + 1) * 128, :], wstgB[:], rd=["wstgB"],
            wr=[("pw2bh", kf), "wstgB"], dsem="dwst")

    for kd in range(ND):
        for jj in range(4):
            dma(s_, wstgA[:],
                w1[kd * 128:(kd + 1) * 128, jj * 1024:(jj + 1) * 1024],
                rd=["wstgA"], wr=["wstgA"], dsem="dwst")
            p.emit(sc, "activation", absA[:], wstgA[:], AF.Abs,
                   rd=["wstgA"], wr=["absA"])
            m1 = y1s[:, 0:1024]
            m2 = y1s[:, 1024:2048]
            p.emit(v_, "tensor_scalar", m1, absA[:], bcols[:, 0:1], 0.8,
                   op0=OP.is_le, op1=OP.mult, rd=["absA", "bcols"], wr=["pm1"])
            p.emit(v_, "tensor_scalar", m2, absA[:], bcols[:, 1:2], 0.8,
                   op0=OP.is_ge, op1=OP.mult, rd=["absA", "bcols"], wr=["pm2"])
            p.emit(v_, "tensor_tensor", m1, m1, m2, op=OP.subtract,
                   rd=["pm1", "pm2"], wr=["pm1"])
            p.emit(v_, "tensor_scalar", m1, m1, 1.0, None, op0=OP.add,
                   rd=["pm1"], wr=["pm1"])
            p.emit(v_, "tensor_tensor",
                   w1b[:, kd * FF + jj * 1024:kd * FF + (jj + 1) * 1024],
                   wstgA[:], m1, op=OP.mult, rd=["pm1", "wstgA"],
                   wr=[("w1b", kd)])

    p.barrier()
    y0t = y0st

    aw2bv = aw2bh.rearrange("(a p) d -> p a d", p=128)
    AWTOK = [("dw2o0", p.cnt["dw2o0"]), ("dw2o1", p.cnt["dw2o1"])]

    def wdat_b(buf, cols=2048):
        return buf[:].bitcast(BF16)[:, 0:cols]

    def wmsk(buf, cols):
        return buf[:].bitcast(BF16)[:, 2048:2048 + cols]

    def bis_mid(js):
        sl = slice(js[0], js[-1] + 1)
        p.emit(v_, "tensor_tensor", bmid[:, sl], blo[:, sl], bhi[:, sl],
               op=OP.add, rd=["blo", "bhi"], wr=["bmid"])
        p.emit(v_, "tensor_scalar", bmid[:, sl], bmid[:, sl], 0.5, None,
               op0=OP.mult, rd=["bmid"], wr=["bmid"])
        p.emit(t_, "matmul", pbank(4, 128, 8), ones1[:], bmid[:], start=True,
               stop=True, rd=["ones1", "bmid", ("pbk", 4)], wr=[("pbk", 4)])
        p.emit(v_, "tensor_copy", bcols[:], pbank(4, 128, 8),
               rd=[("pbk", 4)], wr=["bcols"])

    def bis_update(js):
        pr, inv, tmp = bst[:, 0:8], bst[:, 8:16], bst[:, 16:24]
        sl = slice(js[0], js[-1] + 1)
        p.emit(v_, "tensor_tensor", pr[:, sl], cnt_loc[:, sl], kvec[:, sl],
               op=OP.is_ge, rd=["cnt_loc", "kvec"], wr=["pr"])
        p.emit(v_, "tensor_scalar", inv[:, sl], pr[:, sl], -1.0, 1.0,
               op0=OP.mult, op1=OP.add, rd=["pr"], wr=["inv"])
        p.emit(v_, "tensor_tensor", tmp[:, sl], pr[:, sl], bmid[:, sl],
               op=OP.mult, rd=["pr", "bmid"], wr=["tmp"])
        p.emit(v_, "tensor_tensor", bhi[:, sl], bhi[:, sl], inv[:, sl],
               op=OP.mult, rd=["bhi", "inv"], wr=["bhi"])
        p.emit(v_, "tensor_tensor", bhi[:, sl], bhi[:, sl], tmp[:, sl],
               op=OP.add, rd=["bhi", "tmp"], wr=["bhi"])
        p.emit(v_, "tensor_tensor", tmp[:, sl], inv[:, sl], bmid[:, sl],
               op=OP.mult, rd=["inv", "bmid"], wr=["tmp"])
        p.emit(v_, "tensor_tensor", blo[:, sl], blo[:, sl], pr[:, sl],
               op=OP.mult, rd=["blo", "pr"], wr=["blo"])
        p.emit(v_, "tensor_tensor", blo[:, sl], blo[:, sl], tmp[:, sl],
               op=OP.add, rd=["blo", "tmp"], wr=["blo"])

    def bis_cnt_small(j, srcap, nn, rdk):
        p.emit(v_, "memset", cntc[:], 0.0, rd=["cntc"], wr=["cntc"])
        p.emit(v_, "tensor_scalar", wmsk(wcnt1[0], 1024)[:, 0:nn], srcap,
               bcols[:, j:j + 1], 0.0, op0=OP.is_lt, op1=OP.add,
               accum_out=cntc[:], rd=rdk + ["bcols", "cntc"],
               wr=["cntc", ("msk1", 0)])
        p.emit(t_, "matmul", pbank(5, 1, 1), cntc[:], onescol[:],
               start=True, stop=True, rd=["cntc", "onescol", ("pbk", 5)],
               wr=[("pbk", 5)])
        p.emit(v_, "tensor_copy", cnt_loc[:, j:j + 1], pbank(5, 1, 1),
               rd=[("pbk", 5)], wr=["cnt_loc"])

    def bis_b_iter():
        bis_mid((4, 5, 6, 7))
        for j, (src_, nn_, key) in ((4, (ab1, NFF, "ab1")),
                                    (5, (ab1, NFF, "ab1")),
                                    (6, (ab2, ND, "ab2")),
                                    (7, (ab2, ND, "ab2"))):
            bis_cnt_small(j, src_[:, 0:nn_], nn_, [key])
        bis_update((4, 5, 6, 7))

    def bis_s1_iter():
        bis_mid((0, 1, 2, 3))
        p.emit(v_, "memset", cacc2[:], 0.0, rd=["cacc2"], wr=["cacc2"])
        for hc in range(16):
            pe = hc % 2
            dma(g, wdat_b(wcnt1[pe]),
                aw1bh[(hc // 2) * 128:(hc // 2) * 128 + 128,
                      (hc % 2) * 2048:(hc % 2) * 2048 + 2048],
                after=AWTOK, wr=[("wcb", pe)], dsem="dcn%d" % pe)
            for j in (0, 1):
                p.emit(v_, "tensor_scalar", wmsk(wcnt1[pe], 2048),
                       wdat_b(wcnt1[pe]), bcols[:, j:j + 1], 0.0,
                       op0=OP.is_lt, op1=OP.add,
                       accum_out=cacc2[:, j * 16 + hc:j * 16 + hc + 1],
                       rd=[("wcb", pe), "bcols"],
                       wr=[("msk1", pe), "cacc2"])
            p.last_w[("wcb", pe)] = p.last_w["cacc2"]
            dma(g, wdat_b(wcnt2[pe]).rearrange("p (a d) -> p a d", d=1024),
                aw2bv[:, 2 * hc:2 * hc + 2, :],
                after=AWTOK, wr=[("wc2", pe)], dsem="dc2%d" % pe)
            for j in (2, 3):
                p.emit(v_, "tensor_scalar", wmsk(wcnt2[pe], 2048),
                       wdat_b(wcnt2[pe]), bcols[:, j:j + 1], 0.0,
                       op0=OP.is_lt, op1=OP.add,
                       accum_out=cacc2[:, 32 + (j - 2) * 16 + hc:
                                       32 + (j - 2) * 16 + hc + 1],
                       rd=[("wc2", pe), "bcols"],
                       wr=[("msk2", pe), "cacc2"])
            p.last_w[("wc2", pe)] = p.last_w["cacc2"]
        for j in (0, 1, 2, 3):
            base = j * 16 if j < 2 else 32 + (j - 2) * 16
            p.emit(v_, "tensor_reduce", cntc[:],
                   cacc2[:, base:base + 16], axis=mybir.AxisListType.X,
                   op=OP.add, rd=["cacc2"], wr=["cntc"])
            p.emit(t_, "matmul", pbank(5, 1, 1), cntc[:], onescol[:],
                   start=True, stop=True,
                   rd=["cntc", "onescol", ("pbk", 5)], wr=[("pbk", 5)])
            p.emit(v_, "tensor_copy", cnt_loc[:, j:j + 1], pbank(5, 1, 1),
                   rd=[("pbk", 5)], wr=["cnt_loc"])
        bis_update((0, 1, 2, 3))

    for (colsrc, bhicol) in ((1, 0), (2, 2)):
        p.emit(t_, "transpose", pbank(4, 1, 128), mxst[:, colsrc:colsrc + 1],
               identf[:], rd=["mxst", "identf", ("pbk", 4)], wr=[("pbk", 4)])
        p.emit(v_, "tensor_reduce", bhi[:, bhicol:bhicol + 1],
               pbank(4, 1, 128), axis=mybir.AxisListType.X, op=OP.max,
               rd=[("pbk", 4)], wr=["bhi"])
    for (src_, nn_, col, key) in ((ab1, NFF, 4, "ab1"), (ab2, ND, 6, "ab2")):
        p.emit(v_, "tensor_reduce", cntc[:], src_[:, 0:nn_],
               axis=mybir.AxisListType.X, op=OP.max, rd=[key], wr=["cntc"])
        p.emit(t_, "transpose", pbank(4, 1, 128), cntc[:], identf[:],
               rd=["cntc", "identf", ("pbk", 4)], wr=[("pbk", 4)])
        p.emit(v_, "tensor_reduce", bhi[:, col:col + 1], pbank(4, 1, 128),
               axis=mybir.AxisListType.X, op=OP.max, rd=[("pbk", 4)],
               wr=["bhi"])
    for col in (0, 2, 4, 6):
        p.emit(v_, "tensor_copy", bhi[:, col + 1:col + 2],
               bhi[:, col:col + 1], rd=["bhi"], wr=["bhi"])
    p.emit(v_, "tensor_scalar", bhi[:], bhi[:], 1.01, None, op0=OP.mult,
           rd=["bhi"], wr=["bhi"])
    n_b_done = [0]
    n_s1_done = [0]

    for tt in range(NTT):
        b = (tt * 512) // L
        t0 = (tt * 512) % L
        dma(s_, sb3(x1b[:], 512), fm3(x1h[b, :, t0:t0 + 512]),
            rd=["x1b"], wr=["x1b"], dsem="dx1b")
        xt = dma(s_, sb3(xln3[:], 512), fm3(xln3h[b, :, t0:t0 + 512]),
                 rd=[("xln3", i) for i in range(ND)], wr=["xln3A"],
                 dsem="dxl3")
        for i in range(ND):
            p.last_w[("xln3", i)] = xt
        dma(s_, sb3(y0t[:], 512), fm3(y0h[b, :, t0:t0 + 512]),
            rd=["y0t"], wr=["y0t"], dsem="dy0t")
        dma(s_, p0row[:], p0h[:, tt * 512:(tt + 1) * 512], rd=["p0row"],
            wr=["p0row"], dsem="dp0r")

        gelu_ffn("y1", None, w1b, [("w1b", kd) for kd in range(ND)], b1p,
                 "b1p", pw2bh, lambda kf: ("pw2bh", kf), b2p, "b2p", y1s,
                 None)

        p.emit(t_, "matmul", pbank(4, 128, 512), ones1[:], p0row[:],
               start=True, stop=True, rd=["ones1", "p0row", ("pbk", 4)],
               wr=[("pbk", 4)])
        for hh in range(ND):
            ap_ = apB[hh % 2]
            p.emit(v_, "tensor_tensor", ap_[:], y0t[:, hh * 512:(hh + 1) * 512],
                   y1s[:, hh * 512:(hh + 1) * 512], op=OP.subtract,
                   rd=["y0t", ("y1_y", hh)], wr=[("apB", hh % 2)])
            p.emit(v_, "tensor_tensor", ap_[:], ap_[:], pbank(4, 128, 512),
                   op=OP.mult, rd=[("apB", hh % 2), ("pbk", 4)],
                   wr=[("apB", hh % 2)])
            p.emit(v_, "tensor_tensor", y1s[:, hh * 512:(hh + 1) * 512],
                   y1s[:, hh * 512:(hh + 1) * 512], ap_[:], op=OP.add,
                   rd=[("apB", hh % 2)], wr=[("y1_y", hh)])

        def ffsrc(i):
            return y1s[:, i * 512:(i + 1) * 512]

        ns["srckeys"] = [("y1_y", i) for i in range(ND)]
        bca, bcb = ln_full("ln4", ffsrc, ND, D, 512)
        for i in range(ND):
            ap_ = apB[i % 2]
            p.emit(v_, "tensor_tensor", ap_[:], ffsrc(i), bca, op=OP.mult,
                   rd=[("y1_y", i), ("pbk", 2)], wr=[("apB", i % 2)])
            p.emit(v_, "tensor_tensor", ap_[:], ap_[:], bcb, op=OP.add,
                   rd=[("apB", i % 2), ("pbk", 3)], wr=[("apB", i % 2)])
            p.emit(v_, "tensor_tensor", x1b[:, i * 512:(i + 1) * 512],
                   x1src(i), ap_[:], op=OP.add, rd=[("apB", i % 2), "x1b"],
                   wr=["x1b"])
        dma(s_, fm3(out_fm[b, :, t0:t0 + 512]), sb3(x1b[:], 512),
            rd=["x1b"], wr=[("outh", tt)])
        p.last_w["x1b"] = p.last_w[("outh", tt)]

    p.wait_all("gpsimd")
    p.wait_all("vector")
    block = es.enter_context(nc.Block())
    p.replay(block)
    es.close()
    return nc



def _prep_shared(W_in, conv_w, conv_b, dt_bias, A_log, D_param, norm_w, W_out,
                 W1, b1, W2, b2, ln_g, ln_b, Wr, br, cfg):
    f32 = np.float32
    NIP = 35
    W_inT = np.ascontiguousarray(np.asarray(W_in).T).astype(f32)
    wpad = np.zeros((cfg.D, NIP * 128), f32)
    wpad[:, :4096] = W_inT[:, :4096]
    wpad[:, 32 * 128:32 * 128 + 64] = W_inT[:, 4096:4160]
    wpad[:, 33 * 128:33 * 128 + 64] = W_inT[:, 4160:4224]
    wpad[:, 34 * 128:34 * 128 + 16] = W_inT[:, 4224:4240]
    w_in_tiles = np.ascontiguousarray(
        wpad.reshape(cfg.D // 128, 128, NIP, 128).transpose(2, 0, 1, 3))
    cw = np.asarray(conv_w).reshape(2176, 4).astype(f32)
    cb = np.asarray(conv_b).reshape(2176).astype(f32)
    cw4 = np.zeros((128, 18 * 4), f32)
    cb18 = np.zeros((128, 18), f32)
    for c in range(16):
        cw4[:, c * 4:(c + 1) * 4] = cw[c * 128:(c + 1) * 128]
        cb18[:, c] = cb[c * 128:(c + 1) * 128]
    cw4[0:64, 16 * 4:17 * 4] = cw[2048:2112]
    cb18[0:64, 16] = cb[2048:2112]
    cw4[0:64, 17 * 4:18 * 4] = cw[2112:2176]
    cb18[0:64, 17] = cb[2112:2176]
    e16 = np.zeros((16, 16 * 128), f32)
    for h in range(16):
        e16[h, h * 128:(h + 1) * 128] = 1.0
    n1 = W1.size
    k1 = max(1, int(n1 * 0.1 / 2))
    n2 = W2.size
    k2 = max(1, int(n2 * 0.1 / 2))
    nb1 = b1.size
    kb1 = max(1, int(nb1 * 0.1 / 2))
    nb2 = b2.size
    kb2 = max(1, int(nb2 * 0.1 / 2))
    NC = cfg.n_cores
    return {
        "w_in_tiles": w_in_tiles.astype(f32),
        "w_outT": np.ascontiguousarray(np.asarray(W_out).T).astype(f32),
        "w1": np.ascontiguousarray(W1).astype(f32),
        "w2": np.ascontiguousarray(W2).astype(f32),
        "conv_w4": cw4,
        "conv_b17": cb18,
        "dtb": np.asarray(dt_bias).reshape(16, 1).astype(f32),
        "alog": np.asarray(A_log).reshape(16, 1).astype(f32),
        "dprm_row": np.asarray(D_param).reshape(1, 16).astype(f32),
        "normw": np.ascontiguousarray(
            np.asarray(norm_w).reshape(16, 128).T).astype(f32),
        "b1c": np.ascontiguousarray(np.asarray(b1).reshape(32, 128).T).astype(f32),
        "b2c": np.ascontiguousarray(np.asarray(b2).reshape(8, 128).T).astype(f32),
        "lgam": np.ascontiguousarray(np.asarray(ln_g).reshape(8, 128).T).astype(f32),
        "lbet": np.ascontiguousarray(np.asarray(ln_b).reshape(8, 128).T).astype(f32),
        "wr_in": np.ascontiguousarray(np.asarray(Wr).reshape(8, 128, 8)
            .transpose(1, 0, 2).reshape(128, 64)).astype(f32),
        "br_in": np.asarray(br).reshape(8, 1).astype(f32),
        "identf_in": np.eye(128, dtype=f32),
        "identb_in": np.eye(128).astype(BF),
        "trineg_in": np.where(
            np.arange(128)[:, None] <= np.arange(128)[None, :], 0.0, NEG
        ).astype(f32),
        "e16_in": e16,
        "kvec_in": np.array([[k1, n1 - k1 + 1, k2, n2 - k2 + 1,
                              kb1, nb1 - kb1 + 1,
                              kb2, nb2 - kb2 + 1]], f32),
        "bnds_in": np.array([[0.0] * 8 + [1024.0] * 8], f32),
    }


def kernel(L0_attn_hidden_states, L0_attn_W_in, L0_attn_conv_w, L0_attn_conv_b,
           L0_attn_dt_bias, L0_attn_A_log_alpha, L0_attn_D_param,
           L0_attn_norm_w, L0_attn_W_out, L0_ffn_W1, L0_ffn_b1, L0_ffn_W2,
           L0_ffn_b2, L0_ffn_ln_gamma, L0_ffn_ln_beta, L0_ffn_Wr, L0_ffn_br):
    cfg = CFG()
    x = np.asarray(L0_attn_hidden_states, np.float32)
    W1 = np.ascontiguousarray(np.asarray(L0_ffn_W1, np.float32))
    W2 = np.ascontiguousarray(np.asarray(L0_ffn_W2, np.float32))
    shared = _prep_shared(L0_attn_W_in, L0_attn_conv_w, L0_attn_conv_b,
                          L0_attn_dt_bias, L0_attn_A_log_alpha, L0_attn_D_param,
                          L0_attn_norm_w, L0_attn_W_out, W1, L0_ffn_b1, W2,
                          L0_ffn_b2, L0_ffn_ln_gamma, L0_ffn_ln_beta,
                          L0_ffn_Wr, L0_ffn_br, cfg)
    nc = build_program(cfg)
    in_maps = []
    for c in range(8):
        m = dict(shared)
        m["x_fm"] = np.ascontiguousarray(
            x[2 * c:2 * c + 2].transpose(0, 2, 1)).astype(np.float32)
        in_maps.append(m)
    res = run_bass_kernel_spmd(nc, in_maps, core_ids=list(range(8)))
    outs = []
    for c in range(8):
        o = np.asarray(res.results[c]["out_fm"])
        outs.append(np.ascontiguousarray(o.transpose(0, 2, 1)))
    return np.concatenate(outs, axis=0).astype(np.float32)


# version 08
# speedup vs baseline: 1.0507x; 1.0507x over previous
import contextlib

import numpy as np
import ml_dtypes
import concourse.bass as bass
import concourse.mybir as mybir
from concourse.bass_utils import run_bass_kernel_spmd

F32 = mybir.dt.float32
BF16 = mybir.dt.bfloat16
AF = mybir.ActivationFunctionType
OP = mybir.AluOpType
BF = ml_dtypes.bfloat16

GELU_C = 0.044715
GELU_S = 2.0 * 0.7978845608028654
EPS = 1e-5
NEG = -1e30


class CFG:
    def __init__(self, L=4096, NB=2, T=512, n_cores=8, bisect_iters=52,
                 debug=None):
        self.L, self.NB, self.T, self.C = L, NB, T, 128
        self.D, self.EI, self.CV, self.DIP = 1024, 2048, 2176, 4240
        self.NH, self.HD, self.NS, self.FF = 16, 128, 64, 4096
        self.n_cores = n_cores
        self.bisect_iters = bisect_iters
        self.debug = debug
        assert L % T == 0 and T % self.C == 0


ENGINES = ("tensor", "vector", "scalar", "gpsimd", "sync")
DSEMS = ("dxb", "dwi0", "dwi1", "dwi2", "dwo0", "dwo1", "dw2i0", "dw2i1", "dw2o0", "dw2o1", "dx1b",
         "dxl3", "dy0t", "dp0r", "dw20", "dw21", "dwst", "dcn0", "dcn1", "dc20", "dc21")
SEMS = ("T", "V", "S", "G", "DS", "DA", "DG", "DC") + DSEMS
ENG_SEM = {"tensor": "T", "vector": "V", "scalar": "S", "gpsimd": "G"}
DMA_SEM = {"sync": "DS", "scalar": "DA", "gpsimd": "DG"}


class Prog:

    def __init__(self, nc):
        self.nc = nc
        self.ops = {e: [] for e in ENGINES}
        self.cnt = {s: 0 for s in SEMS}
        self.hiw = {e: {s: 0 for s in SEMS} for e in ENGINES}
        self.last_w = {}
        self.readers = {}
        self.sems = {}
        self.pending_rd = {e: [] for e in ENGINES}

    def make_sems(self, es):
        for s in SEMS:
            self.sems[s] = es.enter_context(self.nc.semaphore("sem_" + s))

    def patch_total(self, keys, sem):
        tok = (sem, self.cnt[sem])
        for k in keys:
            self.last_w[k] = tok
            self.readers[k] = []

    def emit(self, eng, fn, *args, rd=(), wr=(), after=(), inc=True, dma=False,
             collective=False, dsem=None, **kw):
        toks = [t for t in after if t is not None]
        for k in rd:
            t = self.last_w.get(k)
            if t is not None:
                toks.append(t)
        for k in wr:
            t = self.last_w.get(k)
            if t is not None:
                toks.append(t)
            toks.extend(self.readers.get(k, ()))
        waits = []
        own = ENG_SEM.get(eng)
        for (sem, val) in toks:
            if sem == own and eng == "tensor":
                continue
            if val > self.hiw[eng][sem]:
                self.hiw[eng][sem] = val
                waits.append((sem, val))
        if dma:
            sem = dsem or ("DC" if collective else DMA_SEM[eng])
            amt = 16
        else:
            sem = ENG_SEM[eng]
            amt = 1
        if inc:
            self.cnt[sem] += amt
        tok = (sem, self.cnt[sem])
        self.ops[eng].append((fn, args, kw, waits, (sem, amt) if inc else None))
        if inc:
            for k in rd:
                self.readers.setdefault(k, []).append(tok)
            for k in self.pending_rd[eng]:
                self.readers.setdefault(k, []).append(tok)
            self.pending_rd[eng] = []
        else:
            self.pending_rd[eng].extend(rd)
        for k in wr:
            for e2 in ENGINES:
                assert k not in self.pending_rd[e2], \
                    f"write to {k} while {e2} has un-inc'd pending read"
            self.last_w[k] = tok
            self.readers[k] = []
        return tok

    def barrier(self):
        snap = dict(self.cnt)
        for e in ENGINES:
            for s in SEMS:
                v = snap[s]
                if s != ENG_SEM.get(e) and v > self.hiw[e][s]:
                    self.hiw[e][s] = v
                    self.ops[e].append((None, None, None, [(s, v)], None))

    def wait_all(self, eng):
        for s in SEMS:
            v = self.cnt[s]
            if v > self.hiw[eng][s]:
                self.hiw[eng][s] = v
                self.ops[eng].append((None, None, None, [(s, v)], None))

    def replay(self, block):
        prog = self

        def section(name):
            def run(eng):
                for (fn, args, kw, waits, inc) in prog.ops[name]:
                    for (sem, val) in waits:
                        eng.wait_ge(prog.sems[sem], val)
                    if fn is None:
                        continue
                    inst = getattr(eng, fn)(*args, **kw)
                    if inc is not None:
                        inst.then_inc(prog.sems[inc[0]], inc[1])
            return run

        block.tensor(section("tensor"))
        block.vector(section("vector"))
        block.scalar(section("scalar"))
        block.gpsimd(section("gpsimd"))
        block.sync(section("sync"))


class Arena:
    def __init__(self, handle, words):
        self.h = handle
        self.words = words
        self.off = 0
        self.marks = []

    def _take(self, n_words):
        n_words = (n_words + 1) & ~1
        assert self.off + n_words <= self.words, \
            f"arena overflow: {self.off}+{n_words} > {self.words}"
        o = self.off
        self.off += n_words
        return o

    def f32(self, parts, n):
        o = self._take(n)
        return self.h[0:parts, o:o + n]

    def bf16(self, parts, n):
        o = self._take((n + 1) // 2)
        return self.h[:].bitcast(BF16)[0:parts, 2 * o:2 * o + n]

    def push(self):
        self.marks.append(self.off)

    def pop(self):
        self.off = self.marks.pop()


def build_program(cfg: CFG):
    nc = bass.Bass("TRN2", target_bir_lowering=False)
    L, NB, T, C = cfg.L, cfg.NB, cfg.T, cfg.C
    D, EI, CV, DIP = cfg.D, cfg.EI, cfg.CV, cfg.DIP
    NH, HD, NS, FF = cfg.NH, cfg.HD, cfg.NS, cfg.FF
    ND, NE, NCV, NFF = D // 128, EI // 128, CV // 128, FF // 128
    NIP = 35
    NCVT = 18
    NT = L // T
    TOK = NB * L
    NTT = TOK // 512
    NCH = T // C

    def din(name, shape, dt=F32):
        return nc.dram_tensor(name, list(shape), dt, kind="ExternalInput").ap()

    def dint(name, shape, dt=F32):
        return nc.dram_tensor(name, list(shape), dt, kind="Internal").ap()

    x_fm = din("x_fm", [NB, D, L])
    w_in_tiles = din("w_in_tiles", [NIP, ND, 128, 128])
    w_outT = din("w_outT", [EI, D])
    w1 = din("w1", [D, FF])
    w2 = din("w2", [FF, D])
    conv_w4 = din("conv_w4", [128, NCVT * 4])
    conv_b17 = din("conv_b17", [128, NCVT])
    dtb_in = din("dtb", [NH, 1])
    alog_in = din("alog", [NH, 1])
    dprm_row = din("dprm_row", [1, NH])
    normw_in = din("normw", [128, NE])
    b1c_in = din("b1c", [128, NFF])
    b2c_in = din("b2c", [128, ND])
    lgam_in = din("lgam", [128, ND])
    lbet_in = din("lbet", [128, ND])
    wr_in = din("wr_in", [128, (D // 128) * 8])
    br_in = din("br_in", [8, 1])
    identf_in = din("identf_in", [128, 128])
    identb_in = din("identb_in", [128, 128], BF16)
    trineg_in = din("trineg_in", [128, 128])
    e16_in = din("e16_in", [NH, NH * 128])
    kvec_in = din("kvec_in", [1, 8])
    bnds_in = din("bnds_in", [1, 16])

    out_fm = nc.dram_tensor("out_fm", [NB, D, L], F32, kind="ExternalOutput").ap()

    x1h = dint("x1h", [NB, D, L])
    xln3h = dint("xln3h", [NB, D, L], BF16)
    y0h = dint("y0h", [NB, D, L], BF16)
    p0h = dint("p0h", [1, TOK])
    w2bh = dint("w2bh", [FF, D], BF16)
    w_inbh = dint("w_inbh", [NIP, ND, 128, 128], BF16)
    w_outbh = dint("w_outbh", [EI, D], BF16)
    dbgh = dint("dbgh", [1, 64])
    pw2bh = dint("pw2bh", [FF, D], BF16)

    es = contextlib.ExitStack()
    AR_P = Arena(es.enter_context(nc.sbuf_tensor("arP", [128, 4100], F32)), 4100)
    AR_A = Arena(es.enter_context(nc.sbuf_tensor("arA", [128, 48700], F32)), 48700)

    p = Prog(nc)
    p.make_sems(es)
    pb = [es.enter_context(nc.psum_tensor(f"pb{i}", [128, 512], F32))
          for i in range(8)]

    def pbank(i, parts=128, n=512):
        return pb[i][0:parts, 0:n]

    def pbank_b(i, parts=128, n=1024):
        return pb[i][:].bitcast(BF16)[0:parts, 0:n]

    identf = AR_P.f32(128, 128)
    identb = AR_P.bf16(128, 128)
    trineg = AR_P.f32(128, 128)
    e16 = AR_P.f32(NH, NH * 128)
    cw4 = AR_P.f32(128, NCVT * 4)
    cb17 = AR_P.f32(128, NCVT)
    dt_bias = AR_P.f32(NH, 1)
    a_log = AR_P.f32(NH, 1)
    d_row = AR_P.f32(1, NH)
    nw = AR_P.f32(128, NE)
    b1c = AR_P.f32(128, NFF)
    b2c = AR_P.f32(128, ND)
    gam = AR_P.f32(128, ND)
    bet = AR_P.f32(128, ND)
    wr_sb = AR_P.f32(128, ND * 8)
    br_sb = AR_P.f32(8, 1)
    kvec = AR_P.f32(1, 8)
    ones1 = AR_P.f32(1, 128)
    ones1b = AR_P.bf16(1, 128)
    onescol = AR_P.f32(128, 1)
    onescol_b = AR_P.bf16(128, 2)[:, 0:1]
    zero16 = AR_P.f32(NH, 1)
    eps_col = AR_P.f32(128, 1)
    one_col = AR_P.f32(128, 1)
    a_col = AR_P.f32(NH, 1)
    d_b = AR_P.f32(128, NH)
    wrg = AR_P.bf16(128, ND * 8)
    cgam = AR_P.f32(8, 1)
    cbet = AR_P.f32(8, 1)
    b1p = AR_P.f32(128, NFF)
    b2p = AR_P.f32(128, ND)
    blo = AR_P.f32(1, 8)
    bhi = AR_P.f32(1, 8)
    bmid = AR_P.f32(1, 8)
    bcols = AR_P.f32(128, 8)
    bst = AR_P.f32(1, 24)
    ab1 = AR_P.f32(128, NFF)
    ab2 = AR_P.f32(128, ND)

    g, s_, v_, t_, sc = "gpsimd", "sync", "vector", "tensor", "scalar"

    def dma(eng, out, in_, rd=(), wr=(), after=(), dsem=None):
        return p.emit(eng, "dma_start", rd=rd, wr=wr, after=after, dma=True,
                      dsem=dsem, out=out, in_=in_)

    def fm3(hbm_ap):
        return hbm_ap.rearrange("(k p) t -> p k t", p=128)

    def sb3(sbuf_ap, t):
        return sbuf_ap.rearrange("p (k t) -> p k t", t=t)

    dma(s_, identf[:], identf_in[:], wr=["identf"])
    dma(s_, trineg[:], trineg_in[:], wr=["trineg"])
    dma(s_, e16[:], e16_in[:], wr=["e16"])
    dma(s_, cw4[:], conv_w4[:], wr=["cw4"])
    dma(s_, cb17[:], conv_b17[:], wr=["cb17"])
    dma(s_, dt_bias[:], dtb_in[:], wr=["dtb"])
    dma(s_, a_log[:], alog_in[:], wr=["alog"])
    dma(s_, d_row[:], dprm_row[:], wr=["drow"])
    dma(s_, nw[:], normw_in[:], wr=["nw"])
    dma(s_, b1c[:], b1c_in[:], wr=["b1c"])
    dma(s_, b2c[:], b2c_in[:], wr=["b2c"])
    dma(s_, gam[:], lgam_in[:], wr=["gam"])
    dma(s_, bet[:], lbet_in[:], wr=["bet"])
    dma(s_, wr_sb[:], wr_in[:], wr=["wr"])
    dma(s_, br_sb[:], br_in[:], wr=["br"])
    dma(s_, kvec[:], kvec_in[:], wr=["kvec"])
    dma(s_, blo[:], bnds_in[:, 0:8], wr=["blo"])
    dma(s_, bhi[:], bnds_in[:, 8:16], wr=["bhi"])
    dma(s_, ab1[:], b1c_in[:], wr=["ab1"])
    dma(s_, ab2[:], b2c_in[:], wr=["ab2"])
    dma(g, identb[:], identb_in[:], wr=["identb"])
    p.patch_total(["identf", "trineg", "e16", "cw4", "cb17", "dtb", "alog",
                   "drow", "nw", "b1c", "b2c", "gam", "bet", "wr", "br",
                   "kvec", "blo", "bhi", "ab1", "ab2"], "DS")
    p.patch_total(["identb"], "DG")

    p.emit(g, "memset", ones1[:], 1.0, wr=["ones1"])
    p.emit(g, "memset", ones1b[:], 1.0, wr=["ones1b"])
    p.emit(g, "memset", onescol[:], 1.0, wr=["onescol"])
    p.emit(g, "memset", zero16[:], 0.0, wr=["zero16"])
    p.emit(g, "memset", bmid[:], 0.0, wr=["bmid"])
    p.emit(g, "memset", eps_col[:], float(EPS), wr=["eps_col"])
    p.emit(g, "memset", one_col[:], 1.0, wr=["one_col"])
    p.emit(v_, "tensor_copy", onescol_b[:], onescol[:], rd=["onescol"],
           wr=["onescolb"])
    p.emit(sc, "activation", a_col[:], a_log[:], AF.Exp, rd=["alog"],
           wr=["a_col"])
    p.emit(v_, "tensor_scalar", a_col[:], a_col[:], -1.0, None, op0=OP.mult,
           rd=["a_col"], wr=["a_col"])
    p.emit(t_, "matmul", pbank(6, 128, NH), ones1[:], d_row[:], start=True,
           stop=True, rd=["ones1", "drow", ("pbk", 6)], wr=[("pbk", 6)])
    p.emit(v_, "tensor_copy", d_b[:], pbank(6, 128, NH), rd=[("pbk", 6)],
           wr=["d_b"])
    p.emit(sc, "activation", ab1[:], ab1[:], AF.Abs,
           rd=["ab1"], wr=["ab1"])
    p.emit(sc, "activation", ab2[:], ab2[:], AF.Abs,
           rd=["ab2"], wr=["ab2"])

    w2stg = [AR_P.bf16(128, 1024) for _ in range(2)]
    for kf in range(NFF):
        pe = kf % 2
        dma(g, w2stg[pe][:], w2[kf * 128:(kf + 1) * 128, :],
            wr=[("w2stg", pe)], dsem="dw2i%d" % pe)
        dma(s_, w2bh[kf * 128:(kf + 1) * 128, :], w2stg[pe][:],
            rd=[("w2stg", pe)], wr=[("w2bh", kf), ("w2stg", pe)],
            dsem="dw2o%d" % pe)
    for e in range(NIP):
        pe = e % 2
        dma(g, w2stg[pe][:].rearrange("p (k q) -> p k q", q=128),
            w_in_tiles[e].rearrange("k p q -> p k q"),
            wr=[("w2stg", pe)], dsem="dw2i%d" % pe)
        dma(s_, w_inbh[e].rearrange("k p q -> p k q"),
            w2stg[pe][:].rearrange("p (k q) -> p k q", q=128),
            rd=[("w2stg", pe)], wr=[("wibh", e), ("w2stg", pe)], dsem="dw2o%d" % pe)
    for ke in range(NE):
        pe = ke % 2
        dma(g, w2stg[pe][:], w_outT[ke * 128:(ke + 1) * 128, :],
            wr=[("w2stg", pe)], dsem="dw2i%d" % pe)
        dma(s_, w_outbh[ke * 128:(ke + 1) * 128, :], w2stg[pe][:],
            rd=[("w2stg", pe)], wr=[("wobh", ke), ("w2stg", pe)],
            dsem="dw2o%d" % pe)

    ns = {}

    def ln_full(tag, src_fn, n_tiles, n_feat, tcols, src_bf16=False):
        arow, brow, crow = ns["arow"], ns["brow"], ns["crow"]
        sqp = ns["sq"]
        oc, ock = (onescol_b, "onescolb") if src_bf16 else (onescol, "onescol")
        s1 = pbank(2, 1, tcols)
        s2 = pbank(3, 1, tcols)
        srckeys = ns["srckeys"]
        for i in range(n_tiles):
            p.emit(t_, "matmul", s1, oc[:], src_fn(i),
                   start=(i == 0), stop=(i == n_tiles - 1),
                   rd=[ock, srckeys[i]],
                   wr=[("pbk", 2)] if i in (0, n_tiles - 1) else (),
                   inc=(i == n_tiles - 1))
        for i in range(n_tiles):
            sq_scr = sqp[i % 2]
            p.emit(sc, "activation", sq_scr[:, 0:tcols], src_fn(i), AF.Square,
                   rd=[srckeys[i]], wr=[("sq_scr", i % 2)])
            p.emit(t_, "matmul", s2, onescol_b[:], sq_scr[:, 0:tcols],
                   start=(i == 0), stop=(i == n_tiles - 1),
                   rd=["onescolb", ("sq_scr", i % 2)],
                   wr=[("pbk", 3)] if i in (0, n_tiles - 1) else (), inc=True)
        mu = arow[:, 0:tcols]
        p.emit(v_, "tensor_scalar", mu, s1, 1.0 / n_feat, None, op0=OP.mult,
               rd=[("pbk", 2)], wr=["arow"])
        mu2 = crow[:, 0:tcols]
        p.emit(v_, "tensor_tensor", mu2, mu, mu, op=OP.mult, rd=["arow"],
               wr=["crow"])
        var = brow[:, 0:tcols]
        p.emit(v_, "scalar_tensor_tensor", var, s2, 1.0 / n_feat, mu2,
               op0=OP.mult, op1=OP.subtract, rd=[("pbk", 3), "crow"],
               wr=["brow"])
        p.emit(sc, "activation", var, var, AF.Ln, bias=eps_col[0:1, :],
               rd=["brow", "eps_col"], wr=["brow"])
        rstd_b = brow[:].bitcast(BF16)[0:1, 0:tcols]
        p.emit(sc, "activation", rstd_b, var, AF.Exp, scale=-0.5, rd=["brow"],
               wr=["brow"])
        brw_b = crow[:].bitcast(BF16)[0:1, 0:tcols]
        p.emit(v_, "scalar_tensor_tensor", brw_b, mu, -1.0, rstd_b,
               op0=OP.mult, op1=OP.mult, rd=["arow", "brow"], wr=["crow"])
        bca = pbank(2, 128, tcols)
        bcb = pbank(3, 128, tcols)
        p.emit(t_, "matmul", bca, ones1b[:], rstd_b, start=True, stop=True,
               rd=["ones1b", "brow", ("pbk", 2)], wr=[("pbk", 2)])
        p.emit(t_, "matmul", bcb, ones1b[:], brw_b, start=True, stop=True,
               rd=["ones1b", "crow", ("pbk", 3)], wr=[("pbk", 3)])
        return bca, bcb

    AR_A.push()
    wi_buf = [AR_A.bf16(128, ND * 128) for _ in range(3)]
    wo_buf = [AR_A.bf16(128, 1024) for _ in range(2)]
    diag = [AR_A.bf16(128, 4 * 128) for _ in range(2)]
    xb = AR_A.f32(128, ND * T)
    xln1 = AR_A.bf16(128, ND * T)
    z_silu = AR_A.bf16(128, NE * T)
    xbc = AR_A.bf16(128, NCVT * (T + 3))
    y_n = AR_A.bf16(128, NE * T)
    m_o = AR_A.f32(128, ND * T)
    sqA = AR_A.bf16(128, 512)
    halo = AR_A.bf16(128, NCVT * 4)
    dt_sp = AR_A.f32(NH, T)
    s_all = AR_A.f32(NH, T)
    sig_all = AR_A.f32(NH, T)
    esc_c = AR_A.f32(NH, 1)
    es_b = [AR_A.f32(NS, C) for _ in range(2)]
    w_st = AR_A.f32(NH, C)
    sT = AR_A.f32(128, NH)
    sigT = AR_A.f32(128, NH)
    wT = AR_A.f32(128, NH)
    escb = AR_A.f32(NS, NH)
    g_s = AR_A.f32(128, C)
    mask_s = [AR_A.f32(128, C) for _ in range(2)]
    gmb = [AR_A.bf16(128, C) for _ in range(2)]
    x_tm = [AR_A.bf16(128, C) for _ in range(2)]
    x_til = [AR_A.bf16(128, C) for _ in range(2)]
    b_tm = AR_A.bf16(128, NS)
    c_til = [AR_A.bf16(NS, C) for _ in range(2)]
    ht = AR_A.f32(NS, NH * HD)
    ht_b = [AR_A.bf16(NS, HD) for _ in range(2)]
    y_g = AR_A.bf16(128, NH * C)
    dx = [AR_A.f32(128, C) for _ in range(2)]
    apA = [AR_A.f32(128, T) for _ in range(2)]
    rsc = AR_A.f32(1, 16)
    arowA = AR_A.f32(1, T)
    browA = AR_A.f32(1, T)
    crowA = AR_A.f32(1, T)
    sqA2 = AR_A.bf16(128, 512)
    ns.update(arow=arowA, brow=browA, crow=crowA, sq=(sqA, sqA2))

    for mt in range(NB * NT):
        b, m = mt // NT, mt % NT
        t0 = m * T

        dma(s_, sb3(xb[:], T), fm3(x_fm[b, :, t0:t0 + T]),
            rd=["xb"], wr=["xb"], dsem="dxb")
        if m == 0:
            p.emit(v_, "memset", ht[:], 0.0,
                   rd=[("ht", h) for h in range(NH)],
                   wr=[("ht", h) for h in range(NH)])

        def xsrc(i):
            return xb[:, i * T:(i + 1) * T]

        ns["srckeys"] = ["xb"] * ND
        bca, bcb = ln_full("ln1", xsrc, ND, D, T)
        for i in range(ND):
            ap_ = apA[i % 2]
            p.emit(v_, "tensor_tensor", ap_[:], xsrc(i), bca, op=OP.mult,
                   rd=["xb", ("pbk", 2)], wr=[("apA", i % 2)])
            p.emit(v_, "tensor_tensor", xln1[:, i * T:(i + 1) * T], ap_[:],
                   bcb, op=OP.add, rd=[("apA", i % 2), ("pbk", 3)],
                   wr=[("xln1", i)])

        for e in range(NIP):
            pe = e % 3
            bk = e % 4
            dma(s_, wi_buf[pe][:].rearrange("p (k q) -> p k q", q=128),
                w_inbh[e].rearrange("k p q -> p k q"),
                after=[p.last_w.get(("wibh", e))],
                wr=[("wi", pe)], dsem="dwi%d" % pe)
            rows_e = 128 if e < 32 else (64 if e < 34 else NH)
            ps = pbank(bk, rows_e, T)
            for kd in range(ND):
                p.emit(t_, "matmul", ps,
                       wi_buf[pe][:, kd * 128:kd * 128 + rows_e],
                       xln1[:, kd * T:(kd + 1) * T],
                       start=(kd == 0), stop=(kd == ND - 1),
                       rd=[("wi", pe), ("xln1", kd)],
                       wr=([("pbk", bk)] if kd in (0, ND - 1) else []) +
                          ([("wi", pe)] if kd == ND - 1 else []),
                       inc=(kd == ND - 1))
            if e < NE:
                p.emit(sc, "activation", z_silu[:, e * T:(e + 1) * T], ps,
                       AF.Silu, rd=[("pbk", bk)], wr=[("zs", e)])
            elif e < 34:
                c = e - NE
                p.emit(v_, "tensor_copy",
                       xbc[0:rows_e, c * (T + 3) + 3:c * (T + 3) + 3 + T], ps,
                       rd=[("pbk", bk)], wr=[("xbc", c)])
            else:
                p.emit(sc, "activation", dt_sp[:, 0:T], ps, AF.Relu,
                       bias=dt_bias[:], rd=[("pbk", bk), "dtb"], wr=["dt_sp"])
                p.emit(sc, "activation", s_all[:, 0:T], ps, AF.Abs,
                       bias=dt_bias[:], rd=[("pbk", bk), "dtb"], wr=["s_all"])
                p.emit(sc, "activation", sig_all[:, 0:T], s_all[:, 0:T],
                       AF.Exp, scale=-1.0, rd=["s_all"], wr=["sig_all"])
                p.emit(sc, "activation", sig_all[:, 0:T], sig_all[:, 0:T],
                       AF.Ln, bias=one_col[0:NH, :], rd=["sig_all", "one_col"],
                       wr=["sig_all"])
                p.emit(v_, "tensor_tensor", dt_sp[:, 0:T], dt_sp[:, 0:T],
                       sig_all[:, 0:T], op=OP.add, rd=["dt_sp", "sig_all"],
                       wr=["dt_sp"])

        for c in range(NCVT):
            base = c * (T + 3)
            rc = 128 if c < 16 else 64
            if m == 0:
                p.emit(v_, "memset", xbc[0:rc, base:base + 3], 0.0,
                       rd=[("xbc", c)], wr=[("xbc", c)])
            else:
                p.emit(v_, "tensor_copy", xbc[0:rc, base:base + 3],
                       halo[0:rc, c * 4:c * 4 + 3], rd=[("halo", c), ("xbc", c)],
                       wr=[("xbc", c)])
        for c in range(NCVT):
            base = c * (T + 3)
            rc = 128 if c < 16 else 64
            pe = c % 2
            bk = c % 4
            for j in range(4):
                p.emit(v_, "tensor_scalar",
                       diag[pe][0:rc, j * 128:j * 128 + rc],
                       identb[0:rc, 0:rc], cw4[0:rc, c * 4 + j:c * 4 + j + 1],
                       None, op0=OP.mult, rd=["identb", "cw4", ("diag", pe)],
                       wr=[("diag", pe)] if j == 3 else ())
            ps = pbank(bk, rc, T)
            for j in range(4):
                p.emit(t_, "matmul", ps, diag[pe][0:rc, j * 128:j * 128 + rc],
                       xbc[0:rc, base + j:base + j + T],
                       start=(j == 0), stop=(j == 3),
                       rd=[("diag", pe), ("xbc", c)],
                       wr=([("pbk", bk)] if j in (0, 3) else []) +
                          ([("diag", pe)] if j == 3 else []),
                       inc=(j == 3))
            p.emit(v_, "tensor_copy", halo[0:rc, c * 4:c * 4 + 3],
                   xbc[0:rc, base + T:base + T + 3], rd=[("xbc", c)],
                   wr=[("halo", c)])
            p.emit(sc, "activation", xbc[0:rc, base + 3:base + 3 + T], ps,
                   AF.Silu, bias=cb17[0:rc, c:c + 1],
                   rd=[("pbk", bk), "cb17", ("halo", c)], wr=[("xbc", c)])

        p.emit(v_, "tensor_scalar", s_all[:, 0:T], dt_sp[:, 0:T], a_col[:],
               None, op0=OP.mult, rd=["dt_sp", "a_col"], wr=["s_all"])
        p.emit(sc, "activation", sig_all[:, 0:T], dt_sp[:, 0:T], AF.Ln,
               rd=["dt_sp"], wr=["sig_all"])
        for ch in range(NCH):
            sl = slice(ch * C, (ch + 1) * C)
            p.emit(v_, "tensor_tensor_scan", s_all[:, sl], s_all[:, sl],
                   s_all[:, sl], zero16[:], op0=OP.add, op1=OP.bypass,
                   rd=["s_all", "zero16"], wr=["s_all"])
        p.emit(v_, "tensor_tensor", sig_all[:, 0:T], s_all[:, 0:T],
               sig_all[:, 0:T], op=OP.subtract, rd=["s_all", "sig_all"],
               wr=["sig_all"])

        for ch in range(NCH):
            t_sl = slice(ch * C, (ch + 1) * C)
            p.emit(t_, "transpose", pbank(4, 128, NH), sig_all[:, t_sl],
                   identf[0:NH, 0:NH], rd=["sig_all", "identf", ("pbk", 4)],
                   wr=[("pbk", 4)])
            p.emit(v_, "tensor_copy", sigT[:, 0:NH], pbank(4, 128, NH),
                   rd=[("pbk", 4)], wr=["sigT"])
            p.emit(v_, "tensor_scalar", w_st[:], sig_all[:, t_sl], -1.0,
                   s_all[:, ch * C + C - 1:ch * C + C], op0=OP.mult,
                   op1=OP.add, rd=["sig_all", "s_all"], wr=["w_st"])
            p.emit(sc, "activation", w_st[:], w_st[:], AF.Exp, rd=["w_st"],
                   wr=["w_st"])
            p.emit(t_, "transpose", pbank(4, 128, NH), w_st[:],
                   identf[0:NH, 0:NH], rd=["w_st", "identf", ("pbk", 4)],
                   wr=[("pbk", 4)])
            p.emit(v_, "tensor_copy", wT[:, 0:NH], pbank(4, 128, NH),
                   rd=[("pbk", 4)], wr=["wT"])
            p.emit(sc, "activation", esc_c[:], s_all[:, ch * C + C - 1:ch * C + C],
                   AF.Exp, rd=["s_all"], wr=["esc_c"])
            p.emit(t_, "transpose", pbank(4, 1, NH), esc_c[:],
                   identf[0:NH, 0:NH],
                   rd=["esc_c", "identf", ("pbk", 4)], wr=[("pbk", 4)])
            p.emit(v_, "tensor_copy", rsc[:, 0:NH], pbank(4, 1, NH),
                   rd=[("pbk", 4)], wr=["rsc"])
            p.emit(t_, "matmul", pbank(5, NS, NH), ones1[:, 0:NS],
                   rsc[:, 0:NH], start=True, stop=True,
                   rd=["ones1", "rsc", ("pbk", 5)], wr=[("pbk", 5)])
            p.emit(v_, "tensor_copy", escb[:, 0:NH], pbank(5, NS, NH),
                   rd=[("pbk", 5)], wr=["escb"])
            bbB = 16 * (T + 3) + 3 + ch * C
            bbC = 17 * (T + 3) + 3 + ch * C
            B_sl = xbc[0:NS, bbB:bbB + C]
            C_sl = xbc[0:NS, bbC:bbC + C]
            p.emit(t_, "matmul", pbank(4, 128, C), B_sl, C_sl, start=True,
                   stop=True, rd=[("xbc", 16), ("xbc", 17), ("pbk", 4)], wr=[("pbk", 4)])
            p.emit(v_, "tensor_copy", g_s[:], pbank(4, 128, C),
                   rd=[("pbk", 4)], wr=["g_s"])
            p.emit(t_, "transpose", pbank_b(4, 128, NS), B_sl,
                   identb[0:NS, 0:NS], rd=[("xbc", 16), "identb", ("pbk", 4)],
                   wr=[("pbk", 4)])
            p.emit(v_, "tensor_copy", b_tm[:, 0:NS], pbank_b(4, 128, NS),
                   rd=[("pbk", 4)], wr=["b_tm"])

            for h in range(NH):
                pe = h % 2
                yb = 6 + pe
                tb = 4 + pe
                rb = 0 + pe
                cb = 2 + pe
                xoff = h * (T + 3) + 3 + ch * C
                p.emit(t_, "transpose", pbank_b(tb, 128, C),
                       xbc[:, xoff:xoff + C], identb[:],
                       rd=[("xbc", h), "identb", ("pbk", tb)], wr=[("pbk", tb)])
                p.emit(sc, "activation", x_tm[pe][:], pbank_b(tb, 128, C),
                       AF.Identity, rd=[("pbk", tb)], wr=[("x_tm", pe)])
                p.emit(t_, "matmul", pbank(rb, 128, C),
                       e16[:, h * 128:(h + 1) * 128], s_all[:, t_sl],
                       start=True, stop=True,
                       rd=["e16", "s_all", ("pbk", rb)], wr=[("pbk", rb)])
                p.emit(v_, "scalar_tensor_tensor", mask_s[pe][:],
                       pbank(rb, 128, C), sigT[:, h:h + 1], trineg[:],
                       op0=OP.subtract, op1=OP.add,
                       rd=[("pbk", rb), "sigT", "trineg"],
                       wr=[("mask", pe)])
                p.emit(sc, "activation", mask_s[pe][:], mask_s[pe][:], AF.Exp,
                       rd=[("mask", pe)], wr=[("mask", pe)])
                p.emit(v_, "tensor_tensor", gmb[pe][:], g_s[:], mask_s[pe][:],
                       op=OP.mult, rd=["g_s", ("mask", pe)], wr=[("gmb", pe)])
                p.emit(sc, "activation", es_b[pe][:], pbank(rb, NS, C),
                       AF.Exp, rd=[("pbk", rb)], wr=[("esb", pe)])
                p.emit(v_, "tensor_tensor", c_til[pe][:, 0:C], es_b[pe][:],
                       C_sl, op=OP.mult, rd=[("esb", pe), ("xbc", 17)],
                       wr=[("ctil", pe)])
                p.emit(v_, "tensor_copy", ht_b[pe][:],
                       ht[:, h * HD:(h + 1) * HD], rd=[("ht", h)],
                       wr=[("htb", pe)])
                p.emit(t_, "matmul", pbank(yb, 128, C), x_tm[pe][:], gmb[pe][:],
                       start=True, stop=False,
                       rd=[("x_tm", pe), ("gmb", pe)], wr=[("pbk", yb)],
                       inc=False)
                p.emit(t_, "matmul", pbank(yb, 128, C), ht_b[pe][:],
                       c_til[pe][:, 0:C], start=False, stop=True,
                       rd=[("htb", pe), ("ctil", pe)], wr=[("pbk", yb)])
                p.emit(v_, "tensor_scalar", x_til[pe][:], x_tm[pe][:],
                       wT[:, h:h + 1], None, op0=OP.mult,
                       rd=[("x_tm", pe), "wT"], wr=[("xtil", pe)])
                p.emit(t_, "matmul", pbank(rb, NS, HD), b_tm[:, 0:NS],
                       x_til[pe][:], start=True, stop=True,
                       rd=["b_tm", ("xtil", pe), ("pbk", rb)], wr=[("pbk", rb)])
                p.emit(v_, "scalar_tensor_tensor", ht[:, h * HD:(h + 1) * HD],
                       ht[:, h * HD:(h + 1) * HD], escb[:, h:h + 1],
                       pbank(rb, NS, HD), op0=OP.mult, op1=OP.add,
                       rd=[("ht", h), "escb", ("pbk", rb), ("htb", pe)],
                       wr=[("ht", h)])
                p.emit(v_, "scalar_tensor_tensor", dx[pe][:],
                       xbc[:, xoff:xoff + C], d_b[:, h:h + 1],
                       pbank(yb, 128, C), op0=OP.mult, op1=OP.add,
                       rd=[("xbc", h), "d_b", ("pbk", yb)],
                       wr=[("dx", pe), ("pbk", yb)])
                p.emit(v_, "tensor_tensor", y_g[:, h * C:(h + 1) * C],
                       dx[pe][:], z_silu[:, h * T + ch * C:h * T + ch * C + C],
                       op=OP.mult, rd=[("dx", pe), ("zs", h)], wr=[("y_g", h)])
            for h in range(NH):
                sq_ = (sqA, sqA2)[h % 2]
                p.emit(sc, "activation", sq_[:, 0:C], y_g[:, h * C:(h + 1) * C],
                       AF.Square, rd=[("y_g", h)], wr=[("sq_scr", h % 2)])
                p.emit(t_, "matmul", pbank(5, 1, C), onescol_b[:],
                       sq_[:, 0:C], start=(h == 0), stop=(h == NH - 1),
                       rd=["onescolb", ("sq_scr", h % 2)],
                       wr=[("pbk", 5)] if h in (0, NH - 1) else (), inc=True)
            p.emit(v_, "tensor_scalar", crowA[0:1, 0:C], pbank(5, 1, C),
                   1.0 / EI, None, op0=OP.mult, rd=[("pbk", 5)], wr=["crow"])
            p.emit(sc, "activation", crowA[0:1, 0:C], crowA[0:1, 0:C], AF.Ln,
                   bias=eps_col[0:1, :], rd=["crow", "eps_col"], wr=["crow"])
            rrb_b = crowA[:].bitcast(BF16)[0:1, 0:C]
            p.emit(sc, "activation", rrb_b, crowA[0:1, 0:C], AF.Exp,
                   scale=-0.5, rd=["crow"], wr=["crow"])
            p.emit(t_, "matmul", pbank(5, 128, C), ones1b[:], rrb_b,
                   start=True, stop=True, rd=["ones1b", "crow", ("pbk", 5)],
                   wr=[("pbk", 5)])
            for h in range(NH):
                p.emit(v_, "scalar_tensor_tensor",
                       y_n[:, h * T + ch * C:h * T + ch * C + C],
                       y_g[:, h * C:(h + 1) * C], nw[:, h:h + 1],
                       pbank(5, 128, C), op0=OP.mult, op1=OP.mult,
                       rd=[("y_g", h), "nw", ("pbk", 5)], wr=[("y_n", h)])

        for ke in range(NE):
            pe = ke % 2
            dma(s_, wo_buf[pe][:], w_outbh[ke * 128:(ke + 1) * 128, :],
                after=[p.last_w.get(("wobh", ke))],
                wr=[("wo", pe)], dsem="dwo%d" % pe)
            for dd in range(ND):
                p.emit(t_, "matmul", pbank(dd, 128, T),
                       wo_buf[pe][:, dd * 128:(dd + 1) * 128],
                       y_n[:, ke * T:(ke + 1) * T],
                       start=(ke == 0), stop=(ke == NE - 1),
                       rd=[("wo", pe), ("y_n", ke)],
                       wr=([("pbk", dd)] if ke in (0, NE - 1) else []) +
                          ([("wo", pe)] if dd == ND - 1 else []),
                       inc=(ke == NE - 1 or dd == ND - 1))
        for dd in range(ND):
            p.emit(v_, "tensor_copy", m_o[:, dd * T:(dd + 1) * T],
                   pbank(dd, 128, T), rd=[("pbk", dd)], wr=[("m_o", dd)])

        def mosrc(i):
            return m_o[:, i * T:(i + 1) * T]

        ns["srckeys"] = [("m_o", i) for i in range(ND)]
        bca, bcb = ln_full("ln2", mosrc, ND, D, T)
        for i in range(ND):
            ap_ = apA[i % 2]
            p.emit(v_, "tensor_tensor", ap_[:], mosrc(i), bca, op=OP.mult,
                   rd=[("m_o", i), ("pbk", 2)], wr=[("apA", i % 2)])
            p.emit(v_, "tensor_tensor", ap_[:], ap_[:], bcb, op=OP.add,
                   rd=[("apA", i % 2), ("pbk", 3)], wr=[("apA", i % 2)])
            p.emit(v_, "tensor_tensor", xb[:, i * T:(i + 1) * T], xsrc(i),
                   ap_[:], op=OP.add, rd=[("apA", i % 2), "xb"], wr=["xb"])
        tgt = out_fm if cfg.debug == "x1" else x1h
        dma(s_, fm3(tgt[b, :, t0:t0 + T]), sb3(xb[:], T),
            rd=["xb"], wr=[("x1h", mt)])
        p.last_w["xb"] = p.last_w[("x1h", mt)]

    AR_A.pop()

    if cfg.debug == "x1":
        p.wait_all("gpsimd")
        block = es.enter_context(nc.Block())
        p.replay(block)
        es.close()
        return nc

    p.barrier()
    AR_A.push()
    w1b = AR_A.bf16(128, ND * FF)
    w2buf = [AR_A.bf16(128, 1024) for _ in range(2)]
    x1b = AR_A.f32(128, ND * 512)
    xln3 = AR_A.bf16(128, ND * 512)
    act = AR_A.bf16(128, NFF * 512)
    h1b = [AR_A.bf16(128, 512) for _ in range(2)]
    x2b = [AR_A.bf16(128, 512) for _ in range(2)]
    ab_ = [AR_A.bf16(128, 512) for _ in range(2)]
    sgb = [AR_A.bf16(128, 512) for _ in range(2)]
    y0st = AR_A.bf16(128, ND * 512)
    lg_sb = AR_A.f32(8, 512)
    lt_sb = AR_A.f32(128, 8)
    nmax = AR_A.f32(128, 1)
    qex = AR_A.f32(128, 8)
    ssum = AR_A.f32(128, 1)
    p0c = AR_A.f32(128, 1)
    p0row = AR_A.f32(1, 512)
    sqB = AR_A.bf16(128, 512)
    sqB2 = AR_A.bf16(128, 512)
    apB = [AR_A.f32(128, 512) for _ in range(2)]
    arowB = AR_A.f32(1, 512)
    browB = AR_A.f32(1, 512)
    crowB = AR_A.f32(1, 512)
    cntc = AR_A.f32(128, 1)
    cnt_loc = AR_A.f32(1, 8)
    cacc = AR_A.f32(1, 128)
    cacc2 = AR_A.f32(128, 64)
    wblk = AR_A.f32(128, 8192)
    wcnt1 = [wblk[:, 0:2048], wblk[:, 2048:4096]]
    wcnt2 = [wblk[:, 4096:6144], wblk[:, 6144:8192]]
    wstgA = wblk[:, 4096:5120]
    absA = wblk[:, 5120:6144]
    wstgB = wblk[:].bitcast(BF16)[:, 2 * 6144:2 * 6144 + 1024]
    y1s = wblk[:, 0:4096]
    ns.update(arow=arowB, brow=browB, crow=crowB, sq=(sqB, sqB2))

    for kd in range(ND):
        dma(g, w1b[:, kd * FF:(kd + 1) * FF], w1[kd * 128:(kd + 1) * 128, :],
            wr=[("w1b", kd)])
    p.patch_total([("w1b", kd) for kd in range(ND)], "DG")

    for i in range(ND):
        p.emit(v_, "tensor_scalar", wrg[:, i * 8:(i + 1) * 8],
               wr_sb[:, i * 8:(i + 1) * 8], gam[:, i:i + 1], None, op0=OP.mult,
               rd=["wr", "gam"], wr=[("wrg", i)])
        p.emit(t_, "matmul", pbank(4, 8, 1), wr_sb[:, i * 8:(i + 1) * 8],
               gam[:, i:i + 1], start=(i == 0), stop=(i == ND - 1),
               rd=["wr", "gam"],
               wr=[("pbk", 4)] if i in (0, ND - 1) else (), inc=(i == ND - 1))
    p.emit(v_, "tensor_copy", cgam[:], pbank(4, 8, 1), rd=[("pbk", 4)],
           wr=["cgam"])
    for i in range(ND):
        p.emit(t_, "matmul", pbank(4, 8, 1), wr_sb[:, i * 8:(i + 1) * 8],
               bet[:, i:i + 1], start=(i == 0), stop=(i == ND - 1),
               rd=["wr", "bet"],
               wr=[("pbk", 4)] if i in (0, ND - 1) else (), inc=(i == ND - 1))
    p.emit(v_, "tensor_tensor", cbet[:], pbank(4, 8, 1), br_sb[:], op=OP.add,
           rd=[("pbk", 4), "br"], wr=["cbet"])

    def gelu_ffn(tag, xln3_keys, w1_ap, w1_keys, bias_ap, bias_key, w2_src,
                 w2_keys, bias2_ap, bias2_key, ydst, ydst_f32):
        for f in range(NFF):
            bk = f % 4
            sp = f % 2
            ps = pbank(bk, 128, 512)
            for kd in range(ND):
                p.emit(t_, "matmul", ps,
                       w1_ap[:, kd * FF + f * 128:kd * FF + (f + 1) * 128],
                       xln3[:, kd * 512:(kd + 1) * 512],
                       start=(kd == 0), stop=(kd == ND - 1),
                       rd=[w1_keys[kd], ("xln3", kd)],
                       wr=[("pbk", bk)] if kd in (0, ND - 1) else (),
                       inc=(kd == ND - 1))
            p.emit(sc, "activation", h1b[sp][:], ps, AF.Identity,
                   bias=bias_ap[:, f:f + 1], rd=[("pbk", bk), bias_key],
                   wr=[("h1b", sp), ("pbk", bk)])
            p.emit(v_, "tensor_tensor", x2b[sp][:], h1b[sp][:], h1b[sp][:],
                   op=OP.mult, rd=[("h1b", sp)], wr=[("x2b", sp)])
            p.emit(v_, "tensor_scalar", ab_[sp][:], x2b[sp][:], GELU_C, 1.0,
                   op0=OP.mult, op1=OP.add, rd=[("x2b", sp)], wr=[("ab_", sp)])
            p.emit(v_, "tensor_tensor", ab_[sp][:], ab_[sp][:], h1b[sp][:],
                   op=OP.mult, rd=[("ab_", sp), ("h1b", sp)], wr=[("ab_", sp)])
            p.emit(sc, "activation", sgb[sp][:], ab_[sp][:], AF.Sigmoid,
                   scale=GELU_S, rd=[("ab_", sp)], wr=[("sgb", sp)])
            p.emit(v_, "tensor_tensor", act[:, f * 512:(f + 1) * 512],
                   h1b[sp][:], sgb[sp][:], op=OP.mult,
                   rd=[("h1b", sp), ("sgb", sp)], wr=[("act", f)])
        for kf in range(NFF):
            pe = kf % 2
            dma(s_, w2buf[pe][:], w2_src[kf * 128:(kf + 1) * 128, :],
                after=[p.last_w.get(w2_keys(kf))],
                wr=[("w2buf", pe)], dsem="dw2%d" % pe)
            for hh in range(ND):
                p.emit(t_, "matmul", pbank(hh, 128, 512),
                       w2buf[pe][:, hh * 128:(hh + 1) * 128],
                       act[:, kf * 512:(kf + 1) * 512],
                       start=(kf == 0), stop=(kf == NFF - 1),
                       rd=[("w2buf", pe), ("act", kf)],
                       wr=([("pbk", hh)] if kf in (0, NFF - 1) else []) +
                          ([("w2buf", pe)] if hh == ND - 1 else []),
                       inc=(kf == NFF - 1 or hh == ND - 1))
        for hh in range(ND):
            p.emit(sc, "activation", ydst[:, hh * 512:(hh + 1) * 512],
                   pbank(hh, 128, 512), AF.Identity,
                   bias=bias2_ap[:, hh:hh + 1],
                   rd=[("pbk", hh), bias2_key], wr=[(tag + "_y", hh)])

    for tt in range(NTT):
        b = (tt * 512) // L
        t0 = (tt * 512) % L
        dma(s_, sb3(x1b[:], 512), fm3(x1h[b, :, t0:t0 + 512]),
            rd=["x1b"], wr=["x1b"], dsem="dx1b")

        def x1src(i):
            return x1b[:, i * 512:(i + 1) * 512]

        ns["srckeys"] = ["x1b"] * ND
        bca, bcb = ln_full("ln3", x1src, ND, D, 512)
        for i in range(ND):
            ap_ = apB[i % 2]
            p.emit(v_, "tensor_tensor", ap_[:], x1src(i), bca, op=OP.mult,
                   rd=["x1b", ("pbk", 2)], wr=[("apB", i % 2)])
            p.emit(v_, "tensor_tensor", xln3[:, i * 512:(i + 1) * 512],
                   ap_[:], bcb, op=OP.add, rd=[("apB", i % 2), ("pbk", 3)],
                   wr=[("xln3", i)])
        dma(s_, fm3(xln3h[b, :, t0:t0 + 512]), sb3(xln3[:], 512),
            rd=[("xln3", i) for i in range(ND)], wr=[("xln3h", tt)])

        def x3src(i):
            return xln3[:, i * 512:(i + 1) * 512]

        ns["srckeys"] = [("xln3", i) for i in range(ND)]
        bca_r, bcb_r = ln_full("lnr", x3src, ND, D, 512, src_bf16=True)
        lps = pbank(4, 8, 512)
        for i in range(ND):
            p.emit(t_, "matmul", lps, wrg[:, i * 8:(i + 1) * 8], x3src(i),
                   start=(i == 0), stop=(i == ND - 1),
                   rd=[("wrg", i), ("xln3", i)],
                   wr=[("pbk", 4)] if i in (0, ND - 1) else (), inc=(i == ND - 1))
        p.emit(v_, "tensor_copy", lg_sb[:], lps, rd=[("pbk", 4)], wr=["lg"])
        p.emit(v_, "tensor_tensor", lg_sb[:], lg_sb[:], pbank(2, 8, 512),
               op=OP.mult, rd=["lg", ("pbk", 2), ("pbk", 4)], wr=["lg"])
        p.emit(v_, "scalar_tensor_tensor", lg_sb[:], pbank(3, 8, 512),
               cgam[:], lg_sb[:], op0=OP.mult, op1=OP.add,
               rd=[("pbk", 3), "cgam", "lg", ("pbk", 4)], wr=["lg"])
        p.emit(v_, "tensor_scalar", lg_sb[:], lg_sb[:], cbet[:], None,
               op0=OP.add, rd=["lg", "cbet"], wr=["lg"])
        for st in range(4):
            p.emit(t_, "transpose", pbank(5, 128, 8),
                   lg_sb[:, st * 128:(st + 1) * 128], identf[0:8, 0:8],
                   rd=["lg", "identf", ("pbk", 5)], wr=[("pbk", 5)])
            p.emit(v_, "tensor_copy", lt_sb[:], pbank(5, 128, 8),
                   rd=[("pbk", 5)], wr=["lt"])
            p.emit(v_, "tensor_reduce", nmax[:], lt_sb[:],
                   axis=mybir.AxisListType.X, op=OP.max, rd=["lt"],
                   wr=["nmax"])
            p.emit(v_, "tensor_scalar", nmax[:], nmax[:], -1.0, None,
                   op0=OP.mult, rd=["nmax"], wr=["nmax"])
            p.emit(sc, "activation", qex[:], lt_sb[:], AF.Exp, bias=nmax[:],
                   rd=["lt", "nmax"], wr=["qex"])
            p.emit(v_, "tensor_reduce", ssum[:], qex[:],
                   axis=mybir.AxisListType.X, op=OP.add, rd=["qex"],
                   wr=["ssum"])
            p.emit(v_, "reciprocal", ssum[:], ssum[:], rd=["ssum"],
                   wr=["ssum"])
            p.emit(v_, "tensor_tensor", p0c[:], qex[:, 0:1], ssum[:],
                   op=OP.mult, rd=["qex", "ssum"], wr=["p0c"])
            p.emit(t_, "transpose", pbank(5, 1, 128), p0c[:], identf[:],
                   rd=["p0c", "identf", ("pbk", 5)], wr=[("pbk", 5)])
            p.emit(v_, "tensor_copy", p0row[:, st * 128:(st + 1) * 128],
                   pbank(5, 1, 128), rd=[("pbk", 5)], wr=["p0row"])
        dma(s_, p0h[:, tt * 512:(tt + 1) * 512], p0row[:], rd=["p0row"],
            wr=[("p0h", tt)])

        gelu_ffn("y0", None, w1b, [("w1b", kd) for kd in range(ND)], b1c,
                 "b1c", w2bh, lambda kf: ("w2bh", kf), b2c, "b2c", y0st, None)
        dma(s_, fm3(y0h[b, :, t0:t0 + 512]), sb3(y0st[:], 512),
            rd=[("y0_y", hh) for hh in range(ND)], wr=[("y0h", tt)])
        for hh in range(ND):
            p.last_w[("y0_y", hh)] = p.last_w[("y0h", tt)]

    p.barrier()
    w1v = w1.rearrange("(a p) f -> p a f", p=128)
    w2v = w2.rearrange("(a p) d -> p a d", p=128)
    w2bv = w2bh.rearrange("(a p) d -> p a d", p=128)

    def bis_update():
        pr, inv, tmp = bst[:, 0:8], bst[:, 8:16], bst[:, 16:24]
        p.emit(v_, "tensor_tensor", pr, cnt_loc[:], kvec[:], op=OP.is_ge,
               rd=["cnt_loc", "kvec"], wr=["pr"])
        p.emit(v_, "tensor_scalar", inv, pr, -1.0, 1.0, op0=OP.mult,
               op1=OP.add, rd=["pr"], wr=["inv"])
        p.emit(v_, "tensor_tensor", tmp, pr, bmid[:], op=OP.mult,
               rd=["pr", "bmid"], wr=["tmp"])
        p.emit(v_, "tensor_tensor", bhi[:], bhi[:], inv, op=OP.mult,
               rd=["bhi", "inv"], wr=["bhi"])
        p.emit(v_, "tensor_tensor", bhi[:], bhi[:], tmp, op=OP.add,
               rd=["bhi", "tmp"], wr=["bhi"])
        p.emit(v_, "tensor_tensor", tmp, inv, bmid[:], op=OP.mult,
               rd=["inv", "bmid"], wr=["tmp"])
        p.emit(v_, "tensor_tensor", blo[:], blo[:], pr, op=OP.mult,
               rd=["blo", "pr"], wr=["blo"])
        p.emit(v_, "tensor_tensor", blo[:], blo[:], tmp, op=OP.add,
               rd=["blo", "tmp"], wr=["blo"])

    def bis_mid():
        p.emit(v_, "tensor_tensor", bmid[:], blo[:], bhi[:], op=OP.add,
               rd=["blo", "bhi"], wr=["bmid"])
        p.emit(v_, "tensor_scalar", bmid[:], bmid[:], 0.5, None, op0=OP.mult,
               rd=["bmid"], wr=["bmid"])
        p.emit(t_, "matmul", pbank(4, 128, 8), ones1[:], bmid[:], start=True,
               stop=True, rd=["ones1", "bmid", ("pbk", 4)], wr=[("pbk", 4)])
        p.emit(v_, "tensor_copy", bcols[:], pbank(4, 128, 8),
               rd=[("pbk", 4)], wr=["bcols"])

    def wdat_b(buf, cols=2048):
        return buf[:].bitcast(BF16)[:, 0:cols]

    def wdat_f(buf, cols=1024):
        return buf[:, 0:cols]

    def wmsk(buf, cols):
        return buf[:].bitcast(BF16)[:, 2048:2048 + cols]

    def cnt_mask_mm(j, buf, srcap, rdk, mcols, first, last, mkey, bank):
        p.emit(v_, "tensor_scalar", wmsk(buf, mcols), srcap,
               bcols[:, j:j + 1], 0.0, op0=OP.is_lt, op1=OP.add,
               rd=rdk + ["bcols"], wr=[mkey])
        nmm = mcols // 512
        for sb in range(nmm):
            p.emit(t_, "matmul", pbank(bank, 1, 512), onescol_b[:],
                   wmsk(buf, mcols)[:, sb * 512:(sb + 1) * 512],
                   start=(first and sb == 0), stop=(last and sb == nmm - 1),
                   rd=["onescolb", mkey], wr=[("pbk", bank)],
                   inc=(sb == nmm - 1))

    def bis_tally(j):
        p.emit(v_, "tensor_reduce", cnt_loc[:, j:j + 1],
               pbank(j, 1, 512), axis=mybir.AxisListType.X, op=OP.add,
               rd=[("pbk", j)], wr=["cnt_loc", ("pbk", j)])

    def bis_cnt_small(j, srcap, nn, rdk):
        p.emit(v_, "memset", cntc[:], 0.0, rd=["cntc"], wr=["cntc"])
        p.emit(v_, "tensor_scalar", wmsk(wcnt1[0], 1024)[:, 0:nn], srcap,
               bcols[:, j:j + 1], 0.0, op0=OP.is_lt, op1=OP.add,
               accum_out=cntc[:], rd=rdk + ["bcols", "cntc"],
               wr=["cntc", ("msk1", 0)])
        p.emit(t_, "matmul", pbank(5, 1, 1), cntc[:], onescol[:],
               start=True, stop=True, rd=["cntc", "onescol", ("pbk", 5)],
               wr=[("pbk", 5)])
        p.emit(v_, "tensor_copy", cnt_loc[:, j:j + 1], pbank(5, 1, 1),
               rd=[("pbk", 5)], wr=["cnt_loc"])

    def bis_update(js):
        pr, inv, tmp = bst[:, 0:8], bst[:, 8:16], bst[:, 16:24]
        sl = slice(js[0], js[-1] + 1)
        p.emit(v_, "tensor_tensor", pr[:, sl], cnt_loc[:, sl], kvec[:, sl],
               op=OP.is_ge, rd=["cnt_loc", "kvec"], wr=["pr"])
        p.emit(v_, "tensor_scalar", inv[:, sl], pr[:, sl], -1.0, 1.0,
               op0=OP.mult, op1=OP.add, rd=["pr"], wr=["inv"])
        p.emit(v_, "tensor_tensor", tmp[:, sl], pr[:, sl], bmid[:, sl],
               op=OP.mult, rd=["pr", "bmid"], wr=["tmp"])
        p.emit(v_, "tensor_tensor", bhi[:, sl], bhi[:, sl], inv[:, sl],
               op=OP.mult, rd=["bhi", "inv"], wr=["bhi"])
        p.emit(v_, "tensor_tensor", bhi[:, sl], bhi[:, sl], tmp[:, sl],
               op=OP.add, rd=["bhi", "tmp"], wr=["bhi"])
        p.emit(v_, "tensor_tensor", tmp[:, sl], inv[:, sl], bmid[:, sl],
               op=OP.mult, rd=["inv", "bmid"], wr=["tmp"])
        p.emit(v_, "tensor_tensor", blo[:, sl], blo[:, sl], pr[:, sl],
               op=OP.mult, rd=["blo", "pr"], wr=["blo"])
        p.emit(v_, "tensor_tensor", blo[:, sl], blo[:, sl], tmp[:, sl],
               op=OP.add, rd=["blo", "tmp"], wr=["blo"])

    def bis_mid(js):
        sl = slice(js[0], js[-1] + 1)
        p.emit(v_, "tensor_tensor", bmid[:, sl], blo[:, sl], bhi[:, sl],
               op=OP.add, rd=["blo", "bhi"], wr=["bmid"])
        p.emit(v_, "tensor_scalar", bmid[:, sl], bmid[:, sl], 0.5, None,
               op0=OP.mult, rd=["bmid"], wr=["bmid"])
        p.emit(t_, "matmul", pbank(4, 128, 8), ones1[:], bmid[:], start=True,
               stop=True, rd=["ones1", "bmid", ("pbk", 4)], wr=[("pbk", 4)])
        p.emit(v_, "tensor_copy", bcols[:], pbank(4, 128, 8),
               rd=[("pbk", 4)], wr=["bcols"])

    for c in range(ND):
        p.emit(sc, "activation", w1b[:, c * FF:(c + 1) * FF],
               w1b[:, c * FF:(c + 1) * FF], AF.Abs, rd=[("w1b", c)],
               wr=[("w1b", c)])

    mxc = cntc
    mx16 = cacc2[:, 0:16]
    for c in range(ND):
        p.emit(v_, "tensor_reduce", mx16[:, c:c + 1],
               w1b[:, c * FF:(c + 1) * FF], axis=mybir.AxisListType.X,
               op=OP.max, rd=[("w1b", c)], wr=["mx16"])
    p.emit(v_, "tensor_reduce", mxc[:], mx16[:, 0:ND],
           axis=mybir.AxisListType.X, op=OP.max, rd=["mx16"], wr=["mxc"])
    p.emit(t_, "transpose", pbank(4, 1, 128), mxc[:], identf[:],
           rd=["mxc", "identf", ("pbk", 4)], wr=[("pbk", 4)])
    p.emit(v_, "tensor_reduce", bhi[:, 0:1], pbank(4, 1, 128),
           axis=mybir.AxisListType.X, op=OP.max, rd=[("pbk", 4)], wr=["bhi"])
    for c in range(16):
        pe = c % 2
        dstb = wdat_b(wcnt1[pe])
        dma(g, dstb.rearrange("p (a d) -> p a d", d=1024),
            w2bv[:, 2 * c:2 * c + 2, :], wr=[("wcb", pe)],
            dsem="dcn%d" % pe)
        p.emit(sc, "activation", dstb, dstb, AF.Abs,
               rd=[("wcb", pe)], wr=[("wcb", pe)])
        p.emit(v_, "tensor_reduce", mx16[:, c:c + 1], dstb,
               axis=mybir.AxisListType.X, op=OP.max,
               rd=[("wcb", pe)], wr=["mx16"])
        p.last_w[("wcb", pe)] = p.last_w["mx16"]
    p.emit(v_, "tensor_reduce", mxc[:], mx16[:, 0:16],
           axis=mybir.AxisListType.X, op=OP.max, rd=["mx16"], wr=["mxc"])
    p.emit(t_, "transpose", pbank(4, 1, 128), mxc[:], identf[:],
           rd=["mxc", "identf", ("pbk", 4)], wr=[("pbk", 4)])
    p.emit(v_, "tensor_reduce", bhi[:, 2:3], pbank(4, 1, 128),
           axis=mybir.AxisListType.X, op=OP.max, rd=[("pbk", 4)], wr=["bhi"])
    for (src_, nn_, col, key) in ((ab1, NFF, 4, "ab1"), (ab2, ND, 6, "ab2")):
        p.emit(v_, "tensor_reduce", mxc[:], src_[:, 0:nn_],
               axis=mybir.AxisListType.X, op=OP.max, rd=[key], wr=["mxc"])
        p.emit(t_, "transpose", pbank(4, 1, 128), mxc[:], identf[:],
               rd=["mxc", "identf", ("pbk", 4)], wr=[("pbk", 4)])
        p.emit(v_, "tensor_reduce", bhi[:, col:col + 1], pbank(4, 1, 128),
               axis=mybir.AxisListType.X, op=OP.max, rd=[("pbk", 4)],
               wr=["bhi"])
    for col in (0, 2, 4, 6):
        p.emit(v_, "tensor_copy", bhi[:, col + 1:col + 2],
               bhi[:, col:col + 1], rd=["bhi"], wr=["bhi"])
    p.emit(v_, "tensor_scalar", bhi[:], bhi[:], 1.01, None, op0=OP.mult,
           rd=["bhi"], wr=["bhi"])

    for it in range(40):
        bis_mid((4, 5, 6, 7))
        for j, (src_, nn_, key) in ((4, (ab1, NFF, "ab1")),
                                    (5, (ab1, NFF, "ab1")),
                                    (6, (ab2, ND, "ab2")),
                                    (7, (ab2, ND, "ab2"))):
            bis_cnt_small(j, src_[:, 0:nn_], nn_, [key])
        bis_update((4, 5, 6, 7))

    dma(s_, dbgh[:, 32:40], blo[:], rd=["blo"], wr=["dbg"])
    for stage in (1, 2):
        n_iter = 16 if stage == 1 else 18
        nch = 16 if stage == 1 else 32
        for it in range(n_iter):
            bis_mid((0, 1, 2, 3))
            for hc in range(nch):
                pe = hc % 2
                first, last = (hc == 0), (hc == nch - 1)
                if stage == 1:
                    srcap = w1b[:, hc * 2048:(hc + 1) * 2048]
                    rdk1 = [("w1b", hc // 2)]
                    mcols = 2048
                else:
                    dma(g, wdat_f(wcnt1[pe]),
                        w1[(hc // 4) * 128:(hc // 4) * 128 + 128,
                           (hc % 4) * 1024:(hc % 4) * 1024 + 1024],
                        wr=[("wcb", pe)], dsem="dcn%d" % pe)
                    p.emit(sc, "activation", wdat_f(wcnt1[pe]),
                           wdat_f(wcnt1[pe]), AF.Abs, rd=[("wcb", pe)],
                           wr=[("wcb", pe)])
                    srcap = wdat_f(wcnt1[pe])
                    rdk1 = [("wcb", pe)]
                    mcols = 1024
                for j in (0, 1):
                    cnt_mask_mm(j, wcnt1[pe], srcap, rdk1, mcols,
                                first, last, ("msk1", pe), bank=j)
                if stage == 2:
                    p.last_w[("wcb", pe)] = p.last_w[("msk1", pe)]
                if stage == 1:
                    dstb = wdat_b(wcnt2[pe])
                    dma(g, dstb.rearrange("p (a d) -> p a d", d=1024),
                        w2bv[:, 2 * hc:2 * hc + 2, :], wr=[("wc2", pe)],
                        dsem="dc2%d" % pe)
                    p.emit(sc, "activation", dstb, dstb, AF.Abs,
                           rd=[("wc2", pe)], wr=[("wc2", pe)])
                    srcap2 = dstb
                    mcols = 2048
                else:
                    dma(g, wdat_f(wcnt2[pe]).rearrange(
                            "p (a d) -> p a d", d=1024),
                        w2v[:, hc:hc + 1, :], wr=[("wc2", pe)],
                        dsem="dc2%d" % pe)
                    p.emit(sc, "activation", wdat_f(wcnt2[pe]),
                           wdat_f(wcnt2[pe]), AF.Abs, rd=[("wc2", pe)],
                           wr=[("wc2", pe)])
                    srcap2 = wdat_f(wcnt2[pe])
                    mcols = 1024
                for j in (2, 3):
                    cnt_mask_mm(j, wcnt2[pe], srcap2, [("wc2", pe)], mcols,
                                first, last, ("msk2", pe), bank=j)
                p.last_w[("wc2", pe)] = p.last_w[("msk2", pe)]
            for j in (0, 1, 2, 3):
                bis_tally(j)
            bis_update((0, 1, 2, 3))
        if stage == 1:
            p.emit(v_, "tensor_scalar", bhi[:, 0:4], blo[:, 0:4], 1.0078125,
                   None, op0=OP.mult, rd=["blo"], wr=["bhi"])
            p.emit(v_, "tensor_scalar", blo[:, 0:4], blo[:, 0:4], 0.9921875,
                   None, op0=OP.mult, rd=["blo"], wr=["blo"])
    p.emit(t_, "matmul", pbank(4, 128, 8), ones1[:], blo[:], start=True,
           stop=True, rd=["ones1", "blo", ("pbk", 4)], wr=[("pbk", 4)])
    p.emit(v_, "tensor_copy", bcols[:], pbank(4, 128, 8), rd=[("pbk", 4)],
           wr=["bcols"])
    p.barrier()

    def perturb(dst, dst_keys, src, src_key, absrc, abs_key, n, jb, jt,
                m1, m2):
        p.emit(v_, "tensor_scalar", m1[:, 0:n], absrc[:, 0:n],
               bcols[:, jb:jb + 1], 0.8, op0=OP.is_le, op1=OP.mult,
               rd=[abs_key, "bcols"], wr=["pm1"])
        p.emit(v_, "tensor_scalar", m2[:, 0:n], absrc[:, 0:n],
               bcols[:, jt:jt + 1], 0.8, op0=OP.is_ge, op1=OP.mult,
               rd=[abs_key, "bcols"], wr=["pm2"])
        p.emit(v_, "tensor_tensor", m1[:, 0:n], m1[:, 0:n], m2[:, 0:n],
               op=OP.subtract, rd=["pm1", "pm2"], wr=["pm1"])
        p.emit(v_, "tensor_scalar", m1[:, 0:n], m1[:, 0:n], 1.0, None,
               op0=OP.add, rd=["pm1"], wr=["pm1"])
        p.emit(v_, "tensor_tensor", dst[:, 0:n], src[:, 0:n], m1[:, 0:n],
               op=OP.mult, rd=["pm1", src_key], wr=dst_keys)

    perturb(b1p, ["b1p"], b1c, "b1c", ab1, "ab1", NFF, 4, 5,
            y1s[:, 0:64], y1s[:, 64:128])
    perturb(b2p, ["b2p"], b2c, "b2c", ab2, "ab2", ND, 6, 7,
            y1s[:, 0:64], y1s[:, 64:128])

    for kf in range(NFF):
        dma(s_, wstgA[:], w2[kf * 128:(kf + 1) * 128, :], rd=["wstgA"],
            wr=["wstgA"], dsem="dwst")
        p.emit(sc, "activation", absA[:], wstgA[:], AF.Abs,
               rd=["wstgA"], wr=["absA"])
        m1 = y1s[:, 0:1024]
        m2 = y1s[:, 1024:2048]
        p.emit(v_, "tensor_scalar", m1, absA[:], bcols[:, 2:3], 0.8,
               op0=OP.is_le, op1=OP.mult, rd=["absA", "bcols"], wr=["pm1"])
        p.emit(v_, "tensor_scalar", m2, absA[:], bcols[:, 3:4], 0.8,
               op0=OP.is_ge, op1=OP.mult, rd=["absA", "bcols"], wr=["pm2"])
        p.emit(v_, "tensor_tensor", m1, m1, m2, op=OP.subtract,
               rd=["pm1", "pm2"], wr=["pm1"])
        p.emit(v_, "tensor_scalar", m1, m1, 1.0, None, op0=OP.add,
               rd=["pm1"], wr=["pm1"])
        p.emit(v_, "tensor_tensor", wstgB[:], wstgA[:], m1, op=OP.mult,
               rd=["pm1", "wstgA"], wr=["wstgB"])
        dma(s_, pw2bh[kf * 128:(kf + 1) * 128, :], wstgB[:], rd=["wstgB"],
            wr=[("pw2bh", kf), "wstgB"], dsem="dwst")

    for kd in range(ND):
        for jj in range(4):
            dma(s_, wstgA[:],
                w1[kd * 128:(kd + 1) * 128, jj * 1024:(jj + 1) * 1024],
                rd=["wstgA"], wr=["wstgA"], dsem="dwst")
            p.emit(sc, "activation", absA[:], wstgA[:], AF.Abs,
                   rd=["wstgA"], wr=["absA"])
            m1 = y1s[:, 0:1024]
            m2 = y1s[:, 1024:2048]
            p.emit(v_, "tensor_scalar", m1, absA[:], bcols[:, 0:1], 0.8,
                   op0=OP.is_le, op1=OP.mult, rd=["absA", "bcols"], wr=["pm1"])
            p.emit(v_, "tensor_scalar", m2, absA[:], bcols[:, 1:2], 0.8,
                   op0=OP.is_ge, op1=OP.mult, rd=["absA", "bcols"], wr=["pm2"])
            p.emit(v_, "tensor_tensor", m1, m1, m2, op=OP.subtract,
                   rd=["pm1", "pm2"], wr=["pm1"])
            p.emit(v_, "tensor_scalar", m1, m1, 1.0, None, op0=OP.add,
                   rd=["pm1"], wr=["pm1"])
            p.emit(v_, "tensor_tensor",
                   w1b[:, kd * FF + jj * 1024:kd * FF + (jj + 1) * 1024],
                   wstgA[:], m1, op=OP.mult, rd=["pm1", "wstgA"],
                   wr=[("w1b", kd)])

    p.barrier()
    y0t = y0st

    for tt in range(NTT):
        b = (tt * 512) // L
        t0 = (tt * 512) % L
        dma(s_, sb3(x1b[:], 512), fm3(x1h[b, :, t0:t0 + 512]),
            rd=["x1b"], wr=["x1b"], dsem="dx1b")
        xt = dma(s_, sb3(xln3[:], 512), fm3(xln3h[b, :, t0:t0 + 512]),
                 rd=[("xln3", i) for i in range(ND)], wr=["xln3A"],
                 dsem="dxl3")
        for i in range(ND):
            p.last_w[("xln3", i)] = xt
        dma(s_, sb3(y0t[:], 512), fm3(y0h[b, :, t0:t0 + 512]),
            rd=["y0t"], wr=["y0t"], dsem="dy0t")
        dma(s_, p0row[:], p0h[:, tt * 512:(tt + 1) * 512], rd=["p0row"],
            wr=["p0row"], dsem="dp0r")

        gelu_ffn("y1", None, w1b, [("w1b", kd) for kd in range(ND)], b1p,
                 "b1p", pw2bh, lambda kf: ("pw2bh", kf), b2p, "b2p", y1s,
                 None)

        p.emit(t_, "matmul", pbank(4, 128, 512), ones1[:], p0row[:],
               start=True, stop=True, rd=["ones1", "p0row", ("pbk", 4)],
               wr=[("pbk", 4)])
        for hh in range(ND):
            ap_ = apB[hh % 2]
            p.emit(v_, "tensor_tensor", ap_[:], y0t[:, hh * 512:(hh + 1) * 512],
                   y1s[:, hh * 512:(hh + 1) * 512], op=OP.subtract,
                   rd=["y0t", ("y1_y", hh)], wr=[("apB", hh % 2)])
            p.emit(v_, "tensor_tensor", ap_[:], ap_[:], pbank(4, 128, 512),
                   op=OP.mult, rd=[("apB", hh % 2), ("pbk", 4)],
                   wr=[("apB", hh % 2)])
            p.emit(v_, "tensor_tensor", y1s[:, hh * 512:(hh + 1) * 512],
                   y1s[:, hh * 512:(hh + 1) * 512], ap_[:], op=OP.add,
                   rd=[("apB", hh % 2)], wr=[("y1_y", hh)])

        def ffsrc(i):
            return y1s[:, i * 512:(i + 1) * 512]

        ns["srckeys"] = [("y1_y", i) for i in range(ND)]
        bca, bcb = ln_full("ln4", ffsrc, ND, D, 512)
        for i in range(ND):
            ap_ = apB[i % 2]
            p.emit(v_, "tensor_tensor", ap_[:], ffsrc(i), bca, op=OP.mult,
                   rd=[("y1_y", i), ("pbk", 2)], wr=[("apB", i % 2)])
            p.emit(v_, "tensor_tensor", ap_[:], ap_[:], bcb, op=OP.add,
                   rd=[("apB", i % 2), ("pbk", 3)], wr=[("apB", i % 2)])
            p.emit(v_, "tensor_tensor", x1b[:, i * 512:(i + 1) * 512],
                   x1src(i), ap_[:], op=OP.add, rd=[("apB", i % 2), "x1b"],
                   wr=["x1b"])
        dma(s_, fm3(out_fm[b, :, t0:t0 + 512]), sb3(x1b[:], 512),
            rd=["x1b"], wr=[("outh", tt)])
        p.last_w["x1b"] = p.last_w[("outh", tt)]

    p.wait_all("gpsimd")
    p.wait_all("vector")
    block = es.enter_context(nc.Block())
    p.replay(block)
    es.close()
    return nc



def _prep_shared(W_in, conv_w, conv_b, dt_bias, A_log, D_param, norm_w, W_out,
                 W1, b1, W2, b2, ln_g, ln_b, Wr, br, cfg):
    f32 = np.float32
    NIP = 35
    W_inT = np.ascontiguousarray(np.asarray(W_in).T).astype(f32)
    wpad = np.zeros((cfg.D, NIP * 128), f32)
    wpad[:, :4096] = W_inT[:, :4096]
    wpad[:, 32 * 128:32 * 128 + 64] = W_inT[:, 4096:4160]
    wpad[:, 33 * 128:33 * 128 + 64] = W_inT[:, 4160:4224]
    wpad[:, 34 * 128:34 * 128 + 16] = W_inT[:, 4224:4240]
    w_in_tiles = np.ascontiguousarray(
        wpad.reshape(cfg.D // 128, 128, NIP, 128).transpose(2, 0, 1, 3))
    cw = np.asarray(conv_w).reshape(2176, 4).astype(f32)
    cb = np.asarray(conv_b).reshape(2176).astype(f32)
    cw4 = np.zeros((128, 18 * 4), f32)
    cb18 = np.zeros((128, 18), f32)
    for c in range(16):
        cw4[:, c * 4:(c + 1) * 4] = cw[c * 128:(c + 1) * 128]
        cb18[:, c] = cb[c * 128:(c + 1) * 128]
    cw4[0:64, 16 * 4:17 * 4] = cw[2048:2112]
    cb18[0:64, 16] = cb[2048:2112]
    cw4[0:64, 17 * 4:18 * 4] = cw[2112:2176]
    cb18[0:64, 17] = cb[2112:2176]
    e16 = np.zeros((16, 16 * 128), f32)
    for h in range(16):
        e16[h, h * 128:(h + 1) * 128] = 1.0
    n1 = W1.size
    k1 = max(1, int(n1 * 0.1 / 2))
    n2 = W2.size
    k2 = max(1, int(n2 * 0.1 / 2))
    nb1 = b1.size
    kb1 = max(1, int(nb1 * 0.1 / 2))
    nb2 = b2.size
    kb2 = max(1, int(nb2 * 0.1 / 2))
    NC = cfg.n_cores
    return {
        "w_in_tiles": w_in_tiles.astype(f32),
        "w_outT": np.ascontiguousarray(np.asarray(W_out).T).astype(f32),
        "w1": np.ascontiguousarray(W1).astype(f32),
        "w2": np.ascontiguousarray(W2).astype(f32),
        "conv_w4": cw4,
        "conv_b17": cb18,
        "dtb": np.asarray(dt_bias).reshape(16, 1).astype(f32),
        "alog": np.asarray(A_log).reshape(16, 1).astype(f32),
        "dprm_row": np.asarray(D_param).reshape(1, 16).astype(f32),
        "normw": np.ascontiguousarray(
            np.asarray(norm_w).reshape(16, 128).T).astype(f32),
        "b1c": np.ascontiguousarray(np.asarray(b1).reshape(32, 128).T).astype(f32),
        "b2c": np.ascontiguousarray(np.asarray(b2).reshape(8, 128).T).astype(f32),
        "lgam": np.ascontiguousarray(np.asarray(ln_g).reshape(8, 128).T).astype(f32),
        "lbet": np.ascontiguousarray(np.asarray(ln_b).reshape(8, 128).T).astype(f32),
        "wr_in": np.ascontiguousarray(np.asarray(Wr).reshape(8, 128, 8)
            .transpose(1, 0, 2).reshape(128, 64)).astype(f32),
        "br_in": np.asarray(br).reshape(8, 1).astype(f32),
        "identf_in": np.eye(128, dtype=f32),
        "identb_in": np.eye(128).astype(BF),
        "trineg_in": np.where(
            np.arange(128)[:, None] <= np.arange(128)[None, :], 0.0, NEG
        ).astype(f32),
        "e16_in": e16,
        "kvec_in": np.array([[k1, n1 - k1 + 1, k2, n2 - k2 + 1,
                              kb1, nb1 - kb1 + 1,
                              kb2, nb2 - kb2 + 1]], f32),
        "bnds_in": np.array([[0.0] * 8 + [1024.0] * 8], f32),
    }


def kernel(L0_attn_hidden_states, L0_attn_W_in, L0_attn_conv_w, L0_attn_conv_b,
           L0_attn_dt_bias, L0_attn_A_log_alpha, L0_attn_D_param,
           L0_attn_norm_w, L0_attn_W_out, L0_ffn_W1, L0_ffn_b1, L0_ffn_W2,
           L0_ffn_b2, L0_ffn_ln_gamma, L0_ffn_ln_beta, L0_ffn_Wr, L0_ffn_br):
    cfg = CFG()
    x = np.asarray(L0_attn_hidden_states, np.float32)
    W1 = np.ascontiguousarray(np.asarray(L0_ffn_W1, np.float32))
    W2 = np.ascontiguousarray(np.asarray(L0_ffn_W2, np.float32))
    shared = _prep_shared(L0_attn_W_in, L0_attn_conv_w, L0_attn_conv_b,
                          L0_attn_dt_bias, L0_attn_A_log_alpha, L0_attn_D_param,
                          L0_attn_norm_w, L0_attn_W_out, W1, L0_ffn_b1, W2,
                          L0_ffn_b2, L0_ffn_ln_gamma, L0_ffn_ln_beta,
                          L0_ffn_Wr, L0_ffn_br, cfg)
    nc = build_program(cfg)
    in_maps = []
    for c in range(8):
        m = dict(shared)
        m["x_fm"] = np.ascontiguousarray(
            x[2 * c:2 * c + 2].transpose(0, 2, 1)).astype(np.float32)
        in_maps.append(m)
    res = run_bass_kernel_spmd(nc, in_maps, core_ids=list(range(8)))
    outs = []
    for c in range(8):
        o = np.asarray(res.results[c]["out_fm"])
        outs.append(np.ascontiguousarray(o.transpose(0, 2, 1)))
    return np.concatenate(outs, axis=0).astype(np.float32)


# version 09
# speedup vs baseline: 1.0622x; 1.0110x over previous
import contextlib

import numpy as np
import ml_dtypes
import concourse.bass as bass
import concourse.mybir as mybir
from concourse.bass_utils import run_bass_kernel_spmd

F32 = mybir.dt.float32
BF16 = mybir.dt.bfloat16
AF = mybir.ActivationFunctionType
OP = mybir.AluOpType
BF = ml_dtypes.bfloat16

GELU_C = 0.044715
GELU_S = 2.0 * 0.7978845608028654
EPS = 1e-5
NEG = -1e30


class CFG:
    def __init__(self, L=4096, NB=2, T=512, n_cores=8, bisect_iters=52,
                 debug=None):
        self.L, self.NB, self.T, self.C = L, NB, T, 128
        self.D, self.EI, self.CV, self.DIP = 1024, 2048, 2176, 4240
        self.NH, self.HD, self.NS, self.FF = 16, 128, 64, 4096
        self.n_cores = n_cores
        self.bisect_iters = bisect_iters
        self.debug = debug
        assert L % T == 0 and T % self.C == 0


ENGINES = ("tensor", "vector", "scalar", "gpsimd", "sync")
DSEMS = ("dxb", "dwi0", "dwi1", "dwi2", "dwo0", "dwo1", "dw2i0", "dw2i1", "dw2o0", "dw2o1", "dx1b",
         "dxl3", "dy0t", "dp0r", "dw20", "dw21", "dwst", "dcn0", "dcn1", "dc20", "dc21")
SEMS = ("T", "V", "S", "G", "DS", "DA", "DG", "DC") + DSEMS
ENG_SEM = {"tensor": "T", "vector": "V", "scalar": "S", "gpsimd": "G"}
DMA_SEM = {"sync": "DS", "scalar": "DA", "gpsimd": "DG"}


class Prog:

    def __init__(self, nc):
        self.nc = nc
        self.ops = {e: [] for e in ENGINES}
        self.cnt = {s: 0 for s in SEMS}
        self.hiw = {e: {s: 0 for s in SEMS} for e in ENGINES}
        self.last_w = {}
        self.readers = {}
        self.sems = {}
        self.pending_rd = {e: [] for e in ENGINES}

    def make_sems(self, es):
        for s in SEMS:
            self.sems[s] = es.enter_context(self.nc.semaphore("sem_" + s))

    def patch_total(self, keys, sem):
        tok = (sem, self.cnt[sem])
        for k in keys:
            self.last_w[k] = tok
            self.readers[k] = []

    def emit(self, eng, fn, *args, rd=(), wr=(), after=(), inc=True, dma=False,
             collective=False, dsem=None, **kw):
        toks = [t for t in after if t is not None]
        for k in rd:
            t = self.last_w.get(k)
            if t is not None:
                toks.append(t)
        for k in wr:
            t = self.last_w.get(k)
            if t is not None:
                toks.append(t)
            toks.extend(self.readers.get(k, ()))
        waits = []
        own = ENG_SEM.get(eng)
        for (sem, val) in toks:
            if sem == own and eng == "tensor":
                continue
            if val > self.hiw[eng][sem]:
                self.hiw[eng][sem] = val
                waits.append((sem, val))
        if dma:
            sem = dsem or ("DC" if collective else DMA_SEM[eng])
            amt = 16
        else:
            sem = ENG_SEM[eng]
            amt = 1
        if inc:
            self.cnt[sem] += amt
        tok = (sem, self.cnt[sem])
        self.ops[eng].append((fn, args, kw, waits, (sem, amt) if inc else None))
        if inc:
            for k in rd:
                self.readers.setdefault(k, []).append(tok)
            for k in self.pending_rd[eng]:
                self.readers.setdefault(k, []).append(tok)
            self.pending_rd[eng] = []
        else:
            self.pending_rd[eng].extend(rd)
        for k in wr:
            for e2 in ENGINES:
                assert k not in self.pending_rd[e2], \
                    f"write to {k} while {e2} has un-inc'd pending read"
            self.last_w[k] = tok
            self.readers[k] = []
        return tok

    def barrier(self):
        snap = dict(self.cnt)
        for e in ENGINES:
            for s in SEMS:
                v = snap[s]
                if s != ENG_SEM.get(e) and v > self.hiw[e][s]:
                    self.hiw[e][s] = v
                    self.ops[e].append((None, None, None, [(s, v)], None))

    def wait_all(self, eng):
        for s in SEMS:
            v = self.cnt[s]
            if v > self.hiw[eng][s]:
                self.hiw[eng][s] = v
                self.ops[eng].append((None, None, None, [(s, v)], None))

    def replay(self, block):
        prog = self

        def section(name):
            def run(eng):
                for (fn, args, kw, waits, inc) in prog.ops[name]:
                    for (sem, val) in waits:
                        eng.wait_ge(prog.sems[sem], val)
                    if fn is None:
                        continue
                    inst = getattr(eng, fn)(*args, **kw)
                    if inc is not None:
                        inst.then_inc(prog.sems[inc[0]], inc[1])
            return run

        block.tensor(section("tensor"))
        block.vector(section("vector"))
        block.scalar(section("scalar"))
        block.gpsimd(section("gpsimd"))
        block.sync(section("sync"))


class Arena:
    def __init__(self, handle, words):
        self.h = handle
        self.words = words
        self.off = 0
        self.marks = []

    def _take(self, n_words):
        n_words = (n_words + 1) & ~1
        assert self.off + n_words <= self.words, \
            f"arena overflow: {self.off}+{n_words} > {self.words}"
        o = self.off
        self.off += n_words
        return o

    def f32(self, parts, n):
        o = self._take(n)
        return self.h[0:parts, o:o + n]

    def bf16(self, parts, n):
        o = self._take((n + 1) // 2)
        return self.h[:].bitcast(BF16)[0:parts, 2 * o:2 * o + n]

    def push(self):
        self.marks.append(self.off)

    def pop(self):
        self.off = self.marks.pop()


def build_program(cfg: CFG):
    nc = bass.Bass("TRN2", target_bir_lowering=False)
    L, NB, T, C = cfg.L, cfg.NB, cfg.T, cfg.C
    D, EI, CV, DIP = cfg.D, cfg.EI, cfg.CV, cfg.DIP
    NH, HD, NS, FF = cfg.NH, cfg.HD, cfg.NS, cfg.FF
    ND, NE, NCV, NFF = D // 128, EI // 128, CV // 128, FF // 128
    NIP = 35
    NCVT = 18
    NT = L // T
    TOK = NB * L
    NTT = TOK // 512
    NCH = T // C

    def din(name, shape, dt=F32):
        return nc.dram_tensor(name, list(shape), dt, kind="ExternalInput").ap()

    def dint(name, shape, dt=F32):
        return nc.dram_tensor(name, list(shape), dt, kind="Internal").ap()

    x_fm = din("x_fm", [NB, D, L])
    w_in_tiles = din("w_in_tiles", [NIP, ND, 128, 128])
    w_outT = din("w_outT", [EI, D])
    w1 = din("w1", [D, FF])
    w2 = din("w2", [FF, D])
    conv_w4 = din("conv_w4", [128, NCVT * 4])
    conv_b17 = din("conv_b17", [128, NCVT])
    dtb_in = din("dtb", [NH, 1])
    alog_in = din("alog", [NH, 1])
    dprm_row = din("dprm_row", [1, NH])
    normw_in = din("normw", [128, NE])
    b1c_in = din("b1c", [128, NFF])
    b2c_in = din("b2c", [128, ND])
    lgam_in = din("lgam", [128, ND])
    lbet_in = din("lbet", [128, ND])
    wr_in = din("wr_in", [128, (D // 128) * 8])
    br_in = din("br_in", [8, 1])
    identf_in = din("identf_in", [128, 128])
    identb_in = din("identb_in", [128, 128], BF16)
    trineg_in = din("trineg_in", [128, 128])
    e16_in = din("e16_in", [NH, NH * 128])
    kvec_in = din("kvec_in", [1, 8])
    bnds_in = din("bnds_in", [1, 16])

    out_fm = nc.dram_tensor("out_fm", [NB, D, L], F32, kind="ExternalOutput").ap()

    x1h = dint("x1h", [NB, D, L])
    xln3h = dint("xln3h", [NB, D, L], BF16)
    y0h = dint("y0h", [NB, D, L], BF16)
    p0h = dint("p0h", [1, TOK])
    w2bh = dint("w2bh", [FF, D], BF16)
    w_inbh = dint("w_inbh", [NIP, ND, 128, 128], BF16)
    w_outbh = dint("w_outbh", [EI, D], BF16)
    dbgh = dint("dbgh", [1, 64])
    pw2bh = dint("pw2bh", [FF, D], BF16)

    es = contextlib.ExitStack()
    AR_P = Arena(es.enter_context(nc.sbuf_tensor("arP", [128, 4100], F32)), 4100)
    AR_A = Arena(es.enter_context(nc.sbuf_tensor("arA", [128, 48700], F32)), 48700)

    p = Prog(nc)
    p.make_sems(es)
    pb = [es.enter_context(nc.psum_tensor(f"pb{i}", [128, 512], F32))
          for i in range(8)]

    def pbank(i, parts=128, n=512):
        return pb[i][0:parts, 0:n]

    def pbank_b(i, parts=128, n=1024):
        return pb[i][:].bitcast(BF16)[0:parts, 0:n]

    identf = AR_P.f32(128, 128)
    identb = AR_P.bf16(128, 128)
    trineg = AR_P.f32(128, 128)
    e16 = AR_P.f32(NH, NH * 128)
    cw4 = AR_P.f32(128, NCVT * 4)
    cb17 = AR_P.f32(128, NCVT)
    dt_bias = AR_P.f32(NH, 1)
    a_log = AR_P.f32(NH, 1)
    d_row = AR_P.f32(1, NH)
    nw = AR_P.f32(128, NE)
    b1c = AR_P.f32(128, NFF)
    b2c = AR_P.f32(128, ND)
    gam = AR_P.f32(128, ND)
    bet = AR_P.f32(128, ND)
    wr_sb = AR_P.f32(128, ND * 8)
    br_sb = AR_P.f32(8, 1)
    kvec = AR_P.f32(1, 8)
    ones1 = AR_P.f32(1, 128)
    ones1b = AR_P.bf16(1, 128)
    onescol = AR_P.f32(128, 1)
    onescol_b = AR_P.bf16(128, 2)[:, 0:1]
    zero16 = AR_P.f32(NH, 1)
    eps_col = AR_P.f32(128, 1)
    one_col = AR_P.f32(128, 1)
    a_col = AR_P.f32(NH, 1)
    d_b = AR_P.f32(128, NH)
    wrg = AR_P.bf16(128, ND * 8)
    cgam = AR_P.f32(8, 1)
    cbet = AR_P.f32(8, 1)
    b1p = AR_P.f32(128, NFF)
    b2p = AR_P.f32(128, ND)
    blo = AR_P.f32(1, 8)
    bhi = AR_P.f32(1, 8)
    bmid = AR_P.f32(1, 8)
    bcols = AR_P.f32(128, 8)
    bst = AR_P.f32(1, 24)
    ab1 = AR_P.f32(128, NFF)
    ab2 = AR_P.f32(128, ND)

    g, s_, v_, t_, sc = "gpsimd", "sync", "vector", "tensor", "scalar"

    def dma(eng, out, in_, rd=(), wr=(), after=(), dsem=None):
        return p.emit(eng, "dma_start", rd=rd, wr=wr, after=after, dma=True,
                      dsem=dsem, out=out, in_=in_)

    def fm3(hbm_ap):
        return hbm_ap.rearrange("(k p) t -> p k t", p=128)

    def sb3(sbuf_ap, t):
        return sbuf_ap.rearrange("p (k t) -> p k t", t=t)

    dma(s_, identf[:], identf_in[:], wr=["identf"])
    dma(s_, trineg[:], trineg_in[:], wr=["trineg"])
    dma(s_, e16[:], e16_in[:], wr=["e16"])
    dma(s_, cw4[:], conv_w4[:], wr=["cw4"])
    dma(s_, cb17[:], conv_b17[:], wr=["cb17"])
    dma(s_, dt_bias[:], dtb_in[:], wr=["dtb"])
    dma(s_, a_log[:], alog_in[:], wr=["alog"])
    dma(s_, d_row[:], dprm_row[:], wr=["drow"])
    dma(s_, nw[:], normw_in[:], wr=["nw"])
    dma(s_, b1c[:], b1c_in[:], wr=["b1c"])
    dma(s_, b2c[:], b2c_in[:], wr=["b2c"])
    dma(s_, gam[:], lgam_in[:], wr=["gam"])
    dma(s_, bet[:], lbet_in[:], wr=["bet"])
    dma(s_, wr_sb[:], wr_in[:], wr=["wr"])
    dma(s_, br_sb[:], br_in[:], wr=["br"])
    dma(s_, kvec[:], kvec_in[:], wr=["kvec"])
    dma(s_, blo[:], bnds_in[:, 0:8], wr=["blo"])
    dma(s_, bhi[:], bnds_in[:, 8:16], wr=["bhi"])
    dma(s_, ab1[:], b1c_in[:], wr=["ab1"])
    dma(s_, ab2[:], b2c_in[:], wr=["ab2"])
    dma(g, identb[:], identb_in[:], wr=["identb"])
    p.patch_total(["identf", "trineg", "e16", "cw4", "cb17", "dtb", "alog",
                   "drow", "nw", "b1c", "b2c", "gam", "bet", "wr", "br",
                   "kvec", "blo", "bhi", "ab1", "ab2"], "DS")
    p.patch_total(["identb"], "DG")

    p.emit(g, "memset", ones1[:], 1.0, wr=["ones1"])
    p.emit(g, "memset", ones1b[:], 1.0, wr=["ones1b"])
    p.emit(g, "memset", onescol[:], 1.0, wr=["onescol"])
    p.emit(g, "memset", zero16[:], 0.0, wr=["zero16"])
    p.emit(g, "memset", bmid[:], 0.0, wr=["bmid"])
    p.emit(g, "memset", eps_col[:], float(EPS), wr=["eps_col"])
    p.emit(g, "memset", one_col[:], 1.0, wr=["one_col"])
    p.emit(v_, "tensor_copy", onescol_b[:], onescol[:], rd=["onescol"],
           wr=["onescolb"])
    p.emit(sc, "activation", a_col[:], a_log[:], AF.Exp, rd=["alog"],
           wr=["a_col"])
    p.emit(v_, "tensor_scalar", a_col[:], a_col[:], -1.0, None, op0=OP.mult,
           rd=["a_col"], wr=["a_col"])
    p.emit(t_, "matmul", pbank(6, 128, NH), ones1[:], d_row[:], start=True,
           stop=True, rd=["ones1", "drow", ("pbk", 6)], wr=[("pbk", 6)])
    p.emit(v_, "tensor_copy", d_b[:], pbank(6, 128, NH), rd=[("pbk", 6)],
           wr=["d_b"])
    p.emit(sc, "activation", ab1[:], ab1[:], AF.Abs,
           rd=["ab1"], wr=["ab1"])
    p.emit(sc, "activation", ab2[:], ab2[:], AF.Abs,
           rd=["ab2"], wr=["ab2"])

    w2stg = [AR_P.bf16(128, 1024) for _ in range(2)]
    for kf in range(NFF):
        pe = kf % 2
        dma(g, w2stg[pe][:], w2[kf * 128:(kf + 1) * 128, :],
            wr=[("w2stg", pe)], dsem="dw2i%d" % pe)
        dma(s_, w2bh[kf * 128:(kf + 1) * 128, :], w2stg[pe][:],
            rd=[("w2stg", pe)], wr=[("w2bh", kf), ("w2stg", pe)],
            dsem="dw2o%d" % pe)
    for e in range(NIP):
        pe = e % 2
        dma(g, w2stg[pe][:].rearrange("p (k q) -> p k q", q=128),
            w_in_tiles[e].rearrange("k p q -> p k q"),
            wr=[("w2stg", pe)], dsem="dw2i%d" % pe)
        dma(s_, w_inbh[e].rearrange("k p q -> p k q"),
            w2stg[pe][:].rearrange("p (k q) -> p k q", q=128),
            rd=[("w2stg", pe)], wr=[("wibh", e), ("w2stg", pe)], dsem="dw2o%d" % pe)
    for ke in range(NE):
        pe = ke % 2
        dma(g, w2stg[pe][:], w_outT[ke * 128:(ke + 1) * 128, :],
            wr=[("w2stg", pe)], dsem="dw2i%d" % pe)
        dma(s_, w_outbh[ke * 128:(ke + 1) * 128, :], w2stg[pe][:],
            rd=[("w2stg", pe)], wr=[("wobh", ke), ("w2stg", pe)],
            dsem="dw2o%d" % pe)

    ns = {}

    def ln_full(tag, src_fn, n_tiles, n_feat, tcols, src_bf16=False):
        arow, brow, crow = ns["arow"], ns["brow"], ns["crow"]
        sqp = ns["sq"]
        oc, ock = (onescol_b, "onescolb") if src_bf16 else (onescol, "onescol")
        s1 = pbank(2, 1, tcols)
        s2 = pbank(3, 1, tcols)
        srckeys = ns["srckeys"]
        for i in range(n_tiles):
            p.emit(t_, "matmul", s1, oc[:], src_fn(i),
                   start=(i == 0), stop=(i == n_tiles - 1),
                   rd=[ock, srckeys[i]],
                   wr=[("pbk", 2)] if i in (0, n_tiles - 1) else (),
                   inc=(i == n_tiles - 1))
        for i in range(n_tiles):
            sq_scr = sqp[i % 2]
            p.emit(sc, "activation", sq_scr[:, 0:tcols], src_fn(i), AF.Square,
                   rd=[srckeys[i]], wr=[("sq_scr", i % 2)])
            p.emit(t_, "matmul", s2, onescol_b[:], sq_scr[:, 0:tcols],
                   start=(i == 0), stop=(i == n_tiles - 1),
                   rd=["onescolb", ("sq_scr", i % 2)],
                   wr=[("pbk", 3)] if i in (0, n_tiles - 1) else (), inc=True)
        mu = arow[:, 0:tcols]
        p.emit(v_, "tensor_scalar", mu, s1, 1.0 / n_feat, None, op0=OP.mult,
               rd=[("pbk", 2)], wr=["arow"])
        mu2 = crow[:, 0:tcols]
        p.emit(v_, "tensor_tensor", mu2, mu, mu, op=OP.mult, rd=["arow"],
               wr=["crow"])
        var = brow[:, 0:tcols]
        p.emit(v_, "scalar_tensor_tensor", var, s2, 1.0 / n_feat, mu2,
               op0=OP.mult, op1=OP.subtract, rd=[("pbk", 3), "crow"],
               wr=["brow"])
        p.emit(sc, "activation", var, var, AF.Ln, bias=eps_col[0:1, :],
               rd=["brow", "eps_col"], wr=["brow"])
        rstd_b = brow[:].bitcast(BF16)[0:1, 0:tcols]
        p.emit(sc, "activation", rstd_b, var, AF.Exp, scale=-0.5, rd=["brow"],
               wr=["brow"])
        brw_b = crow[:].bitcast(BF16)[0:1, 0:tcols]
        p.emit(v_, "scalar_tensor_tensor", brw_b, mu, -1.0, rstd_b,
               op0=OP.mult, op1=OP.mult, rd=["arow", "brow"], wr=["crow"])
        bca = pbank(2, 128, tcols)
        bcb = pbank(3, 128, tcols)
        p.emit(t_, "matmul", bca, ones1b[:], rstd_b, start=True, stop=True,
               rd=["ones1b", "brow", ("pbk", 2)], wr=[("pbk", 2)])
        p.emit(t_, "matmul", bcb, ones1b[:], brw_b, start=True, stop=True,
               rd=["ones1b", "crow", ("pbk", 3)], wr=[("pbk", 3)])
        return bca, bcb

    AR_A.push()
    wi_buf = [AR_A.bf16(128, ND * 128) for _ in range(3)]
    wo_buf = [AR_A.bf16(128, 1024) for _ in range(2)]
    diag = [AR_A.bf16(128, 4 * 128) for _ in range(2)]
    xb = AR_A.f32(128, ND * T)
    xln1 = AR_A.bf16(128, ND * T)
    z_silu = AR_A.bf16(128, NE * T)
    xbc = AR_A.bf16(128, NCVT * (T + 3))
    y_n = AR_A.bf16(128, NE * T)
    m_o = AR_A.f32(128, ND * T)
    sqA = AR_A.bf16(128, 512)
    halo = AR_A.bf16(128, NCVT * 4)
    dt_sp = AR_A.f32(NH, T)
    s_all = AR_A.f32(NH, T)
    sig_all = AR_A.f32(NH, T)
    esc_c = AR_A.f32(NH, 1)
    es_b = [AR_A.f32(NS, C) for _ in range(2)]
    w_st = AR_A.f32(NH, C)
    sT = AR_A.f32(128, NH)
    sigT = AR_A.f32(128, NH)
    wT = AR_A.f32(128, NH)
    escb = AR_A.f32(NS, NH)
    g_s = AR_A.f32(128, C)
    mask_s = [AR_A.f32(128, C) for _ in range(2)]
    gmb = [AR_A.bf16(128, C) for _ in range(2)]
    x_tm = [AR_A.bf16(128, C) for _ in range(2)]
    x_til = [AR_A.bf16(128, C) for _ in range(2)]
    b_tm = AR_A.bf16(128, NS)
    c_til = [AR_A.bf16(NS, C) for _ in range(2)]
    ht = AR_A.f32(NS, NH * HD)
    ht_b = [AR_A.bf16(NS, HD) for _ in range(2)]
    y_g = AR_A.bf16(128, NH * C)
    dx = [AR_A.f32(128, C) for _ in range(2)]
    apA = [AR_A.f32(128, T) for _ in range(2)]
    rsc = AR_A.f32(1, 16)
    arowA = AR_A.f32(1, T)
    browA = AR_A.f32(1, T)
    crowA = AR_A.f32(1, T)
    sqA2 = AR_A.bf16(128, 512)
    ns.update(arow=arowA, brow=browA, crow=crowA, sq=(sqA, sqA2))

    for mt in range(NB * NT):
        b, m = mt // NT, mt % NT
        t0 = m * T

        dma(s_, sb3(xb[:], T), fm3(x_fm[b, :, t0:t0 + T]),
            rd=["xb"], wr=["xb"], dsem="dxb")
        if m == 0:
            p.emit(v_, "memset", ht[:], 0.0,
                   rd=[("ht", h) for h in range(NH)],
                   wr=[("ht", h) for h in range(NH)])

        def xsrc(i):
            return xb[:, i * T:(i + 1) * T]

        ns["srckeys"] = ["xb"] * ND
        bca, bcb = ln_full("ln1", xsrc, ND, D, T)
        for i in range(ND):
            ap_ = apA[i % 2]
            p.emit(v_, "tensor_tensor", ap_[:], xsrc(i), bca, op=OP.mult,
                   rd=["xb", ("pbk", 2)], wr=[("apA", i % 2)])
            p.emit(v_, "tensor_tensor", xln1[:, i * T:(i + 1) * T], ap_[:],
                   bcb, op=OP.add, rd=[("apA", i % 2), ("pbk", 3)],
                   wr=[("xln1", i)])

        for e in range(NIP):
            pe = e % 3
            dma(s_, wi_buf[pe][:].rearrange("p (k q) -> p k q", q=128),
                w_inbh[e].rearrange("k p q -> p k q"),
                after=[p.last_w.get(("wibh", e))],
                wr=[("wi", pe)], dsem="dwi%d" % pe)
            rows_e = 128 if e < 32 else (64 if e < 34 else NH)
            ps = pbank(pe, rows_e, T)
            for kd in range(ND):
                p.emit(t_, "matmul", ps,
                       wi_buf[pe][:, kd * 128:kd * 128 + rows_e],
                       xln1[:, kd * T:(kd + 1) * T],
                       start=(kd == 0), stop=(kd == ND - 1),
                       rd=[("wi", pe), ("xln1", kd)],
                       wr=([("pbk", pe)] if kd in (0, ND - 1) else []) +
                          ([("wi", pe)] if kd == ND - 1 else []),
                       inc=(kd == ND - 1))
            if e < NE:
                p.emit(sc, "activation", z_silu[:, e * T:(e + 1) * T], ps,
                       AF.Silu, rd=[("pbk", pe)], wr=[("zs", e)])
            elif e < 34:
                c = e - NE
                p.emit(v_, "tensor_copy",
                       xbc[0:rows_e, c * (T + 3) + 3:c * (T + 3) + 3 + T], ps,
                       rd=[("pbk", pe)], wr=[("xbc", c)])
            else:
                p.emit(sc, "activation", dt_sp[:, 0:T], ps, AF.Relu,
                       bias=dt_bias[:], rd=[("pbk", pe), "dtb"], wr=["dt_sp"])
                p.emit(sc, "activation", s_all[:, 0:T], ps, AF.Abs,
                       bias=dt_bias[:], rd=[("pbk", pe), "dtb"], wr=["s_all"])
                p.emit(sc, "activation", sig_all[:, 0:T], s_all[:, 0:T],
                       AF.Exp, scale=-1.0, rd=["s_all"], wr=["sig_all"])
                p.emit(sc, "activation", sig_all[:, 0:T], sig_all[:, 0:T],
                       AF.Ln, bias=one_col[0:NH, :], rd=["sig_all", "one_col"],
                       wr=["sig_all"])
                p.emit(v_, "tensor_tensor", dt_sp[:, 0:T], dt_sp[:, 0:T],
                       sig_all[:, 0:T], op=OP.add, rd=["dt_sp", "sig_all"],
                       wr=["dt_sp"])

        for c in range(NCVT):
            base = c * (T + 3)
            rc = 128 if c < 16 else 64
            if m == 0:
                p.emit(v_, "memset", xbc[0:rc, base:base + 3], 0.0,
                       rd=[("xbc", c)], wr=[("xbc", c)])
            else:
                p.emit(v_, "tensor_copy", xbc[0:rc, base:base + 3],
                       halo[0:rc, c * 4:c * 4 + 3], rd=[("halo", c), ("xbc", c)],
                       wr=[("xbc", c)])
        for c in range(NCVT):
            base = c * (T + 3)
            rc = 128 if c < 16 else 64
            pe = c % 2
            for j in range(4):
                p.emit(v_, "tensor_scalar",
                       diag[pe][0:rc, j * 128:j * 128 + rc],
                       identb[0:rc, 0:rc], cw4[0:rc, c * 4 + j:c * 4 + j + 1],
                       None, op0=OP.mult, rd=["identb", "cw4", ("diag", pe)],
                       wr=[("diag", pe)] if j == 3 else ())
            ps = pbank(pe, rc, T)
            for j in range(4):
                p.emit(t_, "matmul", ps, diag[pe][0:rc, j * 128:j * 128 + rc],
                       xbc[0:rc, base + j:base + j + T],
                       start=(j == 0), stop=(j == 3),
                       rd=[("diag", pe), ("xbc", c)],
                       wr=([("pbk", pe)] if j in (0, 3) else []) +
                          ([("diag", pe)] if j == 3 else []),
                       inc=(j == 3))
            p.emit(v_, "tensor_copy", halo[0:rc, c * 4:c * 4 + 3],
                   xbc[0:rc, base + T:base + T + 3], rd=[("xbc", c)],
                   wr=[("halo", c)])
            p.emit(sc, "activation", xbc[0:rc, base + 3:base + 3 + T], ps,
                   AF.Silu, bias=cb17[0:rc, c:c + 1],
                   rd=[("pbk", pe), "cb17", ("halo", c)], wr=[("xbc", c)])

        p.emit(v_, "tensor_scalar", s_all[:, 0:T], dt_sp[:, 0:T], a_col[:],
               None, op0=OP.mult, rd=["dt_sp", "a_col"], wr=["s_all"])
        p.emit(sc, "activation", sig_all[:, 0:T], dt_sp[:, 0:T], AF.Ln,
               rd=["dt_sp"], wr=["sig_all"])
        for ch in range(NCH):
            sl = slice(ch * C, (ch + 1) * C)
            p.emit(v_, "tensor_tensor_scan", s_all[:, sl], s_all[:, sl],
                   s_all[:, sl], zero16[:], op0=OP.add, op1=OP.bypass,
                   rd=["s_all", "zero16"], wr=["s_all"])
        p.emit(v_, "tensor_tensor", sig_all[:, 0:T], s_all[:, 0:T],
               sig_all[:, 0:T], op=OP.subtract, rd=["s_all", "sig_all"],
               wr=["sig_all"])

        for ch in range(NCH):
            t_sl = slice(ch * C, (ch + 1) * C)
            p.emit(t_, "transpose", pbank(4, 128, NH), sig_all[:, t_sl],
                   identf[0:NH, 0:NH], rd=["sig_all", "identf", ("pbk", 4)],
                   wr=[("pbk", 4)])
            p.emit(v_, "tensor_copy", sigT[:, 0:NH], pbank(4, 128, NH),
                   rd=[("pbk", 4)], wr=["sigT"])
            p.emit(v_, "tensor_scalar", w_st[:], sig_all[:, t_sl], -1.0,
                   s_all[:, ch * C + C - 1:ch * C + C], op0=OP.mult,
                   op1=OP.add, rd=["sig_all", "s_all"], wr=["w_st"])
            p.emit(sc, "activation", w_st[:], w_st[:], AF.Exp, rd=["w_st"],
                   wr=["w_st"])
            p.emit(t_, "transpose", pbank(4, 128, NH), w_st[:],
                   identf[0:NH, 0:NH], rd=["w_st", "identf", ("pbk", 4)],
                   wr=[("pbk", 4)])
            p.emit(v_, "tensor_copy", wT[:, 0:NH], pbank(4, 128, NH),
                   rd=[("pbk", 4)], wr=["wT"])
            p.emit(sc, "activation", esc_c[:], s_all[:, ch * C + C - 1:ch * C + C],
                   AF.Exp, rd=["s_all"], wr=["esc_c"])
            p.emit(t_, "transpose", pbank(4, 1, NH), esc_c[:],
                   identf[0:NH, 0:NH],
                   rd=["esc_c", "identf", ("pbk", 4)], wr=[("pbk", 4)])
            p.emit(v_, "tensor_copy", rsc[:, 0:NH], pbank(4, 1, NH),
                   rd=[("pbk", 4)], wr=["rsc"])
            p.emit(t_, "matmul", pbank(5, NS, NH), ones1[:, 0:NS],
                   rsc[:, 0:NH], start=True, stop=True,
                   rd=["ones1", "rsc", ("pbk", 5)], wr=[("pbk", 5)])
            p.emit(v_, "tensor_copy", escb[:, 0:NH], pbank(5, NS, NH),
                   rd=[("pbk", 5)], wr=["escb"])
            bbB = 16 * (T + 3) + 3 + ch * C
            bbC = 17 * (T + 3) + 3 + ch * C
            B_sl = xbc[0:NS, bbB:bbB + C]
            C_sl = xbc[0:NS, bbC:bbC + C]
            p.emit(t_, "matmul", pbank(4, 128, C), B_sl, C_sl, start=True,
                   stop=True, rd=[("xbc", 16), ("xbc", 17), ("pbk", 4)], wr=[("pbk", 4)])
            p.emit(v_, "tensor_copy", g_s[:], pbank(4, 128, C),
                   rd=[("pbk", 4)], wr=["g_s"])
            p.emit(t_, "transpose", pbank_b(4, 128, NS), B_sl,
                   identb[0:NS, 0:NS], rd=[("xbc", 16), "identb", ("pbk", 4)],
                   wr=[("pbk", 4)])
            p.emit(v_, "tensor_copy", b_tm[:, 0:NS], pbank_b(4, 128, NS),
                   rd=[("pbk", 4)], wr=["b_tm"])

            for h in range(NH):
                pe = h % 2
                yb = 6 + pe
                tb = 4 + pe
                rb = 0 + pe
                cb = 2 + pe
                xoff = h * (T + 3) + 3 + ch * C
                p.emit(t_, "transpose", pbank_b(tb, 128, C),
                       xbc[:, xoff:xoff + C], identb[:],
                       rd=[("xbc", h), "identb", ("pbk", tb)], wr=[("pbk", tb)])
                p.emit(sc, "activation", x_tm[pe][:], pbank_b(tb, 128, C),
                       AF.Identity, rd=[("pbk", tb)], wr=[("x_tm", pe)])
                p.emit(t_, "matmul", pbank(rb, 128, C),
                       e16[:, h * 128:(h + 1) * 128], s_all[:, t_sl],
                       start=True, stop=True,
                       rd=["e16", "s_all", ("pbk", rb)], wr=[("pbk", rb)])
                p.emit(v_, "scalar_tensor_tensor", mask_s[pe][:],
                       pbank(rb, 128, C), sigT[:, h:h + 1], trineg[:],
                       op0=OP.subtract, op1=OP.add,
                       rd=[("pbk", rb), "sigT", "trineg"],
                       wr=[("mask", pe)])
                p.emit(sc, "activation", mask_s[pe][:], mask_s[pe][:], AF.Exp,
                       rd=[("mask", pe)], wr=[("mask", pe)])
                p.emit(v_, "tensor_tensor", gmb[pe][:], g_s[:], mask_s[pe][:],
                       op=OP.mult, rd=["g_s", ("mask", pe)], wr=[("gmb", pe)])
                p.emit(sc, "activation", es_b[pe][:], pbank(rb, NS, C),
                       AF.Exp, rd=[("pbk", rb)], wr=[("esb", pe)])
                p.emit(v_, "tensor_tensor", c_til[pe][:, 0:C], es_b[pe][:],
                       C_sl, op=OP.mult, rd=[("esb", pe), ("xbc", 17)],
                       wr=[("ctil", pe)])
                p.emit(v_, "tensor_copy", ht_b[pe][:],
                       ht[:, h * HD:(h + 1) * HD], rd=[("ht", h)],
                       wr=[("htb", pe)])
                p.emit(t_, "matmul", pbank(yb, 128, C), x_tm[pe][:], gmb[pe][:],
                       start=True, stop=False,
                       rd=[("x_tm", pe), ("gmb", pe)], wr=[("pbk", yb)],
                       inc=False)
                p.emit(t_, "matmul", pbank(yb, 128, C), ht_b[pe][:],
                       c_til[pe][:, 0:C], start=False, stop=True,
                       rd=[("htb", pe), ("ctil", pe)], wr=[("pbk", yb)])
                p.emit(v_, "tensor_scalar", x_til[pe][:], x_tm[pe][:],
                       wT[:, h:h + 1], None, op0=OP.mult,
                       rd=[("x_tm", pe), "wT"], wr=[("xtil", pe)])
                p.emit(t_, "matmul", pbank(rb, NS, HD), b_tm[:, 0:NS],
                       x_til[pe][:], start=True, stop=True,
                       rd=["b_tm", ("xtil", pe), ("pbk", rb)], wr=[("pbk", rb)])
                p.emit(v_, "scalar_tensor_tensor", ht[:, h * HD:(h + 1) * HD],
                       ht[:, h * HD:(h + 1) * HD], escb[:, h:h + 1],
                       pbank(rb, NS, HD), op0=OP.mult, op1=OP.add,
                       rd=[("ht", h), "escb", ("pbk", rb), ("htb", pe)],
                       wr=[("ht", h)])
                p.emit(v_, "scalar_tensor_tensor", dx[pe][:],
                       xbc[:, xoff:xoff + C], d_b[:, h:h + 1],
                       pbank(yb, 128, C), op0=OP.mult, op1=OP.add,
                       rd=[("xbc", h), "d_b", ("pbk", yb)],
                       wr=[("dx", pe), ("pbk", yb)])
                p.emit(v_, "tensor_tensor", y_g[:, h * C:(h + 1) * C],
                       dx[pe][:], z_silu[:, h * T + ch * C:h * T + ch * C + C],
                       op=OP.mult, rd=[("dx", pe), ("zs", h)], wr=[("y_g", h)])
            for h in range(NH):
                sq_ = (sqA, sqA2)[h % 2]
                p.emit(sc, "activation", sq_[:, 0:C], y_g[:, h * C:(h + 1) * C],
                       AF.Square, rd=[("y_g", h)], wr=[("sq_scr", h % 2)])
                p.emit(t_, "matmul", pbank(5, 1, C), onescol_b[:],
                       sq_[:, 0:C], start=(h == 0), stop=(h == NH - 1),
                       rd=["onescolb", ("sq_scr", h % 2)],
                       wr=[("pbk", 5)] if h in (0, NH - 1) else (), inc=True)
            p.emit(v_, "tensor_scalar", crowA[0:1, 0:C], pbank(5, 1, C),
                   1.0 / EI, None, op0=OP.mult, rd=[("pbk", 5)], wr=["crow"])
            p.emit(sc, "activation", crowA[0:1, 0:C], crowA[0:1, 0:C], AF.Ln,
                   bias=eps_col[0:1, :], rd=["crow", "eps_col"], wr=["crow"])
            rrb_b = crowA[:].bitcast(BF16)[0:1, 0:C]
            p.emit(sc, "activation", rrb_b, crowA[0:1, 0:C], AF.Exp,
                   scale=-0.5, rd=["crow"], wr=["crow"])
            p.emit(t_, "matmul", pbank(5, 128, C), ones1b[:], rrb_b,
                   start=True, stop=True, rd=["ones1b", "crow", ("pbk", 5)],
                   wr=[("pbk", 5)])
            for h in range(NH):
                p.emit(v_, "scalar_tensor_tensor",
                       y_n[:, h * T + ch * C:h * T + ch * C + C],
                       y_g[:, h * C:(h + 1) * C], nw[:, h:h + 1],
                       pbank(5, 128, C), op0=OP.mult, op1=OP.mult,
                       rd=[("y_g", h), "nw", ("pbk", 5)], wr=[("y_n", h)])

        for ke in range(NE):
            pe = ke % 2
            dma(s_, wo_buf[pe][:], w_outbh[ke * 128:(ke + 1) * 128, :],
                after=[p.last_w.get(("wobh", ke))],
                wr=[("wo", pe)], dsem="dwo%d" % pe)
            for dd in range(ND):
                p.emit(t_, "matmul", pbank(dd, 128, T),
                       wo_buf[pe][:, dd * 128:(dd + 1) * 128],
                       y_n[:, ke * T:(ke + 1) * T],
                       start=(ke == 0), stop=(ke == NE - 1),
                       rd=[("wo", pe), ("y_n", ke)],
                       wr=([("pbk", dd)] if ke in (0, NE - 1) else []) +
                          ([("wo", pe)] if dd == ND - 1 else []),
                       inc=(ke == NE - 1 or dd == ND - 1))
        for dd in range(ND):
            p.emit(v_, "tensor_copy", m_o[:, dd * T:(dd + 1) * T],
                   pbank(dd, 128, T), rd=[("pbk", dd)], wr=[("m_o", dd)])

        def mosrc(i):
            return m_o[:, i * T:(i + 1) * T]

        ns["srckeys"] = [("m_o", i) for i in range(ND)]
        bca, bcb = ln_full("ln2", mosrc, ND, D, T)
        for i in range(ND):
            ap_ = apA[i % 2]
            p.emit(v_, "tensor_tensor", ap_[:], mosrc(i), bca, op=OP.mult,
                   rd=[("m_o", i), ("pbk", 2)], wr=[("apA", i % 2)])
            p.emit(v_, "tensor_tensor", ap_[:], ap_[:], bcb, op=OP.add,
                   rd=[("apA", i % 2), ("pbk", 3)], wr=[("apA", i % 2)])
            p.emit(v_, "tensor_tensor", xb[:, i * T:(i + 1) * T], xsrc(i),
                   ap_[:], op=OP.add, rd=[("apA", i % 2), "xb"], wr=["xb"])
        tgt = out_fm if cfg.debug == "x1" else x1h
        dma(s_, fm3(tgt[b, :, t0:t0 + T]), sb3(xb[:], T),
            rd=["xb"], wr=[("x1h", mt)])
        p.last_w["xb"] = p.last_w[("x1h", mt)]

    AR_A.pop()

    if cfg.debug == "x1":
        p.wait_all("gpsimd")
        block = es.enter_context(nc.Block())
        p.replay(block)
        es.close()
        return nc

    p.barrier()
    AR_A.push()
    w1b = AR_A.bf16(128, ND * FF)
    w2buf = [AR_A.bf16(128, 1024) for _ in range(2)]
    x1b = [AR_A.bf16(128, ND * 512) for _ in range(2)]
    xln3 = AR_A.bf16(128, ND * 512)
    act = AR_A.bf16(128, NFF * 512)
    h1b = [AR_A.bf16(128, 512) for _ in range(2)]
    x2b = [AR_A.bf16(128, 512) for _ in range(2)]
    ab_ = [AR_A.bf16(128, 512) for _ in range(2)]
    sgb = [AR_A.bf16(128, 512) for _ in range(2)]
    y0st = AR_A.bf16(128, ND * 512)
    lg_sb = AR_A.f32(8, 512)
    lt_sb = AR_A.f32(128, 8)
    nmax = AR_A.f32(128, 1)
    qex = AR_A.f32(128, 8)
    ssum = AR_A.f32(128, 1)
    p0c = AR_A.f32(128, 1)
    p0row = AR_A.f32(1, 512)
    sqB = AR_A.bf16(128, 512)
    sqB2 = AR_A.bf16(128, 512)
    apB = [AR_A.f32(128, 512) for _ in range(2)]
    arowB = AR_A.f32(1, 512)
    browB = AR_A.f32(1, 512)
    crowB = AR_A.f32(1, 512)
    cntc = AR_A.f32(128, 1)
    cnt_loc = AR_A.f32(1, 8)
    cacc = AR_A.f32(1, 128)
    cacc2 = AR_A.f32(128, 64)
    wblk = AR_A.f32(128, 8192)
    wcnt1 = [wblk[:, 0:2048], wblk[:, 2048:4096]]
    wcnt2 = [wblk[:, 4096:6144], wblk[:, 6144:8192]]
    wstgA = wblk[:, 4096:5120]
    absA = wblk[:, 5120:6144]
    wstgB = wblk[:].bitcast(BF16)[:, 2 * 6144:2 * 6144 + 1024]
    y1s = wblk[:, 0:4096]
    ns.update(arow=arowB, brow=browB, crow=crowB, sq=(sqB, sqB2))

    for kd in range(ND):
        dma(g, w1b[:, kd * FF:(kd + 1) * FF], w1[kd * 128:(kd + 1) * 128, :],
            wr=[("w1b", kd)])
    p.patch_total([("w1b", kd) for kd in range(ND)], "DG")

    for i in range(ND):
        p.emit(v_, "tensor_scalar", wrg[:, i * 8:(i + 1) * 8],
               wr_sb[:, i * 8:(i + 1) * 8], gam[:, i:i + 1], None, op0=OP.mult,
               rd=["wr", "gam"], wr=[("wrg", i)])
        p.emit(t_, "matmul", pbank(4, 8, 1), wr_sb[:, i * 8:(i + 1) * 8],
               gam[:, i:i + 1], start=(i == 0), stop=(i == ND - 1),
               rd=["wr", "gam"],
               wr=[("pbk", 4)] if i in (0, ND - 1) else (), inc=(i == ND - 1))
    p.emit(v_, "tensor_copy", cgam[:], pbank(4, 8, 1), rd=[("pbk", 4)],
           wr=["cgam"])
    for i in range(ND):
        p.emit(t_, "matmul", pbank(4, 8, 1), wr_sb[:, i * 8:(i + 1) * 8],
               bet[:, i:i + 1], start=(i == 0), stop=(i == ND - 1),
               rd=["wr", "bet"],
               wr=[("pbk", 4)] if i in (0, ND - 1) else (), inc=(i == ND - 1))
    p.emit(v_, "tensor_tensor", cbet[:], pbank(4, 8, 1), br_sb[:], op=OP.add,
           rd=[("pbk", 4), "br"], wr=["cbet"])

    def gelu_ffn(tag, xln3_keys, w1_ap, w1_keys, bias_ap, bias_key, w2_src,
                 w2_keys, bias2_ap, bias2_key, ydst, ydst_f32):
        for f in range(NFF):
            bk = f % 2
            ps = pbank(bk, 128, 512)
            for kd in range(ND):
                p.emit(t_, "matmul", ps,
                       w1_ap[:, kd * FF + f * 128:kd * FF + (f + 1) * 128],
                       xln3[:, kd * 512:(kd + 1) * 512],
                       start=(kd == 0), stop=(kd == ND - 1),
                       rd=[w1_keys[kd], ("xln3", kd)],
                       wr=[("pbk", bk)] if kd in (0, ND - 1) else (),
                       inc=(kd == ND - 1))
            p.emit(sc, "activation", h1b[bk][:], ps, AF.Identity,
                   bias=bias_ap[:, f:f + 1], rd=[("pbk", bk), bias_key],
                   wr=[("h1b", bk), ("pbk", bk)])
            p.emit(v_, "tensor_tensor", x2b[bk][:], h1b[bk][:], h1b[bk][:],
                   op=OP.mult, rd=[("h1b", bk)], wr=[("x2b", bk)])
            p.emit(v_, "tensor_scalar", ab_[bk][:], x2b[bk][:], GELU_C, 1.0,
                   op0=OP.mult, op1=OP.add, rd=[("x2b", bk)], wr=[("ab_", bk)])
            p.emit(v_, "tensor_tensor", ab_[bk][:], ab_[bk][:], h1b[bk][:],
                   op=OP.mult, rd=[("ab_", bk), ("h1b", bk)], wr=[("ab_", bk)])
            p.emit(sc, "activation", sgb[bk][:], ab_[bk][:], AF.Sigmoid,
                   scale=GELU_S, rd=[("ab_", bk)], wr=[("sgb", bk)])
            p.emit(v_, "tensor_tensor", act[:, f * 512:(f + 1) * 512],
                   h1b[bk][:], sgb[bk][:], op=OP.mult,
                   rd=[("h1b", bk), ("sgb", bk)], wr=[("act", f)])
        for kf in range(NFF):
            pe = kf % 2
            dma(s_, w2buf[pe][:], w2_src[kf * 128:(kf + 1) * 128, :],
                after=[p.last_w.get(w2_keys(kf))],
                wr=[("w2buf", pe)], dsem="dw2%d" % pe)
            for hh in range(ND):
                p.emit(t_, "matmul", pbank(hh, 128, 512),
                       w2buf[pe][:, hh * 128:(hh + 1) * 128],
                       act[:, kf * 512:(kf + 1) * 512],
                       start=(kf == 0), stop=(kf == NFF - 1),
                       rd=[("w2buf", pe), ("act", kf)],
                       wr=([("pbk", hh)] if kf in (0, NFF - 1) else []) +
                          ([("w2buf", pe)] if hh == ND - 1 else []),
                       inc=(kf == NFF - 1 or hh == ND - 1))
        for hh in range(ND):
            p.emit(sc, "activation", ydst[:, hh * 512:(hh + 1) * 512],
                   pbank(hh, 128, 512), AF.Identity,
                   bias=bias2_ap[:, hh:hh + 1],
                   rd=[("pbk", hh), bias2_key], wr=[(tag + "_y", hh)])

    def gelu_h1(w1_ap, w1_keys, bias_ap, bias_key):
        for f in range(NFF):
            bk = f % 2
            ps = pbank(bk, 128, 512)
            for kd in range(ND):
                p.emit(t_, "matmul", ps,
                       w1_ap[:, kd * FF + f * 128:kd * FF + (f + 1) * 128],
                       xln3[:, kd * 512:(kd + 1) * 512],
                       start=(kd == 0), stop=(kd == ND - 1),
                       rd=[w1_keys[kd], ("xln3", kd)],
                       wr=[("pbk", bk)] if kd in (0, ND - 1) else (),
                       inc=(kd == ND - 1))
            p.emit(sc, "activation", h1b[bk][:], ps, AF.Identity,
                   bias=bias_ap[:, f:f + 1], rd=[("pbk", bk), bias_key],
                   wr=[("h1b", bk), ("pbk", bk)])
            p.emit(v_, "tensor_tensor", x2b[bk][:], h1b[bk][:], h1b[bk][:],
                   op=OP.mult, rd=[("h1b", bk)], wr=[("x2b", bk)])
            p.emit(v_, "tensor_scalar", ab_[bk][:], x2b[bk][:], GELU_C, 1.0,
                   op0=OP.mult, op1=OP.add, rd=[("x2b", bk)], wr=[("ab_", bk)])
            p.emit(v_, "tensor_tensor", ab_[bk][:], ab_[bk][:], h1b[bk][:],
                   op=OP.mult, rd=[("ab_", bk), ("h1b", bk)], wr=[("ab_", bk)])
            p.emit(sc, "activation", sgb[bk][:], ab_[bk][:], AF.Sigmoid,
                   scale=GELU_S, rd=[("ab_", bk)], wr=[("sgb", bk)])
            p.emit(v_, "tensor_tensor", act[:, f * 512:(f + 1) * 512],
                   h1b[bk][:], sgb[bk][:], op=OP.mult,
                   rd=[("h1b", bk), ("sgb", bk)], wr=[("act", f)])

    def gelu_y(tag, w2_src, w2_keys, bias2_ap, bias2_key, ydst):
        for kf in range(NFF):
            pe = kf % 2
            dma(s_, w2buf[pe][:], w2_src[kf * 128:(kf + 1) * 128, :],
                after=[p.last_w.get(w2_keys(kf))],
                wr=[("w2buf", pe)], dsem="dw2%d" % pe)
            for hh in range(ND):
                p.emit(t_, "matmul", pbank(hh, 128, 512),
                       w2buf[pe][:, hh * 128:(hh + 1) * 128],
                       act[:, kf * 512:(kf + 1) * 512],
                       start=(kf == 0), stop=(kf == NFF - 1),
                       rd=[("w2buf", pe), ("act", kf)],
                       wr=([("pbk", hh)] if kf in (0, NFF - 1) else []) +
                          ([("w2buf", pe)] if hh == ND - 1 else []),
                       inc=(kf == NFF - 1 or hh == ND - 1))
        for hh in range(ND):
            p.emit(sc, "activation", ydst[:, hh * 512:(hh + 1) * 512],
                   pbank(hh, 128, 512), AF.Identity,
                   bias=bias2_ap[:, hh:hh + 1],
                   rd=[("pbk", hh), bias2_key], wr=[(tag + "_y", hh)])

    for tt in range(NTT):
        b = (tt * 512) // L
        t0 = (tt * 512) % L
        dma(g, sb3(x1b[0][:], 512), fm3(x1h[b, :, t0:t0 + 512]),
            rd=["x1b"], wr=["x1b"], dsem="dx1b")

        def x1src(i):
            return x1b[0][:, i * 512:(i + 1) * 512]

        ns["srckeys"] = ["x1b"] * ND
        bca, bcb = ln_full("ln3", x1src, ND, D, 512, src_bf16=True)
        for i in range(ND):
            ap_ = apB[i % 2]
            p.emit(v_, "tensor_tensor", ap_[:], x1src(i), bca, op=OP.mult,
                   rd=["x1b", ("pbk", 2)], wr=[("apB", i % 2)])
            p.emit(v_, "tensor_tensor", xln3[:, i * 512:(i + 1) * 512],
                   ap_[:], bcb, op=OP.add, rd=[("apB", i % 2), ("pbk", 3)],
                   wr=[("xln3", i)])
        dma(s_, fm3(xln3h[b, :, t0:t0 + 512]), sb3(xln3[:], 512),
            rd=[("xln3", i) for i in range(ND)], wr=[("xln3h", tt)])

        def x3src(i):
            return xln3[:, i * 512:(i + 1) * 512]

        ns["srckeys"] = [("xln3", i) for i in range(ND)]
        bca_r, bcb_r = ln_full("lnr", x3src, ND, D, 512, src_bf16=True)
        lps = pbank(4, 8, 512)
        for i in range(ND):
            p.emit(t_, "matmul", lps, wrg[:, i * 8:(i + 1) * 8], x3src(i),
                   start=(i == 0), stop=(i == ND - 1),
                   rd=[("wrg", i), ("xln3", i)],
                   wr=[("pbk", 4)] if i in (0, ND - 1) else (), inc=(i == ND - 1))
        p.emit(v_, "tensor_copy", lg_sb[:], lps, rd=[("pbk", 4)], wr=["lg"])
        p.emit(v_, "tensor_tensor", lg_sb[:], lg_sb[:], pbank(2, 8, 512),
               op=OP.mult, rd=["lg", ("pbk", 2), ("pbk", 4)], wr=["lg"])
        p.emit(v_, "scalar_tensor_tensor", lg_sb[:], pbank(3, 8, 512),
               cgam[:], lg_sb[:], op0=OP.mult, op1=OP.add,
               rd=[("pbk", 3), "cgam", "lg", ("pbk", 4)], wr=["lg"])
        p.emit(v_, "tensor_scalar", lg_sb[:], lg_sb[:], cbet[:], None,
               op0=OP.add, rd=["lg", "cbet"], wr=["lg"])
        for st in range(4):
            p.emit(t_, "transpose", pbank(5, 128, 8),
                   lg_sb[:, st * 128:(st + 1) * 128], identf[0:8, 0:8],
                   rd=["lg", "identf", ("pbk", 5)], wr=[("pbk", 5)])
            p.emit(v_, "tensor_copy", lt_sb[:], pbank(5, 128, 8),
                   rd=[("pbk", 5)], wr=["lt"])
            p.emit(v_, "tensor_reduce", nmax[:], lt_sb[:],
                   axis=mybir.AxisListType.X, op=OP.max, rd=["lt"],
                   wr=["nmax"])
            p.emit(v_, "tensor_scalar", nmax[:], nmax[:], -1.0, None,
                   op0=OP.mult, rd=["nmax"], wr=["nmax"])
            p.emit(sc, "activation", qex[:], lt_sb[:], AF.Exp, bias=nmax[:],
                   rd=["lt", "nmax"], wr=["qex"])
            p.emit(v_, "tensor_reduce", ssum[:], qex[:],
                   axis=mybir.AxisListType.X, op=OP.add, rd=["qex"],
                   wr=["ssum"])
            p.emit(v_, "reciprocal", ssum[:], ssum[:], rd=["ssum"],
                   wr=["ssum"])
            p.emit(v_, "tensor_tensor", p0c[:], qex[:, 0:1], ssum[:],
                   op=OP.mult, rd=["qex", "ssum"], wr=["p0c"])
            p.emit(t_, "transpose", pbank(5, 1, 128), p0c[:], identf[:],
                   rd=["p0c", "identf", ("pbk", 5)], wr=[("pbk", 5)])
            p.emit(v_, "tensor_copy", p0row[:, st * 128:(st + 1) * 128],
                   pbank(5, 1, 128), rd=[("pbk", 5)], wr=["p0row"])
        dma(s_, p0h[:, tt * 512:(tt + 1) * 512], p0row[:], rd=["p0row"],
            wr=[("p0h", tt)])

        gelu_ffn("y0", None, w1b, [("w1b", kd) for kd in range(ND)], b1c,
                 "b1c", w2bh, lambda kf: ("w2bh", kf), b2c, "b2c", y0st, None)
        dma(s_, fm3(y0h[b, :, t0:t0 + 512]), sb3(y0st[:], 512),
            rd=[("y0_y", hh) for hh in range(ND)], wr=[("y0h", tt)])
        for hh in range(ND):
            p.last_w[("y0_y", hh)] = p.last_w[("y0h", tt)]

    p.barrier()
    w1v = w1.rearrange("(a p) f -> p a f", p=128)
    w2v = w2.rearrange("(a p) d -> p a d", p=128)
    w2bv = w2bh.rearrange("(a p) d -> p a d", p=128)

    def bis_update():
        pr, inv, tmp = bst[:, 0:8], bst[:, 8:16], bst[:, 16:24]
        p.emit(v_, "tensor_tensor", pr, cnt_loc[:], kvec[:], op=OP.is_ge,
               rd=["cnt_loc", "kvec"], wr=["pr"])
        p.emit(v_, "tensor_scalar", inv, pr, -1.0, 1.0, op0=OP.mult,
               op1=OP.add, rd=["pr"], wr=["inv"])
        p.emit(v_, "tensor_tensor", tmp, pr, bmid[:], op=OP.mult,
               rd=["pr", "bmid"], wr=["tmp"])
        p.emit(v_, "tensor_tensor", bhi[:], bhi[:], inv, op=OP.mult,
               rd=["bhi", "inv"], wr=["bhi"])
        p.emit(v_, "tensor_tensor", bhi[:], bhi[:], tmp, op=OP.add,
               rd=["bhi", "tmp"], wr=["bhi"])
        p.emit(v_, "tensor_tensor", tmp, inv, bmid[:], op=OP.mult,
               rd=["inv", "bmid"], wr=["tmp"])
        p.emit(v_, "tensor_tensor", blo[:], blo[:], pr, op=OP.mult,
               rd=["blo", "pr"], wr=["blo"])
        p.emit(v_, "tensor_tensor", blo[:], blo[:], tmp, op=OP.add,
               rd=["blo", "tmp"], wr=["blo"])

    def bis_mid():
        p.emit(v_, "tensor_tensor", bmid[:], blo[:], bhi[:], op=OP.add,
               rd=["blo", "bhi"], wr=["bmid"])
        p.emit(v_, "tensor_scalar", bmid[:], bmid[:], 0.5, None, op0=OP.mult,
               rd=["bmid"], wr=["bmid"])
        p.emit(t_, "matmul", pbank(4, 128, 8), ones1[:], bmid[:], start=True,
               stop=True, rd=["ones1", "bmid", ("pbk", 4)], wr=[("pbk", 4)])
        p.emit(v_, "tensor_copy", bcols[:], pbank(4, 128, 8),
               rd=[("pbk", 4)], wr=["bcols"])

    def wdat_b(buf, cols=2048):
        return buf[:].bitcast(BF16)[:, 0:cols]

    def wdat_f(buf, cols=1024):
        return buf[:, 0:cols]

    def wmsk(buf, cols):
        return buf[:].bitcast(BF16)[:, 2048:2048 + cols]

    def cnt_mask_mm(j, buf, srcap, rdk, mcols, first, last, mkey, bank):
        p.emit(v_, "tensor_scalar", wmsk(buf, mcols), srcap,
               bcols[:, j:j + 1], 0.0, op0=OP.is_lt, op1=OP.add,
               rd=rdk + ["bcols"], wr=[mkey])
        nmm = mcols // 512
        for sb in range(nmm):
            p.emit(t_, "matmul", pbank(bank, 1, 512), onescol_b[:],
                   wmsk(buf, mcols)[:, sb * 512:(sb + 1) * 512],
                   start=(first and sb == 0), stop=(last and sb == nmm - 1),
                   rd=["onescolb", mkey], wr=[("pbk", bank)],
                   inc=(sb == nmm - 1))

    def bis_tally(j):
        p.emit(v_, "tensor_reduce", cnt_loc[:, j:j + 1],
               pbank(j, 1, 512), axis=mybir.AxisListType.X, op=OP.add,
               rd=[("pbk", j)], wr=["cnt_loc", ("pbk", j)])

    def bis_cnt_small(j, srcap, nn, rdk):
        p.emit(v_, "memset", cntc[:], 0.0, rd=["cntc"], wr=["cntc"])
        p.emit(v_, "tensor_scalar", wmsk(wcnt1[0], 1024)[:, 0:nn], srcap,
               bcols[:, j:j + 1], 0.0, op0=OP.is_lt, op1=OP.add,
               accum_out=cntc[:], rd=rdk + ["bcols", "cntc"],
               wr=["cntc", ("msk1", 0)])
        p.emit(t_, "matmul", pbank(5, 1, 1), cntc[:], onescol[:],
               start=True, stop=True, rd=["cntc", "onescol", ("pbk", 5)],
               wr=[("pbk", 5)])
        p.emit(v_, "tensor_copy", cnt_loc[:, j:j + 1], pbank(5, 1, 1),
               rd=[("pbk", 5)], wr=["cnt_loc"])

    def bis_update(js):
        pr, inv, tmp = bst[:, 0:8], bst[:, 8:16], bst[:, 16:24]
        sl = slice(js[0], js[-1] + 1)
        p.emit(v_, "tensor_tensor", pr[:, sl], cnt_loc[:, sl], kvec[:, sl],
               op=OP.is_ge, rd=["cnt_loc", "kvec"], wr=["pr"])
        p.emit(v_, "tensor_scalar", inv[:, sl], pr[:, sl], -1.0, 1.0,
               op0=OP.mult, op1=OP.add, rd=["pr"], wr=["inv"])
        p.emit(v_, "tensor_tensor", tmp[:, sl], pr[:, sl], bmid[:, sl],
               op=OP.mult, rd=["pr", "bmid"], wr=["tmp"])
        p.emit(v_, "tensor_tensor", bhi[:, sl], bhi[:, sl], inv[:, sl],
               op=OP.mult, rd=["bhi", "inv"], wr=["bhi"])
        p.emit(v_, "tensor_tensor", bhi[:, sl], bhi[:, sl], tmp[:, sl],
               op=OP.add, rd=["bhi", "tmp"], wr=["bhi"])
        p.emit(v_, "tensor_tensor", tmp[:, sl], inv[:, sl], bmid[:, sl],
               op=OP.mult, rd=["inv", "bmid"], wr=["tmp"])
        p.emit(v_, "tensor_tensor", blo[:, sl], blo[:, sl], pr[:, sl],
               op=OP.mult, rd=["blo", "pr"], wr=["blo"])
        p.emit(v_, "tensor_tensor", blo[:, sl], blo[:, sl], tmp[:, sl],
               op=OP.add, rd=["blo", "tmp"], wr=["blo"])

    def bis_mid(js):
        sl = slice(js[0], js[-1] + 1)
        p.emit(v_, "tensor_tensor", bmid[:, sl], blo[:, sl], bhi[:, sl],
               op=OP.add, rd=["blo", "bhi"], wr=["bmid"])
        p.emit(v_, "tensor_scalar", bmid[:, sl], bmid[:, sl], 0.5, None,
               op0=OP.mult, rd=["bmid"], wr=["bmid"])
        p.emit(t_, "matmul", pbank(4, 128, 8), ones1[:], bmid[:], start=True,
               stop=True, rd=["ones1", "bmid", ("pbk", 4)], wr=[("pbk", 4)])
        p.emit(v_, "tensor_copy", bcols[:], pbank(4, 128, 8),
               rd=[("pbk", 4)], wr=["bcols"])

    for c in range(ND):
        p.emit(sc, "activation", w1b[:, c * FF:(c + 1) * FF],
               w1b[:, c * FF:(c + 1) * FF], AF.Abs, rd=[("w1b", c)],
               wr=[("w1b", c)])

    mxc = cntc
    mx16 = cacc2[:, 0:16]
    for c in range(ND):
        p.emit(v_, "tensor_reduce", mx16[:, c:c + 1],
               w1b[:, c * FF:(c + 1) * FF], axis=mybir.AxisListType.X,
               op=OP.max, rd=[("w1b", c)], wr=["mx16"])
    p.emit(v_, "tensor_reduce", mxc[:], mx16[:, 0:ND],
           axis=mybir.AxisListType.X, op=OP.max, rd=["mx16"], wr=["mxc"])
    p.emit(t_, "transpose", pbank(4, 1, 128), mxc[:], identf[:],
           rd=["mxc", "identf", ("pbk", 4)], wr=[("pbk", 4)])
    p.emit(v_, "tensor_reduce", bhi[:, 0:1], pbank(4, 1, 128),
           axis=mybir.AxisListType.X, op=OP.max, rd=[("pbk", 4)], wr=["bhi"])
    for c in range(16):
        pe = c % 2
        dstb = wdat_b(wcnt1[pe])
        dma(g, dstb.rearrange("p (a d) -> p a d", d=1024),
            w2bv[:, 2 * c:2 * c + 2, :], wr=[("wcb", pe)],
            dsem="dcn%d" % pe)
        p.emit(sc, "activation", dstb, dstb, AF.Abs,
               rd=[("wcb", pe)], wr=[("wcb", pe)])
        p.emit(v_, "tensor_reduce", mx16[:, c:c + 1], dstb,
               axis=mybir.AxisListType.X, op=OP.max,
               rd=[("wcb", pe)], wr=["mx16"])
        p.last_w[("wcb", pe)] = p.last_w["mx16"]
    p.emit(v_, "tensor_reduce", mxc[:], mx16[:, 0:16],
           axis=mybir.AxisListType.X, op=OP.max, rd=["mx16"], wr=["mxc"])
    p.emit(t_, "transpose", pbank(4, 1, 128), mxc[:], identf[:],
           rd=["mxc", "identf", ("pbk", 4)], wr=[("pbk", 4)])
    p.emit(v_, "tensor_reduce", bhi[:, 2:3], pbank(4, 1, 128),
           axis=mybir.AxisListType.X, op=OP.max, rd=[("pbk", 4)], wr=["bhi"])
    for (src_, nn_, col, key) in ((ab1, NFF, 4, "ab1"), (ab2, ND, 6, "ab2")):
        p.emit(v_, "tensor_reduce", mxc[:], src_[:, 0:nn_],
               axis=mybir.AxisListType.X, op=OP.max, rd=[key], wr=["mxc"])
        p.emit(t_, "transpose", pbank(4, 1, 128), mxc[:], identf[:],
               rd=["mxc", "identf", ("pbk", 4)], wr=[("pbk", 4)])
        p.emit(v_, "tensor_reduce", bhi[:, col:col + 1], pbank(4, 1, 128),
               axis=mybir.AxisListType.X, op=OP.max, rd=[("pbk", 4)],
               wr=["bhi"])
    for col in (0, 2, 4, 6):
        p.emit(v_, "tensor_copy", bhi[:, col + 1:col + 2],
               bhi[:, col:col + 1], rd=["bhi"], wr=["bhi"])
    p.emit(v_, "tensor_scalar", bhi[:], bhi[:], 1.01, None, op0=OP.mult,
           rd=["bhi"], wr=["bhi"])

    for it in range(40):
        bis_mid((4, 5, 6, 7))
        for j, (src_, nn_, key) in ((4, (ab1, NFF, "ab1")),
                                    (5, (ab1, NFF, "ab1")),
                                    (6, (ab2, ND, "ab2")),
                                    (7, (ab2, ND, "ab2"))):
            bis_cnt_small(j, src_[:, 0:nn_], nn_, [key])
        bis_update((4, 5, 6, 7))

    dma(s_, dbgh[:, 32:40], blo[:], rd=["blo"], wr=["dbg"])
    for stage in (1, 2):
        n_iter = 16 if stage == 1 else 18
        nch = 16 if stage == 1 else 32
        for it in range(n_iter):
            bis_mid((0, 1, 2, 3))
            for hc in range(nch):
                pe = hc % 2
                first, last = (hc == 0), (hc == nch - 1)
                if stage == 1:
                    srcap = w1b[:, hc * 2048:(hc + 1) * 2048]
                    rdk1 = [("w1b", hc // 2)]
                    mcols = 2048
                else:
                    dma(g, wdat_f(wcnt1[pe]),
                        w1[(hc // 4) * 128:(hc // 4) * 128 + 128,
                           (hc % 4) * 1024:(hc % 4) * 1024 + 1024],
                        wr=[("wcb", pe)], dsem="dcn%d" % pe)
                    p.emit(sc, "activation", wdat_f(wcnt1[pe]),
                           wdat_f(wcnt1[pe]), AF.Abs, rd=[("wcb", pe)],
                           wr=[("wcb", pe)])
                    srcap = wdat_f(wcnt1[pe])
                    rdk1 = [("wcb", pe)]
                    mcols = 1024
                for j in (0, 1):
                    cnt_mask_mm(j, wcnt1[pe], srcap, rdk1, mcols,
                                first, last, ("msk1", pe), bank=j)
                if stage == 2:
                    p.last_w[("wcb", pe)] = p.last_w[("msk1", pe)]
                if stage == 1:
                    dstb = wdat_b(wcnt2[pe])
                    dma(g, dstb.rearrange("p (a d) -> p a d", d=1024),
                        w2bv[:, 2 * hc:2 * hc + 2, :], wr=[("wc2", pe)],
                        dsem="dc2%d" % pe)
                    p.emit(sc, "activation", dstb, dstb, AF.Abs,
                           rd=[("wc2", pe)], wr=[("wc2", pe)])
                    srcap2 = dstb
                    mcols = 2048
                else:
                    dma(g, wdat_f(wcnt2[pe]).rearrange(
                            "p (a d) -> p a d", d=1024),
                        w2v[:, hc:hc + 1, :], wr=[("wc2", pe)],
                        dsem="dc2%d" % pe)
                    p.emit(sc, "activation", wdat_f(wcnt2[pe]),
                           wdat_f(wcnt2[pe]), AF.Abs, rd=[("wc2", pe)],
                           wr=[("wc2", pe)])
                    srcap2 = wdat_f(wcnt2[pe])
                    mcols = 1024
                for j in (2, 3):
                    cnt_mask_mm(j, wcnt2[pe], srcap2, [("wc2", pe)], mcols,
                                first, last, ("msk2", pe), bank=j)
                p.last_w[("wc2", pe)] = p.last_w[("msk2", pe)]
            for j in (0, 1, 2, 3):
                bis_tally(j)
            bis_update((0, 1, 2, 3))
        if stage == 1:
            p.emit(v_, "tensor_scalar", bhi[:, 0:4], blo[:, 0:4], 1.0078125,
                   None, op0=OP.mult, rd=["blo"], wr=["bhi"])
            p.emit(v_, "tensor_scalar", blo[:, 0:4], blo[:, 0:4], 0.9921875,
                   None, op0=OP.mult, rd=["blo"], wr=["blo"])
    p.emit(t_, "matmul", pbank(4, 128, 8), ones1[:], blo[:], start=True,
           stop=True, rd=["ones1", "blo", ("pbk", 4)], wr=[("pbk", 4)])
    p.emit(v_, "tensor_copy", bcols[:], pbank(4, 128, 8), rd=[("pbk", 4)],
           wr=["bcols"])
    p.barrier()

    def perturb(dst, dst_keys, src, src_key, absrc, abs_key, n, jb, jt,
                m1, m2):
        p.emit(v_, "tensor_scalar", m1[:, 0:n], absrc[:, 0:n],
               bcols[:, jb:jb + 1], 0.8, op0=OP.is_le, op1=OP.mult,
               rd=[abs_key, "bcols"], wr=["pm1"])
        p.emit(v_, "tensor_scalar", m2[:, 0:n], absrc[:, 0:n],
               bcols[:, jt:jt + 1], 0.8, op0=OP.is_ge, op1=OP.mult,
               rd=[abs_key, "bcols"], wr=["pm2"])
        p.emit(v_, "tensor_tensor", m1[:, 0:n], m1[:, 0:n], m2[:, 0:n],
               op=OP.subtract, rd=["pm1", "pm2"], wr=["pm1"])
        p.emit(v_, "tensor_scalar", m1[:, 0:n], m1[:, 0:n], 1.0, None,
               op0=OP.add, rd=["pm1"], wr=["pm1"])
        p.emit(v_, "tensor_tensor", dst[:, 0:n], src[:, 0:n], m1[:, 0:n],
               op=OP.mult, rd=["pm1", src_key], wr=dst_keys)

    perturb(b1p, ["b1p"], b1c, "b1c", ab1, "ab1", NFF, 4, 5,
            y1s[:, 0:64], y1s[:, 64:128])
    perturb(b2p, ["b2p"], b2c, "b2c", ab2, "ab2", ND, 6, 7,
            y1s[:, 0:64], y1s[:, 64:128])

    for kf in range(NFF):
        dma(s_, wstgA[:], w2[kf * 128:(kf + 1) * 128, :], rd=["wstgA"],
            wr=["wstgA"], dsem="dwst")
        p.emit(sc, "activation", absA[:], wstgA[:], AF.Abs,
               rd=["wstgA"], wr=["absA"])
        m1 = y1s[:, 0:1024]
        m2 = y1s[:, 1024:2048]
        p.emit(v_, "tensor_scalar", m1, absA[:], bcols[:, 2:3], 0.8,
               op0=OP.is_le, op1=OP.mult, rd=["absA", "bcols"], wr=["pm1"])
        p.emit(v_, "tensor_scalar", m2, absA[:], bcols[:, 3:4], 0.8,
               op0=OP.is_ge, op1=OP.mult, rd=["absA", "bcols"], wr=["pm2"])
        p.emit(v_, "tensor_tensor", m1, m1, m2, op=OP.subtract,
               rd=["pm1", "pm2"], wr=["pm1"])
        p.emit(v_, "tensor_scalar", m1, m1, 1.0, None, op0=OP.add,
               rd=["pm1"], wr=["pm1"])
        p.emit(v_, "tensor_tensor", wstgB[:], wstgA[:], m1, op=OP.mult,
               rd=["pm1", "wstgA"], wr=["wstgB"])
        dma(s_, pw2bh[kf * 128:(kf + 1) * 128, :], wstgB[:], rd=["wstgB"],
            wr=[("pw2bh", kf), "wstgB"], dsem="dwst")

    for kd in range(ND):
        for jj in range(4):
            dma(s_, wstgA[:],
                w1[kd * 128:(kd + 1) * 128, jj * 1024:(jj + 1) * 1024],
                rd=["wstgA"], wr=["wstgA"], dsem="dwst")
            p.emit(sc, "activation", absA[:], wstgA[:], AF.Abs,
                   rd=["wstgA"], wr=["absA"])
            m1 = y1s[:, 0:1024]
            m2 = y1s[:, 1024:2048]
            p.emit(v_, "tensor_scalar", m1, absA[:], bcols[:, 0:1], 0.8,
                   op0=OP.is_le, op1=OP.mult, rd=["absA", "bcols"], wr=["pm1"])
            p.emit(v_, "tensor_scalar", m2, absA[:], bcols[:, 1:2], 0.8,
                   op0=OP.is_ge, op1=OP.mult, rd=["absA", "bcols"], wr=["pm2"])
            p.emit(v_, "tensor_tensor", m1, m1, m2, op=OP.subtract,
                   rd=["pm1", "pm2"], wr=["pm1"])
            p.emit(v_, "tensor_scalar", m1, m1, 1.0, None, op0=OP.add,
                   rd=["pm1"], wr=["pm1"])
            p.emit(v_, "tensor_tensor",
                   w1b[:, kd * FF + jj * 1024:kd * FF + (jj + 1) * 1024],
                   wstgA[:], m1, op=OP.mult, rd=["pm1", "wstgA"],
                   wr=[("w1b", kd)])

    p.barrier()
    y0t = y0st

    def b2_loads1(tt):
        b = (tt * 512) // L
        t0 = (tt * 512) % L
        dma(g, sb3(x1b[tt % 2][:], 512), fm3(x1h[b, :, t0:t0 + 512]),
            rd=[("x1b", tt % 2)], wr=[("x1b", tt % 2)], dsem="dx1b")
        xt = dma(s_, sb3(xln3[:], 512), fm3(xln3h[b, :, t0:t0 + 512]),
                 rd=[("xln3", i) for i in range(ND)], wr=["xln3A"],
                 dsem="dxl3")
        for i in range(ND):
            p.last_w[("xln3", i)] = xt

    def b2_loads2(tt):
        b = (tt * 512) // L
        t0 = (tt * 512) % L
        dma(s_, sb3(y0t[:], 512), fm3(y0h[b, :, t0:t0 + 512]),
            rd=["y0t"], wr=["y0t"], dsem="dy0t")
        dma(s_, p0row[:], p0h[:, tt * 512:(tt + 1) * 512], rd=["p0row"],
            wr=["p0row"], dsem="dp0r")

    def b2_tail(tt):
        b = (tt * 512) // L
        t0 = (tt * 512) % L
        p.emit(t_, "matmul", pbank(4, 128, 512), ones1[:], p0row[:],
               start=True, stop=True, rd=["ones1", "p0row", ("pbk", 4)],
               wr=[("pbk", 4)])
        for hh in range(ND):
            ap_ = apB[hh % 2]
            p.emit(v_, "tensor_tensor", ap_[:], y0t[:, hh * 512:(hh + 1) * 512],
                   y1s[:, hh * 512:(hh + 1) * 512], op=OP.subtract,
                   rd=["y0t", ("y1_y", hh)], wr=[("apB", hh % 2)])
            p.emit(v_, "tensor_tensor", ap_[:], ap_[:], pbank(4, 128, 512),
                   op=OP.mult, rd=[("apB", hh % 2), ("pbk", 4)],
                   wr=[("apB", hh % 2)])
            p.emit(v_, "tensor_tensor", y1s[:, hh * 512:(hh + 1) * 512],
                   y1s[:, hh * 512:(hh + 1) * 512], ap_[:], op=OP.add,
                   rd=[("apB", hh % 2)], wr=[("y1_y", hh)])

        def ffsrc(i):
            return y1s[:, i * 512:(i + 1) * 512]

        ns["srckeys"] = [("y1_y", i) for i in range(ND)]
        bca, bcb = ln_full("ln4", ffsrc, ND, D, 512)
        for i in range(ND):
            ap_ = apB[i % 2]
            p.emit(v_, "tensor_tensor", ap_[:], ffsrc(i), bca, op=OP.mult,
                   rd=[("y1_y", i), ("pbk", 2)], wr=[("apB", i % 2)])
            p.emit(v_, "tensor_tensor", ap_[:], ap_[:], bcb, op=OP.add,
                   rd=[("apB", i % 2), ("pbk", 3)], wr=[("apB", i % 2)])
            p.emit(v_, "tensor_tensor", y1s[:, i * 512:(i + 1) * 512],
                   x1b[tt % 2][:, i * 512:(i + 1) * 512], ap_[:], op=OP.add,
                   rd=[("apB", i % 2), ("x1b", tt % 2)], wr=[("y1_y", i)])
        dma(s_, fm3(out_fm[b, :, t0:t0 + 512]),
            sb3(y1s[:, 0:ND * 512], 512),
            rd=[("y1_y", i) for i in range(ND)], wr=[("outh", tt)])
        for i in range(ND):
            p.last_w[("y1_y", i)] = p.last_w[("outh", tt)]

    b2_loads1(0)
    b2_loads2(0)
    gelu_h1(w1b, [("w1b", kd) for kd in range(ND)], b1p, "b1p")
    for tt in range(NTT):
        gelu_y("y1", pw2bh, lambda kf: ("pw2bh", kf), b2p, "b2p", y1s)
        if tt + 1 < NTT:
            b2_loads1(tt + 1)
            gelu_h1(w1b, [("w1b", kd) for kd in range(ND)], b1p, "b1p")
        b2_tail(tt)
        if tt + 1 < NTT:
            b2_loads2(tt + 1)

    p.wait_all("gpsimd")
    p.wait_all("vector")
    block = es.enter_context(nc.Block())
    p.replay(block)
    es.close()
    return nc



def _prep_shared(W_in, conv_w, conv_b, dt_bias, A_log, D_param, norm_w, W_out,
                 W1, b1, W2, b2, ln_g, ln_b, Wr, br, cfg):
    f32 = np.float32
    NIP = 35
    W_inT = np.ascontiguousarray(np.asarray(W_in).T).astype(f32)
    wpad = np.zeros((cfg.D, NIP * 128), f32)
    wpad[:, :4096] = W_inT[:, :4096]
    wpad[:, 32 * 128:32 * 128 + 64] = W_inT[:, 4096:4160]
    wpad[:, 33 * 128:33 * 128 + 64] = W_inT[:, 4160:4224]
    wpad[:, 34 * 128:34 * 128 + 16] = W_inT[:, 4224:4240]
    w_in_tiles = np.ascontiguousarray(
        wpad.reshape(cfg.D // 128, 128, NIP, 128).transpose(2, 0, 1, 3))
    cw = np.asarray(conv_w).reshape(2176, 4).astype(f32)
    cb = np.asarray(conv_b).reshape(2176).astype(f32)
    cw4 = np.zeros((128, 18 * 4), f32)
    cb18 = np.zeros((128, 18), f32)
    for c in range(16):
        cw4[:, c * 4:(c + 1) * 4] = cw[c * 128:(c + 1) * 128]
        cb18[:, c] = cb[c * 128:(c + 1) * 128]
    cw4[0:64, 16 * 4:17 * 4] = cw[2048:2112]
    cb18[0:64, 16] = cb[2048:2112]
    cw4[0:64, 17 * 4:18 * 4] = cw[2112:2176]
    cb18[0:64, 17] = cb[2112:2176]
    e16 = np.zeros((16, 16 * 128), f32)
    for h in range(16):
        e16[h, h * 128:(h + 1) * 128] = 1.0
    n1 = W1.size
    k1 = max(1, int(n1 * 0.1 / 2))
    n2 = W2.size
    k2 = max(1, int(n2 * 0.1 / 2))
    nb1 = b1.size
    kb1 = max(1, int(nb1 * 0.1 / 2))
    nb2 = b2.size
    kb2 = max(1, int(nb2 * 0.1 / 2))
    NC = cfg.n_cores
    return {
        "w_in_tiles": w_in_tiles.astype(f32),
        "w_outT": np.ascontiguousarray(np.asarray(W_out).T).astype(f32),
        "w1": np.ascontiguousarray(W1).astype(f32),
        "w2": np.ascontiguousarray(W2).astype(f32),
        "conv_w4": cw4,
        "conv_b17": cb18,
        "dtb": np.asarray(dt_bias).reshape(16, 1).astype(f32),
        "alog": np.asarray(A_log).reshape(16, 1).astype(f32),
        "dprm_row": np.asarray(D_param).reshape(1, 16).astype(f32),
        "normw": np.ascontiguousarray(
            np.asarray(norm_w).reshape(16, 128).T).astype(f32),
        "b1c": np.ascontiguousarray(np.asarray(b1).reshape(32, 128).T).astype(f32),
        "b2c": np.ascontiguousarray(np.asarray(b2).reshape(8, 128).T).astype(f32),
        "lgam": np.ascontiguousarray(np.asarray(ln_g).reshape(8, 128).T).astype(f32),
        "lbet": np.ascontiguousarray(np.asarray(ln_b).reshape(8, 128).T).astype(f32),
        "wr_in": np.ascontiguousarray(np.asarray(Wr).reshape(8, 128, 8)
            .transpose(1, 0, 2).reshape(128, 64)).astype(f32),
        "br_in": np.asarray(br).reshape(8, 1).astype(f32),
        "identf_in": np.eye(128, dtype=f32),
        "identb_in": np.eye(128).astype(BF),
        "trineg_in": np.where(
            np.arange(128)[:, None] <= np.arange(128)[None, :], 0.0, NEG
        ).astype(f32),
        "e16_in": e16,
        "kvec_in": np.array([[k1, n1 - k1 + 1, k2, n2 - k2 + 1,
                              kb1, nb1 - kb1 + 1,
                              kb2, nb2 - kb2 + 1]], f32),
        "bnds_in": np.array([[0.0] * 8 + [1024.0] * 8], f32),
    }


def kernel(L0_attn_hidden_states, L0_attn_W_in, L0_attn_conv_w, L0_attn_conv_b,
           L0_attn_dt_bias, L0_attn_A_log_alpha, L0_attn_D_param,
           L0_attn_norm_w, L0_attn_W_out, L0_ffn_W1, L0_ffn_b1, L0_ffn_W2,
           L0_ffn_b2, L0_ffn_ln_gamma, L0_ffn_ln_beta, L0_ffn_Wr, L0_ffn_br):
    cfg = CFG()
    x = np.asarray(L0_attn_hidden_states, np.float32)
    W1 = np.ascontiguousarray(np.asarray(L0_ffn_W1, np.float32))
    W2 = np.ascontiguousarray(np.asarray(L0_ffn_W2, np.float32))
    shared = _prep_shared(L0_attn_W_in, L0_attn_conv_w, L0_attn_conv_b,
                          L0_attn_dt_bias, L0_attn_A_log_alpha, L0_attn_D_param,
                          L0_attn_norm_w, L0_attn_W_out, W1, L0_ffn_b1, W2,
                          L0_ffn_b2, L0_ffn_ln_gamma, L0_ffn_ln_beta,
                          L0_ffn_Wr, L0_ffn_br, cfg)
    nc = build_program(cfg)
    in_maps = []
    for c in range(8):
        m = dict(shared)
        m["x_fm"] = np.ascontiguousarray(
            x[2 * c:2 * c + 2].transpose(0, 2, 1)).astype(np.float32)
        in_maps.append(m)
    res = run_bass_kernel_spmd(nc, in_maps, core_ids=list(range(8)))
    outs = []
    for c in range(8):
        o = np.asarray(res.results[c]["out_fm"])
        outs.append(np.ascontiguousarray(o.transpose(0, 2, 1)))
    return np.concatenate(outs, axis=0).astype(np.float32)


# version 10
# speedup vs baseline: 1.0631x; 1.0008x over previous
import contextlib

import numpy as np
import ml_dtypes
import concourse.bass as bass
import concourse.mybir as mybir
from concourse.bass_utils import run_bass_kernel_spmd

F32 = mybir.dt.float32
BF16 = mybir.dt.bfloat16
AF = mybir.ActivationFunctionType
OP = mybir.AluOpType
BF = ml_dtypes.bfloat16

GELU_C = 0.044715
GELU_S = 2.0 * 0.7978845608028654
EPS = 1e-5
NEG = -1e30


class CFG:
    def __init__(self, L=4096, NB=2, T=512, n_cores=8, bisect_iters=52,
                 debug=None):
        self.L, self.NB, self.T, self.C = L, NB, T, 128
        self.D, self.EI, self.CV, self.DIP = 1024, 2048, 2176, 4240
        self.NH, self.HD, self.NS, self.FF = 16, 128, 64, 4096
        self.n_cores = n_cores
        self.bisect_iters = bisect_iters
        self.debug = debug
        assert L % T == 0 and T % self.C == 0


ENGINES = ("tensor", "vector", "scalar", "gpsimd", "sync")
DSEMS = ("dxb", "dwi0", "dwi1", "dwi2", "dwo0", "dwo1", "dw2i0", "dw2i1", "dw2o0", "dw2o1", "dx1b",
         "dxl3", "dy0t", "dp0r", "dw20", "dw21", "dwst", "dcn0", "dcn1", "dcn2", "dcn3", "dc20", "dc21", "dc22", "dc23")
SEMS = ("T", "V", "S", "G", "DS", "DA", "DG", "DC") + DSEMS
ENG_SEM = {"tensor": "T", "vector": "V", "scalar": "S", "gpsimd": "G"}
DMA_SEM = {"sync": "DS", "scalar": "DA", "gpsimd": "DG"}


class Prog:

    def __init__(self, nc):
        self.nc = nc
        self.ops = {e: [] for e in ENGINES}
        self.cnt = {s: 0 for s in SEMS}
        self.hiw = {e: {s: 0 for s in SEMS} for e in ENGINES}
        self.last_w = {}
        self.readers = {}
        self.sems = {}
        self.pending_rd = {e: [] for e in ENGINES}

    def make_sems(self, es):
        for s in SEMS:
            self.sems[s] = es.enter_context(self.nc.semaphore("sem_" + s))

    def patch_total(self, keys, sem):
        tok = (sem, self.cnt[sem])
        for k in keys:
            self.last_w[k] = tok
            self.readers[k] = []

    def emit(self, eng, fn, *args, rd=(), wr=(), after=(), inc=True, dma=False,
             collective=False, dsem=None, **kw):
        toks = [t for t in after if t is not None]
        for k in rd:
            t = self.last_w.get(k)
            if t is not None:
                toks.append(t)
        for k in wr:
            t = self.last_w.get(k)
            if t is not None:
                toks.append(t)
            toks.extend(self.readers.get(k, ()))
        waits = []
        own = ENG_SEM.get(eng)
        for (sem, val) in toks:
            if sem == own and eng == "tensor":
                continue
            if val > self.hiw[eng][sem]:
                self.hiw[eng][sem] = val
                waits.append((sem, val))
        if dma:
            sem = dsem or ("DC" if collective else DMA_SEM[eng])
            amt = 16
        else:
            sem = ENG_SEM[eng]
            amt = 1
        if inc:
            self.cnt[sem] += amt
        tok = (sem, self.cnt[sem])
        self.ops[eng].append((fn, args, kw, waits, (sem, amt) if inc else None))
        if inc:
            for k in rd:
                self.readers.setdefault(k, []).append(tok)
            for k in self.pending_rd[eng]:
                self.readers.setdefault(k, []).append(tok)
            self.pending_rd[eng] = []
        else:
            self.pending_rd[eng].extend(rd)
        for k in wr:
            for e2 in ENGINES:
                assert k not in self.pending_rd[e2], \
                    f"write to {k} while {e2} has un-inc'd pending read"
            self.last_w[k] = tok
            self.readers[k] = []
        return tok

    def barrier(self):
        snap = dict(self.cnt)
        for e in ENGINES:
            for s in SEMS:
                v = snap[s]
                if s != ENG_SEM.get(e) and v > self.hiw[e][s]:
                    self.hiw[e][s] = v
                    self.ops[e].append((None, None, None, [(s, v)], None))

    def wait_all(self, eng):
        for s in SEMS:
            v = self.cnt[s]
            if v > self.hiw[eng][s]:
                self.hiw[eng][s] = v
                self.ops[eng].append((None, None, None, [(s, v)], None))

    def replay(self, block):
        prog = self

        def section(name):
            def run(eng):
                for (fn, args, kw, waits, inc) in prog.ops[name]:
                    for (sem, val) in waits:
                        eng.wait_ge(prog.sems[sem], val)
                    if fn is None:
                        continue
                    inst = getattr(eng, fn)(*args, **kw)
                    if inc is not None:
                        inst.then_inc(prog.sems[inc[0]], inc[1])
            return run

        block.tensor(section("tensor"))
        block.vector(section("vector"))
        block.scalar(section("scalar"))
        block.gpsimd(section("gpsimd"))
        block.sync(section("sync"))


class Arena:
    def __init__(self, handle, words):
        self.h = handle
        self.words = words
        self.off = 0
        self.marks = []

    def _take(self, n_words):
        n_words = (n_words + 1) & ~1
        assert self.off + n_words <= self.words, \
            f"arena overflow: {self.off}+{n_words} > {self.words}"
        o = self.off
        self.off += n_words
        return o

    def f32(self, parts, n):
        o = self._take(n)
        return self.h[0:parts, o:o + n]

    def bf16(self, parts, n):
        o = self._take((n + 1) // 2)
        return self.h[:].bitcast(BF16)[0:parts, 2 * o:2 * o + n]

    def push(self):
        self.marks.append(self.off)

    def pop(self):
        self.off = self.marks.pop()


def build_program(cfg: CFG):
    nc = bass.Bass("TRN2", target_bir_lowering=False)
    L, NB, T, C = cfg.L, cfg.NB, cfg.T, cfg.C
    D, EI, CV, DIP = cfg.D, cfg.EI, cfg.CV, cfg.DIP
    NH, HD, NS, FF = cfg.NH, cfg.HD, cfg.NS, cfg.FF
    ND, NE, NCV, NFF = D // 128, EI // 128, CV // 128, FF // 128
    NIP = 35
    NCVT = 18
    NT = L // T
    TOK = NB * L
    NTT = TOK // 512
    NCH = T // C

    def din(name, shape, dt=F32):
        return nc.dram_tensor(name, list(shape), dt, kind="ExternalInput").ap()

    def dint(name, shape, dt=F32):
        return nc.dram_tensor(name, list(shape), dt, kind="Internal").ap()

    x_fm = din("x_fm", [NB, D, L])
    w_in_tiles = din("w_in_tiles", [NIP, ND, 128, 128])
    w_outT = din("w_outT", [EI, D])
    w1 = din("w1", [D, FF])
    w2 = din("w2", [FF, D])
    conv_w4 = din("conv_w4", [128, NCVT * 4])
    conv_b17 = din("conv_b17", [128, NCVT])
    dtb_in = din("dtb", [NH, 1])
    alog_in = din("alog", [NH, 1])
    dprm_row = din("dprm_row", [1, NH])
    normw_in = din("normw", [128, NE])
    b1c_in = din("b1c", [128, NFF])
    b2c_in = din("b2c", [128, ND])
    lgam_in = din("lgam", [128, ND])
    lbet_in = din("lbet", [128, ND])
    wr_in = din("wr_in", [128, (D // 128) * 8])
    br_in = din("br_in", [8, 1])
    identf_in = din("identf_in", [128, 128])
    identb_in = din("identb_in", [128, 128], BF16)
    trineg_in = din("trineg_in", [128, 128])
    e16_in = din("e16_in", [NH, NH * 128])
    kvec_in = din("kvec_in", [1, 8])
    bnds_in = din("bnds_in", [1, 16])

    out_fm = nc.dram_tensor("out_fm", [NB, D, L], F32, kind="ExternalOutput").ap()

    x1h = dint("x1h", [NB, D, L])
    xln3h = dint("xln3h", [NB, D, L], BF16)
    y0h = dint("y0h", [NB, D, L], BF16)
    p0h = dint("p0h", [1, TOK])
    w2bh = dint("w2bh", [FF, D], BF16)
    w_inbh = dint("w_inbh", [NIP, ND, 128, 128], BF16)
    w_outbh = dint("w_outbh", [EI, D], BF16)
    dbgh = dint("dbgh", [1, 64])
    pw2bh = dint("pw2bh", [FF, D], BF16)

    es = contextlib.ExitStack()
    AR_P = Arena(es.enter_context(nc.sbuf_tensor("arP", [128, 4100], F32)), 4100)
    AR_A = Arena(es.enter_context(nc.sbuf_tensor("arA", [128, 48700], F32)), 48700)

    p = Prog(nc)
    p.make_sems(es)
    pb = [es.enter_context(nc.psum_tensor(f"pb{i}", [128, 512], F32))
          for i in range(8)]

    def pbank(i, parts=128, n=512):
        return pb[i][0:parts, 0:n]

    def pbank_b(i, parts=128, n=1024):
        return pb[i][:].bitcast(BF16)[0:parts, 0:n]

    identf = AR_P.f32(128, 128)
    identb = AR_P.bf16(128, 128)
    trineg = AR_P.f32(128, 128)
    e16 = AR_P.f32(NH, NH * 128)
    cw4 = AR_P.f32(128, NCVT * 4)
    cb17 = AR_P.f32(128, NCVT)
    dt_bias = AR_P.f32(NH, 1)
    a_log = AR_P.f32(NH, 1)
    d_row = AR_P.f32(1, NH)
    nw = AR_P.f32(128, NE)
    b1c = AR_P.f32(128, NFF)
    b2c = AR_P.f32(128, ND)
    gam = AR_P.f32(128, ND)
    bet = AR_P.f32(128, ND)
    wr_sb = AR_P.f32(128, ND * 8)
    br_sb = AR_P.f32(8, 1)
    kvec = AR_P.f32(1, 8)
    ones1 = AR_P.f32(1, 128)
    ones1b = AR_P.bf16(1, 128)
    onescol = AR_P.f32(128, 1)
    onescol_b = AR_P.bf16(128, 2)[:, 0:1]
    zero16 = AR_P.f32(NH, 1)
    eps_col = AR_P.f32(128, 1)
    one_col = AR_P.f32(128, 1)
    a_col = AR_P.f32(NH, 1)
    d_b = AR_P.f32(128, NH)
    wrg = AR_P.bf16(128, ND * 8)
    cgam = AR_P.f32(8, 1)
    cbet = AR_P.f32(8, 1)
    b1p = AR_P.f32(128, NFF)
    b2p = AR_P.f32(128, ND)
    blo = AR_P.f32(1, 8)
    bhi = AR_P.f32(1, 8)
    bmid = AR_P.f32(1, 8)
    bcols = AR_P.f32(128, 8)
    bst = AR_P.f32(1, 24)
    ab1 = AR_P.f32(128, NFF)
    ab2 = AR_P.f32(128, ND)

    g, s_, v_, t_, sc = "gpsimd", "sync", "vector", "tensor", "scalar"

    def dma(eng, out, in_, rd=(), wr=(), after=(), dsem=None):
        return p.emit(eng, "dma_start", rd=rd, wr=wr, after=after, dma=True,
                      dsem=dsem, out=out, in_=in_)

    def fm3(hbm_ap):
        return hbm_ap.rearrange("(k p) t -> p k t", p=128)

    def sb3(sbuf_ap, t):
        return sbuf_ap.rearrange("p (k t) -> p k t", t=t)

    dma(s_, identf[:], identf_in[:], wr=["identf"])
    dma(s_, trineg[:], trineg_in[:], wr=["trineg"])
    dma(s_, e16[:], e16_in[:], wr=["e16"])
    dma(s_, cw4[:], conv_w4[:], wr=["cw4"])
    dma(s_, cb17[:], conv_b17[:], wr=["cb17"])
    dma(s_, dt_bias[:], dtb_in[:], wr=["dtb"])
    dma(s_, a_log[:], alog_in[:], wr=["alog"])
    dma(s_, d_row[:], dprm_row[:], wr=["drow"])
    dma(s_, nw[:], normw_in[:], wr=["nw"])
    dma(s_, b1c[:], b1c_in[:], wr=["b1c"])
    dma(s_, b2c[:], b2c_in[:], wr=["b2c"])
    dma(s_, gam[:], lgam_in[:], wr=["gam"])
    dma(s_, bet[:], lbet_in[:], wr=["bet"])
    dma(s_, wr_sb[:], wr_in[:], wr=["wr"])
    dma(s_, br_sb[:], br_in[:], wr=["br"])
    dma(s_, kvec[:], kvec_in[:], wr=["kvec"])
    dma(s_, blo[:], bnds_in[:, 0:8], wr=["blo"])
    dma(s_, bhi[:], bnds_in[:, 8:16], wr=["bhi"])
    dma(s_, ab1[:], b1c_in[:], wr=["ab1"])
    dma(s_, ab2[:], b2c_in[:], wr=["ab2"])
    dma(g, identb[:], identb_in[:], wr=["identb"])
    p.patch_total(["identf", "trineg", "e16", "cw4", "cb17", "dtb", "alog",
                   "drow", "nw", "b1c", "b2c", "gam", "bet", "wr", "br",
                   "kvec", "blo", "bhi", "ab1", "ab2"], "DS")
    p.patch_total(["identb"], "DG")

    p.emit(g, "memset", ones1[:], 1.0, wr=["ones1"])
    p.emit(g, "memset", ones1b[:], 1.0, wr=["ones1b"])
    p.emit(g, "memset", onescol[:], 1.0, wr=["onescol"])
    p.emit(g, "memset", zero16[:], 0.0, wr=["zero16"])
    p.emit(g, "memset", bmid[:], 0.0, wr=["bmid"])
    p.emit(g, "memset", eps_col[:], float(EPS), wr=["eps_col"])
    p.emit(g, "memset", one_col[:], 1.0, wr=["one_col"])
    p.emit(v_, "tensor_copy", onescol_b[:], onescol[:], rd=["onescol"],
           wr=["onescolb"])
    p.emit(sc, "activation", a_col[:], a_log[:], AF.Exp, rd=["alog"],
           wr=["a_col"])
    p.emit(v_, "tensor_scalar", a_col[:], a_col[:], -1.0, None, op0=OP.mult,
           rd=["a_col"], wr=["a_col"])
    p.emit(t_, "matmul", pbank(6, 128, NH), ones1[:], d_row[:], start=True,
           stop=True, rd=["ones1", "drow", ("pbk", 6)], wr=[("pbk", 6)])
    p.emit(v_, "tensor_copy", d_b[:], pbank(6, 128, NH), rd=[("pbk", 6)],
           wr=["d_b"])
    p.emit(sc, "activation", ab1[:], ab1[:], AF.Abs,
           rd=["ab1"], wr=["ab1"])
    p.emit(sc, "activation", ab2[:], ab2[:], AF.Abs,
           rd=["ab2"], wr=["ab2"])

    w2stg = [AR_P.bf16(128, 1024) for _ in range(2)]
    for kf in range(NFF):
        pe = kf % 2
        dma(g, w2stg[pe][:], w2[kf * 128:(kf + 1) * 128, :],
            wr=[("w2stg", pe)], dsem="dw2i%d" % pe)
        dma(s_, w2bh[kf * 128:(kf + 1) * 128, :], w2stg[pe][:],
            rd=[("w2stg", pe)], wr=[("w2bh", kf), ("w2stg", pe)],
            dsem="dw2o%d" % pe)
    for e in range(NIP):
        pe = e % 2
        dma(g, w2stg[pe][:].rearrange("p (k q) -> p k q", q=128),
            w_in_tiles[e].rearrange("k p q -> p k q"),
            wr=[("w2stg", pe)], dsem="dw2i%d" % pe)
        dma(s_, w_inbh[e].rearrange("k p q -> p k q"),
            w2stg[pe][:].rearrange("p (k q) -> p k q", q=128),
            rd=[("w2stg", pe)], wr=[("wibh", e), ("w2stg", pe)], dsem="dw2o%d" % pe)
    for ke in range(NE):
        pe = ke % 2
        dma(g, w2stg[pe][:], w_outT[ke * 128:(ke + 1) * 128, :],
            wr=[("w2stg", pe)], dsem="dw2i%d" % pe)
        dma(s_, w_outbh[ke * 128:(ke + 1) * 128, :], w2stg[pe][:],
            rd=[("w2stg", pe)], wr=[("wobh", ke), ("w2stg", pe)],
            dsem="dw2o%d" % pe)

    ns = {}

    def ln_full(tag, src_fn, n_tiles, n_feat, tcols, src_bf16=False):
        arow, brow, crow = ns["arow"], ns["brow"], ns["crow"]
        sqp = ns["sq"]
        oc, ock = (onescol_b, "onescolb") if src_bf16 else (onescol, "onescol")
        s1 = pbank(2, 1, tcols)
        s2 = pbank(3, 1, tcols)
        srckeys = ns["srckeys"]
        for i in range(n_tiles):
            p.emit(t_, "matmul", s1, oc[:], src_fn(i),
                   start=(i == 0), stop=(i == n_tiles - 1),
                   rd=[ock, srckeys[i]],
                   wr=[("pbk", 2)] if i in (0, n_tiles - 1) else (),
                   inc=(i == n_tiles - 1))
        for i in range(n_tiles):
            sq_scr = sqp[i % 2]
            p.emit(sc, "activation", sq_scr[:, 0:tcols], src_fn(i), AF.Square,
                   rd=[srckeys[i]], wr=[("sq_scr", i % 2)])
            p.emit(t_, "matmul", s2, onescol_b[:], sq_scr[:, 0:tcols],
                   start=(i == 0), stop=(i == n_tiles - 1),
                   rd=["onescolb", ("sq_scr", i % 2)],
                   wr=[("pbk", 3)] if i in (0, n_tiles - 1) else (), inc=True)
        mu = arow[:, 0:tcols]
        p.emit(v_, "tensor_scalar", mu, s1, 1.0 / n_feat, None, op0=OP.mult,
               rd=[("pbk", 2)], wr=["arow"])
        mu2 = crow[:, 0:tcols]
        p.emit(v_, "tensor_tensor", mu2, mu, mu, op=OP.mult, rd=["arow"],
               wr=["crow"])
        var = brow[:, 0:tcols]
        p.emit(v_, "scalar_tensor_tensor", var, s2, 1.0 / n_feat, mu2,
               op0=OP.mult, op1=OP.subtract, rd=[("pbk", 3), "crow"],
               wr=["brow"])
        p.emit(sc, "activation", var, var, AF.Ln, bias=eps_col[0:1, :],
               rd=["brow", "eps_col"], wr=["brow"])
        rstd_b = brow[:].bitcast(BF16)[0:1, 0:tcols]
        p.emit(sc, "activation", rstd_b, var, AF.Exp, scale=-0.5, rd=["brow"],
               wr=["brow"])
        brw_b = crow[:].bitcast(BF16)[0:1, 0:tcols]
        p.emit(v_, "scalar_tensor_tensor", brw_b, mu, -1.0, rstd_b,
               op0=OP.mult, op1=OP.mult, rd=["arow", "brow"], wr=["crow"])
        bca = pbank(2, 128, tcols)
        bcb = pbank(3, 128, tcols)
        p.emit(t_, "matmul", bca, ones1b[:], rstd_b, start=True, stop=True,
               rd=["ones1b", "brow", ("pbk", 2)], wr=[("pbk", 2)])
        p.emit(t_, "matmul", bcb, ones1b[:], brw_b, start=True, stop=True,
               rd=["ones1b", "crow", ("pbk", 3)], wr=[("pbk", 3)])
        return bca, bcb

    AR_A.push()
    wi_buf = [AR_A.bf16(128, ND * 128) for _ in range(3)]
    wo_buf = [AR_A.bf16(128, 1024) for _ in range(2)]
    diag = [AR_A.bf16(128, 4 * 128) for _ in range(2)]
    xb = AR_A.f32(128, ND * T)
    xln1 = AR_A.bf16(128, ND * T)
    z_silu = AR_A.bf16(128, NE * T)
    xbc = AR_A.bf16(128, NCVT * (T + 3))
    y_n = AR_A.bf16(128, NE * T)
    m_o = AR_A.f32(128, ND * T)
    sqA = AR_A.bf16(128, 512)
    halo = AR_A.bf16(128, NCVT * 4)
    dt_sp = AR_A.f32(NH, T)
    s_all = AR_A.f32(NH, T)
    sig_all = AR_A.f32(NH, T)
    esc_c = AR_A.f32(NH, 1)
    es_b = [AR_A.f32(NS, C) for _ in range(2)]
    w_st = AR_A.f32(NH, C)
    sT = AR_A.f32(128, NH)
    sigT = AR_A.f32(128, NH)
    wT = AR_A.f32(128, NH)
    escb = AR_A.f32(NS, NH)
    g_s = AR_A.f32(128, C)
    mask_s = [AR_A.f32(128, C) for _ in range(2)]
    gmb = [AR_A.bf16(128, C) for _ in range(2)]
    x_tm = [AR_A.bf16(128, C) for _ in range(2)]
    x_til = [AR_A.bf16(128, C) for _ in range(2)]
    b_tm = AR_A.bf16(128, NS)
    c_til = [AR_A.bf16(NS, C) for _ in range(2)]
    ht = AR_A.f32(NS, NH * HD)
    ht_b = [AR_A.bf16(NS, HD) for _ in range(2)]
    y_g = AR_A.bf16(128, NH * C)
    dx = [AR_A.f32(128, C) for _ in range(2)]
    apA = [AR_A.f32(128, T) for _ in range(2)]
    rsc = AR_A.f32(1, 16)
    arowA = AR_A.f32(1, T)
    browA = AR_A.f32(1, T)
    crowA = AR_A.f32(1, T)
    sqA2 = AR_A.bf16(128, 512)
    ns.update(arow=arowA, brow=browA, crow=crowA, sq=(sqA, sqA2))

    for mt in range(NB * NT):
        b, m = mt // NT, mt % NT
        t0 = m * T

        dma(s_, sb3(xb[:], T), fm3(x_fm[b, :, t0:t0 + T]),
            rd=["xb"], wr=["xb"], dsem="dxb")
        if m == 0:
            p.emit(v_, "memset", ht[:], 0.0,
                   rd=[("ht", h) for h in range(NH)],
                   wr=[("ht", h) for h in range(NH)])

        def xsrc(i):
            return xb[:, i * T:(i + 1) * T]

        ns["srckeys"] = ["xb"] * ND
        bca, bcb = ln_full("ln1", xsrc, ND, D, T)
        for i in range(ND):
            ap_ = apA[i % 2]
            p.emit(v_, "tensor_tensor", ap_[:], xsrc(i), bca, op=OP.mult,
                   rd=["xb", ("pbk", 2)], wr=[("apA", i % 2)])
            p.emit(v_, "tensor_tensor", xln1[:, i * T:(i + 1) * T], ap_[:],
                   bcb, op=OP.add, rd=[("apA", i % 2), ("pbk", 3)],
                   wr=[("xln1", i)])

        for e in range(NIP):
            pe = e % 3
            dma(s_, wi_buf[pe][:].rearrange("p (k q) -> p k q", q=128),
                w_inbh[e].rearrange("k p q -> p k q"),
                after=[p.last_w.get(("wibh", e))],
                wr=[("wi", pe)], dsem="dwi%d" % pe)
            rows_e = 128 if e < 32 else (64 if e < 34 else NH)
            ps = pbank(pe, rows_e, T)
            for kd in range(ND):
                p.emit(t_, "matmul", ps,
                       wi_buf[pe][:, kd * 128:kd * 128 + rows_e],
                       xln1[:, kd * T:(kd + 1) * T],
                       start=(kd == 0), stop=(kd == ND - 1),
                       rd=[("wi", pe), ("xln1", kd)],
                       wr=([("pbk", pe)] if kd in (0, ND - 1) else []) +
                          ([("wi", pe)] if kd == ND - 1 else []),
                       inc=(kd == ND - 1))
            if e < NE:
                p.emit(sc, "activation", z_silu[:, e * T:(e + 1) * T], ps,
                       AF.Silu, rd=[("pbk", pe)], wr=[("zs", e)])
            elif e < 34:
                c = e - NE
                p.emit(v_, "tensor_copy",
                       xbc[0:rows_e, c * (T + 3) + 3:c * (T + 3) + 3 + T], ps,
                       rd=[("pbk", pe)], wr=[("xbc", c)])
            else:
                p.emit(sc, "activation", dt_sp[:, 0:T], ps, AF.Relu,
                       bias=dt_bias[:], rd=[("pbk", pe), "dtb"], wr=["dt_sp"])
                p.emit(sc, "activation", s_all[:, 0:T], ps, AF.Abs,
                       bias=dt_bias[:], rd=[("pbk", pe), "dtb"], wr=["s_all"])
                p.emit(sc, "activation", sig_all[:, 0:T], s_all[:, 0:T],
                       AF.Exp, scale=-1.0, rd=["s_all"], wr=["sig_all"])
                p.emit(sc, "activation", sig_all[:, 0:T], sig_all[:, 0:T],
                       AF.Ln, bias=one_col[0:NH, :], rd=["sig_all", "one_col"],
                       wr=["sig_all"])
                p.emit(v_, "tensor_tensor", dt_sp[:, 0:T], dt_sp[:, 0:T],
                       sig_all[:, 0:T], op=OP.add, rd=["dt_sp", "sig_all"],
                       wr=["dt_sp"])

        for c in range(NCVT):
            base = c * (T + 3)
            rc = 128 if c < 16 else 64
            if m == 0:
                p.emit(v_, "memset", xbc[0:rc, base:base + 3], 0.0,
                       rd=[("xbc", c)], wr=[("xbc", c)])
            else:
                p.emit(v_, "tensor_copy", xbc[0:rc, base:base + 3],
                       halo[0:rc, c * 4:c * 4 + 3], rd=[("halo", c), ("xbc", c)],
                       wr=[("xbc", c)])
        for c in range(NCVT):
            base = c * (T + 3)
            rc = 128 if c < 16 else 64
            pe = c % 2
            for j in range(4):
                p.emit(v_, "tensor_scalar",
                       diag[pe][0:rc, j * 128:j * 128 + rc],
                       identb[0:rc, 0:rc], cw4[0:rc, c * 4 + j:c * 4 + j + 1],
                       None, op0=OP.mult, rd=["identb", "cw4", ("diag", pe)],
                       wr=[("diag", pe)] if j == 3 else ())
            ps = pbank(pe, rc, T)
            for j in range(4):
                p.emit(t_, "matmul", ps, diag[pe][0:rc, j * 128:j * 128 + rc],
                       xbc[0:rc, base + j:base + j + T],
                       start=(j == 0), stop=(j == 3),
                       rd=[("diag", pe), ("xbc", c)],
                       wr=([("pbk", pe)] if j in (0, 3) else []) +
                          ([("diag", pe)] if j == 3 else []),
                       inc=(j == 3))
            p.emit(v_, "tensor_copy", halo[0:rc, c * 4:c * 4 + 3],
                   xbc[0:rc, base + T:base + T + 3], rd=[("xbc", c)],
                   wr=[("halo", c)])
            p.emit(sc, "activation", xbc[0:rc, base + 3:base + 3 + T], ps,
                   AF.Silu, bias=cb17[0:rc, c:c + 1],
                   rd=[("pbk", pe), "cb17", ("halo", c)], wr=[("xbc", c)])

        p.emit(v_, "tensor_scalar", s_all[:, 0:T], dt_sp[:, 0:T], a_col[:],
               None, op0=OP.mult, rd=["dt_sp", "a_col"], wr=["s_all"])
        p.emit(sc, "activation", sig_all[:, 0:T], dt_sp[:, 0:T], AF.Ln,
               rd=["dt_sp"], wr=["sig_all"])
        for ch in range(NCH):
            sl = slice(ch * C, (ch + 1) * C)
            p.emit(v_, "tensor_tensor_scan", s_all[:, sl], s_all[:, sl],
                   s_all[:, sl], zero16[:], op0=OP.add, op1=OP.bypass,
                   rd=["s_all", "zero16"], wr=["s_all"])
        p.emit(v_, "tensor_tensor", sig_all[:, 0:T], s_all[:, 0:T],
               sig_all[:, 0:T], op=OP.subtract, rd=["s_all", "sig_all"],
               wr=["sig_all"])

        for ch in range(NCH):
            t_sl = slice(ch * C, (ch + 1) * C)
            p.emit(t_, "transpose", pbank(4, 128, NH), sig_all[:, t_sl],
                   identf[0:NH, 0:NH], rd=["sig_all", "identf", ("pbk", 4)],
                   wr=[("pbk", 4)])
            p.emit(v_, "tensor_copy", sigT[:, 0:NH], pbank(4, 128, NH),
                   rd=[("pbk", 4)], wr=["sigT"])
            p.emit(v_, "tensor_scalar", w_st[:], sig_all[:, t_sl], -1.0,
                   s_all[:, ch * C + C - 1:ch * C + C], op0=OP.mult,
                   op1=OP.add, rd=["sig_all", "s_all"], wr=["w_st"])
            p.emit(sc, "activation", w_st[:], w_st[:], AF.Exp, rd=["w_st"],
                   wr=["w_st"])
            p.emit(t_, "transpose", pbank(4, 128, NH), w_st[:],
                   identf[0:NH, 0:NH], rd=["w_st", "identf", ("pbk", 4)],
                   wr=[("pbk", 4)])
            p.emit(v_, "tensor_copy", wT[:, 0:NH], pbank(4, 128, NH),
                   rd=[("pbk", 4)], wr=["wT"])
            p.emit(sc, "activation", esc_c[:], s_all[:, ch * C + C - 1:ch * C + C],
                   AF.Exp, rd=["s_all"], wr=["esc_c"])
            p.emit(t_, "transpose", pbank(4, 1, NH), esc_c[:],
                   identf[0:NH, 0:NH],
                   rd=["esc_c", "identf", ("pbk", 4)], wr=[("pbk", 4)])
            p.emit(v_, "tensor_copy", rsc[:, 0:NH], pbank(4, 1, NH),
                   rd=[("pbk", 4)], wr=["rsc"])
            p.emit(t_, "matmul", pbank(5, NS, NH), ones1[:, 0:NS],
                   rsc[:, 0:NH], start=True, stop=True,
                   rd=["ones1", "rsc", ("pbk", 5)], wr=[("pbk", 5)])
            p.emit(v_, "tensor_copy", escb[:, 0:NH], pbank(5, NS, NH),
                   rd=[("pbk", 5)], wr=["escb"])
            bbB = 16 * (T + 3) + 3 + ch * C
            bbC = 17 * (T + 3) + 3 + ch * C
            B_sl = xbc[0:NS, bbB:bbB + C]
            C_sl = xbc[0:NS, bbC:bbC + C]
            p.emit(t_, "matmul", pbank(4, 128, C), B_sl, C_sl, start=True,
                   stop=True, rd=[("xbc", 16), ("xbc", 17), ("pbk", 4)], wr=[("pbk", 4)])
            p.emit(v_, "tensor_copy", g_s[:], pbank(4, 128, C),
                   rd=[("pbk", 4)], wr=["g_s"])
            p.emit(t_, "transpose", pbank_b(4, 128, NS), B_sl,
                   identb[0:NS, 0:NS], rd=[("xbc", 16), "identb", ("pbk", 4)],
                   wr=[("pbk", 4)])
            p.emit(v_, "tensor_copy", b_tm[:, 0:NS], pbank_b(4, 128, NS),
                   rd=[("pbk", 4)], wr=["b_tm"])

            for h in range(NH):
                pe = h % 2
                yb = 6 + pe
                tb = 4 + pe
                rb = 0 + pe
                cb = 2 + pe
                xoff = h * (T + 3) + 3 + ch * C
                p.emit(t_, "transpose", pbank_b(tb, 128, C),
                       xbc[:, xoff:xoff + C], identb[:],
                       rd=[("xbc", h), "identb", ("pbk", tb)], wr=[("pbk", tb)])
                p.emit(sc, "activation", x_tm[pe][:], pbank_b(tb, 128, C),
                       AF.Identity, rd=[("pbk", tb)], wr=[("x_tm", pe)])
                p.emit(t_, "matmul", pbank(rb, 128, C),
                       e16[:, h * 128:(h + 1) * 128], s_all[:, t_sl],
                       start=True, stop=True,
                       rd=["e16", "s_all", ("pbk", rb)], wr=[("pbk", rb)])
                p.emit(v_, "scalar_tensor_tensor", mask_s[pe][:],
                       pbank(rb, 128, C), sigT[:, h:h + 1], trineg[:],
                       op0=OP.subtract, op1=OP.add,
                       rd=[("pbk", rb), "sigT", "trineg"],
                       wr=[("mask", pe)])
                p.emit(sc, "activation", mask_s[pe][:], mask_s[pe][:], AF.Exp,
                       rd=[("mask", pe)], wr=[("mask", pe)])
                p.emit(v_, "tensor_tensor", gmb[pe][:], g_s[:], mask_s[pe][:],
                       op=OP.mult, rd=["g_s", ("mask", pe)], wr=[("gmb", pe)])
                p.emit(sc, "activation", es_b[pe][:], pbank(rb, NS, C),
                       AF.Exp, rd=[("pbk", rb)], wr=[("esb", pe)])
                p.emit(v_, "tensor_tensor", c_til[pe][:, 0:C], es_b[pe][:],
                       C_sl, op=OP.mult, rd=[("esb", pe), ("xbc", 17)],
                       wr=[("ctil", pe)])
                p.emit(v_, "tensor_copy", ht_b[pe][:],
                       ht[:, h * HD:(h + 1) * HD], rd=[("ht", h)],
                       wr=[("htb", pe)])
                p.emit(t_, "matmul", pbank(yb, 128, C), x_tm[pe][:], gmb[pe][:],
                       start=True, stop=False,
                       rd=[("x_tm", pe), ("gmb", pe)], wr=[("pbk", yb)],
                       inc=False)
                p.emit(t_, "matmul", pbank(yb, 128, C), ht_b[pe][:],
                       c_til[pe][:, 0:C], start=False, stop=True,
                       rd=[("htb", pe), ("ctil", pe)], wr=[("pbk", yb)])
                p.emit(v_, "tensor_scalar", x_til[pe][:], x_tm[pe][:],
                       wT[:, h:h + 1], None, op0=OP.mult,
                       rd=[("x_tm", pe), "wT"], wr=[("xtil", pe)])
                p.emit(t_, "matmul", pbank(rb, NS, HD), b_tm[:, 0:NS],
                       x_til[pe][:], start=True, stop=True,
                       rd=["b_tm", ("xtil", pe), ("pbk", rb)], wr=[("pbk", rb)])
                p.emit(v_, "scalar_tensor_tensor", ht[:, h * HD:(h + 1) * HD],
                       ht[:, h * HD:(h + 1) * HD], escb[:, h:h + 1],
                       pbank(rb, NS, HD), op0=OP.mult, op1=OP.add,
                       rd=[("ht", h), "escb", ("pbk", rb), ("htb", pe)],
                       wr=[("ht", h)])
                p.emit(v_, "scalar_tensor_tensor", dx[pe][:],
                       xbc[:, xoff:xoff + C], d_b[:, h:h + 1],
                       pbank(yb, 128, C), op0=OP.mult, op1=OP.add,
                       rd=[("xbc", h), "d_b", ("pbk", yb)],
                       wr=[("dx", pe), ("pbk", yb)])
                p.emit(v_, "tensor_tensor", y_g[:, h * C:(h + 1) * C],
                       dx[pe][:], z_silu[:, h * T + ch * C:h * T + ch * C + C],
                       op=OP.mult, rd=[("dx", pe), ("zs", h)], wr=[("y_g", h)])
            for h in range(NH):
                sq_ = (sqA, sqA2)[h % 2]
                p.emit(sc, "activation", sq_[:, 0:C], y_g[:, h * C:(h + 1) * C],
                       AF.Square, rd=[("y_g", h)], wr=[("sq_scr", h % 2)])
                p.emit(t_, "matmul", pbank(5, 1, C), onescol_b[:],
                       sq_[:, 0:C], start=(h == 0), stop=(h == NH - 1),
                       rd=["onescolb", ("sq_scr", h % 2)],
                       wr=[("pbk", 5)] if h in (0, NH - 1) else (), inc=True)
            p.emit(v_, "tensor_scalar", crowA[0:1, 0:C], pbank(5, 1, C),
                   1.0 / EI, None, op0=OP.mult, rd=[("pbk", 5)], wr=["crow"])
            p.emit(sc, "activation", crowA[0:1, 0:C], crowA[0:1, 0:C], AF.Ln,
                   bias=eps_col[0:1, :], rd=["crow", "eps_col"], wr=["crow"])
            rrb_b = crowA[:].bitcast(BF16)[0:1, 0:C]
            p.emit(sc, "activation", rrb_b, crowA[0:1, 0:C], AF.Exp,
                   scale=-0.5, rd=["crow"], wr=["crow"])
            p.emit(t_, "matmul", pbank(5, 128, C), ones1b[:], rrb_b,
                   start=True, stop=True, rd=["ones1b", "crow", ("pbk", 5)],
                   wr=[("pbk", 5)])
            for h in range(NH):
                p.emit(v_, "scalar_tensor_tensor",
                       y_n[:, h * T + ch * C:h * T + ch * C + C],
                       y_g[:, h * C:(h + 1) * C], nw[:, h:h + 1],
                       pbank(5, 128, C), op0=OP.mult, op1=OP.mult,
                       rd=[("y_g", h), "nw", ("pbk", 5)], wr=[("y_n", h)])

        for ke in range(NE):
            pe = ke % 2
            dma(s_, wo_buf[pe][:], w_outbh[ke * 128:(ke + 1) * 128, :],
                after=[p.last_w.get(("wobh", ke))],
                wr=[("wo", pe)], dsem="dwo%d" % pe)
            for dd in range(ND):
                p.emit(t_, "matmul", pbank(dd, 128, T),
                       wo_buf[pe][:, dd * 128:(dd + 1) * 128],
                       y_n[:, ke * T:(ke + 1) * T],
                       start=(ke == 0), stop=(ke == NE - 1),
                       rd=[("wo", pe), ("y_n", ke)],
                       wr=([("pbk", dd)] if ke in (0, NE - 1) else []) +
                          ([("wo", pe)] if dd == ND - 1 else []),
                       inc=(ke == NE - 1 or dd == ND - 1))
        for dd in range(ND):
            p.emit(v_, "tensor_copy", m_o[:, dd * T:(dd + 1) * T],
                   pbank(dd, 128, T), rd=[("pbk", dd)], wr=[("m_o", dd)])

        def mosrc(i):
            return m_o[:, i * T:(i + 1) * T]

        ns["srckeys"] = [("m_o", i) for i in range(ND)]
        bca, bcb = ln_full("ln2", mosrc, ND, D, T)
        for i in range(ND):
            ap_ = apA[i % 2]
            p.emit(v_, "tensor_tensor", ap_[:], mosrc(i), bca, op=OP.mult,
                   rd=[("m_o", i), ("pbk", 2)], wr=[("apA", i % 2)])
            p.emit(v_, "tensor_tensor", ap_[:], ap_[:], bcb, op=OP.add,
                   rd=[("apA", i % 2), ("pbk", 3)], wr=[("apA", i % 2)])
            p.emit(v_, "tensor_tensor", xb[:, i * T:(i + 1) * T], xsrc(i),
                   ap_[:], op=OP.add, rd=[("apA", i % 2), "xb"], wr=["xb"])
        tgt = out_fm if cfg.debug == "x1" else x1h
        dma(s_, fm3(tgt[b, :, t0:t0 + T]), sb3(xb[:], T),
            rd=["xb"], wr=[("x1h", mt)])
        p.last_w["xb"] = p.last_w[("x1h", mt)]

    AR_A.pop()

    if cfg.debug == "x1":
        p.wait_all("gpsimd")
        block = es.enter_context(nc.Block())
        p.replay(block)
        es.close()
        return nc

    p.barrier()
    AR_A.push()
    w1b = AR_A.bf16(128, ND * FF)
    w2buf = [AR_A.bf16(128, 1024) for _ in range(2)]
    x1b = [AR_A.bf16(128, ND * 512) for _ in range(2)]
    xln3 = AR_A.bf16(128, ND * 512)
    actW = AR_A.f32(128, 8192)
    act = actW[:].bitcast(BF16)[:, 0:NFF * 512]
    h1b = [AR_A.bf16(128, 512) for _ in range(2)]
    x2b = [AR_A.bf16(128, 512) for _ in range(2)]
    ab_ = [AR_A.bf16(128, 512) for _ in range(2)]
    sgb = [AR_A.bf16(128, 512) for _ in range(2)]
    y0st = AR_A.bf16(128, ND * 512)
    lg_sb = AR_A.f32(8, 512)
    lt_sb = AR_A.f32(128, 8)
    nmax = AR_A.f32(128, 1)
    qex = AR_A.f32(128, 8)
    ssum = AR_A.f32(128, 1)
    p0c = AR_A.f32(128, 1)
    p0row = AR_A.f32(1, 512)
    sqB = AR_A.bf16(128, 512)
    sqB2 = AR_A.bf16(128, 512)
    apB = [AR_A.f32(128, 512) for _ in range(2)]
    arowB = AR_A.f32(1, 512)
    browB = AR_A.f32(1, 512)
    crowB = AR_A.f32(1, 512)
    cntc = AR_A.f32(128, 1)
    cnt_loc = AR_A.f32(1, 8)
    cacc = AR_A.f32(1, 128)
    cacc2 = AR_A.f32(128, 64)
    wblk = AR_A.f32(128, 8192)
    wcnt1 = [wblk[:, 0:2048], wblk[:, 2048:4096]]
    wcnt2 = [wblk[:, 4096:6144], wblk[:, 6144:8192]]
    wstgA = wblk[:, 4096:5120]
    absA = wblk[:, 5120:6144]
    wstgB = wblk[:].bitcast(BF16)[:, 2 * 6144:2 * 6144 + 1024]
    y1s = wblk[:, 0:4096]
    ns.update(arow=arowB, brow=browB, crow=crowB, sq=(sqB, sqB2))

    for kd in range(ND):
        dma(g, w1b[:, kd * FF:(kd + 1) * FF], w1[kd * 128:(kd + 1) * 128, :],
            wr=[("w1b", kd)])
    p.patch_total([("w1b", kd) for kd in range(ND)], "DG")

    for i in range(ND):
        p.emit(v_, "tensor_scalar", wrg[:, i * 8:(i + 1) * 8],
               wr_sb[:, i * 8:(i + 1) * 8], gam[:, i:i + 1], None, op0=OP.mult,
               rd=["wr", "gam"], wr=[("wrg", i)])
        p.emit(t_, "matmul", pbank(4, 8, 1), wr_sb[:, i * 8:(i + 1) * 8],
               gam[:, i:i + 1], start=(i == 0), stop=(i == ND - 1),
               rd=["wr", "gam"],
               wr=[("pbk", 4)] if i in (0, ND - 1) else (), inc=(i == ND - 1))
    p.emit(v_, "tensor_copy", cgam[:], pbank(4, 8, 1), rd=[("pbk", 4)],
           wr=["cgam"])
    for i in range(ND):
        p.emit(t_, "matmul", pbank(4, 8, 1), wr_sb[:, i * 8:(i + 1) * 8],
               bet[:, i:i + 1], start=(i == 0), stop=(i == ND - 1),
               rd=["wr", "bet"],
               wr=[("pbk", 4)] if i in (0, ND - 1) else (), inc=(i == ND - 1))
    p.emit(v_, "tensor_tensor", cbet[:], pbank(4, 8, 1), br_sb[:], op=OP.add,
           rd=[("pbk", 4), "br"], wr=["cbet"])

    def gelu_ffn(tag, xln3_keys, w1_ap, w1_keys, bias_ap, bias_key, w2_src,
                 w2_keys, bias2_ap, bias2_key, ydst, ydst_f32):
        for f in range(NFF):
            bk = f % 2
            ps = pbank(bk, 128, 512)
            for kd in range(ND):
                p.emit(t_, "matmul", ps,
                       w1_ap[:, kd * FF + f * 128:kd * FF + (f + 1) * 128],
                       xln3[:, kd * 512:(kd + 1) * 512],
                       start=(kd == 0), stop=(kd == ND - 1),
                       rd=[w1_keys[kd], ("xln3", kd)],
                       wr=[("pbk", bk)] if kd in (0, ND - 1) else (),
                       inc=(kd == ND - 1))
            p.emit(sc, "activation", h1b[bk][:], ps, AF.Identity,
                   bias=bias_ap[:, f:f + 1], rd=[("pbk", bk), bias_key],
                   wr=[("h1b", bk), ("pbk", bk)])
            p.emit(v_, "tensor_tensor", x2b[bk][:], h1b[bk][:], h1b[bk][:],
                   op=OP.mult, rd=[("h1b", bk)], wr=[("x2b", bk)])
            p.emit(v_, "tensor_scalar", ab_[bk][:], x2b[bk][:], GELU_C, 1.0,
                   op0=OP.mult, op1=OP.add, rd=[("x2b", bk)], wr=[("ab_", bk)])
            p.emit(v_, "tensor_tensor", ab_[bk][:], ab_[bk][:], h1b[bk][:],
                   op=OP.mult, rd=[("ab_", bk), ("h1b", bk)], wr=[("ab_", bk)])
            p.emit(sc, "activation", sgb[bk][:], ab_[bk][:], AF.Sigmoid,
                   scale=GELU_S, rd=[("ab_", bk)], wr=[("sgb", bk)])
            p.emit(v_, "tensor_tensor", act[:, f * 512:(f + 1) * 512],
                   h1b[bk][:], sgb[bk][:], op=OP.mult,
                   rd=[("h1b", bk), ("sgb", bk)], wr=[("act", f)])
        for kf in range(NFF):
            pe = kf % 2
            dma(s_, w2buf[pe][:], w2_src[kf * 128:(kf + 1) * 128, :],
                after=[p.last_w.get(w2_keys(kf))],
                wr=[("w2buf", pe)], dsem="dw2%d" % pe)
            for hh in range(ND):
                p.emit(t_, "matmul", pbank(hh, 128, 512),
                       w2buf[pe][:, hh * 128:(hh + 1) * 128],
                       act[:, kf * 512:(kf + 1) * 512],
                       start=(kf == 0), stop=(kf == NFF - 1),
                       rd=[("w2buf", pe), ("act", kf)],
                       wr=([("pbk", hh)] if kf in (0, NFF - 1) else []) +
                          ([("w2buf", pe)] if hh == ND - 1 else []),
                       inc=(kf == NFF - 1 or hh == ND - 1))
        for hh in range(ND):
            p.emit(sc, "activation", ydst[:, hh * 512:(hh + 1) * 512],
                   pbank(hh, 128, 512), AF.Identity,
                   bias=bias2_ap[:, hh:hh + 1],
                   rd=[("pbk", hh), bias2_key], wr=[(tag + "_y", hh)])

    def gelu_h1(w1_ap, w1_keys, bias_ap, bias_key):
        for f in range(NFF):
            bk = f % 2
            ps = pbank(bk, 128, 512)
            for kd in range(ND):
                p.emit(t_, "matmul", ps,
                       w1_ap[:, kd * FF + f * 128:kd * FF + (f + 1) * 128],
                       xln3[:, kd * 512:(kd + 1) * 512],
                       start=(kd == 0), stop=(kd == ND - 1),
                       rd=[w1_keys[kd], ("xln3", kd)],
                       wr=[("pbk", bk)] if kd in (0, ND - 1) else (),
                       inc=(kd == ND - 1))
            p.emit(sc, "activation", h1b[bk][:], ps, AF.Identity,
                   bias=bias_ap[:, f:f + 1], rd=[("pbk", bk), bias_key],
                   wr=[("h1b", bk), ("pbk", bk)])
            p.emit(v_, "tensor_tensor", x2b[bk][:], h1b[bk][:], h1b[bk][:],
                   op=OP.mult, rd=[("h1b", bk)], wr=[("x2b", bk)])
            p.emit(v_, "tensor_scalar", ab_[bk][:], x2b[bk][:], GELU_C, 1.0,
                   op0=OP.mult, op1=OP.add, rd=[("x2b", bk)], wr=[("ab_", bk)])
            p.emit(v_, "tensor_tensor", ab_[bk][:], ab_[bk][:], h1b[bk][:],
                   op=OP.mult, rd=[("ab_", bk), ("h1b", bk)], wr=[("ab_", bk)])
            p.emit(sc, "activation", sgb[bk][:], ab_[bk][:], AF.Sigmoid,
                   scale=GELU_S, rd=[("ab_", bk)], wr=[("sgb", bk)])
            p.emit(v_, "tensor_tensor", act[:, f * 512:(f + 1) * 512],
                   h1b[bk][:], sgb[bk][:], op=OP.mult,
                   rd=[("h1b", bk), ("sgb", bk)], wr=[("act", f)])

    def gelu_y(tag, w2_src, w2_keys, bias2_ap, bias2_key, ydst):
        for kf in range(NFF):
            pe = kf % 2
            dma(s_, w2buf[pe][:], w2_src[kf * 128:(kf + 1) * 128, :],
                after=[p.last_w.get(w2_keys(kf))],
                wr=[("w2buf", pe)], dsem="dw2%d" % pe)
            for hh in range(ND):
                p.emit(t_, "matmul", pbank(hh, 128, 512),
                       w2buf[pe][:, hh * 128:(hh + 1) * 128],
                       act[:, kf * 512:(kf + 1) * 512],
                       start=(kf == 0), stop=(kf == NFF - 1),
                       rd=[("w2buf", pe), ("act", kf)],
                       wr=([("pbk", hh)] if kf in (0, NFF - 1) else []) +
                          ([("w2buf", pe)] if hh == ND - 1 else []),
                       inc=(kf == NFF - 1 or hh == ND - 1))
        for hh in range(ND):
            p.emit(sc, "activation", ydst[:, hh * 512:(hh + 1) * 512],
                   pbank(hh, 128, 512), AF.Identity,
                   bias=bias2_ap[:, hh:hh + 1],
                   rd=[("pbk", hh), bias2_key], wr=[(tag + "_y", hh)])

    for tt in range(NTT):
        b = (tt * 512) // L
        t0 = (tt * 512) % L
        dma(g, sb3(x1b[0][:], 512), fm3(x1h[b, :, t0:t0 + 512]),
            rd=["x1b"], wr=["x1b"], dsem="dx1b")

        def x1src(i):
            return x1b[0][:, i * 512:(i + 1) * 512]

        ns["srckeys"] = ["x1b"] * ND
        bca, bcb = ln_full("ln3", x1src, ND, D, 512, src_bf16=True)
        for i in range(ND):
            ap_ = apB[i % 2]
            p.emit(v_, "tensor_tensor", ap_[:], x1src(i), bca, op=OP.mult,
                   rd=["x1b", ("pbk", 2)], wr=[("apB", i % 2)])
            p.emit(v_, "tensor_tensor", xln3[:, i * 512:(i + 1) * 512],
                   ap_[:], bcb, op=OP.add, rd=[("apB", i % 2), ("pbk", 3)],
                   wr=[("xln3", i)])
        dma(s_, fm3(xln3h[b, :, t0:t0 + 512]), sb3(xln3[:], 512),
            rd=[("xln3", i) for i in range(ND)], wr=[("xln3h", tt)])

        def x3src(i):
            return xln3[:, i * 512:(i + 1) * 512]

        ns["srckeys"] = [("xln3", i) for i in range(ND)]
        bca_r, bcb_r = ln_full("lnr", x3src, ND, D, 512, src_bf16=True)
        lps = pbank(4, 8, 512)
        for i in range(ND):
            p.emit(t_, "matmul", lps, wrg[:, i * 8:(i + 1) * 8], x3src(i),
                   start=(i == 0), stop=(i == ND - 1),
                   rd=[("wrg", i), ("xln3", i)],
                   wr=[("pbk", 4)] if i in (0, ND - 1) else (), inc=(i == ND - 1))
        p.emit(v_, "tensor_copy", lg_sb[:], lps, rd=[("pbk", 4)], wr=["lg"])
        p.emit(v_, "tensor_tensor", lg_sb[:], lg_sb[:], pbank(2, 8, 512),
               op=OP.mult, rd=["lg", ("pbk", 2), ("pbk", 4)], wr=["lg"])
        p.emit(v_, "scalar_tensor_tensor", lg_sb[:], pbank(3, 8, 512),
               cgam[:], lg_sb[:], op0=OP.mult, op1=OP.add,
               rd=[("pbk", 3), "cgam", "lg", ("pbk", 4)], wr=["lg"])
        p.emit(v_, "tensor_scalar", lg_sb[:], lg_sb[:], cbet[:], None,
               op0=OP.add, rd=["lg", "cbet"], wr=["lg"])
        for st in range(4):
            p.emit(t_, "transpose", pbank(5, 128, 8),
                   lg_sb[:, st * 128:(st + 1) * 128], identf[0:8, 0:8],
                   rd=["lg", "identf", ("pbk", 5)], wr=[("pbk", 5)])
            p.emit(v_, "tensor_copy", lt_sb[:], pbank(5, 128, 8),
                   rd=[("pbk", 5)], wr=["lt"])
            p.emit(v_, "tensor_reduce", nmax[:], lt_sb[:],
                   axis=mybir.AxisListType.X, op=OP.max, rd=["lt"],
                   wr=["nmax"])
            p.emit(v_, "tensor_scalar", nmax[:], nmax[:], -1.0, None,
                   op0=OP.mult, rd=["nmax"], wr=["nmax"])
            p.emit(sc, "activation", qex[:], lt_sb[:], AF.Exp, bias=nmax[:],
                   rd=["lt", "nmax"], wr=["qex"])
            p.emit(v_, "tensor_reduce", ssum[:], qex[:],
                   axis=mybir.AxisListType.X, op=OP.add, rd=["qex"],
                   wr=["ssum"])
            p.emit(v_, "reciprocal", ssum[:], ssum[:], rd=["ssum"],
                   wr=["ssum"])
            p.emit(v_, "tensor_tensor", p0c[:], qex[:, 0:1], ssum[:],
                   op=OP.mult, rd=["qex", "ssum"], wr=["p0c"])
            p.emit(t_, "transpose", pbank(5, 1, 128), p0c[:], identf[:],
                   rd=["p0c", "identf", ("pbk", 5)], wr=[("pbk", 5)])
            p.emit(v_, "tensor_copy", p0row[:, st * 128:(st + 1) * 128],
                   pbank(5, 1, 128), rd=[("pbk", 5)], wr=["p0row"])
        dma(s_, p0h[:, tt * 512:(tt + 1) * 512], p0row[:], rd=["p0row"],
            wr=[("p0h", tt)])

        gelu_ffn("y0", None, w1b, [("w1b", kd) for kd in range(ND)], b1c,
                 "b1c", w2bh, lambda kf: ("w2bh", kf), b2c, "b2c", y0st, None)
        dma(s_, fm3(y0h[b, :, t0:t0 + 512]), sb3(y0st[:], 512),
            rd=[("y0_y", hh) for hh in range(ND)], wr=[("y0h", tt)])
        for hh in range(ND):
            p.last_w[("y0_y", hh)] = p.last_w[("y0h", tt)]

    p.barrier()
    w1v = w1.rearrange("(a p) f -> p a f", p=128)
    w2v = w2.rearrange("(a p) d -> p a d", p=128)
    w2bv = w2bh.rearrange("(a p) d -> p a d", p=128)

    def bis_update():
        pr, inv, tmp = bst[:, 0:8], bst[:, 8:16], bst[:, 16:24]
        p.emit(v_, "tensor_tensor", pr, cnt_loc[:], kvec[:], op=OP.is_ge,
               rd=["cnt_loc", "kvec"], wr=["pr"])
        p.emit(v_, "tensor_scalar", inv, pr, -1.0, 1.0, op0=OP.mult,
               op1=OP.add, rd=["pr"], wr=["inv"])
        p.emit(v_, "tensor_tensor", tmp, pr, bmid[:], op=OP.mult,
               rd=["pr", "bmid"], wr=["tmp"])
        p.emit(v_, "tensor_tensor", bhi[:], bhi[:], inv, op=OP.mult,
               rd=["bhi", "inv"], wr=["bhi"])
        p.emit(v_, "tensor_tensor", bhi[:], bhi[:], tmp, op=OP.add,
               rd=["bhi", "tmp"], wr=["bhi"])
        p.emit(v_, "tensor_tensor", tmp, inv, bmid[:], op=OP.mult,
               rd=["inv", "bmid"], wr=["tmp"])
        p.emit(v_, "tensor_tensor", blo[:], blo[:], pr, op=OP.mult,
               rd=["blo", "pr"], wr=["blo"])
        p.emit(v_, "tensor_tensor", blo[:], blo[:], tmp, op=OP.add,
               rd=["blo", "tmp"], wr=["blo"])

    def bis_mid():
        p.emit(v_, "tensor_tensor", bmid[:], blo[:], bhi[:], op=OP.add,
               rd=["blo", "bhi"], wr=["bmid"])
        p.emit(v_, "tensor_scalar", bmid[:], bmid[:], 0.5, None, op0=OP.mult,
               rd=["bmid"], wr=["bmid"])
        p.emit(t_, "matmul", pbank(4, 128, 8), ones1[:], bmid[:], start=True,
               stop=True, rd=["ones1", "bmid", ("pbk", 4)], wr=[("pbk", 4)])
        p.emit(v_, "tensor_copy", bcols[:], pbank(4, 128, 8),
               rd=[("pbk", 4)], wr=["bcols"])

    def wdat_b(buf, cols=2048):
        return buf[:].bitcast(BF16)[:, 0:cols]

    def wdat_f(buf, cols=1024):
        return buf[:, 0:cols]

    def wmsk(buf, cols):
        return buf[:].bitcast(BF16)[:, 2048:2048 + cols]

    def cnt_mask_mm(j, buf, srcap, rdk, mcols, first, last, mkey, bank):
        p.emit(v_, "tensor_scalar", wmsk(buf, mcols), srcap,
               bcols[:, j:j + 1], 0.0, op0=OP.is_lt, op1=OP.add,
               rd=rdk + ["bcols"], wr=[mkey])
        nmm = mcols // 512
        for sb in range(nmm):
            p.emit(t_, "matmul", pbank(bank, 1, 512), onescol_b[:],
                   wmsk(buf, mcols)[:, sb * 512:(sb + 1) * 512],
                   start=(first and sb == 0), stop=(last and sb == nmm - 1),
                   rd=["onescolb", mkey], wr=[("pbk", bank)],
                   inc=(sb == nmm - 1))

    def bis_tally(j):
        p.emit(v_, "tensor_reduce", cnt_loc[:, j:j + 1],
               pbank(j, 1, 512), axis=mybir.AxisListType.X, op=OP.add,
               rd=[("pbk", j)], wr=["cnt_loc", ("pbk", j)])

    def bis_cnt_small(j, srcap, nn, rdk):
        p.emit(v_, "memset", cntc[:], 0.0, rd=["cntc"], wr=["cntc"])
        p.emit(v_, "tensor_scalar", wmsk(wcnt1[0], 1024)[:, 0:nn], srcap,
               bcols[:, j:j + 1], 0.0, op0=OP.is_lt, op1=OP.add,
               accum_out=cntc[:], rd=rdk + ["bcols", "cntc"],
               wr=["cntc", ("msk1", 0)])
        p.emit(t_, "matmul", pbank(5, 1, 1), cntc[:], onescol[:],
               start=True, stop=True, rd=["cntc", "onescol", ("pbk", 5)],
               wr=[("pbk", 5)])
        p.emit(v_, "tensor_copy", cnt_loc[:, j:j + 1], pbank(5, 1, 1),
               rd=[("pbk", 5)], wr=["cnt_loc"])

    def bis_update(js):
        pr, inv, tmp = bst[:, 0:8], bst[:, 8:16], bst[:, 16:24]
        sl = slice(js[0], js[-1] + 1)
        p.emit(v_, "tensor_tensor", pr[:, sl], cnt_loc[:, sl], kvec[:, sl],
               op=OP.is_ge, rd=["cnt_loc", "kvec"], wr=["pr"])
        p.emit(v_, "tensor_scalar", inv[:, sl], pr[:, sl], -1.0, 1.0,
               op0=OP.mult, op1=OP.add, rd=["pr"], wr=["inv"])
        p.emit(v_, "tensor_tensor", tmp[:, sl], pr[:, sl], bmid[:, sl],
               op=OP.mult, rd=["pr", "bmid"], wr=["tmp"])
        p.emit(v_, "tensor_tensor", bhi[:, sl], bhi[:, sl], inv[:, sl],
               op=OP.mult, rd=["bhi", "inv"], wr=["bhi"])
        p.emit(v_, "tensor_tensor", bhi[:, sl], bhi[:, sl], tmp[:, sl],
               op=OP.add, rd=["bhi", "tmp"], wr=["bhi"])
        p.emit(v_, "tensor_tensor", tmp[:, sl], inv[:, sl], bmid[:, sl],
               op=OP.mult, rd=["inv", "bmid"], wr=["tmp"])
        p.emit(v_, "tensor_tensor", blo[:, sl], blo[:, sl], pr[:, sl],
               op=OP.mult, rd=["blo", "pr"], wr=["blo"])
        p.emit(v_, "tensor_tensor", blo[:, sl], blo[:, sl], tmp[:, sl],
               op=OP.add, rd=["blo", "tmp"], wr=["blo"])

    def bis_mid(js):
        sl = slice(js[0], js[-1] + 1)
        p.emit(v_, "tensor_tensor", bmid[:, sl], blo[:, sl], bhi[:, sl],
               op=OP.add, rd=["blo", "bhi"], wr=["bmid"])
        p.emit(v_, "tensor_scalar", bmid[:, sl], bmid[:, sl], 0.5, None,
               op0=OP.mult, rd=["bmid"], wr=["bmid"])
        p.emit(t_, "matmul", pbank(4, 128, 8), ones1[:], bmid[:], start=True,
               stop=True, rd=["ones1", "bmid", ("pbk", 4)], wr=[("pbk", 4)])
        p.emit(v_, "tensor_copy", bcols[:], pbank(4, 128, 8),
               rd=[("pbk", 4)], wr=["bcols"])

    for c in range(ND):
        p.emit(sc, "activation", w1b[:, c * FF:(c + 1) * FF],
               w1b[:, c * FF:(c + 1) * FF], AF.Abs, rd=[("w1b", c)],
               wr=[("w1b", c)])

    mxc = cntc
    mx16 = cacc2[:, 0:16]
    for c in range(ND):
        p.emit(v_, "tensor_reduce", mx16[:, c:c + 1],
               w1b[:, c * FF:(c + 1) * FF], axis=mybir.AxisListType.X,
               op=OP.max, rd=[("w1b", c)], wr=["mx16"])
    p.emit(v_, "tensor_reduce", mxc[:], mx16[:, 0:ND],
           axis=mybir.AxisListType.X, op=OP.max, rd=["mx16"], wr=["mxc"])
    p.emit(t_, "transpose", pbank(4, 1, 128), mxc[:], identf[:],
           rd=["mxc", "identf", ("pbk", 4)], wr=[("pbk", 4)])
    p.emit(v_, "tensor_reduce", bhi[:, 0:1], pbank(4, 1, 128),
           axis=mybir.AxisListType.X, op=OP.max, rd=[("pbk", 4)], wr=["bhi"])
    for c in range(16):
        pe = c % 2
        dstb = wdat_b(wcnt1[pe])
        dma(g, dstb.rearrange("p (a d) -> p a d", d=1024),
            w2bv[:, 2 * c:2 * c + 2, :], wr=[("wcb", pe)],
            dsem="dcn%d" % pe)
        p.emit(sc, "activation", dstb, dstb, AF.Abs,
               rd=[("wcb", pe)], wr=[("wcb", pe)])
        p.emit(v_, "tensor_reduce", mx16[:, c:c + 1], dstb,
               axis=mybir.AxisListType.X, op=OP.max,
               rd=[("wcb", pe)], wr=["mx16"])
        p.last_w[("wcb", pe)] = p.last_w["mx16"]
    p.emit(v_, "tensor_reduce", mxc[:], mx16[:, 0:16],
           axis=mybir.AxisListType.X, op=OP.max, rd=["mx16"], wr=["mxc"])
    p.emit(t_, "transpose", pbank(4, 1, 128), mxc[:], identf[:],
           rd=["mxc", "identf", ("pbk", 4)], wr=[("pbk", 4)])
    p.emit(v_, "tensor_reduce", bhi[:, 2:3], pbank(4, 1, 128),
           axis=mybir.AxisListType.X, op=OP.max, rd=[("pbk", 4)], wr=["bhi"])
    for (src_, nn_, col, key) in ((ab1, NFF, 4, "ab1"), (ab2, ND, 6, "ab2")):
        p.emit(v_, "tensor_reduce", mxc[:], src_[:, 0:nn_],
               axis=mybir.AxisListType.X, op=OP.max, rd=[key], wr=["mxc"])
        p.emit(t_, "transpose", pbank(4, 1, 128), mxc[:], identf[:],
               rd=["mxc", "identf", ("pbk", 4)], wr=[("pbk", 4)])
        p.emit(v_, "tensor_reduce", bhi[:, col:col + 1], pbank(4, 1, 128),
               axis=mybir.AxisListType.X, op=OP.max, rd=[("pbk", 4)],
               wr=["bhi"])
    for col in (0, 2, 4, 6):
        p.emit(v_, "tensor_copy", bhi[:, col + 1:col + 2],
               bhi[:, col:col + 1], rd=["bhi"], wr=["bhi"])
    p.emit(v_, "tensor_scalar", bhi[:], bhi[:], 1.01, None, op0=OP.mult,
           rd=["bhi"], wr=["bhi"])

    for it in range(40):
        bis_mid((4, 5, 6, 7))
        for j, (src_, nn_, key) in ((4, (ab1, NFF, "ab1")),
                                    (5, (ab1, NFF, "ab1")),
                                    (6, (ab2, ND, "ab2")),
                                    (7, (ab2, ND, "ab2"))):
            bis_cnt_small(j, src_[:, 0:nn_], nn_, [key])
        bis_update((4, 5, 6, 7))

    dma(s_, dbgh[:, 32:40], blo[:], rd=["blo"], wr=["dbg"])
    bcnt1 = [wcnt1[0], wcnt1[1], actW[:, 0:2048], actW[:, 2048:4096]]
    bcnt2 = [wcnt2[0], wcnt2[1], actW[:, 4096:6144], actW[:, 6144:8192]]
    for stage in (1, 2):
        n_iter = 16 if stage == 1 else 18
        nch = 16 if stage == 1 else 32
        nbuf = 2 if stage == 1 else 4
        for it in range(n_iter):
            bis_mid((0, 1, 2, 3))
            for hc in range(nch):
                pe = hc % nbuf
                first, last = (hc == 0), (hc == nch - 1)
                if stage == 1:
                    srcap = w1b[:, hc * 2048:(hc + 1) * 2048]
                    rdk1 = [("w1b", hc // 2)]
                    mcols = 2048
                else:
                    dma(g, wdat_f(bcnt1[pe]),
                        w1[(hc // 4) * 128:(hc // 4) * 128 + 128,
                           (hc % 4) * 1024:(hc % 4) * 1024 + 1024],
                        wr=[("wcb", pe)], dsem="dcn%d" % pe)
                    p.emit(sc, "activation", wdat_f(bcnt1[pe]),
                           wdat_f(bcnt1[pe]), AF.Abs, rd=[("wcb", pe)],
                           wr=[("wcb", pe)])
                    srcap = wdat_f(bcnt1[pe])
                    rdk1 = [("wcb", pe)]
                    mcols = 1024
                for j in (0, 1):
                    cnt_mask_mm(j, bcnt1[pe], srcap, rdk1, mcols,
                                first, last, ("msk1", pe), bank=j)
                if stage == 2:
                    p.last_w[("wcb", pe)] = p.last_w[("msk1", pe)]
                if stage == 1:
                    dstb = wdat_b(bcnt2[pe])
                    dma(g, dstb.rearrange("p (a d) -> p a d", d=1024),
                        w2bv[:, 2 * hc:2 * hc + 2, :], wr=[("wc2", pe)],
                        dsem="dc2%d" % pe)
                    p.emit(sc, "activation", dstb, dstb, AF.Abs,
                           rd=[("wc2", pe)], wr=[("wc2", pe)])
                    srcap2 = dstb
                    mcols = 2048
                else:
                    dma(g, wdat_f(bcnt2[pe]).rearrange(
                            "p (a d) -> p a d", d=1024),
                        w2v[:, hc:hc + 1, :], wr=[("wc2", pe)],
                        dsem="dc2%d" % pe)
                    p.emit(sc, "activation", wdat_f(bcnt2[pe]),
                           wdat_f(bcnt2[pe]), AF.Abs, rd=[("wc2", pe)],
                           wr=[("wc2", pe)])
                    srcap2 = wdat_f(bcnt2[pe])
                    mcols = 1024
                for j in (2, 3):
                    cnt_mask_mm(j, bcnt2[pe], srcap2, [("wc2", pe)], mcols,
                                first, last, ("msk2", pe), bank=j)
                p.last_w[("wc2", pe)] = p.last_w[("msk2", pe)]
            for j in (0, 1, 2, 3):
                bis_tally(j)
            bis_update((0, 1, 2, 3))
        if stage == 1:
            p.emit(v_, "tensor_scalar", bhi[:, 0:4], blo[:, 0:4], 1.0078125,
                   None, op0=OP.mult, rd=["blo"], wr=["bhi"])
            p.emit(v_, "tensor_scalar", blo[:, 0:4], blo[:, 0:4], 0.9921875,
                   None, op0=OP.mult, rd=["blo"], wr=["blo"])
    p.emit(t_, "matmul", pbank(4, 128, 8), ones1[:], blo[:], start=True,
           stop=True, rd=["ones1", "blo", ("pbk", 4)], wr=[("pbk", 4)])
    p.emit(v_, "tensor_copy", bcols[:], pbank(4, 128, 8), rd=[("pbk", 4)],
           wr=["bcols"])
    p.barrier()

    def perturb(dst, dst_keys, src, src_key, absrc, abs_key, n, jb, jt,
                m1, m2):
        p.emit(v_, "tensor_scalar", m1[:, 0:n], absrc[:, 0:n],
               bcols[:, jb:jb + 1], 0.8, op0=OP.is_le, op1=OP.mult,
               rd=[abs_key, "bcols"], wr=["pm1"])
        p.emit(v_, "tensor_scalar", m2[:, 0:n], absrc[:, 0:n],
               bcols[:, jt:jt + 1], 0.8, op0=OP.is_ge, op1=OP.mult,
               rd=[abs_key, "bcols"], wr=["pm2"])
        p.emit(v_, "tensor_tensor", m1[:, 0:n], m1[:, 0:n], m2[:, 0:n],
               op=OP.subtract, rd=["pm1", "pm2"], wr=["pm1"])
        p.emit(v_, "tensor_scalar", m1[:, 0:n], m1[:, 0:n], 1.0, None,
               op0=OP.add, rd=["pm1"], wr=["pm1"])
        p.emit(v_, "tensor_tensor", dst[:, 0:n], src[:, 0:n], m1[:, 0:n],
               op=OP.mult, rd=["pm1", src_key], wr=dst_keys)

    perturb(b1p, ["b1p"], b1c, "b1c", ab1, "ab1", NFF, 4, 5,
            y1s[:, 0:64], y1s[:, 64:128])
    perturb(b2p, ["b2p"], b2c, "b2c", ab2, "ab2", ND, 6, 7,
            y1s[:, 0:64], y1s[:, 64:128])

    for kf in range(NFF):
        dma(s_, wstgA[:], w2[kf * 128:(kf + 1) * 128, :], rd=["wstgA"],
            wr=["wstgA"], dsem="dwst")
        p.emit(sc, "activation", absA[:], wstgA[:], AF.Abs,
               rd=["wstgA"], wr=["absA"])
        m1 = y1s[:, 0:1024]
        m2 = y1s[:, 1024:2048]
        p.emit(v_, "tensor_scalar", m1, absA[:], bcols[:, 2:3], 0.8,
               op0=OP.is_le, op1=OP.mult, rd=["absA", "bcols"], wr=["pm1"])
        p.emit(v_, "tensor_scalar", m2, absA[:], bcols[:, 3:4], 0.8,
               op0=OP.is_ge, op1=OP.mult, rd=["absA", "bcols"], wr=["pm2"])
        p.emit(v_, "tensor_tensor", m1, m1, m2, op=OP.subtract,
               rd=["pm1", "pm2"], wr=["pm1"])
        p.emit(v_, "tensor_scalar", m1, m1, 1.0, None, op0=OP.add,
               rd=["pm1"], wr=["pm1"])
        p.emit(v_, "tensor_tensor", wstgB[:], wstgA[:], m1, op=OP.mult,
               rd=["pm1", "wstgA"], wr=["wstgB"])
        dma(s_, pw2bh[kf * 128:(kf + 1) * 128, :], wstgB[:], rd=["wstgB"],
            wr=[("pw2bh", kf), "wstgB"], dsem="dwst")

    for kd in range(ND):
        for jj in range(4):
            dma(s_, wstgA[:],
                w1[kd * 128:(kd + 1) * 128, jj * 1024:(jj + 1) * 1024],
                rd=["wstgA"], wr=["wstgA"], dsem="dwst")
            p.emit(sc, "activation", absA[:], wstgA[:], AF.Abs,
                   rd=["wstgA"], wr=["absA"])
            m1 = y1s[:, 0:1024]
            m2 = y1s[:, 1024:2048]
            p.emit(v_, "tensor_scalar", m1, absA[:], bcols[:, 0:1], 0.8,
                   op0=OP.is_le, op1=OP.mult, rd=["absA", "bcols"], wr=["pm1"])
            p.emit(v_, "tensor_scalar", m2, absA[:], bcols[:, 1:2], 0.8,
                   op0=OP.is_ge, op1=OP.mult, rd=["absA", "bcols"], wr=["pm2"])
            p.emit(v_, "tensor_tensor", m1, m1, m2, op=OP.subtract,
                   rd=["pm1", "pm2"], wr=["pm1"])
            p.emit(v_, "tensor_scalar", m1, m1, 1.0, None, op0=OP.add,
                   rd=["pm1"], wr=["pm1"])
            p.emit(v_, "tensor_tensor",
                   w1b[:, kd * FF + jj * 1024:kd * FF + (jj + 1) * 1024],
                   wstgA[:], m1, op=OP.mult, rd=["pm1", "wstgA"],
                   wr=[("w1b", kd)])

    p.barrier()
    y0t = y0st

    def b2_loads1(tt):
        b = (tt * 512) // L
        t0 = (tt * 512) % L
        dma(g, sb3(x1b[tt % 2][:], 512), fm3(x1h[b, :, t0:t0 + 512]),
            rd=[("x1b", tt % 2)], wr=[("x1b", tt % 2)], dsem="dx1b")
        xt = dma(s_, sb3(xln3[:], 512), fm3(xln3h[b, :, t0:t0 + 512]),
                 rd=[("xln3", i) for i in range(ND)], wr=["xln3A"],
                 dsem="dxl3")
        for i in range(ND):
            p.last_w[("xln3", i)] = xt

    def b2_loads2(tt):
        b = (tt * 512) // L
        t0 = (tt * 512) % L
        dma(s_, sb3(y0t[:], 512), fm3(y0h[b, :, t0:t0 + 512]),
            rd=["y0t"], wr=["y0t"], dsem="dy0t")
        dma(s_, p0row[:], p0h[:, tt * 512:(tt + 1) * 512], rd=["p0row"],
            wr=["p0row"], dsem="dp0r")

    def b2_tail(tt):
        b = (tt * 512) // L
        t0 = (tt * 512) % L
        p.emit(t_, "matmul", pbank(4, 128, 512), ones1[:], p0row[:],
               start=True, stop=True, rd=["ones1", "p0row", ("pbk", 4)],
               wr=[("pbk", 4)])
        for hh in range(ND):
            ap_ = apB[hh % 2]
            p.emit(v_, "tensor_tensor", ap_[:], y0t[:, hh * 512:(hh + 1) * 512],
                   y1s[:, hh * 512:(hh + 1) * 512], op=OP.subtract,
                   rd=["y0t", ("y1_y", hh)], wr=[("apB", hh % 2)])
            p.emit(v_, "tensor_tensor", ap_[:], ap_[:], pbank(4, 128, 512),
                   op=OP.mult, rd=[("apB", hh % 2), ("pbk", 4)],
                   wr=[("apB", hh % 2)])
            p.emit(v_, "tensor_tensor", y1s[:, hh * 512:(hh + 1) * 512],
                   y1s[:, hh * 512:(hh + 1) * 512], ap_[:], op=OP.add,
                   rd=[("apB", hh % 2)], wr=[("y1_y", hh)])

        def ffsrc(i):
            return y1s[:, i * 512:(i + 1) * 512]

        ns["srckeys"] = [("y1_y", i) for i in range(ND)]
        bca, bcb = ln_full("ln4", ffsrc, ND, D, 512)
        for i in range(ND):
            ap_ = apB[i % 2]
            p.emit(v_, "tensor_tensor", ap_[:], ffsrc(i), bca, op=OP.mult,
                   rd=[("y1_y", i), ("pbk", 2)], wr=[("apB", i % 2)])
            p.emit(v_, "tensor_tensor", ap_[:], ap_[:], bcb, op=OP.add,
                   rd=[("apB", i % 2), ("pbk", 3)], wr=[("apB", i % 2)])
            p.emit(v_, "tensor_tensor", y1s[:, i * 512:(i + 1) * 512],
                   x1b[tt % 2][:, i * 512:(i + 1) * 512], ap_[:], op=OP.add,
                   rd=[("apB", i % 2), ("x1b", tt % 2)], wr=[("y1_y", i)])
        dma(s_, fm3(out_fm[b, :, t0:t0 + 512]),
            sb3(y1s[:, 0:ND * 512], 512),
            rd=[("y1_y", i) for i in range(ND)], wr=[("outh", tt)])
        for i in range(ND):
            p.last_w[("y1_y", i)] = p.last_w[("outh", tt)]

    b2_loads1(0)
    b2_loads2(0)
    gelu_h1(w1b, [("w1b", kd) for kd in range(ND)], b1p, "b1p")
    for tt in range(NTT):
        gelu_y("y1", pw2bh, lambda kf: ("pw2bh", kf), b2p, "b2p", y1s)
        if tt + 1 < NTT:
            b2_loads1(tt + 1)
            gelu_h1(w1b, [("w1b", kd) for kd in range(ND)], b1p, "b1p")
        b2_tail(tt)
        if tt + 1 < NTT:
            b2_loads2(tt + 1)

    p.wait_all("gpsimd")
    p.wait_all("vector")
    block = es.enter_context(nc.Block())
    p.replay(block)
    es.close()
    return nc



def _prep_shared(W_in, conv_w, conv_b, dt_bias, A_log, D_param, norm_w, W_out,
                 W1, b1, W2, b2, ln_g, ln_b, Wr, br, cfg):
    f32 = np.float32
    NIP = 35
    W_inT = np.ascontiguousarray(np.asarray(W_in).T).astype(f32)
    wpad = np.zeros((cfg.D, NIP * 128), f32)
    wpad[:, :4096] = W_inT[:, :4096]
    wpad[:, 32 * 128:32 * 128 + 64] = W_inT[:, 4096:4160]
    wpad[:, 33 * 128:33 * 128 + 64] = W_inT[:, 4160:4224]
    wpad[:, 34 * 128:34 * 128 + 16] = W_inT[:, 4224:4240]
    w_in_tiles = np.ascontiguousarray(
        wpad.reshape(cfg.D // 128, 128, NIP, 128).transpose(2, 0, 1, 3))
    cw = np.asarray(conv_w).reshape(2176, 4).astype(f32)
    cb = np.asarray(conv_b).reshape(2176).astype(f32)
    cw4 = np.zeros((128, 18 * 4), f32)
    cb18 = np.zeros((128, 18), f32)
    for c in range(16):
        cw4[:, c * 4:(c + 1) * 4] = cw[c * 128:(c + 1) * 128]
        cb18[:, c] = cb[c * 128:(c + 1) * 128]
    cw4[0:64, 16 * 4:17 * 4] = cw[2048:2112]
    cb18[0:64, 16] = cb[2048:2112]
    cw4[0:64, 17 * 4:18 * 4] = cw[2112:2176]
    cb18[0:64, 17] = cb[2112:2176]
    e16 = np.zeros((16, 16 * 128), f32)
    for h in range(16):
        e16[h, h * 128:(h + 1) * 128] = 1.0
    n1 = W1.size
    k1 = max(1, int(n1 * 0.1 / 2))
    n2 = W2.size
    k2 = max(1, int(n2 * 0.1 / 2))
    nb1 = b1.size
    kb1 = max(1, int(nb1 * 0.1 / 2))
    nb2 = b2.size
    kb2 = max(1, int(nb2 * 0.1 / 2))
    NC = cfg.n_cores
    return {
        "w_in_tiles": w_in_tiles.astype(f32),
        "w_outT": np.ascontiguousarray(np.asarray(W_out).T).astype(f32),
        "w1": np.ascontiguousarray(W1).astype(f32),
        "w2": np.ascontiguousarray(W2).astype(f32),
        "conv_w4": cw4,
        "conv_b17": cb18,
        "dtb": np.asarray(dt_bias).reshape(16, 1).astype(f32),
        "alog": np.asarray(A_log).reshape(16, 1).astype(f32),
        "dprm_row": np.asarray(D_param).reshape(1, 16).astype(f32),
        "normw": np.ascontiguousarray(
            np.asarray(norm_w).reshape(16, 128).T).astype(f32),
        "b1c": np.ascontiguousarray(np.asarray(b1).reshape(32, 128).T).astype(f32),
        "b2c": np.ascontiguousarray(np.asarray(b2).reshape(8, 128).T).astype(f32),
        "lgam": np.ascontiguousarray(np.asarray(ln_g).reshape(8, 128).T).astype(f32),
        "lbet": np.ascontiguousarray(np.asarray(ln_b).reshape(8, 128).T).astype(f32),
        "wr_in": np.ascontiguousarray(np.asarray(Wr).reshape(8, 128, 8)
            .transpose(1, 0, 2).reshape(128, 64)).astype(f32),
        "br_in": np.asarray(br).reshape(8, 1).astype(f32),
        "identf_in": np.eye(128, dtype=f32),
        "identb_in": np.eye(128).astype(BF),
        "trineg_in": np.where(
            np.arange(128)[:, None] <= np.arange(128)[None, :], 0.0, NEG
        ).astype(f32),
        "e16_in": e16,
        "kvec_in": np.array([[k1, n1 - k1 + 1, k2, n2 - k2 + 1,
                              kb1, nb1 - kb1 + 1,
                              kb2, nb2 - kb2 + 1]], f32),
        "bnds_in": np.array([[0.0] * 8 + [1024.0] * 8], f32),
    }


def kernel(L0_attn_hidden_states, L0_attn_W_in, L0_attn_conv_w, L0_attn_conv_b,
           L0_attn_dt_bias, L0_attn_A_log_alpha, L0_attn_D_param,
           L0_attn_norm_w, L0_attn_W_out, L0_ffn_W1, L0_ffn_b1, L0_ffn_W2,
           L0_ffn_b2, L0_ffn_ln_gamma, L0_ffn_ln_beta, L0_ffn_Wr, L0_ffn_br):
    cfg = CFG()
    x = np.asarray(L0_attn_hidden_states, np.float32)
    W1 = np.ascontiguousarray(np.asarray(L0_ffn_W1, np.float32))
    W2 = np.ascontiguousarray(np.asarray(L0_ffn_W2, np.float32))
    shared = _prep_shared(L0_attn_W_in, L0_attn_conv_w, L0_attn_conv_b,
                          L0_attn_dt_bias, L0_attn_A_log_alpha, L0_attn_D_param,
                          L0_attn_norm_w, L0_attn_W_out, W1, L0_ffn_b1, W2,
                          L0_ffn_b2, L0_ffn_ln_gamma, L0_ffn_ln_beta,
                          L0_ffn_Wr, L0_ffn_br, cfg)
    nc = build_program(cfg)
    in_maps = []
    for c in range(8):
        m = dict(shared)
        m["x_fm"] = np.ascontiguousarray(
            x[2 * c:2 * c + 2].transpose(0, 2, 1)).astype(np.float32)
        in_maps.append(m)
    res = run_bass_kernel_spmd(nc, in_maps, core_ids=list(range(8)))
    outs = []
    for c in range(8):
        o = np.asarray(res.results[c]["out_fm"])
        outs.append(np.ascontiguousarray(o.transpose(0, 2, 1)))
    return np.concatenate(outs, axis=0).astype(np.float32)


# version 12 (final)
# speedup vs baseline: 1.0760x; 1.0122x over previous
import contextlib

import numpy as np
import ml_dtypes
import concourse.bass as bass
import concourse.mybir as mybir
from concourse.bass_utils import run_bass_kernel_spmd

F32 = mybir.dt.float32
BF16 = mybir.dt.bfloat16
AF = mybir.ActivationFunctionType
OP = mybir.AluOpType
BF = ml_dtypes.bfloat16

GELU_C = 0.044715
GELU_S = 2.0 * 0.7978845608028654
EPS = 1e-5
NEG = -1e30


class CFG:
    def __init__(self, L=4096, NB=2, T=512, n_cores=8, bisect_iters=52,
                 debug=None):
        self.L, self.NB, self.T, self.C = L, NB, T, 128
        self.D, self.EI, self.CV, self.DIP = 1024, 2048, 2176, 4240
        self.NH, self.HD, self.NS, self.FF = 16, 128, 64, 4096
        self.n_cores = n_cores
        self.bisect_iters = bisect_iters
        self.debug = debug
        assert L % T == 0 and T % self.C == 0


ENGINES = ("tensor", "vector", "scalar", "gpsimd", "sync")
DSEMS = ("dxb", "dwi0", "dwi1", "dwi2", "dwo0", "dwo1", "dw2i0", "dw2i1", "dw2o0", "dw2o1", "dx1b",
         "dxl3", "dy0t", "dp0r", "dw20", "dw21", "dwst", "dcn0", "dcn1", "dcn2", "dcn3", "dc20", "dc21", "dc22", "dc23")
SEMS = ("T", "V", "S", "G", "DS", "DA", "DG", "DC") + DSEMS
ENG_SEM = {"tensor": "T", "vector": "V", "scalar": "S", "gpsimd": "G"}
DMA_SEM = {"sync": "DS", "scalar": "DA", "gpsimd": "DG"}


class Prog:

    def __init__(self, nc):
        self.nc = nc
        self.ops = {e: [] for e in ENGINES}
        self.cnt = {s: 0 for s in SEMS}
        self.hiw = {e: {s: 0 for s in SEMS} for e in ENGINES}
        self.last_w = {}
        self.readers = {}
        self.sems = {}
        self.pending_rd = {e: [] for e in ENGINES}

    def make_sems(self, es):
        for s in SEMS:
            self.sems[s] = es.enter_context(self.nc.semaphore("sem_" + s))

    def patch_total(self, keys, sem):
        tok = (sem, self.cnt[sem])
        for k in keys:
            self.last_w[k] = tok
            self.readers[k] = []

    def emit(self, eng, fn, *args, rd=(), wr=(), after=(), inc=True, dma=False,
             collective=False, dsem=None, **kw):
        toks = [t for t in after if t is not None]
        for k in rd:
            t = self.last_w.get(k)
            if t is not None:
                toks.append(t)
        for k in wr:
            t = self.last_w.get(k)
            if t is not None:
                toks.append(t)
            toks.extend(self.readers.get(k, ()))
        waits = []
        own = ENG_SEM.get(eng)
        for (sem, val) in toks:
            if sem == own and eng == "tensor":
                continue
            if val > self.hiw[eng][sem]:
                self.hiw[eng][sem] = val
                waits.append((sem, val))
        if dma:
            sem = dsem or ("DC" if collective else DMA_SEM[eng])
            amt = 16
        else:
            sem = ENG_SEM[eng]
            amt = 1
        if inc:
            self.cnt[sem] += amt
        tok = (sem, self.cnt[sem])
        self.ops[eng].append((fn, args, kw, waits, (sem, amt) if inc else None))
        if inc:
            for k in rd:
                self.readers.setdefault(k, []).append(tok)
            for k in self.pending_rd[eng]:
                self.readers.setdefault(k, []).append(tok)
            self.pending_rd[eng] = []
        else:
            self.pending_rd[eng].extend(rd)
        for k in wr:
            for e2 in ENGINES:
                assert k not in self.pending_rd[e2], \
                    f"write to {k} while {e2} has un-inc'd pending read"
            self.last_w[k] = tok
            self.readers[k] = []
        return tok

    def barrier(self):
        snap = dict(self.cnt)
        for e in ENGINES:
            for s in SEMS:
                v = snap[s]
                if s != ENG_SEM.get(e) and v > self.hiw[e][s]:
                    self.hiw[e][s] = v
                    self.ops[e].append((None, None, None, [(s, v)], None))

    def wait_all(self, eng):
        for s in SEMS:
            v = self.cnt[s]
            if v > self.hiw[eng][s]:
                self.hiw[eng][s] = v
                self.ops[eng].append((None, None, None, [(s, v)], None))

    def replay(self, block):
        prog = self

        def section(name):
            def run(eng):
                for (fn, args, kw, waits, inc) in prog.ops[name]:
                    for (sem, val) in waits:
                        eng.wait_ge(prog.sems[sem], val)
                    if fn is None:
                        continue
                    inst = getattr(eng, fn)(*args, **kw)
                    if inc is not None:
                        inst.then_inc(prog.sems[inc[0]], inc[1])
            return run

        block.tensor(section("tensor"))
        block.vector(section("vector"))
        block.scalar(section("scalar"))
        block.gpsimd(section("gpsimd"))
        block.sync(section("sync"))


class Arena:
    def __init__(self, handle, words):
        self.h = handle
        self.words = words
        self.off = 0
        self.marks = []

    def _take(self, n_words):
        n_words = (n_words + 1) & ~1
        assert self.off + n_words <= self.words, \
            f"arena overflow: {self.off}+{n_words} > {self.words}"
        o = self.off
        self.off += n_words
        return o

    def f32(self, parts, n):
        o = self._take(n)
        return self.h[0:parts, o:o + n]

    def bf16(self, parts, n):
        o = self._take((n + 1) // 2)
        return self.h[:].bitcast(BF16)[0:parts, 2 * o:2 * o + n]

    def push(self):
        self.marks.append(self.off)

    def pop(self):
        self.off = self.marks.pop()


def build_program(cfg: CFG):
    nc = bass.Bass("TRN2", target_bir_lowering=False)
    L, NB, T, C = cfg.L, cfg.NB, cfg.T, cfg.C
    D, EI, CV, DIP = cfg.D, cfg.EI, cfg.CV, cfg.DIP
    NH, HD, NS, FF = cfg.NH, cfg.HD, cfg.NS, cfg.FF
    ND, NE, NCV, NFF = D // 128, EI // 128, CV // 128, FF // 128
    NIP = 35
    NCVT = 18
    NT = L // T
    TOK = NB * L
    NTT = TOK // 512
    NCH = T // C

    def din(name, shape, dt=F32):
        return nc.dram_tensor(name, list(shape), dt, kind="ExternalInput").ap()

    def dint(name, shape, dt=F32):
        return nc.dram_tensor(name, list(shape), dt, kind="Internal").ap()

    x_fm = din("x_fm", [NB, D, L])
    w_in_tiles = din("w_in_tiles", [NIP, ND, 128, 128])
    w_outT = din("w_outT", [EI, D])
    w1 = din("w1", [D, FF])
    w2 = din("w2", [FF, D])
    conv_w4 = din("conv_w4", [128, NCVT * 4])
    conv_b17 = din("conv_b17", [128, NCVT])
    dtb_in = din("dtb", [NH, 1])
    alog_in = din("alog", [NH, 1])
    dprm_row = din("dprm_row", [1, NH])
    normw_in = din("normw", [128, NE])
    b1c_in = din("b1c", [128, NFF])
    b2c_in = din("b2c", [128, ND])
    lgam_in = din("lgam", [128, ND])
    lbet_in = din("lbet", [128, ND])
    wr_in = din("wr_in", [128, (D // 128) * 8])
    br_in = din("br_in", [8, 1])
    identf_in = din("identf_in", [128, 128])
    identb_in = din("identb_in", [128, 128], BF16)
    trineg_in = din("trineg_in", [128, 128])
    e16_in = din("e16_in", [NH, NH * 128])
    kvec_in = din("kvec_in", [1, 8])
    bnds_in = din("bnds_in", [1, 16])

    out_fm = nc.dram_tensor("out_fm", [NB, D, L], F32, kind="ExternalOutput").ap()

    x1h = dint("x1h", [NB, D, L])
    xln3h = dint("xln3h", [NB, D, L], BF16)
    y0h = dint("y0h", [NB, D, L], BF16)
    p0h = dint("p0h", [1, TOK])
    w2bh = dint("w2bh", [FF, D], BF16)
    w_inbh = dint("w_inbh", [NIP, ND, 128, 128], BF16)
    w_outbh = dint("w_outbh", [EI, D], BF16)
    dbgh = dint("dbgh", [1, 64])
    pw2bh = dint("pw2bh", [FF, D], BF16)

    es = contextlib.ExitStack()
    AR_P = Arena(es.enter_context(nc.sbuf_tensor("arP", [128, 4100], F32)), 4100)
    AR_A = Arena(es.enter_context(nc.sbuf_tensor("arA", [128, 48700], F32)), 48700)

    p = Prog(nc)
    p.make_sems(es)
    pb = [es.enter_context(nc.psum_tensor(f"pb{i}", [128, 512], F32))
          for i in range(8)]

    def pbank(i, parts=128, n=512):
        return pb[i][0:parts, 0:n]

    def pbank_b(i, parts=128, n=1024):
        return pb[i][:].bitcast(BF16)[0:parts, 0:n]

    identf = AR_P.f32(128, 128)
    identb = AR_P.bf16(128, 128)
    trineg = AR_P.f32(128, 128)
    e16 = AR_P.f32(NH, NH * 128)
    cw4 = AR_P.f32(128, NCVT * 4)
    cb17 = AR_P.f32(128, NCVT)
    dt_bias = AR_P.f32(NH, 1)
    a_log = AR_P.f32(NH, 1)
    d_row = AR_P.f32(1, NH)
    nw = AR_P.f32(128, NE)
    b1c = AR_P.f32(128, NFF)
    b2c = AR_P.f32(128, ND)
    gam = AR_P.f32(128, ND)
    bet = AR_P.f32(128, ND)
    wr_sb = AR_P.f32(128, ND * 8)
    br_sb = AR_P.f32(8, 1)
    kvec = AR_P.f32(1, 8)
    ones1 = AR_P.f32(1, 128)
    ones1b = AR_P.bf16(1, 128)
    onescol = AR_P.f32(128, 1)
    onescol_b = AR_P.bf16(128, 2)[:, 0:1]
    zero16 = AR_P.f32(NH, 1)
    eps_col = AR_P.f32(128, 1)
    one_col = AR_P.f32(128, 1)
    a_col = AR_P.f32(NH, 1)
    d_b = AR_P.f32(128, NH)
    wrg = AR_P.bf16(128, ND * 8)
    cgam = AR_P.f32(8, 1)
    cbet = AR_P.f32(8, 1)
    b1p = AR_P.f32(128, NFF)
    b2p = AR_P.f32(128, ND)
    blo = AR_P.f32(1, 8)
    bhi = AR_P.f32(1, 8)
    bmid = AR_P.f32(1, 8)
    bcols = AR_P.f32(128, 8)
    bst = AR_P.f32(1, 24)
    ab1 = AR_P.f32(128, NFF)
    ab2 = AR_P.f32(128, ND)

    g, s_, v_, t_, sc = "gpsimd", "sync", "vector", "tensor", "scalar"

    def dma(eng, out, in_, rd=(), wr=(), after=(), dsem=None):
        return p.emit(eng, "dma_start", rd=rd, wr=wr, after=after, dma=True,
                      dsem=dsem, out=out, in_=in_)

    def fm3(hbm_ap):
        return hbm_ap.rearrange("(k p) t -> p k t", p=128)

    def sb3(sbuf_ap, t):
        return sbuf_ap.rearrange("p (k t) -> p k t", t=t)

    dma(s_, identf[:], identf_in[:], wr=["identf"])
    dma(s_, trineg[:], trineg_in[:], wr=["trineg"])
    dma(s_, e16[:], e16_in[:], wr=["e16"])
    dma(s_, cw4[:], conv_w4[:], wr=["cw4"])
    dma(s_, cb17[:], conv_b17[:], wr=["cb17"])
    dma(s_, dt_bias[:], dtb_in[:], wr=["dtb"])
    dma(s_, a_log[:], alog_in[:], wr=["alog"])
    dma(s_, d_row[:], dprm_row[:], wr=["drow"])
    dma(s_, nw[:], normw_in[:], wr=["nw"])
    dma(s_, b1c[:], b1c_in[:], wr=["b1c"])
    dma(s_, b2c[:], b2c_in[:], wr=["b2c"])
    dma(s_, gam[:], lgam_in[:], wr=["gam"])
    dma(s_, bet[:], lbet_in[:], wr=["bet"])
    dma(s_, wr_sb[:], wr_in[:], wr=["wr"])
    dma(s_, br_sb[:], br_in[:], wr=["br"])
    dma(s_, kvec[:], kvec_in[:], wr=["kvec"])
    dma(s_, blo[:], bnds_in[:, 0:8], wr=["blo"])
    dma(s_, bhi[:], bnds_in[:, 8:16], wr=["bhi"])
    dma(s_, ab1[:], b1c_in[:], wr=["ab1"])
    dma(s_, ab2[:], b2c_in[:], wr=["ab2"])
    dma(g, identb[:], identb_in[:], wr=["identb"])
    p.patch_total(["identf", "trineg", "e16", "cw4", "cb17", "dtb", "alog",
                   "drow", "nw", "b1c", "b2c", "gam", "bet", "wr", "br",
                   "kvec", "blo", "bhi", "ab1", "ab2"], "DS")
    p.patch_total(["identb"], "DG")

    p.emit(g, "memset", ones1[:], 1.0, wr=["ones1"])
    p.emit(g, "memset", ones1b[:], 1.0, wr=["ones1b"])
    p.emit(g, "memset", onescol[:], 1.0, wr=["onescol"])
    p.emit(g, "memset", zero16[:], 0.0, wr=["zero16"])
    p.emit(g, "memset", bmid[:], 0.0, wr=["bmid"])
    p.emit(g, "memset", eps_col[:], float(EPS), wr=["eps_col"])
    p.emit(g, "memset", one_col[:], 1.0, wr=["one_col"])
    p.emit(v_, "tensor_copy", onescol_b[:], onescol[:], rd=["onescol"],
           wr=["onescolb"])
    p.emit(sc, "activation", a_col[:], a_log[:], AF.Exp, rd=["alog"],
           wr=["a_col"])
    p.emit(v_, "tensor_scalar", a_col[:], a_col[:], -1.0, None, op0=OP.mult,
           rd=["a_col"], wr=["a_col"])
    p.emit(t_, "matmul", pbank(6, 128, NH), ones1[:], d_row[:], start=True,
           stop=True, rd=["ones1", "drow", ("pbk", 6)], wr=[("pbk", 6)])
    p.emit(v_, "tensor_copy", d_b[:], pbank(6, 128, NH), rd=[("pbk", 6)],
           wr=["d_b"])
    p.emit(sc, "activation", ab1[:], ab1[:], AF.Abs,
           rd=["ab1"], wr=["ab1"])
    p.emit(sc, "activation", ab2[:], ab2[:], AF.Abs,
           rd=["ab2"], wr=["ab2"])

    w2stg = [AR_P.bf16(128, 1024) for _ in range(2)]
    for kf in range(NFF):
        pe = kf % 2
        dma(g, w2stg[pe][:], w2[kf * 128:(kf + 1) * 128, :],
            wr=[("w2stg", pe)], dsem="dw2i%d" % pe)
        dma(s_, w2bh[kf * 128:(kf + 1) * 128, :], w2stg[pe][:],
            rd=[("w2stg", pe)], wr=[("w2bh", kf), ("w2stg", pe)],
            dsem="dw2o%d" % pe)
    for e in range(NIP):
        pe = e % 2
        dma(g, w2stg[pe][:].rearrange("p (k q) -> p k q", q=128),
            w_in_tiles[e].rearrange("k p q -> p k q"),
            wr=[("w2stg", pe)], dsem="dw2i%d" % pe)
        dma(s_, w_inbh[e].rearrange("k p q -> p k q"),
            w2stg[pe][:].rearrange("p (k q) -> p k q", q=128),
            rd=[("w2stg", pe)], wr=[("wibh", e), ("w2stg", pe)], dsem="dw2o%d" % pe)
    for ke in range(NE):
        pe = ke % 2
        dma(g, w2stg[pe][:], w_outT[ke * 128:(ke + 1) * 128, :],
            wr=[("w2stg", pe)], dsem="dw2i%d" % pe)
        dma(s_, w_outbh[ke * 128:(ke + 1) * 128, :], w2stg[pe][:],
            rd=[("w2stg", pe)], wr=[("wobh", ke), ("w2stg", pe)],
            dsem="dw2o%d" % pe)

    ns = {}

    def ln_full(tag, src_fn, n_tiles, n_feat, tcols, src_bf16=False):
        arow, brow, crow = ns["arow"], ns["brow"], ns["crow"]
        sqp = ns["sq"]
        oc, ock = (onescol_b, "onescolb") if src_bf16 else (onescol, "onescol")
        s1 = pbank(2, 1, tcols)
        s2 = pbank(3, 1, tcols)
        srckeys = ns["srckeys"]
        for i in range(n_tiles):
            p.emit(t_, "matmul", s1, oc[:], src_fn(i),
                   start=(i == 0), stop=(i == n_tiles - 1),
                   rd=[ock, srckeys[i]],
                   wr=[("pbk", 2)] if i in (0, n_tiles - 1) else (),
                   inc=(i == n_tiles - 1))
        for i in range(n_tiles):
            sq_scr = sqp[i % 2]
            p.emit(sc, "activation", sq_scr[:, 0:tcols], src_fn(i), AF.Square,
                   rd=[srckeys[i]], wr=[("sq_scr", i % 2)])
            p.emit(t_, "matmul", s2, onescol_b[:], sq_scr[:, 0:tcols],
                   start=(i == 0), stop=(i == n_tiles - 1),
                   rd=["onescolb", ("sq_scr", i % 2)],
                   wr=[("pbk", 3)] if i in (0, n_tiles - 1) else (), inc=True)
        mu = arow[:, 0:tcols]
        p.emit(v_, "tensor_scalar", mu, s1, 1.0 / n_feat, None, op0=OP.mult,
               rd=[("pbk", 2)], wr=["arow"])
        mu2 = crow[:, 0:tcols]
        p.emit(v_, "tensor_tensor", mu2, mu, mu, op=OP.mult, rd=["arow"],
               wr=["crow"])
        var = brow[:, 0:tcols]
        p.emit(v_, "scalar_tensor_tensor", var, s2, 1.0 / n_feat, mu2,
               op0=OP.mult, op1=OP.subtract, rd=[("pbk", 3), "crow"],
               wr=["brow"])
        p.emit(sc, "activation", var, var, AF.Ln, bias=eps_col[0:1, :],
               rd=["brow", "eps_col"], wr=["brow"])
        rstd_b = brow[:].bitcast(BF16)[0:1, 0:tcols]
        p.emit(sc, "activation", rstd_b, var, AF.Exp, scale=-0.5, rd=["brow"],
               wr=["brow"])
        brw_b = crow[:].bitcast(BF16)[0:1, 0:tcols]
        p.emit(v_, "scalar_tensor_tensor", brw_b, mu, -1.0, rstd_b,
               op0=OP.mult, op1=OP.mult, rd=["arow", "brow"], wr=["crow"])
        bca = pbank(2, 128, tcols)
        bcb = pbank(3, 128, tcols)
        p.emit(t_, "matmul", bca, ones1b[:], rstd_b, start=True, stop=True,
               rd=["ones1b", "brow", ("pbk", 2)], wr=[("pbk", 2)])
        p.emit(t_, "matmul", bcb, ones1b[:], brw_b, start=True, stop=True,
               rd=["ones1b", "crow", ("pbk", 3)], wr=[("pbk", 3)])
        return bca, bcb

    AR_A.push()
    wi_buf = [AR_A.bf16(128, ND * 128) for _ in range(3)]
    wo_buf = [AR_A.bf16(128, 1024) for _ in range(2)]
    diag = [AR_A.bf16(128, 4 * 128) for _ in range(2)]
    xb = [AR_A.f32(128, ND * T) for _ in range(2)]
    xln1 = AR_A.bf16(128, ND * T)
    z_silu = AR_A.bf16(128, NE * T)
    xbc = AR_A.bf16(128, NCVT * (T + 3))
    y_n = AR_A.bf16(128, NE * T)
    m_o = AR_A.f32(128, ND * T)
    sqA = AR_A.bf16(128, 512)
    halo = AR_A.bf16(128, NCVT * 4)
    dt_sp = AR_A.f32(NH, T)
    s_all = AR_A.f32(NH, T)
    sig_all = AR_A.f32(NH, T)
    esc_c = AR_A.f32(NH, 1)
    es_b = [AR_A.f32(NS, C) for _ in range(2)]
    w_st = AR_A.f32(NH, C)
    sT = AR_A.f32(128, NH)
    sigT = AR_A.f32(128, NH)
    wT = AR_A.f32(128, NH)
    escb = AR_A.f32(NS, NH)
    g_s = AR_A.f32(128, C)
    mask_s = [AR_A.f32(128, C) for _ in range(2)]
    gmb = [AR_A.bf16(128, C) for _ in range(2)]
    x_tm = [AR_A.bf16(128, C) for _ in range(2)]
    x_til = [AR_A.bf16(128, C) for _ in range(2)]
    b_tm = AR_A.bf16(128, NS)
    c_til = [AR_A.bf16(NS, C) for _ in range(2)]
    ht = AR_A.f32(NS, NH * HD)
    ht_b = [AR_A.bf16(NS, HD) for _ in range(2)]
    y_g = AR_A.bf16(128, NH * C)
    dx = [AR_A.f32(128, C) for _ in range(2)]
    apA = [AR_A.f32(128, T) for _ in range(2)]
    rsc = AR_A.f32(1, 16)
    arowA = AR_A.f32(1, T)
    browA = AR_A.f32(1, T)
    crowA = AR_A.f32(1, T)
    sqA2 = AR_A.bf16(128, 512)
    ns.update(arow=arowA, brow=browA, crow=crowA, sq=(sqA, sqA2))

    for mt in range(NB * NT):
        b, m = mt // NT, mt % NT
        t0 = m * T
        xp = mt % 2
        xv = xb[xp]

        dma(g, sb3(xv[:], T), fm3(x_fm[b, :, t0:t0 + T]),
            rd=[("xb", xp)], wr=[("xb", xp)], dsem="dxb")
        if m == 0:
            p.emit(v_, "memset", ht[:], 0.0,
                   rd=[("ht", h) for h in range(NH)],
                   wr=[("ht", h) for h in range(NH)])

        def xsrc(i):
            return xv[:, i * T:(i + 1) * T]

        ns["srckeys"] = [("xb", xp)] * ND
        bca, bcb = ln_full("ln1", xsrc, ND, D, T)
        for i in range(ND):
            ap_ = apA[i % 2]
            p.emit(v_, "tensor_tensor", ap_[:], xsrc(i), bca, op=OP.mult,
                   rd=[("xb", xp), ("pbk", 2)], wr=[("apA", i % 2)])
            p.emit(v_, "tensor_tensor", xln1[:, i * T:(i + 1) * T], ap_[:],
                   bcb, op=OP.add, rd=[("apA", i % 2), ("pbk", 3)],
                   wr=[("xln1", i)])

        for e in range(NIP):
            pe = e % 3
            dma(s_, wi_buf[pe][:].rearrange("p (k q) -> p k q", q=128),
                w_inbh[e].rearrange("k p q -> p k q"),
                after=[p.last_w.get(("wibh", e))],
                wr=[("wi", pe)], dsem="dwi%d" % pe)
            rows_e = 128 if e < 32 else (64 if e < 34 else NH)
            ps = pbank(pe, rows_e, T)
            for kd in range(ND):
                p.emit(t_, "matmul", ps,
                       wi_buf[pe][:, kd * 128:kd * 128 + rows_e],
                       xln1[:, kd * T:(kd + 1) * T],
                       start=(kd == 0), stop=(kd == ND - 1),
                       rd=[("wi", pe), ("xln1", kd)],
                       wr=([("pbk", pe)] if kd in (0, ND - 1) else []) +
                          ([("wi", pe)] if kd == ND - 1 else []),
                       inc=(kd == ND - 1))
            if e < NE:
                p.emit(sc, "activation", z_silu[:, e * T:(e + 1) * T], ps,
                       AF.Silu, rd=[("pbk", pe)], wr=[("zs", e)])
            elif e < 34:
                c = e - NE
                p.emit(v_, "tensor_copy",
                       xbc[0:rows_e, c * (T + 3) + 3:c * (T + 3) + 3 + T], ps,
                       rd=[("pbk", pe)], wr=[("xbc", c)])
            else:
                p.emit(sc, "activation", dt_sp[:, 0:T], ps, AF.Relu,
                       bias=dt_bias[:], rd=[("pbk", pe), "dtb"], wr=["dt_sp"])
                p.emit(sc, "activation", s_all[:, 0:T], ps, AF.Abs,
                       bias=dt_bias[:], rd=[("pbk", pe), "dtb"], wr=["s_all"])
                p.emit(sc, "activation", sig_all[:, 0:T], s_all[:, 0:T],
                       AF.Exp, scale=-1.0, rd=["s_all"], wr=["sig_all"])
                p.emit(sc, "activation", sig_all[:, 0:T], sig_all[:, 0:T],
                       AF.Ln, bias=one_col[0:NH, :], rd=["sig_all", "one_col"],
                       wr=["sig_all"])
                p.emit(v_, "tensor_tensor", dt_sp[:, 0:T], dt_sp[:, 0:T],
                       sig_all[:, 0:T], op=OP.add, rd=["dt_sp", "sig_all"],
                       wr=["dt_sp"])

        for c in range(NCVT):
            base = c * (T + 3)
            rc = 128 if c < 16 else 64
            if m == 0:
                p.emit(v_, "memset", xbc[0:rc, base:base + 3], 0.0,
                       rd=[("xbc", c)], wr=[("xbc", c)])
            else:
                p.emit(v_, "tensor_copy", xbc[0:rc, base:base + 3],
                       halo[0:rc, c * 4:c * 4 + 3], rd=[("halo", c), ("xbc", c)],
                       wr=[("xbc", c)])
        for c in range(NCVT):
            base = c * (T + 3)
            rc = 128 if c < 16 else 64
            pe = c % 2
            for j in range(4):
                p.emit(v_, "tensor_scalar",
                       diag[pe][0:rc, j * 128:j * 128 + rc],
                       identb[0:rc, 0:rc], cw4[0:rc, c * 4 + j:c * 4 + j + 1],
                       None, op0=OP.mult, rd=["identb", "cw4", ("diag", pe)],
                       wr=[("diag", pe)] if j == 3 else ())
            ps = pbank(pe, rc, T)
            for j in range(4):
                p.emit(t_, "matmul", ps, diag[pe][0:rc, j * 128:j * 128 + rc],
                       xbc[0:rc, base + j:base + j + T],
                       start=(j == 0), stop=(j == 3),
                       rd=[("diag", pe), ("xbc", c)],
                       wr=([("pbk", pe)] if j in (0, 3) else []) +
                          ([("diag", pe)] if j == 3 else []),
                       inc=(j == 3))
            p.emit(v_, "tensor_copy", halo[0:rc, c * 4:c * 4 + 3],
                   xbc[0:rc, base + T:base + T + 3], rd=[("xbc", c)],
                   wr=[("halo", c)])
            p.emit(sc, "activation", xbc[0:rc, base + 3:base + 3 + T], ps,
                   AF.Silu, bias=cb17[0:rc, c:c + 1],
                   rd=[("pbk", pe), "cb17", ("halo", c)], wr=[("xbc", c)])

        p.emit(v_, "tensor_scalar", s_all[:, 0:T], dt_sp[:, 0:T], a_col[:],
               None, op0=OP.mult, rd=["dt_sp", "a_col"], wr=["s_all"])
        p.emit(sc, "activation", sig_all[:, 0:T], dt_sp[:, 0:T], AF.Ln,
               rd=["dt_sp"], wr=["sig_all"])
        for ch in range(NCH):
            sl = slice(ch * C, (ch + 1) * C)
            p.emit(v_, "tensor_tensor_scan", s_all[:, sl], s_all[:, sl],
                   s_all[:, sl], zero16[:], op0=OP.add, op1=OP.bypass,
                   rd=["s_all", "zero16"], wr=["s_all"])
        p.emit(v_, "tensor_tensor", sig_all[:, 0:T], s_all[:, 0:T],
               sig_all[:, 0:T], op=OP.subtract, rd=["s_all", "sig_all"],
               wr=["sig_all"])

        for ch in range(NCH):
            t_sl = slice(ch * C, (ch + 1) * C)
            p.emit(t_, "transpose", pbank(4, 128, NH), sig_all[:, t_sl],
                   identf[0:NH, 0:NH], rd=["sig_all", "identf", ("pbk", 4)],
                   wr=[("pbk", 4)])
            p.emit(v_, "tensor_copy", sigT[:, 0:NH], pbank(4, 128, NH),
                   rd=[("pbk", 4)], wr=["sigT"])
            p.emit(v_, "tensor_scalar", w_st[:], sig_all[:, t_sl], -1.0,
                   s_all[:, ch * C + C - 1:ch * C + C], op0=OP.mult,
                   op1=OP.add, rd=["sig_all", "s_all"], wr=["w_st"])
            p.emit(sc, "activation", w_st[:], w_st[:], AF.Exp, rd=["w_st"],
                   wr=["w_st"])
            p.emit(t_, "transpose", pbank(4, 128, NH), w_st[:],
                   identf[0:NH, 0:NH], rd=["w_st", "identf", ("pbk", 4)],
                   wr=[("pbk", 4)])
            p.emit(v_, "tensor_copy", wT[:, 0:NH], pbank(4, 128, NH),
                   rd=[("pbk", 4)], wr=["wT"])
            p.emit(sc, "activation", esc_c[:], s_all[:, ch * C + C - 1:ch * C + C],
                   AF.Exp, rd=["s_all"], wr=["esc_c"])
            p.emit(t_, "transpose", pbank(4, 1, NH), esc_c[:],
                   identf[0:NH, 0:NH],
                   rd=["esc_c", "identf", ("pbk", 4)], wr=[("pbk", 4)])
            p.emit(v_, "tensor_copy", rsc[:, 0:NH], pbank(4, 1, NH),
                   rd=[("pbk", 4)], wr=["rsc"])
            p.emit(t_, "matmul", pbank(5, NS, NH), ones1[:, 0:NS],
                   rsc[:, 0:NH], start=True, stop=True,
                   rd=["ones1", "rsc", ("pbk", 5)], wr=[("pbk", 5)])
            p.emit(v_, "tensor_copy", escb[:, 0:NH], pbank(5, NS, NH),
                   rd=[("pbk", 5)], wr=["escb"])
            bbB = 16 * (T + 3) + 3 + ch * C
            bbC = 17 * (T + 3) + 3 + ch * C
            B_sl = xbc[0:NS, bbB:bbB + C]
            C_sl = xbc[0:NS, bbC:bbC + C]
            p.emit(t_, "matmul", pbank(4, 128, C), B_sl, C_sl, start=True,
                   stop=True, rd=[("xbc", 16), ("xbc", 17), ("pbk", 4)], wr=[("pbk", 4)])
            p.emit(v_, "tensor_copy", g_s[:], pbank(4, 128, C),
                   rd=[("pbk", 4)], wr=["g_s"])
            p.emit(t_, "transpose", pbank_b(4, 128, NS), B_sl,
                   identb[0:NS, 0:NS], rd=[("xbc", 16), "identb", ("pbk", 4)],
                   wr=[("pbk", 4)])
            p.emit(v_, "tensor_copy", b_tm[:, 0:NS], pbank_b(4, 128, NS),
                   rd=[("pbk", 4)], wr=["b_tm"])

            for h in range(NH):
                pe = h % 2
                yb = 6 + pe
                tb = 4 + pe
                rb = 0 + pe
                cb = 2 + pe
                xoff = h * (T + 3) + 3 + ch * C
                p.emit(t_, "transpose", pbank_b(tb, 128, C),
                       xbc[:, xoff:xoff + C], identb[:],
                       rd=[("xbc", h), "identb", ("pbk", tb)], wr=[("pbk", tb)])
                p.emit(sc, "activation", x_tm[pe][:], pbank_b(tb, 128, C),
                       AF.Identity, rd=[("pbk", tb)], wr=[("x_tm", pe)])
                p.emit(t_, "matmul", pbank(rb, 128, C),
                       e16[:, h * 128:(h + 1) * 128], s_all[:, t_sl],
                       start=True, stop=True,
                       rd=["e16", "s_all", ("pbk", rb)], wr=[("pbk", rb)])
                p.emit(v_, "scalar_tensor_tensor", mask_s[pe][:],
                       pbank(rb, 128, C), sigT[:, h:h + 1], trineg[:],
                       op0=OP.subtract, op1=OP.add,
                       rd=[("pbk", rb), "sigT", "trineg"],
                       wr=[("mask", pe)])
                p.emit(sc, "activation", mask_s[pe][:], mask_s[pe][:], AF.Exp,
                       rd=[("mask", pe)], wr=[("mask", pe)])
                p.emit(v_, "tensor_tensor", gmb[pe][:], g_s[:], mask_s[pe][:],
                       op=OP.mult, rd=["g_s", ("mask", pe)], wr=[("gmb", pe)])
                p.emit(sc, "activation", es_b[pe][:], pbank(rb, NS, C),
                       AF.Exp, rd=[("pbk", rb)], wr=[("esb", pe)])
                p.emit(v_, "tensor_tensor", c_til[pe][:, 0:C], es_b[pe][:],
                       C_sl, op=OP.mult, rd=[("esb", pe), ("xbc", 17)],
                       wr=[("ctil", pe)])
                p.emit(v_, "tensor_copy", ht_b[pe][:],
                       ht[:, h * HD:(h + 1) * HD], rd=[("ht", h)],
                       wr=[("htb", pe)])
                p.emit(t_, "matmul", pbank(yb, 128, C), x_tm[pe][:], gmb[pe][:],
                       start=True, stop=False,
                       rd=[("x_tm", pe), ("gmb", pe)], wr=[("pbk", yb)],
                       inc=False)
                p.emit(t_, "matmul", pbank(yb, 128, C), ht_b[pe][:],
                       c_til[pe][:, 0:C], start=False, stop=True,
                       rd=[("htb", pe), ("ctil", pe)], wr=[("pbk", yb)])
                p.emit(v_, "tensor_scalar", x_til[pe][:], x_tm[pe][:],
                       wT[:, h:h + 1], None, op0=OP.mult,
                       rd=[("x_tm", pe), "wT"], wr=[("xtil", pe)])
                p.emit(t_, "matmul", pbank(rb, NS, HD), b_tm[:, 0:NS],
                       x_til[pe][:], start=True, stop=True,
                       rd=["b_tm", ("xtil", pe), ("pbk", rb)], wr=[("pbk", rb)])
                p.emit(v_, "scalar_tensor_tensor", ht[:, h * HD:(h + 1) * HD],
                       ht[:, h * HD:(h + 1) * HD], escb[:, h:h + 1],
                       pbank(rb, NS, HD), op0=OP.mult, op1=OP.add,
                       rd=[("ht", h), "escb", ("pbk", rb), ("htb", pe)],
                       wr=[("ht", h)])
                p.emit(v_, "scalar_tensor_tensor", dx[pe][:],
                       xbc[:, xoff:xoff + C], d_b[:, h:h + 1],
                       pbank(yb, 128, C), op0=OP.mult, op1=OP.add,
                       rd=[("xbc", h), "d_b", ("pbk", yb)],
                       wr=[("dx", pe), ("pbk", yb)])
                p.emit(v_, "tensor_tensor", y_g[:, h * C:(h + 1) * C],
                       dx[pe][:], z_silu[:, h * T + ch * C:h * T + ch * C + C],
                       op=OP.mult, rd=[("dx", pe), ("zs", h)], wr=[("y_g", h)])
            for h in range(NH):
                sq_ = (sqA, sqA2)[h % 2]
                p.emit(sc, "activation", sq_[:, 0:C], y_g[:, h * C:(h + 1) * C],
                       AF.Square, rd=[("y_g", h)], wr=[("sq_scr", h % 2)])
                p.emit(t_, "matmul", pbank(5, 1, C), onescol_b[:],
                       sq_[:, 0:C], start=(h == 0), stop=(h == NH - 1),
                       rd=["onescolb", ("sq_scr", h % 2)],
                       wr=[("pbk", 5)] if h in (0, NH - 1) else (), inc=True)
            p.emit(v_, "tensor_scalar", crowA[0:1, 0:C], pbank(5, 1, C),
                   1.0 / EI, None, op0=OP.mult, rd=[("pbk", 5)], wr=["crow"])
            p.emit(sc, "activation", crowA[0:1, 0:C], crowA[0:1, 0:C], AF.Ln,
                   bias=eps_col[0:1, :], rd=["crow", "eps_col"], wr=["crow"])
            rrb_b = crowA[:].bitcast(BF16)[0:1, 0:C]
            p.emit(sc, "activation", rrb_b, crowA[0:1, 0:C], AF.Exp,
                   scale=-0.5, rd=["crow"], wr=["crow"])
            p.emit(t_, "matmul", pbank(5, 128, C), ones1b[:], rrb_b,
                   start=True, stop=True, rd=["ones1b", "crow", ("pbk", 5)],
                   wr=[("pbk", 5)])
            for h in range(NH):
                p.emit(v_, "scalar_tensor_tensor",
                       y_n[:, h * T + ch * C:h * T + ch * C + C],
                       y_g[:, h * C:(h + 1) * C], nw[:, h:h + 1],
                       pbank(5, 128, C), op0=OP.mult, op1=OP.mult,
                       rd=[("y_g", h), "nw", ("pbk", 5)], wr=[("y_n", h)])

        for ke in range(NE):
            pe = ke % 2
            dma(s_, wo_buf[pe][:], w_outbh[ke * 128:(ke + 1) * 128, :],
                after=[p.last_w.get(("wobh", ke))],
                wr=[("wo", pe)], dsem="dwo%d" % pe)
            for dd in range(ND):
                p.emit(t_, "matmul", pbank(dd, 128, T),
                       wo_buf[pe][:, dd * 128:(dd + 1) * 128],
                       y_n[:, ke * T:(ke + 1) * T],
                       start=(ke == 0), stop=(ke == NE - 1),
                       rd=[("wo", pe), ("y_n", ke)],
                       wr=([("pbk", dd)] if ke in (0, NE - 1) else []) +
                          ([("wo", pe)] if dd == ND - 1 else []),
                       inc=(ke == NE - 1 or dd == ND - 1))
        for dd in range(ND):
            p.emit(v_, "tensor_copy", m_o[:, dd * T:(dd + 1) * T],
                   pbank(dd, 128, T), rd=[("pbk", dd)], wr=[("m_o", dd)])

        def mosrc(i):
            return m_o[:, i * T:(i + 1) * T]

        ns["srckeys"] = [("m_o", i) for i in range(ND)]
        bca, bcb = ln_full("ln2", mosrc, ND, D, T)
        for i in range(ND):
            ap_ = apA[i % 2]
            p.emit(v_, "tensor_tensor", ap_[:], mosrc(i), bca, op=OP.mult,
                   rd=[("m_o", i), ("pbk", 2)], wr=[("apA", i % 2)])
            p.emit(v_, "tensor_tensor", ap_[:], ap_[:], bcb, op=OP.add,
                   rd=[("apA", i % 2), ("pbk", 3)], wr=[("apA", i % 2)])
            p.emit(v_, "tensor_tensor", xv[:, i * T:(i + 1) * T], xsrc(i),
                   ap_[:], op=OP.add, rd=[("apA", i % 2), ("xb", xp)],
                   wr=[("xb", xp)])
        tgt = out_fm if cfg.debug == "x1" else x1h
        dma(s_, fm3(tgt[b, :, t0:t0 + T]), sb3(xv[:], T),
            rd=[("xb", xp)], wr=[("x1h", mt)])
        p.last_w[("xb", xp)] = p.last_w[("x1h", mt)]

    AR_A.pop()

    if cfg.debug == "x1":
        p.wait_all("gpsimd")
        block = es.enter_context(nc.Block())
        p.replay(block)
        es.close()
        return nc

    p.barrier()
    AR_A.push()
    w1b = AR_A.bf16(128, ND * FF)
    w2buf = [AR_A.bf16(128, 1024) for _ in range(2)]
    x1b = [AR_A.bf16(128, ND * 512) for _ in range(2)]
    xln3 = AR_A.bf16(128, ND * 512)
    actW = AR_A.f32(128, 8192)
    act = actW[:].bitcast(BF16)[:, 0:NFF * 512]
    h1b = [AR_A.bf16(128, 512) for _ in range(2)]
    x2b = [AR_A.bf16(128, 512) for _ in range(2)]
    ab_ = [AR_A.bf16(128, 512) for _ in range(2)]
    sgb = [AR_A.bf16(128, 512) for _ in range(2)]
    y0st = AR_A.bf16(128, ND * 512)
    lg_sb = AR_A.f32(8, 512)
    lt_sb = AR_A.f32(128, 8)
    nmax = AR_A.f32(128, 1)
    qex = AR_A.f32(128, 8)
    ssum = AR_A.f32(128, 1)
    p0c = AR_A.f32(128, 1)
    p0row = AR_A.f32(1, 512)
    sqB = AR_A.bf16(128, 512)
    sqB2 = AR_A.bf16(128, 512)
    apB = [AR_A.f32(128, 512) for _ in range(2)]
    arowB = AR_A.f32(1, 512)
    browB = AR_A.f32(1, 512)
    crowB = AR_A.f32(1, 512)
    cntc = AR_A.f32(128, 1)
    cnt_loc = AR_A.f32(1, 8)
    cacc = AR_A.f32(1, 128)
    cacc2 = AR_A.f32(128, 64)
    wblk = AR_A.f32(128, 8192)
    wcnt1 = [wblk[:, 0:2048], wblk[:, 2048:4096]]
    wcnt2 = [wblk[:, 4096:6144], wblk[:, 6144:8192]]
    wstgA = wblk[:, 4096:5120]
    absA = wblk[:, 5120:6144]
    wstgB = wblk[:].bitcast(BF16)[:, 2 * 6144:2 * 6144 + 1024]
    y1s = wblk[:, 0:4096]
    ns.update(arow=arowB, brow=browB, crow=crowB, sq=(sqB, sqB2))

    for kd in range(ND):
        dma(g, w1b[:, kd * FF:(kd + 1) * FF], w1[kd * 128:(kd + 1) * 128, :],
            wr=[("w1b", kd)])
    p.patch_total([("w1b", kd) for kd in range(ND)], "DG")

    for i in range(ND):
        p.emit(v_, "tensor_scalar", wrg[:, i * 8:(i + 1) * 8],
               wr_sb[:, i * 8:(i + 1) * 8], gam[:, i:i + 1], None, op0=OP.mult,
               rd=["wr", "gam"], wr=[("wrg", i)])
        p.emit(t_, "matmul", pbank(4, 8, 1), wr_sb[:, i * 8:(i + 1) * 8],
               gam[:, i:i + 1], start=(i == 0), stop=(i == ND - 1),
               rd=["wr", "gam"],
               wr=[("pbk", 4)] if i in (0, ND - 1) else (), inc=(i == ND - 1))
    p.emit(v_, "tensor_copy", cgam[:], pbank(4, 8, 1), rd=[("pbk", 4)],
           wr=["cgam"])
    for i in range(ND):
        p.emit(t_, "matmul", pbank(4, 8, 1), wr_sb[:, i * 8:(i + 1) * 8],
               bet[:, i:i + 1], start=(i == 0), stop=(i == ND - 1),
               rd=["wr", "bet"],
               wr=[("pbk", 4)] if i in (0, ND - 1) else (), inc=(i == ND - 1))
    p.emit(v_, "tensor_tensor", cbet[:], pbank(4, 8, 1), br_sb[:], op=OP.add,
           rd=[("pbk", 4), "br"], wr=["cbet"])

    def gelu_ffn(tag, xln3_keys, w1_ap, w1_keys, bias_ap, bias_key, w2_src,
                 w2_keys, bias2_ap, bias2_key, ydst, ydst_f32):
        for f in range(NFF):
            bk = f % 2
            ps = pbank(bk, 128, 512)
            for kd in range(ND):
                p.emit(t_, "matmul", ps,
                       w1_ap[:, kd * FF + f * 128:kd * FF + (f + 1) * 128],
                       xln3[:, kd * 512:(kd + 1) * 512],
                       start=(kd == 0), stop=(kd == ND - 1),
                       rd=[w1_keys[kd], ("xln3", kd)],
                       wr=[("pbk", bk)] if kd in (0, ND - 1) else (),
                       inc=(kd == ND - 1))
            p.emit(sc, "activation", h1b[bk][:], ps, AF.Identity,
                   bias=bias_ap[:, f:f + 1], rd=[("pbk", bk), bias_key],
                   wr=[("h1b", bk), ("pbk", bk)])
            p.emit(v_, "tensor_tensor", x2b[bk][:], h1b[bk][:], h1b[bk][:],
                   op=OP.mult, rd=[("h1b", bk)], wr=[("x2b", bk)])
            p.emit(v_, "tensor_scalar", ab_[bk][:], x2b[bk][:], GELU_C, 1.0,
                   op0=OP.mult, op1=OP.add, rd=[("x2b", bk)], wr=[("ab_", bk)])
            p.emit(v_, "tensor_tensor", ab_[bk][:], ab_[bk][:], h1b[bk][:],
                   op=OP.mult, rd=[("ab_", bk), ("h1b", bk)], wr=[("ab_", bk)])
            p.emit(sc, "activation", sgb[bk][:], ab_[bk][:], AF.Sigmoid,
                   scale=GELU_S, rd=[("ab_", bk)], wr=[("sgb", bk)])
            p.emit(v_, "tensor_tensor", act[:, f * 512:(f + 1) * 512],
                   h1b[bk][:], sgb[bk][:], op=OP.mult,
                   rd=[("h1b", bk), ("sgb", bk)], wr=[("act", f)])
        for kf in range(NFF):
            pe = kf % 2
            dma(s_, w2buf[pe][:], w2_src[kf * 128:(kf + 1) * 128, :],
                after=[p.last_w.get(w2_keys(kf))],
                wr=[("w2buf", pe)], dsem="dw2%d" % pe)
            for hh in range(ND):
                p.emit(t_, "matmul", pbank(hh, 128, 512),
                       w2buf[pe][:, hh * 128:(hh + 1) * 128],
                       act[:, kf * 512:(kf + 1) * 512],
                       start=(kf == 0), stop=(kf == NFF - 1),
                       rd=[("w2buf", pe), ("act", kf)],
                       wr=([("pbk", hh)] if kf in (0, NFF - 1) else []) +
                          ([("w2buf", pe)] if hh == ND - 1 else []),
                       inc=(kf == NFF - 1 or hh == ND - 1))
        for hh in range(ND):
            p.emit(sc, "activation", ydst[:, hh * 512:(hh + 1) * 512],
                   pbank(hh, 128, 512), AF.Identity,
                   bias=bias2_ap[:, hh:hh + 1],
                   rd=[("pbk", hh), bias2_key], wr=[(tag + "_y", hh)])

    def gelu_h1(w1_ap, w1_keys, bias_ap, bias_key):
        for f in range(NFF):
            bk = f % 2
            ps = pbank(bk, 128, 512)
            for kd in range(ND):
                p.emit(t_, "matmul", ps,
                       w1_ap[:, kd * FF + f * 128:kd * FF + (f + 1) * 128],
                       xln3[:, kd * 512:(kd + 1) * 512],
                       start=(kd == 0), stop=(kd == ND - 1),
                       rd=[w1_keys[kd], ("xln3", kd)],
                       wr=[("pbk", bk)] if kd in (0, ND - 1) else (),
                       inc=(kd == ND - 1))
            p.emit(sc, "activation", h1b[bk][:], ps, AF.Identity,
                   bias=bias_ap[:, f:f + 1], rd=[("pbk", bk), bias_key],
                   wr=[("h1b", bk), ("pbk", bk)])
            p.emit(v_, "tensor_tensor", x2b[bk][:], h1b[bk][:], h1b[bk][:],
                   op=OP.mult, rd=[("h1b", bk)], wr=[("x2b", bk)])
            p.emit(v_, "tensor_scalar", ab_[bk][:], x2b[bk][:], GELU_C, 1.0,
                   op0=OP.mult, op1=OP.add, rd=[("x2b", bk)], wr=[("ab_", bk)])
            p.emit(v_, "tensor_tensor", ab_[bk][:], ab_[bk][:], h1b[bk][:],
                   op=OP.mult, rd=[("ab_", bk), ("h1b", bk)], wr=[("ab_", bk)])
            p.emit(sc, "activation", sgb[bk][:], ab_[bk][:], AF.Sigmoid,
                   scale=GELU_S, rd=[("ab_", bk)], wr=[("sgb", bk)])
            p.emit(v_, "tensor_tensor", act[:, f * 512:(f + 1) * 512],
                   h1b[bk][:], sgb[bk][:], op=OP.mult,
                   rd=[("h1b", bk), ("sgb", bk)], wr=[("act", f)])

    def gelu_y(tag, w2_src, w2_keys, bias2_ap, bias2_key, ydst):
        for kf in range(NFF):
            pe = kf % 2
            dma(s_, w2buf[pe][:], w2_src[kf * 128:(kf + 1) * 128, :],
                after=[p.last_w.get(w2_keys(kf))],
                wr=[("w2buf", pe)], dsem="dw2%d" % pe)
            for hh in range(ND):
                p.emit(t_, "matmul", pbank(hh, 128, 512),
                       w2buf[pe][:, hh * 128:(hh + 1) * 128],
                       act[:, kf * 512:(kf + 1) * 512],
                       start=(kf == 0), stop=(kf == NFF - 1),
                       rd=[("w2buf", pe), ("act", kf)],
                       wr=([("pbk", hh)] if kf in (0, NFF - 1) else []) +
                          ([("w2buf", pe)] if hh == ND - 1 else []),
                       inc=(kf == NFF - 1 or hh == ND - 1))
        for hh in range(ND):
            p.emit(sc, "activation", ydst[:, hh * 512:(hh + 1) * 512],
                   pbank(hh, 128, 512), AF.Identity,
                   bias=bias2_ap[:, hh:hh + 1],
                   rd=[("pbk", hh), bias2_key], wr=[(tag + "_y", hh)])

    for tt in range(NTT):
        b = (tt * 512) // L
        t0 = (tt * 512) % L
        dma(g, sb3(x1b[0][:], 512), fm3(x1h[b, :, t0:t0 + 512]),
            rd=["x1b"], wr=["x1b"], dsem="dx1b")

        def x1src(i):
            return x1b[0][:, i * 512:(i + 1) * 512]

        ns["srckeys"] = ["x1b"] * ND
        bca, bcb = ln_full("ln3", x1src, ND, D, 512, src_bf16=True)
        for i in range(ND):
            ap_ = apB[i % 2]
            p.emit(v_, "tensor_tensor", ap_[:], x1src(i), bca, op=OP.mult,
                   rd=["x1b", ("pbk", 2)], wr=[("apB", i % 2)])
            p.emit(v_, "tensor_tensor", xln3[:, i * 512:(i + 1) * 512],
                   ap_[:], bcb, op=OP.add, rd=[("apB", i % 2), ("pbk", 3)],
                   wr=[("xln3", i)])
        dma(s_, fm3(xln3h[b, :, t0:t0 + 512]), sb3(xln3[:], 512),
            rd=[("xln3", i) for i in range(ND)], wr=[("xln3h", tt)])

        def x3src(i):
            return xln3[:, i * 512:(i + 1) * 512]

        ns["srckeys"] = [("xln3", i) for i in range(ND)]
        bca_r, bcb_r = ln_full("lnr", x3src, ND, D, 512, src_bf16=True)
        lps = pbank(4, 8, 512)
        for i in range(ND):
            p.emit(t_, "matmul", lps, wrg[:, i * 8:(i + 1) * 8], x3src(i),
                   start=(i == 0), stop=(i == ND - 1),
                   rd=[("wrg", i), ("xln3", i)],
                   wr=[("pbk", 4)] if i in (0, ND - 1) else (), inc=(i == ND - 1))
        p.emit(v_, "tensor_copy", lg_sb[:], lps, rd=[("pbk", 4)], wr=["lg"])
        p.emit(v_, "tensor_tensor", lg_sb[:], lg_sb[:], pbank(2, 8, 512),
               op=OP.mult, rd=["lg", ("pbk", 2), ("pbk", 4)], wr=["lg"])
        p.emit(v_, "scalar_tensor_tensor", lg_sb[:], pbank(3, 8, 512),
               cgam[:], lg_sb[:], op0=OP.mult, op1=OP.add,
               rd=[("pbk", 3), "cgam", "lg", ("pbk", 4)], wr=["lg"])
        p.emit(v_, "tensor_scalar", lg_sb[:], lg_sb[:], cbet[:], None,
               op0=OP.add, rd=["lg", "cbet"], wr=["lg"])
        for st in range(4):
            p.emit(t_, "transpose", pbank(5, 128, 8),
                   lg_sb[:, st * 128:(st + 1) * 128], identf[0:8, 0:8],
                   rd=["lg", "identf", ("pbk", 5)], wr=[("pbk", 5)])
            p.emit(v_, "tensor_copy", lt_sb[:], pbank(5, 128, 8),
                   rd=[("pbk", 5)], wr=["lt"])
            p.emit(v_, "tensor_reduce", nmax[:], lt_sb[:],
                   axis=mybir.AxisListType.X, op=OP.max, rd=["lt"],
                   wr=["nmax"])
            p.emit(v_, "tensor_scalar", nmax[:], nmax[:], -1.0, None,
                   op0=OP.mult, rd=["nmax"], wr=["nmax"])
            p.emit(sc, "activation", qex[:], lt_sb[:], AF.Exp, bias=nmax[:],
                   rd=["lt", "nmax"], wr=["qex"])
            p.emit(v_, "tensor_reduce", ssum[:], qex[:],
                   axis=mybir.AxisListType.X, op=OP.add, rd=["qex"],
                   wr=["ssum"])
            p.emit(v_, "reciprocal", ssum[:], ssum[:], rd=["ssum"],
                   wr=["ssum"])
            p.emit(v_, "tensor_tensor", p0c[:], qex[:, 0:1], ssum[:],
                   op=OP.mult, rd=["qex", "ssum"], wr=["p0c"])
            p.emit(t_, "transpose", pbank(5, 1, 128), p0c[:], identf[:],
                   rd=["p0c", "identf", ("pbk", 5)], wr=[("pbk", 5)])
            p.emit(v_, "tensor_copy", p0row[:, st * 128:(st + 1) * 128],
                   pbank(5, 1, 128), rd=[("pbk", 5)], wr=["p0row"])
        dma(s_, p0h[:, tt * 512:(tt + 1) * 512], p0row[:], rd=["p0row"],
            wr=[("p0h", tt)])

        gelu_ffn("y0", None, w1b, [("w1b", kd) for kd in range(ND)], b1c,
                 "b1c", w2bh, lambda kf: ("w2bh", kf), b2c, "b2c", y0st, None)
        dma(s_, fm3(y0h[b, :, t0:t0 + 512]), sb3(y0st[:], 512),
            rd=[("y0_y", hh) for hh in range(ND)], wr=[("y0h", tt)])
        for hh in range(ND):
            p.last_w[("y0_y", hh)] = p.last_w[("y0h", tt)]

    p.barrier()
    w1v = w1.rearrange("(a p) f -> p a f", p=128)
    w2v = w2.rearrange("(a p) d -> p a d", p=128)
    w2bv = w2bh.rearrange("(a p) d -> p a d", p=128)

    def bis_update():
        pr, inv, tmp = bst[:, 0:8], bst[:, 8:16], bst[:, 16:24]
        p.emit(v_, "tensor_tensor", pr, cnt_loc[:], kvec[:], op=OP.is_ge,
               rd=["cnt_loc", "kvec"], wr=["pr"])
        p.emit(v_, "tensor_scalar", inv, pr, -1.0, 1.0, op0=OP.mult,
               op1=OP.add, rd=["pr"], wr=["inv"])
        p.emit(v_, "tensor_tensor", tmp, pr, bmid[:], op=OP.mult,
               rd=["pr", "bmid"], wr=["tmp"])
        p.emit(v_, "tensor_tensor", bhi[:], bhi[:], inv, op=OP.mult,
               rd=["bhi", "inv"], wr=["bhi"])
        p.emit(v_, "tensor_tensor", bhi[:], bhi[:], tmp, op=OP.add,
               rd=["bhi", "tmp"], wr=["bhi"])
        p.emit(v_, "tensor_tensor", tmp, inv, bmid[:], op=OP.mult,
               rd=["inv", "bmid"], wr=["tmp"])
        p.emit(v_, "tensor_tensor", blo[:], blo[:], pr, op=OP.mult,
               rd=["blo", "pr"], wr=["blo"])
        p.emit(v_, "tensor_tensor", blo[:], blo[:], tmp, op=OP.add,
               rd=["blo", "tmp"], wr=["blo"])

    def bis_mid():
        p.emit(v_, "tensor_tensor", bmid[:], blo[:], bhi[:], op=OP.add,
               rd=["blo", "bhi"], wr=["bmid"])
        p.emit(v_, "tensor_scalar", bmid[:], bmid[:], 0.5, None, op0=OP.mult,
               rd=["bmid"], wr=["bmid"])
        p.emit(t_, "matmul", pbank(4, 128, 8), ones1[:], bmid[:], start=True,
               stop=True, rd=["ones1", "bmid", ("pbk", 4)], wr=[("pbk", 4)])
        p.emit(v_, "tensor_copy", bcols[:], pbank(4, 128, 8),
               rd=[("pbk", 4)], wr=["bcols"])

    def wdat_b(buf, cols=2048):
        return buf[:].bitcast(BF16)[:, 0:cols]

    def wdat_f(buf, cols=1024):
        return buf[:, 0:cols]

    def wmsk(buf, cols):
        return buf[:].bitcast(BF16)[:, 2048:2048 + cols]

    def cnt_mask_mm(j, buf, srcap, rdk, mcols, first, last, mkey, bank):
        p.emit(v_, "tensor_scalar", wmsk(buf, mcols), srcap,
               bcols[:, j:j + 1], 0.0, op0=OP.is_lt, op1=OP.add,
               rd=rdk + ["bcols"], wr=[mkey])
        nmm = mcols // 512
        for sb in range(nmm):
            p.emit(t_, "matmul", pbank(bank, 1, 512), onescol_b[:],
                   wmsk(buf, mcols)[:, sb * 512:(sb + 1) * 512],
                   start=(first and sb == 0), stop=(last and sb == nmm - 1),
                   rd=["onescolb", mkey], wr=[("pbk", bank)],
                   inc=(sb == nmm - 1))

    def bis_tally(j):
        p.emit(v_, "tensor_reduce", cnt_loc[:, j:j + 1],
               pbank(j, 1, 512), axis=mybir.AxisListType.X, op=OP.add,
               rd=[("pbk", j)], wr=["cnt_loc", ("pbk", j)])

    def bis_cnt_small(j, srcap, nn, rdk):
        p.emit(v_, "memset", cntc[:], 0.0, rd=["cntc"], wr=["cntc"])
        p.emit(v_, "tensor_scalar", wmsk(wcnt1[0], 1024)[:, 0:nn], srcap,
               bcols[:, j:j + 1], 0.0, op0=OP.is_lt, op1=OP.add,
               accum_out=cntc[:], rd=rdk + ["bcols", "cntc"],
               wr=["cntc", ("msk1", 0)])
        p.emit(t_, "matmul", pbank(5, 1, 1), cntc[:], onescol[:],
               start=True, stop=True, rd=["cntc", "onescol", ("pbk", 5)],
               wr=[("pbk", 5)])
        p.emit(v_, "tensor_copy", cnt_loc[:, j:j + 1], pbank(5, 1, 1),
               rd=[("pbk", 5)], wr=["cnt_loc"])

    def bis_update(js):
        pr, inv, tmp = bst[:, 0:8], bst[:, 8:16], bst[:, 16:24]
        sl = slice(js[0], js[-1] + 1)
        p.emit(v_, "tensor_tensor", pr[:, sl], cnt_loc[:, sl], kvec[:, sl],
               op=OP.is_ge, rd=["cnt_loc", "kvec"], wr=["pr"])
        p.emit(v_, "tensor_scalar", inv[:, sl], pr[:, sl], -1.0, 1.0,
               op0=OP.mult, op1=OP.add, rd=["pr"], wr=["inv"])
        p.emit(v_, "tensor_tensor", tmp[:, sl], pr[:, sl], bmid[:, sl],
               op=OP.mult, rd=["pr", "bmid"], wr=["tmp"])
        p.emit(v_, "tensor_tensor", bhi[:, sl], bhi[:, sl], inv[:, sl],
               op=OP.mult, rd=["bhi", "inv"], wr=["bhi"])
        p.emit(v_, "tensor_tensor", bhi[:, sl], bhi[:, sl], tmp[:, sl],
               op=OP.add, rd=["bhi", "tmp"], wr=["bhi"])
        p.emit(v_, "tensor_tensor", tmp[:, sl], inv[:, sl], bmid[:, sl],
               op=OP.mult, rd=["inv", "bmid"], wr=["tmp"])
        p.emit(v_, "tensor_tensor", blo[:, sl], blo[:, sl], pr[:, sl],
               op=OP.mult, rd=["blo", "pr"], wr=["blo"])
        p.emit(v_, "tensor_tensor", blo[:, sl], blo[:, sl], tmp[:, sl],
               op=OP.add, rd=["blo", "tmp"], wr=["blo"])

    def bis_mid(js):
        sl = slice(js[0], js[-1] + 1)
        p.emit(v_, "tensor_tensor", bmid[:, sl], blo[:, sl], bhi[:, sl],
               op=OP.add, rd=["blo", "bhi"], wr=["bmid"])
        p.emit(v_, "tensor_scalar", bmid[:, sl], bmid[:, sl], 0.5, None,
               op0=OP.mult, rd=["bmid"], wr=["bmid"])
        p.emit(t_, "matmul", pbank(4, 128, 8), ones1[:], bmid[:], start=True,
               stop=True, rd=["ones1", "bmid", ("pbk", 4)], wr=[("pbk", 4)])
        p.emit(v_, "tensor_copy", bcols[:], pbank(4, 128, 8),
               rd=[("pbk", 4)], wr=["bcols"])

    for c in range(ND):
        p.emit(sc, "activation", w1b[:, c * FF:(c + 1) * FF],
               w1b[:, c * FF:(c + 1) * FF], AF.Abs, rd=[("w1b", c)],
               wr=[("w1b", c)])

    mxc = cntc
    mx16 = cacc2[:, 0:16]
    for c in range(ND):
        p.emit(v_, "tensor_reduce", mx16[:, c:c + 1],
               w1b[:, c * FF:(c + 1) * FF], axis=mybir.AxisListType.X,
               op=OP.max, rd=[("w1b", c)], wr=["mx16"])
    p.emit(v_, "tensor_reduce", mxc[:], mx16[:, 0:ND],
           axis=mybir.AxisListType.X, op=OP.max, rd=["mx16"], wr=["mxc"])
    p.emit(t_, "transpose", pbank(4, 1, 128), mxc[:], identf[:],
           rd=["mxc", "identf", ("pbk", 4)], wr=[("pbk", 4)])
    p.emit(v_, "tensor_reduce", bhi[:, 0:1], pbank(4, 1, 128),
           axis=mybir.AxisListType.X, op=OP.max, rd=[("pbk", 4)], wr=["bhi"])
    for c in range(16):
        pe = c % 2
        dstb = wdat_b(wcnt1[pe])
        dma(g, dstb.rearrange("p (a d) -> p a d", d=1024),
            w2bv[:, 2 * c:2 * c + 2, :], wr=[("wcb", pe)],
            dsem="dcn%d" % pe)
        p.emit(sc, "activation", dstb, dstb, AF.Abs,
               rd=[("wcb", pe)], wr=[("wcb", pe)])
        p.emit(v_, "tensor_reduce", mx16[:, c:c + 1], dstb,
               axis=mybir.AxisListType.X, op=OP.max,
               rd=[("wcb", pe)], wr=["mx16"])
        p.last_w[("wcb", pe)] = p.last_w["mx16"]
    p.emit(v_, "tensor_reduce", mxc[:], mx16[:, 0:16],
           axis=mybir.AxisListType.X, op=OP.max, rd=["mx16"], wr=["mxc"])
    p.emit(t_, "transpose", pbank(4, 1, 128), mxc[:], identf[:],
           rd=["mxc", "identf", ("pbk", 4)], wr=[("pbk", 4)])
    p.emit(v_, "tensor_reduce", bhi[:, 2:3], pbank(4, 1, 128),
           axis=mybir.AxisListType.X, op=OP.max, rd=[("pbk", 4)], wr=["bhi"])
    for (src_, nn_, col, key) in ((ab1, NFF, 4, "ab1"), (ab2, ND, 6, "ab2")):
        p.emit(v_, "tensor_reduce", mxc[:], src_[:, 0:nn_],
               axis=mybir.AxisListType.X, op=OP.max, rd=[key], wr=["mxc"])
        p.emit(t_, "transpose", pbank(4, 1, 128), mxc[:], identf[:],
               rd=["mxc", "identf", ("pbk", 4)], wr=[("pbk", 4)])
        p.emit(v_, "tensor_reduce", bhi[:, col:col + 1], pbank(4, 1, 128),
               axis=mybir.AxisListType.X, op=OP.max, rd=[("pbk", 4)],
               wr=["bhi"])
    for col in (0, 2, 4, 6):
        p.emit(v_, "tensor_copy", bhi[:, col + 1:col + 2],
               bhi[:, col:col + 1], rd=["bhi"], wr=["bhi"])
    p.emit(v_, "tensor_scalar", bhi[:], bhi[:], 1.01, None, op0=OP.mult,
           rd=["bhi"], wr=["bhi"])

    for it in range(40):
        bis_mid((4, 5, 6, 7))
        for j, (src_, nn_, key) in ((4, (ab1, NFF, "ab1")),
                                    (5, (ab1, NFF, "ab1")),
                                    (6, (ab2, ND, "ab2")),
                                    (7, (ab2, ND, "ab2"))):
            bis_cnt_small(j, src_[:, 0:nn_], nn_, [key])
        bis_update((4, 5, 6, 7))

    dma(s_, dbgh[:, 32:40], blo[:], rd=["blo"], wr=["dbg"])
    bcnt1 = [wcnt1[0], wcnt1[1], actW[:, 0:2048], actW[:, 2048:4096]]
    bcnt2 = [wcnt2[0], wcnt2[1], actW[:, 4096:6144], actW[:, 6144:8192]]
    for stage in (1, 2):
        n_iter = 16 if stage == 1 else 18
        nch = 16 if stage == 1 else 32
        nbuf = 2 if stage == 1 else 4
        for it in range(n_iter):
            bis_mid((0, 1, 2, 3))
            for hc in range(nch):
                pe = hc % nbuf
                first, last = (hc == 0), (hc == nch - 1)
                if stage == 1:
                    srcap = w1b[:, hc * 2048:(hc + 1) * 2048]
                    rdk1 = [("w1b", hc // 2)]
                    mcols = 2048
                else:
                    dma(g, wdat_f(bcnt1[pe]),
                        w1[(hc // 4) * 128:(hc // 4) * 128 + 128,
                           (hc % 4) * 1024:(hc % 4) * 1024 + 1024],
                        wr=[("wcb", pe)], dsem="dcn%d" % pe)
                    p.emit(sc, "activation", wdat_f(bcnt1[pe]),
                           wdat_f(bcnt1[pe]), AF.Abs, rd=[("wcb", pe)],
                           wr=[("wcb", pe)])
                    srcap = wdat_f(bcnt1[pe])
                    rdk1 = [("wcb", pe)]
                    mcols = 1024
                for j in (0, 1):
                    cnt_mask_mm(j, bcnt1[pe], srcap, rdk1, mcols,
                                first, last, ("msk1", pe), bank=j)
                if stage == 2:
                    p.last_w[("wcb", pe)] = p.last_w[("msk1", pe)]
                if stage == 1:
                    dstb = wdat_b(bcnt2[pe])
                    dma(g, dstb.rearrange("p (a d) -> p a d", d=1024),
                        w2bv[:, 2 * hc:2 * hc + 2, :], wr=[("wc2", pe)],
                        dsem="dc2%d" % pe)
                    p.emit(sc, "activation", dstb, dstb, AF.Abs,
                           rd=[("wc2", pe)], wr=[("wc2", pe)])
                    srcap2 = dstb
                    mcols = 2048
                else:
                    dma(g, wdat_f(bcnt2[pe]).rearrange(
                            "p (a d) -> p a d", d=1024),
                        w2v[:, hc:hc + 1, :], wr=[("wc2", pe)],
                        dsem="dc2%d" % pe)
                    p.emit(sc, "activation", wdat_f(bcnt2[pe]),
                           wdat_f(bcnt2[pe]), AF.Abs, rd=[("wc2", pe)],
                           wr=[("wc2", pe)])
                    srcap2 = wdat_f(bcnt2[pe])
                    mcols = 1024
                for j in (2, 3):
                    cnt_mask_mm(j, bcnt2[pe], srcap2, [("wc2", pe)], mcols,
                                first, last, ("msk2", pe), bank=j)
                p.last_w[("wc2", pe)] = p.last_w[("msk2", pe)]
            for j in (0, 1, 2, 3):
                bis_tally(j)
            bis_update((0, 1, 2, 3))
        if stage == 1:
            p.emit(v_, "tensor_scalar", bhi[:, 0:4], blo[:, 0:4], 1.0078125,
                   None, op0=OP.mult, rd=["blo"], wr=["bhi"])
            p.emit(v_, "tensor_scalar", blo[:, 0:4], blo[:, 0:4], 0.9921875,
                   None, op0=OP.mult, rd=["blo"], wr=["blo"])
    p.emit(t_, "matmul", pbank(4, 128, 8), ones1[:], blo[:], start=True,
           stop=True, rd=["ones1", "blo", ("pbk", 4)], wr=[("pbk", 4)])
    p.emit(v_, "tensor_copy", bcols[:], pbank(4, 128, 8), rd=[("pbk", 4)],
           wr=["bcols"])
    p.barrier()

    def perturb(dst, dst_keys, src, src_key, absrc, abs_key, n, jb, jt,
                m1, m2):
        p.emit(v_, "tensor_scalar", m1[:, 0:n], absrc[:, 0:n],
               bcols[:, jb:jb + 1], 0.8, op0=OP.is_le, op1=OP.mult,
               rd=[abs_key, "bcols"], wr=["pm1"])
        p.emit(v_, "tensor_scalar", m2[:, 0:n], absrc[:, 0:n],
               bcols[:, jt:jt + 1], 0.8, op0=OP.is_ge, op1=OP.mult,
               rd=[abs_key, "bcols"], wr=["pm2"])
        p.emit(v_, "tensor_tensor", m1[:, 0:n], m1[:, 0:n], m2[:, 0:n],
               op=OP.subtract, rd=["pm1", "pm2"], wr=["pm1"])
        p.emit(v_, "tensor_scalar", m1[:, 0:n], m1[:, 0:n], 1.0, None,
               op0=OP.add, rd=["pm1"], wr=["pm1"])
        p.emit(v_, "tensor_tensor", dst[:, 0:n], src[:, 0:n], m1[:, 0:n],
               op=OP.mult, rd=["pm1", src_key], wr=dst_keys)

    perturb(b1p, ["b1p"], b1c, "b1c", ab1, "ab1", NFF, 4, 5,
            y1s[:, 0:64], y1s[:, 64:128])
    perturb(b2p, ["b2p"], b2c, "b2c", ab2, "ab2", ND, 6, 7,
            y1s[:, 0:64], y1s[:, 64:128])

    for kf in range(NFF):
        dma(s_, wstgA[:], w2[kf * 128:(kf + 1) * 128, :], rd=["wstgA"],
            wr=["wstgA"], dsem="dwst")
        p.emit(sc, "activation", absA[:], wstgA[:], AF.Abs,
               rd=["wstgA"], wr=["absA"])
        m1 = y1s[:, 0:1024]
        m2 = y1s[:, 1024:2048]
        p.emit(v_, "tensor_scalar", m1, absA[:], bcols[:, 2:3], 0.8,
               op0=OP.is_le, op1=OP.mult, rd=["absA", "bcols"], wr=["pm1"])
        p.emit(v_, "tensor_scalar", m2, absA[:], bcols[:, 3:4], 0.8,
               op0=OP.is_ge, op1=OP.mult, rd=["absA", "bcols"], wr=["pm2"])
        p.emit(v_, "tensor_tensor", m1, m1, m2, op=OP.subtract,
               rd=["pm1", "pm2"], wr=["pm1"])
        p.emit(v_, "tensor_scalar", m1, m1, 1.0, None, op0=OP.add,
               rd=["pm1"], wr=["pm1"])
        p.emit(v_, "tensor_tensor", wstgB[:], wstgA[:], m1, op=OP.mult,
               rd=["pm1", "wstgA"], wr=["wstgB"])
        dma(s_, pw2bh[kf * 128:(kf + 1) * 128, :], wstgB[:], rd=["wstgB"],
            wr=[("pw2bh", kf), "wstgB"], dsem="dwst")

    for kd in range(ND):
        for jj in range(4):
            dma(s_, wstgA[:],
                w1[kd * 128:(kd + 1) * 128, jj * 1024:(jj + 1) * 1024],
                rd=["wstgA"], wr=["wstgA"], dsem="dwst")
            p.emit(sc, "activation", absA[:], wstgA[:], AF.Abs,
                   rd=["wstgA"], wr=["absA"])
            m1 = y1s[:, 0:1024]
            m2 = y1s[:, 1024:2048]
            p.emit(v_, "tensor_scalar", m1, absA[:], bcols[:, 0:1], 0.8,
                   op0=OP.is_le, op1=OP.mult, rd=["absA", "bcols"], wr=["pm1"])
            p.emit(v_, "tensor_scalar", m2, absA[:], bcols[:, 1:2], 0.8,
                   op0=OP.is_ge, op1=OP.mult, rd=["absA", "bcols"], wr=["pm2"])
            p.emit(v_, "tensor_tensor", m1, m1, m2, op=OP.subtract,
                   rd=["pm1", "pm2"], wr=["pm1"])
            p.emit(v_, "tensor_scalar", m1, m1, 1.0, None, op0=OP.add,
                   rd=["pm1"], wr=["pm1"])
            p.emit(v_, "tensor_tensor",
                   w1b[:, kd * FF + jj * 1024:kd * FF + (jj + 1) * 1024],
                   wstgA[:], m1, op=OP.mult, rd=["pm1", "wstgA"],
                   wr=[("w1b", kd)])

    p.barrier()
    y0t = y0st

    def b2_loads1(tt):
        b = (tt * 512) // L
        t0 = (tt * 512) % L
        dma(g, sb3(x1b[tt % 2][:], 512), fm3(x1h[b, :, t0:t0 + 512]),
            rd=[("x1b", tt % 2)], wr=[("x1b", tt % 2)], dsem="dx1b")
        xt = dma(s_, sb3(xln3[:], 512), fm3(xln3h[b, :, t0:t0 + 512]),
                 rd=[("xln3", i) for i in range(ND)], wr=["xln3A"],
                 dsem="dxl3")
        for i in range(ND):
            p.last_w[("xln3", i)] = xt

    def b2_loads2(tt):
        b = (tt * 512) // L
        t0 = (tt * 512) % L
        dma(s_, sb3(y0t[:], 512), fm3(y0h[b, :, t0:t0 + 512]),
            rd=["y0t"], wr=["y0t"], dsem="dy0t")
        dma(s_, p0row[:], p0h[:, tt * 512:(tt + 1) * 512], rd=["p0row"],
            wr=["p0row"], dsem="dp0r")

    def b2_tail(tt):
        b = (tt * 512) // L
        t0 = (tt * 512) % L
        p.emit(t_, "matmul", pbank(4, 128, 512), ones1[:], p0row[:],
               start=True, stop=True, rd=["ones1", "p0row", ("pbk", 4)],
               wr=[("pbk", 4)])
        for hh in range(ND):
            ap_ = apB[hh % 2]
            p.emit(v_, "tensor_tensor", ap_[:], y0t[:, hh * 512:(hh + 1) * 512],
                   y1s[:, hh * 512:(hh + 1) * 512], op=OP.subtract,
                   rd=["y0t", ("y1_y", hh)], wr=[("apB", hh % 2)])
            p.emit(v_, "tensor_tensor", ap_[:], ap_[:], pbank(4, 128, 512),
                   op=OP.mult, rd=[("apB", hh % 2), ("pbk", 4)],
                   wr=[("apB", hh % 2)])
            p.emit(v_, "tensor_tensor", y1s[:, hh * 512:(hh + 1) * 512],
                   y1s[:, hh * 512:(hh + 1) * 512], ap_[:], op=OP.add,
                   rd=[("apB", hh % 2)], wr=[("y1_y", hh)])

        def ffsrc(i):
            return y1s[:, i * 512:(i + 1) * 512]

        ns["srckeys"] = [("y1_y", i) for i in range(ND)]
        bca, bcb = ln_full("ln4", ffsrc, ND, D, 512)
        for i in range(ND):
            ap_ = apB[i % 2]
            p.emit(v_, "tensor_tensor", ap_[:], ffsrc(i), bca, op=OP.mult,
                   rd=[("y1_y", i), ("pbk", 2)], wr=[("apB", i % 2)])
            p.emit(v_, "tensor_tensor", ap_[:], ap_[:], bcb, op=OP.add,
                   rd=[("apB", i % 2), ("pbk", 3)], wr=[("apB", i % 2)])
            p.emit(v_, "tensor_tensor", y1s[:, i * 512:(i + 1) * 512],
                   x1b[tt % 2][:, i * 512:(i + 1) * 512], ap_[:], op=OP.add,
                   rd=[("apB", i % 2), ("x1b", tt % 2)], wr=[("y1_y", i)])
        dma(s_, fm3(out_fm[b, :, t0:t0 + 512]),
            sb3(y1s[:, 0:ND * 512], 512),
            rd=[("y1_y", i) for i in range(ND)], wr=[("outh", tt)])
        for i in range(ND):
            p.last_w[("y1_y", i)] = p.last_w[("outh", tt)]

    b2_loads1(0)
    b2_loads2(0)
    gelu_h1(w1b, [("w1b", kd) for kd in range(ND)], b1p, "b1p")
    for tt in range(NTT):
        gelu_y("y1", pw2bh, lambda kf: ("pw2bh", kf), b2p, "b2p", y1s)
        if tt + 1 < NTT:
            b2_loads1(tt + 1)
            gelu_h1(w1b, [("w1b", kd) for kd in range(ND)], b1p, "b1p")
        b2_tail(tt)
        if tt + 1 < NTT:
            b2_loads2(tt + 1)

    p.wait_all("gpsimd")
    p.wait_all("vector")
    block = es.enter_context(nc.Block())
    p.replay(block)
    es.close()
    return nc



def _prep_shared(W_in, conv_w, conv_b, dt_bias, A_log, D_param, norm_w, W_out,
                 W1, b1, W2, b2, ln_g, ln_b, Wr, br, cfg):
    f32 = np.float32
    NIP = 35
    W_inT = np.ascontiguousarray(np.asarray(W_in).T).astype(f32)
    wpad = np.zeros((cfg.D, NIP * 128), f32)
    wpad[:, :4096] = W_inT[:, :4096]
    wpad[:, 32 * 128:32 * 128 + 64] = W_inT[:, 4096:4160]
    wpad[:, 33 * 128:33 * 128 + 64] = W_inT[:, 4160:4224]
    wpad[:, 34 * 128:34 * 128 + 16] = W_inT[:, 4224:4240]
    w_in_tiles = np.ascontiguousarray(
        wpad.reshape(cfg.D // 128, 128, NIP, 128).transpose(2, 0, 1, 3))
    cw = np.asarray(conv_w).reshape(2176, 4).astype(f32)
    cb = np.asarray(conv_b).reshape(2176).astype(f32)
    cw4 = np.zeros((128, 18 * 4), f32)
    cb18 = np.zeros((128, 18), f32)
    for c in range(16):
        cw4[:, c * 4:(c + 1) * 4] = cw[c * 128:(c + 1) * 128]
        cb18[:, c] = cb[c * 128:(c + 1) * 128]
    cw4[0:64, 16 * 4:17 * 4] = cw[2048:2112]
    cb18[0:64, 16] = cb[2048:2112]
    cw4[0:64, 17 * 4:18 * 4] = cw[2112:2176]
    cb18[0:64, 17] = cb[2112:2176]
    e16 = np.zeros((16, 16 * 128), f32)
    for h in range(16):
        e16[h, h * 128:(h + 1) * 128] = 1.0
    n1 = W1.size
    k1 = max(1, int(n1 * 0.1 / 2))
    n2 = W2.size
    k2 = max(1, int(n2 * 0.1 / 2))
    nb1 = b1.size
    kb1 = max(1, int(nb1 * 0.1 / 2))
    nb2 = b2.size
    kb2 = max(1, int(nb2 * 0.1 / 2))
    NC = cfg.n_cores
    return {
        "w_in_tiles": w_in_tiles.astype(f32),
        "w_outT": np.ascontiguousarray(np.asarray(W_out).T).astype(f32),
        "w1": np.ascontiguousarray(W1).astype(f32),
        "w2": np.ascontiguousarray(W2).astype(f32),
        "conv_w4": cw4,
        "conv_b17": cb18,
        "dtb": np.asarray(dt_bias).reshape(16, 1).astype(f32),
        "alog": np.asarray(A_log).reshape(16, 1).astype(f32),
        "dprm_row": np.asarray(D_param).reshape(1, 16).astype(f32),
        "normw": np.ascontiguousarray(
            np.asarray(norm_w).reshape(16, 128).T).astype(f32),
        "b1c": np.ascontiguousarray(np.asarray(b1).reshape(32, 128).T).astype(f32),
        "b2c": np.ascontiguousarray(np.asarray(b2).reshape(8, 128).T).astype(f32),
        "lgam": np.ascontiguousarray(np.asarray(ln_g).reshape(8, 128).T).astype(f32),
        "lbet": np.ascontiguousarray(np.asarray(ln_b).reshape(8, 128).T).astype(f32),
        "wr_in": np.ascontiguousarray(np.asarray(Wr).reshape(8, 128, 8)
            .transpose(1, 0, 2).reshape(128, 64)).astype(f32),
        "br_in": np.asarray(br).reshape(8, 1).astype(f32),
        "identf_in": np.eye(128, dtype=f32),
        "identb_in": np.eye(128).astype(BF),
        "trineg_in": np.where(
            np.arange(128)[:, None] <= np.arange(128)[None, :], 0.0, NEG
        ).astype(f32),
        "e16_in": e16,
        "kvec_in": np.array([[k1, n1 - k1 + 1, k2, n2 - k2 + 1,
                              kb1, nb1 - kb1 + 1,
                              kb2, nb2 - kb2 + 1]], f32),
        "bnds_in": np.array([[0.0] * 8 + [1024.0] * 8], f32),
    }


def kernel(L0_attn_hidden_states, L0_attn_W_in, L0_attn_conv_w, L0_attn_conv_b,
           L0_attn_dt_bias, L0_attn_A_log_alpha, L0_attn_D_param,
           L0_attn_norm_w, L0_attn_W_out, L0_ffn_W1, L0_ffn_b1, L0_ffn_W2,
           L0_ffn_b2, L0_ffn_ln_gamma, L0_ffn_ln_beta, L0_ffn_Wr, L0_ffn_br):
    cfg = CFG()
    x = np.asarray(L0_attn_hidden_states, np.float32)
    W1 = np.ascontiguousarray(np.asarray(L0_ffn_W1, np.float32))
    W2 = np.ascontiguousarray(np.asarray(L0_ffn_W2, np.float32))
    shared = _prep_shared(L0_attn_W_in, L0_attn_conv_w, L0_attn_conv_b,
                          L0_attn_dt_bias, L0_attn_A_log_alpha, L0_attn_D_param,
                          L0_attn_norm_w, L0_attn_W_out, W1, L0_ffn_b1, W2,
                          L0_ffn_b2, L0_ffn_ln_gamma, L0_ffn_ln_beta,
                          L0_ffn_Wr, L0_ffn_br, cfg)
    nc = build_program(cfg)
    in_maps = []
    for c in range(8):
        m = dict(shared)
        m["x_fm"] = np.ascontiguousarray(
            x[2 * c:2 * c + 2].transpose(0, 2, 1)).astype(np.float32)
        in_maps.append(m)
    res = run_bass_kernel_spmd(nc, in_maps, core_ids=list(range(8)))
    outs = []
    for c in range(8):
        o = np.asarray(res.results[c]["out_fm"])
        outs.append(np.ascontiguousarray(o.transpose(0, 2, 1)))
    return np.concatenate(outs, axis=0).astype(np.float32)
